# version 06
# speedup vs baseline: 1.2330x; 1.0258x over previous
import contextlib
import math

import numpy as np
import ml_dtypes

import concourse.bass as bass
import concourse.mybir as mybir
from concourse.bass_utils import run_bass_kernel_spmd

F32 = mybir.dt.float32
BF16 = mybir.dt.bfloat16
AF = mybir.ActivationFunctionType
ALU = mybir.AluOpType
AX = mybir.AxisListType

NCORES = 8
T = 4096
D = 1024
NH = 16
DK = 64
ATT_SCALE = 0.125
LNEPS = 1e-5

BF = ml_dtypes.bfloat16



class SemT:
    def __init__(self, handle, name):
        self.h = handle
        self.v = 0
        self.name = name


class Eng:

    def __init__(self, raw, self_sem, default_dep=False):
        self.e = raw
        self.sem = self_sem
        self.default_dep = default_dep
        self.last = None
        self.last_inc = None

    def _pre(self):
        if self.last is not None:
            if self.last_inc is None:
                self.sem.v += 1
                self.last.then_inc(self.sem.h, 1)
                self.last_inc = (self.sem.h, self.sem.v)
            h, v = self.last_inc
            self.e.wait_ge(h, v)
            self.last = None
            self.last_inc = None

    def op(self, f, *a, inc=None, dep=None, **kw):
        if dep is None:
            dep = self.default_dep
        if dep:
            self._pre()
        inst = f(*a, **kw)
        self.last = inst
        if inc is not None:
            st, n = inc
            inst.then_inc(st.h, n)
            st.v += n
            self.last_inc = (st.h, st.v)
        else:
            self.last_inc = None
        return inst

    def wait(self, st: SemT, v: int):
        self.e.wait_ge(st.h, v)

    def dma(self, dst, src, st: SemT, dep=False):
        if dep:
            self._pre()
        inst = self.e.dma_start(dst, src)
        inst.then_inc(st.h, 16)
        st.v += 16
        return inst


class K:

    def __init__(self, name):
        self.nc = bass.Bass("TRN2", target_bir_lowering=False)
        self.es = contextlib.ExitStack()
        self.cur = self.es
        self.name = name
        self.sems = {}

    @contextlib.contextmanager
    def scope(self):
        prev = self.cur
        st = contextlib.ExitStack()
        self.cur = st
        try:
            yield
        finally:
            self.cur = prev
            st.close()

    def sem(self, name) -> SemT:
        if name not in self.sems:
            h = self.es.enter_context(self.nc.semaphore(name))
            self.sems[name] = SemT(h, name)
        return self.sems[name]

    def sbuf(self, name, shape, dt):
        return self.cur.enter_context(self.nc.sbuf_tensor(name, shape, dt))

    def psum(self, name, shape, dt=F32):
        return self.cur.enter_context(self.nc.psum_tensor(name, shape, dt))

    def dram(self, name, shape, dt, kind=None):
        if kind is None:
            return self.nc.dram_tensor(name, shape, dt)
        return self.nc.dram_tensor(name, shape, dt, kind=kind)

    def eng(self, raw, which) -> Eng:
        return Eng(raw, self.sem("self_" + which),
                   default_dep=which.startswith("gpsimd"))

    def close(self):
        self.es.close()


def host_consts():
    ident_bf = np.eye(128, dtype=BF)
    ident_f32 = np.eye(128, dtype=np.float32)
    ones_f32 = np.ones((1, 128), np.float32)
    ones_col_bf = np.ones((128, 1), dtype=BF)
    masks = np.zeros((4, 128, 512), np.float32)
    s = np.arange(128)[:, None]
    q = np.arange(512)[None, :]
    for m in range(4):
        masks[m] = (q >= s + 128 * m).astype(np.float32)
    return {
        "c_ident_bf": ident_bf,
        "c_ident_f32": ident_f32,
        "c_ones_f32": ones_f32,
        "c_ones_col_bf": ones_col_bf,
        "c_masks_bf": masks.astype(BF),
    }


def declare_and_load_consts(k: K):
    ci = k.dram("c_ident_bf", [128, 128], BF16, kind="ExternalInput")
    cif = k.dram("c_ident_f32", [128, 128], F32, kind="ExternalInput")
    co = k.dram("c_ones_f32", [1, 128], F32, kind="ExternalInput")
    cob = k.dram("c_ones_col_bf", [128, 1], BF16, kind="ExternalInput")
    cm = k.dram("c_masks_bf", [4, 128, 512], BF16, kind="ExternalInput")
    k.ident_bf = k.sbuf("ident_bf_sb", [128, 128], BF16)
    k.ident_f32 = k.sbuf("ident_f32_sb", [128, 128], F32)
    k.ones_f32 = k.sbuf("ones_f32_sb", [1, 128], F32)
    k.ones_col_bf = k.sbuf("ones_col_bf_sb", [128, 1], BF16)
    k.masks_bf = k.sbuf("masks_bf_sb", [128, 4 * 512], BF16)
    ACT_CONSTS = [LNEPS, -0.5, ATT_SCALE, -1.0, 0.5]
    k.cst_sb = k.sbuf("actcst_sb", [128, len(ACT_CONSTS)], F32)
    sc = k.sem("s_cst")
    with k.nc.Block() as blk:
        @blk.sync
        def _(s_raw):
            s = k.eng(s_raw, "sync")
            s.dma(k.ident_bf[:, :], ci[:, :], sc)
            s.dma(k.ident_f32[:, :], cif[:, :], sc)
            s.dma(k.ones_f32[:, :], co[:, :], sc)
            s.dma(k.ones_col_bf[:, :], cob[:, :], sc)
            s.dma(k.masks_bf.rearrange("p (m q) -> p m q", m=4),
                  cm.rearrange("m p q -> p m q"), sc)
            s_raw.wait_ge(sc.h, sc.v)

        @blk.vector
        def _(v_raw):
            v = k.eng(v_raw, "vector")
            for i, val in enumerate(ACT_CONSTS):
                v.op(v_raw.memset, k.cst_sb[:, i:i + 1], val)
    for i, val in enumerate(ACT_CONSTS):
        k.nc.const_aps.register(F32, val, k.cst_sb[:, i:i + 1])


def bcast_rows(k: K, jobs, tag):
    nc = k.nc
    total = sum(src.shape[1] for src, dst in jobs)
    row_sb = k.sbuf(f"bcrow_{tag}", [1, total], F32)
    chunks = []
    off = 0
    loads = []
    for src, dst in jobs:
        N = src.shape[1]
        loads.append((row_sb[0:1, off:off + N], src))
        for c0 in range(0, N, 512):
            c1 = min(N, c0 + 512)
            chunks.append((row_sb[0:1, off + c0:off + c1], dst[:, c0:c1], c1 - c0))
        off += N
    ps = k.psum(f"bcps_{tag}", [128, 4 * 512], F32)
    sld, smm, sev = k.sem("s_xld"), k.sem("t_mm"), k.sem("s_ev")
    b_ld, b_mm, b_ev = sld.v, smm.v, sev.v
    with nc.Block() as blk:
        @blk.sync
        def _(s_raw):
            s = k.eng(s_raw, "sync")
            for dst, src in loads:
                s.dma(dst, src, sld)

        @blk.tensor
        def _(t):
            t.wait_ge(sld.h, b_ld + 16 * len(loads))
            for n, (src, dst, w) in enumerate(chunks):
                if n >= 4:
                    t.wait_ge(sev.h, b_ev + n - 3)
                mm = t.matmul(ps[:, (n % 4) * 512:(n % 4) * 512 + w],
                              k.ones_f32[0:1, :], src, start=True, stop=True)
                mm.then_inc(smm.h, 1)
                smm.v += 1

        @blk.scalar
        def _(sc_raw):
            sc = k.eng(sc_raw, "scalar")
            for n, (src, dst, w) in enumerate(chunks):
                sc.wait(smm, b_mm + n + 1)
                sc.op(sc_raw.copy, dst, ps[:, (n % 4) * 512:(n % 4) * 512 + w],
                      inc=(sev, 1))



def tw_sbuf(k: K, src_ap, dst_sb, R, C, tag):
    nc = k.nc
    smm, sev = k.sem("t_mm"), k.sem("s_ev")
    slds = [k.sem("g_wld0"), k.sem("g_wld1")]
    nrb, ncb = R // 128, C // 128
    wrow = k.sbuf(f"twrow_{tag}", [128, 2 * C], BF16)
    pst = k.psum(f"twps_{tag}", [128, 2 * 1024], BF16)
    b_lds = [s.v for s in slds]
    b_mm, b_ev = smm.v, sev.v

    with nc.Block() as blk:
        @blk.gpsimd
        def _(g_raw):
            g = k.eng(g_raw, "gpsimd")
            for rb in range(nrb):
                if rb >= 2:
                    g_raw.wait_ge(smm.h, b_mm + (rb - 1) * ncb)
                g.dma(wrow[:, (rb % 2) * C:(rb % 2) * C + C],
                      src_ap[rb * 128:(rb + 1) * 128, :], slds[rb % 2])

        @blk.tensor
        def _(t):
            n = 0
            for rb in range(nrb):
                t.wait_ge(slds[rb % 2].h, b_lds[rb % 2] + 16 * (rb // 2 + 1))
                for cb in range(ncb):
                    if n >= 2:
                        t.wait_ge(sev.h, b_ev + n - 1)
                    mm = t.transpose(
                        pst[:, (n % 2) * 1024:(n % 2) * 1024 + 128],
                        wrow[:, (rb % 2) * C + cb * 128:(rb % 2) * C + cb * 128 + 128],
                        k.ident_bf[:, :])
                    mm.then_inc(smm.h, 1)
                    smm.v += 1
                    n += 1

        @blk.scalar
        def _(sc_raw):
            sc = k.eng(sc_raw, "scalar")
            n = 0
            for rb in range(nrb):
                for cb in range(ncb):
                    sc.wait(smm, b_mm + n + 1)
                    sc.op(sc_raw.copy,
                          dst_sb[:, cb * R + rb * 128:cb * R + rb * 128 + 128],
                          pst[:, (n % 2) * 1024:(n % 2) * 1024 + 128],
                          inc=(sev, 1))
                    n += 1


def tx_sbuf(k: K, src_x_hbm, dst_sb, tag, Tn=T, Dn=D):
    nc = k.nc
    smm, sev, svv = k.sem("t_mm"), k.sem("s_ev"), k.sem("v_ev")
    slds = [k.sem("s_xld0"), k.sem("s_xld1")]
    ntt, ndt = Tn // 128, Dn // 128
    xrow = k.sbuf(f"txrow_{tag}", [128, 2 * Dn], F32)
    xrow_bf = k.sbuf(f"txrowb_{tag}", [128, 2 * Dn], BF16)
    pst = k.psum(f"txps_{tag}", [128, 2 * 1024], BF16)
    b_lds = [s.v for s in slds]
    b_mm, b_ev, b_vv = smm.v, sev.v, svv.v

    with nc.Block() as blk:
        @blk.sync
        def _(s_raw):
            s = k.eng(s_raw, "sync")
            for tt in range(ntt):
                if tt >= 2:
                    s_raw.wait_ge(svv.h, b_vv + tt - 1)
                s.dma(xrow[:, (tt % 2) * Dn:(tt % 2) * Dn + Dn],
                      src_x_hbm[tt * 128:(tt + 1) * 128, :], slds[tt % 2])

        @blk.vector
        def _(v_raw):
            v = k.eng(v_raw, "vector")
            for tt in range(ntt):
                v.wait(slds[tt % 2], b_lds[tt % 2] + 16 * (tt // 2 + 1))
                if tt >= 2:
                    v.wait(smm, b_mm + (tt - 1) * ndt)
                v.op(v_raw.tensor_copy,
                     xrow_bf[:, (tt % 2) * Dn:(tt % 2) * Dn + Dn],
                     xrow[:, (tt % 2) * Dn:(tt % 2) * Dn + Dn],
                     inc=(svv, 1))

        @blk.tensor
        def _(t):
            n = 0
            for tt in range(ntt):
                t.wait_ge(svv.h, b_vv + tt + 1)
                for dt in range(ndt):
                    if n >= 2:
                        t.wait_ge(sev.h, b_ev + n - 1)
                    mm = t.transpose(
                        pst[:, (n % 2) * 1024:(n % 2) * 1024 + 128],
                        xrow_bf[:, (tt % 2) * Dn + dt * 128:(tt % 2) * Dn + dt * 128 + 128],
                        k.ident_bf[:, :])
                    mm.then_inc(smm.h, 1)
                    smm.v += 1
                    n += 1

        @blk.scalar
        def _(sc_raw):
            sc = k.eng(sc_raw, "scalar")
            n = 0
            for tt in range(ntt):
                for dt in range(ndt):
                    sc.wait(smm, b_mm + n + 1)
                    sc.op(sc_raw.copy,
                          dst_sb[:, dt * Tn + tt * 128:dt * Tn + tt * 128 + 128],
                          pst[:, (n % 2) * 1024:(n % 2) * 1024 + 128],
                          inc=(sev, 1))
                    n += 1



def emit_qkv(k: K, xT_sb, lhsT_tile, rhs_tile_v, qt_hbm, kt_hbm, v_sb, tag):
    nc = k.nc
    smm, sev = k.sem("t_mm"), k.sem("s_ev")
    ssts = [k.sem(f"s_st{i}") for i in range(4)]
    svv = k.sem("v_ev")
    b_mm, b_ev, b_vv = smm.v, sev.v, svv.v
    b_sts = [s.v for s in ssts]
    pq = k.psum(f"qkvps_{tag}", [128, 4 * 512], F32)
    ev = k.sbuf(f"qkvev_{tag}", [128, 4 * 512], BF16)

    jobs = [("v", st, nb) for st in range(32) for nb in range(2)]
    jobs += [(kind, mb, nb) for kind in ("q", "k") for mb in range(8) for nb in range(8)]
    nstores_before = []
    ns = 0
    for kind, a, nb in jobs:
        nstores_before.append(ns)
        if kind != "v":
            ns += 1

    with nc.Block() as blk:
        @blk.vector
        def _(v_raw):
            v = k.eng(v_raw, "vector")
            v.op(v_raw.memset, v_sb[:, :], 1.0, inc=(svv, 1))

        @blk.tensor
        def _(t):
            t.wait_ge(svv.h, b_vv + 1)
            for n, (kind, a, nb) in enumerate(jobs):
                bank = n % 4
                if n >= 4:
                    t.wait_ge(sev.h, b_ev + n - 3)
                for kt in range(8):
                    if kind == "v":
                        mm = t.matmul(pq[:, bank * 512:bank * 512 + 512],
                                      xT_sb[:, kt * T + a * 128:kt * T + a * 128 + 128],
                                      rhs_tile_v(kt, nb),
                                      start=(kt == 0), stop=(kt == 7))
                    else:
                        mm = t.matmul(pq[:, bank * 512:bank * 512 + 512],
                                      lhsT_tile(kind, kt, a),
                                      xT_sb[:, kt * T + nb * 512:kt * T + nb * 512 + 512],
                                      start=(kt == 0), stop=(kt == 7))
                mm.then_inc(smm.h, 1)
                smm.v += 1

        @blk.scalar
        def _(sc_raw):
            sc = k.eng(sc_raw, "scalar")
            for n, (kind, a, nb) in enumerate(jobs):
                bank = n % 4
                sc.wait(smm, b_mm + n + 1)
                if kind == "v":
                    out = v_sb.rearrange("p (st h dv) -> p st h dv", st=32, dv=65)[
                        :, a, nb * 8:nb * 8 + 8, 0:64]
                    sc.op(sc_raw.copy, out, pq[:, bank * 512:bank * 512 + 512],
                          inc=(sev, 1))
                else:
                    j = nstores_before[n]
                    if j >= 4:
                        sc.wait(ssts[j % 4], b_sts[j % 4] + 16 * (j // 4))
                    sc.op(sc_raw.copy, ev[:, bank * 512:bank * 512 + 512],
                          pq[:, bank * 512:bank * 512 + 512], inc=(sev, 1))

        @blk.sync
        def _(s_raw):
            s = k.eng(s_raw, "sync")
            j = 0
            for n, (kind, a, nb) in enumerate(jobs):
                if kind == "v":
                    continue
                bank = n % 4
                s.wait(sev, b_ev + n + 1)
                dst = qt_hbm if kind == "q" else kt_hbm
                s.dma(dst[a * 128:(a + 1) * 128, nb * 512:(nb + 1) * 512],
                      ev[:, bank * 512:bank * 512 + 512], ssts[j % 4])
                j += 1
            for ss in ssts:
                s_raw.wait_ge(ss.h, ss.v)



def emit_att_bounds(k: K, qt_hbm, kt_hbm, cbias_sb, tag):
    nc = k.nc
    smm, svv, ssc = k.sem("t_mm"), k.sem("v_ev"), k.sem("s_ev")
    slds = [k.sem("s_xld0"), k.sem("s_xld1")]
    b_lds = [s.v for s in slds]
    b_mm, b_vv, b_sc = smm.v, svv.v, ssc.v
    slab = k.sbuf(f"bnd_slab_{tag}", [128, 2 * T], BF16)
    sq = k.sbuf(f"bnd_sq_{tag}", [128, 2 * T], BF16)
    nmax = k.sbuf(f"bnd_nmax_{tag}", [1, 33], F32)
    cneg = k.sbuf(f"bnd_cneg_{tag}", [1, 16], F32)
    ps_n = k.psum(f"bnd_ps_{tag}", [1, 2 * 512], F32)
    ps_c = k.psum(f"bnd_psc_{tag}", [128, 16], F32)
    n_chunk = T // 512

    with nc.Block() as blk:
        @blk.sync
        def _(s_raw):
            s = k.eng(s_raw, "sync")
            for i in range(16):
                src = qt_hbm if i < 8 else kt_hbm
                r0 = (i % 8) * 128
                if i >= 2:
                    s_raw.wait_ge(ssc.h, b_sc + (i - 1))
                s.dma(slab[:, (i % 2) * T:(i % 2) * T + T], src[r0:r0 + 128, :],
                      slds[i % 2])

        @blk.scalar
        def _(sc_raw):
            sc = k.eng(sc_raw, "scalar")
            for i in range(16):
                sc.wait(slds[i % 2], b_lds[i % 2] + 16 * (i // 2 + 1))
                if i >= 2:
                    sc.wait(smm, b_mm + (i - 1) * 2 * n_chunk)
                sc.op(sc_raw.activation, sq[:, (i % 2) * T:(i % 2) * T + T],
                      slab[:, (i % 2) * T:(i % 2) * T + T], AF.Square,
                      inc=(ssc, 1))

        @blk.tensor
        def _(t):
            n = 0
            for i in range(16):
                t.wait_ge(ssc.h, b_sc + i + 1)
                for half in range(2):
                    for ch in range(n_chunk):
                        if n >= 2:
                            t.wait_ge(svv.h, b_vv + 2 * (n - 1))
                        mm = t.matmul(
                            ps_n[:, (n % 2) * 512:(n % 2) * 512 + 512],
                            k.ones_col_bf[half * 64:half * 64 + 64, 0:1],
                            sq[half * 64:half * 64 + 64,
                               (i % 2) * T + ch * 512:(i % 2) * T + ch * 512 + 512],
                            start=True, stop=True)
                        mm.then_inc(smm.h, 1)
                        smm.v += 1
                        n += 1

        @blk.vector
        def _(v_raw):
            v = k.eng(v_raw, "vector")
            v.op(v_raw.memset, nmax[:, :], 0.0)
            n = 0
            for i in range(16):
                for half in range(2):
                    hcol = (0 if i < 8 else 16) + (i % 8) * 2 + half
                    for ch in range(n_chunk):
                        v.wait(smm, b_mm + n + 1)
                        v.op(v_raw.reduce_max, nmax[0:1, 32:33],
                             ps_n[0:1, (n % 2) * 512:(n % 2) * 512 + 512],
                             axis=AX.X)
                        v.op(v_raw.tensor_tensor, nmax[0:1, hcol:hcol + 1],
                             nmax[0:1, hcol:hcol + 1], nmax[0:1, 32:33],
                             ALU.max, inc=(svv, 2), dep=True)
                        n += 1

    with nc.Block() as blk:
        @blk.vector
        def _(v_raw):
            v = k.eng(v_raw, "vector")
            v.op(v_raw.tensor_tensor, cneg[0:1, 0:16], nmax[0:1, 0:16],
                 nmax[0:1, 16:32], ALU.mult)

    with nc.Block() as blk:
        @blk.scalar
        def _(sc_raw):
            sc = k.eng(sc_raw, "scalar")
            sc.op(sc_raw.activation, cneg[0:1, :], cneg[0:1, :], AF.Sqrt)

    with nc.Block() as blk:
        @blk.tensor
        def _(t):
            t.matmul(ps_c[:, 0:16], k.ones_f32[0:1, :], cneg[0:1, 0:16],
                     start=True, stop=True).then_inc(smm.h, 1)
            smm.v += 1

    with nc.Block() as blk:
        @blk.vector
        def _(v_raw):
            v = k.eng(v_raw, "vector")
            v.op(v_raw.tensor_scalar, cbias_sb[:, 0:16], ps_c[:, 0:16],
                 -1.02 * ATT_SCALE, None, ALU.mult)



def emit_attention(k: K, qt_hbm, kt_hbm, v_sb, cbias_sb, ctxT_hbm, tag):
    nc = k.nc
    s_qks = [k.sem("s_qkld0"), k.sem("s_qkld1")]
    b_qks = [s.v for s in s_qks]
    s_sc = k.sem("t_sc")
    s_ex = k.sem("s_exp")
    s_mk = k.sem("v_msk")
    s_av = k.sem("t_av")
    s_rc = k.sem("v_rec")
    s_rb = k.sem("t_rbc")
    s_re = k.sem("s_rev")
    s_cm = k.sem("v_cmul")
    s_sts = [k.sem(f"g_cst{i}") for i in range(4)]
    b_sts = [s.v for s in s_sts]
    b_sc, b_ex, b_mk, b_av = s_sc.v, s_ex.v, s_mk.v, s_av.v
    b_rc, b_rb, b_re, b_cm = s_rc.v, s_rb.v, s_re.v, s_cm.v

    qkt = k.sbuf(f"qkt_{tag}", [64, 4 * T], BF16)
    expp = k.sbuf(f"expp_{tag}", [128, 4 * 512], BF16)
    rrec = k.sbuf(f"rrec_{tag}", [1, 2 * 512], F32)
    rbc = k.sbuf(f"rbc_{tag}", [64, 2 * 512], F32)
    ctxo = k.sbuf(f"ctxo_{tag}", [64, 4 * 512], BF16)
    ps_sc = k.psum(f"ps_sc_{tag}", [128, 4 * 512], F32)
    ps_cx = k.psum(f"ps_cx_{tag}", [65, 2 * 512], F32)
    ps_rb = k.psum(f"ps_rb_{tag}", [64, 2 * 512], F32)

    def ntiles(qb):
        return 4 * qb + 4

    TPH = sum(ntiles(qb) for qb in range(8))
    wlist = [(h, qb) for h in range(NH) for qb in range(8)]
    cumtiles = [0]
    for (h, qb) in wlist:
        cumtiles.append(cumtiles[-1] + ntiles(qb))
    pm_of = {}
    pm = 0
    for w, (h, qb) in enumerate(wlist):
        for st in range(ntiles(qb)):
            if st >= 4 * qb:
                pm_of[(w, st)] = pm
                pm += 1
    NPART = pm

    v_aug = v_sb.rearrange("p (st h dv) -> p st h dv", st=32, dv=65)

    with nc.Block() as blk:
        @blk.sync
        def _(s_raw):
            s = k.eng(s_raw, "sync")
            for h in range(NH):
                if h >= 2:
                    s_raw.wait_ge(s_av.h, b_av + cumtiles[(h - 1) * 8])
                s.dma(qkt[:, (h % 2) * 2 * T:(h % 2) * 2 * T + T],
                      qt_hbm[h * 64:(h + 1) * 64, :], s_qks[h % 2])
                s.dma(qkt[:, (h % 2) * 2 * T + T:(h % 2) * 2 * T + 2 * T],
                      kt_hbm[h * 64:(h + 1) * 64, :], s_qks[h % 2])

        @blk.tensor
        def _(t):
            for w, (h, qb) in enumerate(wlist):
                hb = h % 2
                if qb == 0:
                    t.wait_ge(s_qks[h % 2].h, b_qks[h % 2] + 32 * (h // 2 + 1))
                g0 = cumtiles[w]
                LA = 2
                nt = ntiles(qb)
                for j in range(nt + LA):
                    if j < nt:
                        st = j
                        g = g0 + st
                        if g >= 4:
                            t.wait_ge(s_ex.h, b_ex + g - 3)
                        mm = t.matmul(
                            ps_sc[:, (g % 4) * 512:(g % 4) * 512 + 512],
                            qkt[:, hb * 2 * T + T + st * 128:hb * 2 * T + T + st * 128 + 128],
                            qkt[:, hb * 2 * T + qb * 512:hb * 2 * T + qb * 512 + 512],
                            start=True, stop=True, skip_group_check=True)
                        mm.then_inc(s_sc.h, 1)
                        s_sc.v += 1
                    if j >= LA:
                        st = j - LA
                        g = g0 + st
                        if st == 0 and w >= 2:
                            t.wait_ge(s_cm.h, b_cm + w - 1)
                        if st >= 4 * qb:
                            t.wait_ge(s_mk.h, b_mk + pm_of[(w, st)] + 1)
                        else:
                            t.wait_ge(s_ex.h, b_ex + g + 1)
                        mm = t.matmul(
                            ps_cx[:, (w % 2) * 512:(w % 2) * 512 + 512],
                            v_aug[:, st, h, :],
                            expp[:, (g % 4) * 512:(g % 4) * 512 + 512],
                            start=(st == 0), stop=(st == nt - 1),
                            skip_group_check=True)
                        mm.then_inc(s_av.h, 1)
                        s_av.v += 1
                if w >= 1:
                    t.wait_ge(s_rc.h, b_rc + w)
                    if w >= 3:
                        t.wait_ge(s_re.h, b_re + w - 2)
                    wb = (w - 1) % 2
                    mm = t.matmul(ps_rb[:, wb * 512:wb * 512 + 512],
                                  k.ones_f32[0:1, 0:64],
                                  rrec[0:1, wb * 512:wb * 512 + 512],
                                  start=True, stop=True, skip_group_check=True)
                    mm.then_inc(s_rb.h, 1)
                    s_rb.v += 1
            t.wait_ge(s_rc.h, b_rc + 128)
            t.wait_ge(s_re.h, b_re + 126)
            wb = 127 % 2
            mm = t.matmul(ps_rb[:, wb * 512:wb * 512 + 512],
                          k.ones_f32[0:1, 0:64],
                          rrec[0:1, wb * 512:wb * 512 + 512],
                          start=True, stop=True, skip_group_check=True)
            mm.then_inc(s_rb.h, 1)
            s_rb.v += 1

        @blk.scalar
        def _(sc_raw):
            sc = k.eng(sc_raw, "scalar")
            for w, (h, qb) in enumerate(wlist):
                g0 = cumtiles[w]
                for st in range(ntiles(qb)):
                    g = g0 + st
                    sc.wait(s_sc, b_sc + g + 1)
                    if g >= 4:
                        sc.wait(s_av, b_av + g - 3)
                    sc.op(sc_raw.activation,
                          expp[:, (g % 4) * 512:(g % 4) * 512 + 512],
                          ps_sc[:, (g % 4) * 512:(g % 4) * 512 + 512],
                          AF.Exp, bias=cbias_sb[:, h:h + 1], scale=ATT_SCALE,
                          inc=(s_ex, 1))
                if w >= 1:
                    sc.wait(s_rb, b_rb + w)
                    if w >= 3:
                        sc.wait(s_cm, b_cm + w - 2)
                    wb = (w - 1) % 2
                    sc.op(sc_raw.copy, rbc[:, wb * 512:wb * 512 + 512],
                          ps_rb[:, wb * 512:wb * 512 + 512], inc=(s_re, 1))
            sc.wait(s_rb, b_rb + 128)
            sc.wait(s_cm, b_cm + 126)
            wb = 127 % 2
            sc.op(sc_raw.copy, rbc[:, wb * 512:wb * 512 + 512],
                  ps_rb[:, wb * 512:wb * 512 + 512], inc=(s_re, 1))

        @blk.vector
        def _(v_raw):
            v = k.eng(v_raw, "vector")

            def do_mult(w):
                wb = w % 2
                (h, qb) = wlist[w]
                v.wait(s_re, b_re + w + 1)
                if w >= 4:
                    v.wait(s_sts[w % 4], b_sts[w % 4] + 16 * (w // 4))
                v.op(v_raw.tensor_tensor,
                     ctxo[:, (w % 4) * 512:(w % 4) * 512 + 512],
                     ps_cx[:, wb * 512:wb * 512 + 512].rearrange(
                         "p n -> p n")[0:64, :] if False else
                     ps_cx[0:64, wb * 512:wb * 512 + 512],
                     rbc[:, wb * 512:wb * 512 + 512],
                     ALU.mult, inc=(s_cm, 1))

            def do_recip(w):
                wb = w % 2
                v.wait(s_av, b_av + cumtiles[w + 1])
                v.op(v_raw.reciprocal, rrec[0:1, wb * 512:wb * 512 + 512],
                     ps_cx[64:65, wb * 512:wb * 512 + 512], inc=(s_rc, 1))

            for w, (h, qb) in enumerate(wlist):
                if w >= 2:
                    do_mult(w - 2)
                if w >= 1:
                    do_recip(w - 1)
                g0 = cumtiles[w]
                for st in range(4 * qb, ntiles(qb)):
                    g = g0 + st
                    m = st - 4 * qb
                    v.wait(s_ex, b_ex + g + 1)
                    v.op(v_raw.tensor_tensor,
                         expp[:, (g % 4) * 512:(g % 4) * 512 + 512],
                         expp[:, (g % 4) * 512:(g % 4) * 512 + 512],
                         k.masks_bf[:, m * 512:m * 512 + 512],
                         ALU.mult, inc=(s_mk, 1))
            do_mult(126)
            do_recip(127)
            do_mult(127)

        @blk.gpsimd
        def _(g_raw):
            g = k.eng(g_raw, "gpsimd")
            for w, (h, qb) in enumerate(wlist):
                g.wait(s_cm, b_cm + w + 1)
                g.dma(ctxT_hbm[h * 64:(h + 1) * 64, qb * 512:(qb + 1) * 512],
                      ctxo[:, (w % 4) * 512:(w % 4) * 512 + 512], s_sts[w % 4])
            for ss in s_sts:
                g_raw.wait_ge(ss.h, ss.v)




def emit_proj_residual_ln(k: K, ctxT_hbm, rhs_tile, out_hbm, tag,
                          resid_hbm=None, resid2_hbm=None, bias_bc=None,
                          ln_scale_bc=None, ln_shift_bc=None,
                          ctx_rows=1024, do_ln=True):
    nc = k.nc
    s_cl = k.sem("s_ctxld")
    s_xls = [k.sem("s_xld2a"), k.sem("s_xld2b")]
    smm, sacc, sout = k.sem("t_mm"), k.sem("v_acc"), k.sem("v_out")
    ssts = [k.sem("g_st0"), k.sem("g_st1")]
    s_lnv, s_lns = k.sem("ln_v"), k.sem("ln_s")
    nkt = ctx_rows // 128
    ctx_sb = k.sbuf(f"proj_ctx_{tag}", [128, nkt * T], BF16)

    with nc.Block() as blk:
        @blk.sync
        def _(s_raw):
            s = k.eng(s_raw, "sync")
            for r in range(nkt):
                s.dma(ctx_sb[:, r * T:(r + 1) * T],
                      ctxT_hbm[r * 128:(r + 1) * 128, :], s_cl)
            s_raw.wait_ge(s_cl.h, s_cl.v)

    b_mm, b_acc, b_out = smm.v, sacc.v, sout.v
    b_sts = [s.v for s in ssts]
    b_xls = [s.v for s in s_xls]
    b_lnv, b_lns = s_lnv.v, s_lns.v
    ps = k.psum(f"proj_ps_{tag}", [128, 4 * 512], F32)
    vt = k.sbuf(f"proj_v_{tag}", [128, 2 * 1024], F32)
    res = k.sbuf(f"proj_res_{tag}", [128, 2 * 1024], F32)
    res2 = (k.sbuf(f"proj_res2_{tag}", [128, 2 * 1024], F32)
            if resid2_hbm is not None else None)
    sqs = k.sbuf(f"proj_sq_{tag}", [128, 1024], F32)
    stats = k.sbuf(f"proj_stats_{tag}", [128, 8], F32)
    outt = k.sbuf(f"proj_out_{tag}", [128, 2 * 1024], F32)
    nres_dma = (1 if resid_hbm is not None else 0) + (1 if resid2_hbm is not None else 0)

    with nc.Block() as blk:
        @blk.sync
        def _(s_raw):
            s = k.eng(s_raw, "sync")
            for qt in range(32):
                if qt >= 2 and nres_dma:
                    s_raw.wait_ge(sacc.h, b_acc + qt - 1)
                if resid_hbm is not None:
                    s.dma(res[:, (qt % 2) * 1024:(qt % 2) * 1024 + 1024],
                          resid_hbm[qt * 128:(qt + 1) * 128, :], s_xls[qt % 2])
                if resid2_hbm is not None:
                    s.dma(res2[:, (qt % 2) * 1024:(qt % 2) * 1024 + 1024],
                          resid2_hbm[qt * 128:(qt + 1) * 128, :], s_xls[qt % 2])

        @blk.tensor
        def _(t):
            for qt in range(32):
                for nb in range(2):
                    n = qt * 2 + nb
                    if n >= 4:
                        t.wait_ge(sacc.h, b_acc + qt - 1)
                    for kt in range(nkt):
                        mm = t.matmul(ps[:, (n % 4) * 512:(n % 4) * 512 + 512],
                                      ctx_sb[:, kt * T + qt * 128:kt * T + qt * 128 + 128],
                                      rhs_tile(kt, nb),
                                      start=(kt == 0), stop=(kt == nkt - 1))
                    mm.then_inc(smm.h, 1)
                    smm.v += 1

        @blk.vector
        def _(v_raw):
            v = k.eng(v_raw, "vector")
            for qt in range(32):
                v.wait(smm, b_mm + 2 * qt + 2)
                if nres_dma:
                    v.wait(s_xls[qt % 2],
                           b_xls[qt % 2] + 16 * nres_dma * (qt // 2 + 1))
                vt_ap = vt[:, (qt % 2) * 1024:(qt % 2) * 1024 + 1024]
                pb = (2 * qt) % 4
                ps_pair = ps[:, pb * 512:pb * 512 + 1024]
                accs = []
                if resid_hbm is not None:
                    accs.append(("tt", res[:, (qt % 2) * 1024:(qt % 2) * 1024 + 1024]))
                if resid2_hbm is not None:
                    accs.append(("tt", res2[:, (qt % 2) * 1024:(qt % 2) * 1024 + 1024]))
                if bias_bc is not None:
                    accs.append(("tt", bias_bc))
                if not accs:
                    v.op(v_raw.tensor_copy, vt_ap, ps_pair, inc=(sacc, 1))
                else:
                    first = accs[0][1]
                    last = (len(accs) == 1)
                    v.op(v_raw.tensor_tensor, vt_ap, ps_pair, first, ALU.add,
                         inc=(sacc, 1) if last else None)
                    for i, (_, other) in enumerate(accs[1:]):
                        last = (i == len(accs) - 2)
                        v.op(v_raw.tensor_tensor, vt_ap, vt_ap, other, ALU.add,
                             inc=(sacc, 1) if last else None)
                out_ap = outt[:, (qt % 2) * 1024:(qt % 2) * 1024 + 1024]
                if qt >= 2:
                    v.wait(ssts[qt % 2], b_sts[qt % 2] + 16 * (qt // 2))
                if do_ln:
                    v.op(v_raw.reduce_sum, stats[:, 1:2], vt_ap, axis=AX.X,
                         inc=(s_lnv, 1))
                    v.wait(s_lns, b_lns + 2 * qt + 1)
                    v.op(v_raw.tensor_scalar, stats[:, 2:3], stats[:, 1:2],
                         1.0 / D, None, ALU.mult, dep=True)
                    v.op(v_raw.tensor_scalar, stats[:, 3:4], stats[:, 2:3],
                         stats[:, 2:3], None, ALU.mult, dep=True)
                    v.op(v_raw.tensor_scalar, stats[:, 4:5], stats[:, 0:1],
                         1.0 / D, None, ALU.mult)
                    v.op(v_raw.tensor_tensor, stats[:, 4:5], stats[:, 4:5],
                         stats[:, 3:4], ALU.subtract, inc=(s_lnv, 1), dep=True)
                    v.wait(s_lns, b_lns + 2 * qt + 2)
                    if ln_scale_bc is not None:
                        v.op(v_raw.tensor_scalar, out_ap, vt_ap, stats[:, 2:3],
                             stats[:, 5:6], ALU.subtract, ALU.mult, dep=True)
                        v.op(v_raw.tensor_tensor, out_ap, out_ap, ln_scale_bc,
                             ALU.mult)
                        v.op(v_raw.tensor_tensor, out_ap, out_ap, ln_shift_bc,
                             ALU.add, inc=(sout, 1))
                    else:
                        v.op(v_raw.tensor_scalar, out_ap, vt_ap, stats[:, 2:3],
                             stats[:, 5:6], ALU.subtract, ALU.mult,
                             inc=(sout, 1), dep=True)
                else:
                    v.op(v_raw.tensor_copy, out_ap, vt_ap, inc=(sout, 1))

        @blk.scalar
        def _(sc_raw):
            sc = k.eng(sc_raw, "scalar")
            if not do_ln:
                return
            for qt in range(32):
                sc.wait(s_lnv, b_lnv + 2 * qt + 1)
                sc.op(sc_raw.activation, sqs[:, :],
                      vt[:, (qt % 2) * 1024:(qt % 2) * 1024 + 1024],
                      AF.Square, accum_out=stats[:, 0:1], inc=(s_lns, 1))
                sc.wait(s_lnv, b_lnv + 2 * qt + 2)
                sc.op(sc_raw.activation, stats[:, 5:6], stats[:, 4:5], AF.Ln,
                      bias=LNEPS)
                sc.op(sc_raw.activation, stats[:, 5:6], stats[:, 5:6], AF.Exp,
                      scale=-0.5, inc=(s_lns, 1))

        @blk.gpsimd
        def _(g_raw):
            g = k.eng(g_raw, "gpsimd")
            for qt in range(32):
                g.wait(sout, b_out + qt + 1)
                g.dma(out_hbm[qt * 128:(qt + 1) * 128, :],
                      outt[:, (qt % 2) * 1024:(qt % 2) * 1024 + 1024],
                      ssts[qt % 2])
            for ss in ssts:
                g_raw.wait_ge(ss.h, ss.v)




def body_l0attn(k, x_in, wq_in, wk_in, wv_in, wo_in, x1_out, pfx="a0"):
    nc = k.nc
    qt_hbm = k.dram(f"{pfx}_scr_qt", [D, T], BF16)
    kt_hbm = k.dram(f"{pfx}_scr_kt", [D, T], BF16)
    ctxT_hbm = k.dram(f"{pfx}_scr_ctxT", [D, T], BF16)

    with contextlib.ExitStack() as ph:
        wqT = ph.enter_context(nc.sbuf_tensor("wqT", [128, 8 * D], BF16))
        wkT = ph.enter_context(nc.sbuf_tensor("wkT", [128, 8 * D], BF16))
        wvT = ph.enter_context(nc.sbuf_tensor("wvT", [128, 8 * D], BF16))
        with k.scope():
            tw_sbuf(k, wq_in[:, :], wqT, D, D, "wq")
        with k.scope():
            tw_sbuf(k, wk_in[:, :], wkT, D, D, "wk")
        with k.scope():
            tw_sbuf(k, wv_in[:, :], wvT, D, D, "wv")

        v_sb = ph.enter_context(nc.sbuf_tensor("vaug0", [128, 32 * 1040], BF16))
        cbias = ph.enter_context(nc.sbuf_tensor("cbias0", [128, 16], F32))

        with contextlib.ExitStack() as phx:
            xT = phx.enter_context(nc.sbuf_tensor("xT0", [128, 8 * T], BF16))
            with k.scope():
                tx_sbuf(k, x_in[:, :], xT, "x0")

            def lhsT_qk(kind, kt, mb):
                w = wqT if kind == "q" else wkT
                return w[:, kt * D + mb * 128:kt * D + mb * 128 + 128]

            def rhs_v(kt, nb):
                return wvT[:, kt * D + nb * 512:kt * D + nb * 512 + 512]

            with k.scope():
                emit_qkv(k, xT, lhsT_qk, rhs_v, qt_hbm, kt_hbm, v_sb, "a0")

        with k.scope():
            emit_att_bounds(k, qt_hbm, kt_hbm, cbias, "a0")
        with k.scope():
            emit_attention(k, qt_hbm, kt_hbm, v_sb, cbias, ctxT_hbm, "a0")

    with contextlib.ExitStack() as ph3:
        woT = ph3.enter_context(nc.sbuf_tensor("woT", [128, 8 * D], BF16))
        with k.scope():
            tw_sbuf(k, wo_in[:, :], woT, D, D, "wo")

        def rhs_o(kt, nb):
            return woT[:, kt * D + nb * 512:kt * D + nb * 512 + 512]

        with k.scope():
            emit_proj_residual_ln(k, ctxT_hbm, rhs_o, x1_out, "a0",
                                  resid_hbm=x_in[:, :])


def build_l0attn():
    k = K("l0attn")
    x_in = k.dram("x_in", [T, D], F32, kind="ExternalInput")
    wq_in = k.dram("wq_in", [D, D], F32, kind="ExternalInput")
    wk_in = k.dram("wk_in", [D, D], F32, kind="ExternalInput")
    wv_in = k.dram("wv_in", [D, D], F32, kind="ExternalInput")
    wo_in = k.dram("wo_in", [D, D], F32, kind="ExternalInput")
    x1_out = k.dram("x1_out", [T, D], F32, kind="ExternalOutput")
    declare_and_load_consts(k)
    body_l0attn(k, x_in, wq_in, wk_in, wv_in, wo_in, x1_out)
    return k


def run_l0attn(x_full, q_proj, k_proj, v_proj, o_proj):
    k = build_l0attn()
    consts = host_consts()
    in_maps = []
    for c in range(NCORES):
        m = dict(consts)
        m["x_in"] = np.ascontiguousarray(x_full[c])
        m["wq_in"] = q_proj
        m["wk_in"] = k_proj
        m["wv_in"] = v_proj
        m["wo_in"] = o_proj
        in_maps.append(m)
    res = run_bass_kernel_spmd(k.nc, in_maps, core_ids=list(range(NCORES)))
    k.close()
    return np.stack([res.results[c]["x1_out"] for c in range(NCORES)])


def _f32(a):
    return np.ascontiguousarray(np.asarray(a, dtype=np.float32))




def tw_hbm(k: K, src_ap, dst_hbm, R, C, tag):
    nc = k.nc
    smm, sev = k.sem("t_mm"), k.sem("s_ev")
    slds = [k.sem("g_wld0"), k.sem("g_wld1")]
    ssts = [k.sem("s_st0"), k.sem("s_st1")]
    nrb, ncb = R // 128, C // 128
    wrow = k.sbuf(f"throw_{tag}", [128, 2 * C], BF16)
    stage = k.sbuf(f"thstg_{tag}", [128, 2 * C], BF16)
    pst = k.psum(f"thps_{tag}", [128, 2 * 1024], BF16)
    b_lds = [s.v for s in slds]
    b_sts = [s.v for s in ssts]
    b_mm, b_ev = smm.v, sev.v

    with nc.Block() as blk:
        @blk.gpsimd
        def _(g_raw):
            g = k.eng(g_raw, "gpsimd")
            for rb in range(nrb):
                if rb >= 2:
                    g_raw.wait_ge(smm.h, b_mm + (rb - 1) * ncb)
                half = C // 2 if C > 2048 else C
                for c0 in range(0, C, half):
                    g.dma(wrow[:, (rb % 2) * C + c0:(rb % 2) * C + c0 + half],
                          src_ap[rb * 128:(rb + 1) * 128, c0:c0 + half],
                          slds[rb % 2])

        nld_per_rb = 2 if C > 2048 else 1

        @blk.tensor
        def _(t):
            n = 0
            for rb in range(nrb):
                t.wait_ge(slds[rb % 2].h,
                          b_lds[rb % 2] + 16 * nld_per_rb * (rb // 2 + 1))
                for cb in range(ncb):
                    if n >= 2:
                        t.wait_ge(sev.h, b_ev + n - 1)
                    mm = t.transpose(
                        pst[:, (n % 2) * 1024:(n % 2) * 1024 + 128],
                        wrow[:, (rb % 2) * C + cb * 128:(rb % 2) * C + cb * 128 + 128],
                        k.ident_bf[:, :])
                    mm.then_inc(smm.h, 1)
                    smm.v += 1
                    n += 1

        @blk.scalar
        def _(sc_raw):
            sc = k.eng(sc_raw, "scalar")
            n = 0
            for rb in range(nrb):
                if rb >= 2:
                    sc.wait(ssts[rb % 2], b_sts[rb % 2] + 16 * (rb // 2))
                for cb in range(ncb):
                    sc.wait(smm, b_mm + n + 1)
                    sc.op(sc_raw.copy,
                          stage[:, (rb % 2) * C + cb * 128:(rb % 2) * C + cb * 128 + 128],
                          pst[:, (n % 2) * 1024:(n % 2) * 1024 + 128],
                          inc=(sev, 1))
                    n += 1

        @blk.sync
        def _(s_raw):
            s = k.eng(s_raw, "sync")
            for rb in range(nrb):
                s.wait(sev, b_ev + (rb + 1) * ncb)
                s.dma(dst_hbm[rb], stage[:, (rb % 2) * C:(rb % 2) * C + C],
                      ssts[rb % 2])
            for ss in ssts:
                s_raw.wait_ge(ss.h, ss.v)


def scale_cols(k: K, src_row_dram, n, dst_sb, tag):
    nc = k.nc
    nj = n // 128
    tmp = k.sbuf(f"sc_tmp_{tag}", [128, 128], F32)
    pst = k.psum(f"sc_ps_{tag}", [128, 128], F32)
    sld = k.sem("s_xld")
    b_ld = sld.v
    with nc.Block() as blk:
        @blk.sync
        def _(s_raw):
            s = k.eng(s_raw, "sync")
            s.dma(tmp[0:nj, :], src_row_dram.rearrange("(j p) -> j p", p=128), sld)
            s_raw.wait_ge(sld.h, sld.v)
    with nc.Block() as blk:
        @blk.tensor
        def _(t):
            t.transpose(pst[:, :], tmp[:, :], k.ident_f32[:, :])
    with nc.Block() as blk:
        @blk.scalar
        def _(sc_raw):
            sc = k.eng(sc_raw, "scalar")
            sc.op(sc_raw.copy, dst_sb[:, 0:nj], pst[:, 0:nj])


def emit_gw12(k: K, x3T_sb, w1t_hbm, w2t_hbm, s1col, s2col, at_hbm, tag):
    nc = k.nc
    smm, sev, svv = k.sem("t_mm"), k.sem("s_ev"), k.sem("v_ev")
    slds = [k.sem("s_xld0"), k.sem("s_xld1")]
    ssts = [k.sem(f"g_st{i}") for i in range(4)]
    b_lds = [s.v for s in slds]
    b_sts = [s.v for s in ssts]
    b_mm, b_ev, b_vv = smm.v, sev.v, svv.v
    slab = k.sbuf(f"gw_slab_{tag}", [128, 4 * 1024], BF16)
    tsil = k.sbuf(f"gw_tsil_{tag}", [128, 2 * 512], F32)
    aev = k.sbuf(f"gw_aev_{tag}", [128, 4 * 512], BF16)
    ps = k.psum(f"gw_ps_{tag}", [128, 4 * 512], F32)

    with nc.Block() as blk:
        @blk.sync
        def _(s_raw):
            s = k.eng(s_raw, "sync")
            for mb in range(32):
                if mb >= 2:
                    s_raw.wait_ge(smm.h, b_mm + 8 * (mb - 1))
                s.dma(slab[:, (mb % 2) * 2048:(mb % 2) * 2048 + 1024],
                      w1t_hbm[mb], slds[mb % 2])
                s.dma(slab[:, (mb % 2) * 2048 + 1024:(mb % 2) * 2048 + 2048],
                      w2t_hbm[mb], slds[mb % 2])

        @blk.tensor
        def _(t):
            n = 0
            for mb in range(32):
                t.wait_ge(slds[mb % 2].h, b_lds[mb % 2] + 32 * (mb // 2 + 1))
                for tb in range(8):
                    if n >= 2:
                        t.wait_ge(svv.h, b_vv + n - 1)
                    for kt in range(8):
                        mm = t.matmul(
                            ps[:, (2 * n % 4) * 512:(2 * n % 4) * 512 + 512],
                            slab[:, (mb % 2) * 2048 + kt * 128:(mb % 2) * 2048 + kt * 128 + 128],
                            x3T_sb[:, kt * T + tb * 512:kt * T + tb * 512 + 512],
                            start=(kt == 0), stop=(kt == 7))
                    for kt in range(8):
                        mm = t.matmul(
                            ps[:, ((2 * n + 1) % 4) * 512:((2 * n + 1) % 4) * 512 + 512],
                            slab[:, (mb % 2) * 2048 + 1024 + kt * 128:(mb % 2) * 2048 + 1024 + kt * 128 + 128],
                            x3T_sb[:, kt * T + tb * 512:kt * T + tb * 512 + 512],
                            start=(kt == 0), stop=(kt == 7))
                    mm.then_inc(smm.h, 1)
                    smm.v += 1
                    n += 1

        @blk.scalar
        def _(sc_raw):
            sc = k.eng(sc_raw, "scalar")
            n = 0
            for mb in range(32):
                for tb in range(8):
                    sc.wait(smm, b_mm + n + 1)
                    if n >= 2:
                        sc.wait(svv, b_vv + n - 1)
                    sc.op(sc_raw.activation, tsil[:, (n % 2) * 512:(n % 2) * 512 + 512],
                          ps[:, (2 * n % 4) * 512:(2 * n % 4) * 512 + 512],
                          AF.Silu, scale=s1col[:, mb:mb + 1], inc=(sev, 1))
                    n += 1

        @blk.vector
        def _(v_raw):
            v = k.eng(v_raw, "vector")
            n = 0
            for mb in range(32):
                for tb in range(8):
                    v.wait(sev, b_ev + n + 1)
                    if n >= 4:
                        v.wait(ssts[n % 4], b_sts[n % 4] + 16 * (n // 4))
                    v.op(v_raw.scalar_tensor_tensor,
                         aev[:, (n % 4) * 512:(n % 4) * 512 + 512],
                         ps[:, ((2 * n + 1) % 4) * 512:((2 * n + 1) % 4) * 512 + 512],
                         s2col[:, mb:mb + 1],
                         tsil[:, (n % 2) * 512:(n % 2) * 512 + 512],
                         ALU.mult, ALU.mult, inc=(svv, 1))
                    n += 1

        @blk.gpsimd
        def _(g_raw):
            g = k.eng(g_raw, "gpsimd")
            n = 0
            for mb in range(32):
                for tb in range(8):
                    g.wait(svv, b_vv + n + 1)
                    g.dma(at_hbm[mb][:, tb * 512:tb * 512 + 512],
                          aev[:, (n % 4) * 512:(n % 4) * 512 + 512], ssts[n % 4])
                    n += 1
            for ss in ssts:
                g_raw.wait_ge(ss.h, ss.v)


def emit_gh_resident(k: K, rT_sb, ht_hbm, scol, out_hbm, tag):
    nc = k.nc
    smm, sev = k.sem("t_mm"), k.sem("s_ev")
    slds = [k.sem("g_wld0"), k.sem("g_wld1")]
    ssts = [k.sem(f"s_st{i}") for i in range(4)]
    b_lds = [s.v for s in slds]
    b_sts = [s.v for s in ssts]
    b_mm, b_ev = smm.v, sev.v
    slab = k.sbuf(f"gh_slab_{tag}", [128, 2 * 4096], BF16)
    ev = k.sbuf(f"gh_ev_{tag}", [128, 4 * 512], BF16)
    ps = k.psum(f"gh_ps_{tag}", [128, 4 * 512], F32)

    with nc.Block() as blk:
        @blk.gpsimd
        def _(g_raw):
            g = k.eng(g_raw, "gpsimd")
            for ot in range(32):
                if ot >= 2:
                    g_raw.wait_ge(smm.h, b_mm + 8 * (ot - 1))
                g.dma(slab[:, (ot % 2) * 4096:(ot % 2) * 4096 + 4096],
                      ht_hbm[ot], slds[ot % 2])

        @blk.tensor
        def _(t):
            n = 0
            for ot in range(32):
                t.wait_ge(slds[ot % 2].h, b_lds[ot % 2] + 16 * (ot // 2 + 1))
                for tb in range(8):
                    if n >= 4:
                        t.wait_ge(sev.h, b_ev + n - 3)
                    for kt in range(32):
                        mm = t.matmul(
                            ps[:, (n % 4) * 512:(n % 4) * 512 + 512],
                            slab[:, (ot % 2) * 4096 + kt * 128:(ot % 2) * 4096 + kt * 128 + 128],
                            rT_sb[:, kt * T + tb * 512:kt * T + tb * 512 + 512],
                            start=(kt == 0), stop=(kt == 31))
                    mm.then_inc(smm.h, 1)
                    smm.v += 1
                    n += 1

        @blk.scalar
        def _(sc_raw):
            sc = k.eng(sc_raw, "scalar")
            n = 0
            for ot in range(32):
                for tb in range(8):
                    sc.wait(smm, b_mm + n + 1)
                    if n >= 4:
                        sc.wait(ssts[n % 4], b_sts[n % 4] + 16 * (n // 4))
                    sc.op(sc_raw.activation, ev[:, (n % 4) * 512:(n % 4) * 512 + 512],
                          ps[:, (n % 4) * 512:(n % 4) * 512 + 512],
                          AF.Gelu, scale=scol[:, ot:ot + 1], inc=(sev, 1))
                    n += 1

        @blk.sync
        def _(s_raw):
            s = k.eng(s_raw, "sync")
            n = 0
            for ot in range(32):
                for tb in range(8):
                    s.wait(sev, b_ev + n + 1)
                    s.dma(out_hbm[ot][:, tb * 512:(tb + 1) * 512],
                          ev[:, (n % 4) * 512:(n % 4) * 512 + 512], ssts[n % 4])
                    n += 1
            for ss in ssts:
                s_raw.wait_ge(ss.h, ss.v)


def emit_gh_blocked(k: K, rT_hbm, ht_hbm, scol, out_hbm, tag):
    nc = k.nc
    s_rl = k.sem("s_rhld")
    smm, sev = k.sem("t_mm"), k.sem("s_ev")
    slds = [k.sem("g_wld0"), k.sem("g_wld1")]
    ssts = [k.sem(f"s_st{i}") for i in range(4)]
    b_lds = [s.v for s in slds]
    b_sts = [s.v for s in ssts]
    b_rl, b_mm, b_ev = s_rl.v, smm.v, sev.v
    rblk = k.sbuf(f"ghb_r_{tag}", [128, 32 * 1024], BF16)
    slab = k.sbuf(f"ghb_slab_{tag}", [128, 2 * 4096], BF16)
    ev = k.sbuf(f"ghb_ev_{tag}", [128, 4 * 512], BF16)
    ps = k.psum(f"ghb_ps_{tag}", [128, 4 * 512], F32)

    with nc.Block() as blk:
        @blk.sync
        def _(s_raw):
            s = k.eng(s_raw, "sync")
            for tbb in range(4):
                if tbb >= 1:
                    s_raw.wait_ge(smm.h, b_mm + 64 * tbb)
                for kt in range(32):
                    s.dma(rblk[:, kt * 1024:kt * 1024 + 1024],
                          rT_hbm[kt][:, tbb * 1024:tbb * 1024 + 1024], s_rl)

        @blk.gpsimd
        def _(g_raw):
            g = k.eng(g_raw, "gpsimd")
            for tbb in range(4):
                for ot in range(32):
                    gi = tbb * 32 + ot
                    if gi >= 2:
                        g_raw.wait_ge(smm.h, b_mm + 2 * (gi - 1))
                    g.dma(slab[:, (gi % 2) * 4096:(gi % 2) * 4096 + 4096],
                          ht_hbm[ot], slds[gi % 2])

        @blk.tensor
        def _(t):
            n = 0
            for tbb in range(4):
                t.wait_ge(s_rl.h, b_rl + 16 * 32 * (tbb + 1))
                for ot in range(32):
                    gi = tbb * 32 + ot
                    t.wait_ge(slds[gi % 2].h, b_lds[gi % 2] + 16 * (gi // 2 + 1))
                    for tb in range(2):
                        if n >= 4:
                            t.wait_ge(sev.h, b_ev + n - 3)
                        for kt in range(32):
                            mm = t.matmul(
                                ps[:, (n % 4) * 512:(n % 4) * 512 + 512],
                                slab[:, (gi % 2) * 4096 + kt * 128:(gi % 2) * 4096 + kt * 128 + 128],
                                rblk[:, kt * 1024 + tb * 512:kt * 1024 + tb * 512 + 512],
                                start=(kt == 0), stop=(kt == 31))
                        mm.then_inc(smm.h, 1)
                        smm.v += 1
                        n += 1

        @blk.scalar
        def _(sc_raw):
            sc = k.eng(sc_raw, "scalar")
            n = 0
            for tbb in range(4):
                for ot in range(32):
                    for tb in range(2):
                        sc.wait(smm, b_mm + n + 1)
                        if n >= 4:
                            sc.wait(ssts[n % 4], b_sts[n % 4] + 16 * (n // 4))
                        sc.op(sc_raw.activation,
                              ev[:, (n % 4) * 512:(n % 4) * 512 + 512],
                              ps[:, (n % 4) * 512:(n % 4) * 512 + 512],
                              AF.Gelu, scale=scol[:, ot:ot + 1], inc=(sev, 1))
                        sc.dma(out_hbm[ot][:, tbb * 1024 + tb * 512:tbb * 1024 + tb * 512 + 512],
                               ev[:, (n % 4) * 512:(n % 4) * 512 + 512],
                               ssts[n % 4], dep=True)
                        n += 1
            for ss in ssts:
                sc_raw.wait_ge(ss.h, ss.v)


def emit_w3_ln(k: K, ct_hbm, w3t_sb, s3col, x3_hbm, y_hbm, tag):
    nc = k.nc
    s_ct = k.sem("s_ctld")
    s_x3s = [k.sem("s_xld2a"), k.sem("s_xld2b")]
    smm, str_, se1, se2 = k.sem("t_mm"), k.sem("t_tr"), k.sem("s_e1"), k.sem("s_e2")
    s_lnv, s_lns = k.sem("ln_v"), k.sem("ln_s")
    sout = k.sem("v_out")
    ssts = [k.sem("g_st0"), k.sem("g_st1")]
    b_x3s = [s.v for s in s_x3s]
    b_sts = [s.v for s in ssts]
    b_ct, b_mm, b_tr = s_ct.v, smm.v, str_.v
    b_e1, b_e2, b_lnv, b_lns = se1.v, se2.v, s_lnv.v, s_lns.v
    b_out = sout.v

    ctblk = k.sbuf(f"w3_ct_{tag}", [128, 32 * 512], BF16)
    o4t = k.sbuf(f"w3_o4t_{tag}", [128, 2 * 512], F32)
    obuf = k.sbuf(f"w3_ob_{tag}", [128, 2 * 4096], F32)
    res = k.sbuf(f"w3_res_{tag}", [128, 2 * 1024], F32)
    outt = k.sbuf(f"w3_out_{tag}", [128, 2 * 1024], F32)
    sqs = k.sbuf(f"w3_sq_{tag}", [128, 1024], F32)
    stats = k.sbuf(f"w3_stats_{tag}", [128, 8], F32)
    ps1 = k.psum(f"w3_ps1_{tag}", [128, 2 * 512], F32)
    ps2 = k.psum(f"w3_ps2_{tag}", [128, 2 * 512], F32)

    with nc.Block() as blk:
        @blk.sync
        def _(s_raw):
            s = k.eng(s_raw, "sync")
            for tb in range(8):
                if tb >= 1:
                    s_raw.wait_ge(smm.h, b_mm + 8 * tb)
                for ht in range(32):
                    s.dma(ctblk[:, ht * 512:ht * 512 + 512],
                          ct_hbm[ht][:, tb * 512:tb * 512 + 512], s_ct)
                for c in range(4):
                    w = tb * 4 + c
                    if w >= 2:
                        s_raw.wait_ge(s_lnv.h, b_lnv + 2 * (w - 2) + 1)
                    s.dma(res[:, (w % 2) * 1024:(w % 2) * 1024 + 1024],
                          x3_hbm[w * 128:(w + 1) * 128, :], s_x3s[w % 2])

        @blk.tensor
        def _(t):
            for tb in range(8):
                t.wait_ge(s_ct.h, b_ct + 16 * 32 * (tb + 1))
                for db in range(8):
                    g = tb * 8 + db
                    if g >= 2:
                        t.wait_ge(se1.h, b_e1 + g - 1)
                    for ht in range(32):
                        mm = t.matmul(
                            ps1[:, (g % 2) * 512:(g % 2) * 512 + 512],
                            w3t_sb[:, db * 4096 + ht * 128:db * 4096 + ht * 128 + 128],
                            ctblk[:, ht * 512:ht * 512 + 512],
                            start=(ht == 0), stop=(ht == 31))
                    mm.then_inc(smm.h, 1)
                    smm.v += 1
                    t.wait_ge(se1.h, b_e1 + g + 1)
                    for c in range(4):
                        trg = g * 4 + c
                        if trg >= 2:
                            t.wait_ge(se2.h, b_e2 + trg - 1)
                        mm = t.transpose(
                            ps2[:, (trg % 2) * 512:(trg % 2) * 512 + 128],
                            o4t[:, (g % 2) * 512 + c * 128:(g % 2) * 512 + c * 128 + 128],
                            k.ident_f32[:, :])
                        mm.then_inc(str_.h, 1)
                        str_.v += 1

        @blk.scalar
        def _(sc_raw):
            sc = k.eng(sc_raw, "scalar")
            for tb in range(8):
                for db in range(8):
                    g = tb * 8 + db
                    sc.wait(smm, b_mm + g + 1)
                    if g >= 2:
                        sc.wait(str_, b_tr + 4 * (g - 1))
                    sc.op(sc_raw.activation, o4t[:, (g % 2) * 512:(g % 2) * 512 + 512],
                          ps1[:, (g % 2) * 512:(g % 2) * 512 + 512],
                          AF.Identity, scale=s3col[:, db:db + 1], inc=(se1, 1))
                    if tb >= 2 and db == 0:
                        sc.wait(sout, b_out + 4 * (tb - 1))
                    for c in range(4):
                        trg = g * 4 + c
                        sc.wait(str_, b_tr + trg + 1)
                        sc.op(sc_raw.copy,
                              obuf[:, (tb % 2) * 4096 + c * 1024 + db * 128:
                                   (tb % 2) * 4096 + c * 1024 + db * 128 + 128],
                              ps2[:, (trg % 2) * 512:(trg % 2) * 512 + 128],
                              inc=(se2, 1))
                for c in range(4):
                    w = tb * 4 + c
                    sc.wait(s_lnv, b_lnv + 2 * w + 1)
                    sc.op(sc_raw.activation, sqs[:, :],
                          obuf[:, (tb % 2) * 4096 + c * 1024:(tb % 2) * 4096 + c * 1024 + 1024],
                          AF.Square, accum_out=stats[:, 0:1], inc=(s_lns, 1))
                    sc.wait(s_lnv, b_lnv + 2 * w + 2)
                    sc.op(sc_raw.activation, stats[:, 5:6], stats[:, 4:5], AF.Ln,
                          bias=LNEPS)
                    sc.op(sc_raw.activation, stats[:, 5:6], stats[:, 5:6], AF.Exp,
                          scale=-0.5, inc=(s_lns, 1))

        @blk.vector
        def _(v_raw):
            v = k.eng(v_raw, "vector")
            for tb in range(8):
                v.wait(se2, b_e2 + 32 * (tb + 1))
                for c in range(4):
                    w = tb * 4 + c
                    ob_ap = obuf[:, (tb % 2) * 4096 + c * 1024:(tb % 2) * 4096 + c * 1024 + 1024]
                    v.wait(s_x3s[w % 2], b_x3s[w % 2] + 16 * (w // 2 + 1))
                    v.op(v_raw.tensor_tensor, ob_ap, ob_ap,
                         res[:, (w % 2) * 1024:(w % 2) * 1024 + 1024], ALU.add)
                    v.op(v_raw.reduce_sum, stats[:, 1:2], ob_ap, axis=AX.X,
                         inc=(s_lnv, 1))
                    v.wait(s_lns, b_lns + 2 * w + 1)
                    v.op(v_raw.tensor_scalar, stats[:, 2:3], stats[:, 1:2],
                         1.0 / D, None, ALU.mult, dep=True)
                    v.op(v_raw.tensor_scalar, stats[:, 3:4], stats[:, 2:3],
                         stats[:, 2:3], None, ALU.mult, dep=True)
                    v.op(v_raw.tensor_scalar, stats[:, 4:5], stats[:, 0:1],
                         1.0 / D, None, ALU.mult)
                    v.op(v_raw.tensor_tensor, stats[:, 4:5], stats[:, 4:5],
                         stats[:, 3:4], ALU.subtract, inc=(s_lnv, 1), dep=True)
                    v.wait(s_lns, b_lns + 2 * w + 2)
                    if w >= 2:
                        v.wait(ssts[w % 2], b_sts[w % 2] + 16 * (w // 2))
                    v.op(v_raw.tensor_scalar,
                         outt[:, (w % 2) * 1024:(w % 2) * 1024 + 1024],
                         ob_ap, stats[:, 2:3], stats[:, 5:6],
                         ALU.subtract, ALU.mult, inc=(sout, 1), dep=True)

        @blk.gpsimd
        def _(g_raw):
            g = k.eng(g_raw, "gpsimd")
            for w in range(32):
                g.wait(sout, b_out + w + 1)
                g.dma(y_hbm[w * 128:(w + 1) * 128, :],
                      outt[:, (w % 2) * 1024:(w % 2) * 1024 + 1024], ssts[w % 2])
            for ss in ssts:
                g_raw.wait_ge(ss.h, ss.v)


def body_l1ffn(k, x3_in, w1_in, w2_in, h1_in, h2_in, w3_in, s1_in, s2_in,
               sh1_in, sh2_in, s3_in, y_out, pfx="lf"):
    nc = k.nc
    w1t_hbm = k.dram(f"{pfx}_scr_w1t", [32, 128, 1024], BF16)
    w2t_hbm = k.dram(f"{pfx}_scr_w2t", [32, 128, 1024], BF16)
    h1t_hbm = k.dram(f"{pfx}_scr_h1t", [32, 128, 4096], BF16)
    h2t_hbm = k.dram(f"{pfx}_scr_h2t", [32, 128, 4096], BF16)
    w3t_hbm = k.dram(f"{pfx}_scr_w3t", [8, 128, 4096], BF16)
    bt_hbm = k.dram(f"{pfx}_scr_bt", [32, 128, T], BF16)
    ct_hbm = k.dram(f"{pfx}_scr_ct", [32, 128, T], BF16)
    at_hbm = k.dram(f"{pfx}_scr_at", [32, 128, T], BF16)

    with k.scope():
        tw_hbm(k, w1_in[:, :], w1t_hbm, 4096, 1024, "w1")
    with k.scope():
        tw_hbm(k, w2_in[:, :], w2t_hbm, 4096, 1024, "w2")
    with k.scope():
        tw_hbm(k, h1_in[:, :], h1t_hbm, 4096, 4096, "h1")
    with k.scope():
        tw_hbm(k, h2_in[:, :], h2t_hbm, 4096, 4096, "h2")
    with k.scope():
        tw_hbm(k, w3_in[:, :], w3t_hbm, 1024, 4096, "w3")

    s1c = k.sbuf("s1c", [128, 32], F32)
    s2c = k.sbuf("s2c", [128, 32], F32)
    sh1c = k.sbuf("sh1c", [128, 32], F32)
    sh2c = k.sbuf("sh2c", [128, 32], F32)
    s3c = k.sbuf("s3c", [128, 8], F32)
    with k.scope():
        scale_cols(k, s1_in, 4096, s1c, "s1")
    with k.scope():
        scale_cols(k, s2_in, 4096, s2c, "s2")
    with k.scope():
        scale_cols(k, sh1_in, 4096, sh1c, "sh1")
    with k.scope():
        scale_cols(k, sh2_in, 4096, sh2c, "sh2")
    with k.scope():
        scale_cols(k, s3_in, 1024, s3c, "s3")

    with contextlib.ExitStack() as phx:
        x3T = phx.enter_context(nc.sbuf_tensor("x3T", [128, 8 * T], BF16))
        with k.scope():
            tx_sbuf(k, x3_in[:, :], x3T, "x3")
        with k.scope():
            emit_gw12(k, x3T, w1t_hbm, w2t_hbm, s1c, s2c, at_hbm, "gw")
    with k.scope():
        emit_gh_blocked(k, at_hbm, h1t_hbm, sh1c, bt_hbm, "h1")
    with k.scope():
        emit_gh_blocked(k, bt_hbm, h2t_hbm, sh2c, ct_hbm, "h2")
    with contextlib.ExitStack() as phw:
        w3t_sb = phw.enter_context(nc.sbuf_tensor("w3t_sb", [128, 8 * 4096], BF16))
        s_wl = k.sem("s_w3ld")
        with nc.Block() as blk:
            @blk.sync
            def _(s_raw):
                s = k.eng(s_raw, "sync")
                for dt in range(8):
                    s.dma(w3t_sb[:, dt * 4096:(dt + 1) * 4096], w3t_hbm[dt], s_wl)
                s_raw.wait_ge(s_wl.h, s_wl.v)
        with k.scope():
            emit_w3_ln(k, ct_hbm, w3t_sb, s3c, x3_in[:, :], y_out, "w3")


def build_l1ffn():
    k = K("l1ffn")
    x3_in = k.dram("x3_in", [T, D], F32, kind="ExternalInput")
    w1_in = k.dram("w1w_in", [4096, 1024], F32, kind="ExternalInput")
    w2_in = k.dram("w2w_in", [4096, 1024], F32, kind="ExternalInput")
    h1_in = k.dram("h1w_in", [4096, 4096], F32, kind="ExternalInput")
    h2_in = k.dram("h2w_in", [4096, 4096], F32, kind="ExternalInput")
    w3_in = k.dram("w3w_in", [1024, 4096], F32, kind="ExternalInput")
    s1_in = k.dram("s1_in", [4096], F32, kind="ExternalInput")
    s2_in = k.dram("s2_in", [4096], F32, kind="ExternalInput")
    sh1_in = k.dram("sh1_in", [4096], F32, kind="ExternalInput")
    sh2_in = k.dram("sh2_in", [4096], F32, kind="ExternalInput")
    s3_in = k.dram("s3_in", [1024], F32, kind="ExternalInput")
    y_out = k.dram("y_out", [T, D], F32, kind="ExternalOutput")
    declare_and_load_consts(k)
    body_l1ffn(k, x3_in, w1_in, w2_in, h1_in, h2_in, w3_in, s1_in, s2_in,
               sh1_in, sh2_in, s3_in, y_out)
    return k


def run_l1ffn(x3_full, w1w, w1s, w2w, w2s, h1w, h1s, h2w, h2s, w3w, w3s):
    k = build_l1ffn()
    consts = host_consts()
    in_maps = []
    for c in range(NCORES):
        m = dict(consts)
        m["x3_in"] = np.ascontiguousarray(x3_full[c])
        m["w1w_in"] = w1w
        m["w2w_in"] = w2w
        m["h1w_in"] = h1w
        m["h2w_in"] = h2w
        m["w3w_in"] = w3w
        m["s1_in"] = w1s
        m["s2_in"] = w2s
        m["sh1_in"] = h1s
        m["sh2_in"] = h2s
        m["s3_in"] = w3s
        in_maps.append(m)
    res = run_bass_kernel_spmd(k.nc, in_maps, core_ids=list(range(NCORES)))
    k.close()
    return np.stack([res.results[c]["y_out"] for c in range(NCORES)])



def tw_sbuf_any(k: K, src_ap, dst_sb, R, C, tag):
    nc = k.nc
    smm, sev = k.sem("t_mm"), k.sem("s_ev")
    slds = [k.sem("g_wld0"), k.sem("g_wld1")]
    b_lds = [s.v for s in slds]
    nrb = (R + 127) // 128
    ncb = (C + 127) // 128
    wrow = k.sbuf(f"tsa_row_{tag}", [128, 2 * C], BF16)
    pst = k.psum(f"tsa_ps_{tag}", [128, 2 * 1024], BF16)
    b_mm, b_ev = smm.v, sev.v

    with nc.Block() as blk:
        @blk.gpsimd
        def _(g_raw):
            g = k.eng(g_raw, "gpsimd")
            for rb in range(nrb):
                r_sz = min(128, R - rb * 128)
                if rb >= 2:
                    g_raw.wait_ge(smm.h, b_mm + (rb - 1) * ncb)
                g.dma(wrow[0:r_sz, (rb % 2) * C:(rb % 2) * C + C],
                      src_ap[rb * 128:rb * 128 + r_sz, :], slds[rb % 2])

        @blk.tensor
        def _(t):
            n = 0
            for rb in range(nrb):
                r_sz = min(128, R - rb * 128)
                t.wait_ge(slds[rb % 2].h, b_lds[rb % 2] + 16 * (rb // 2 + 1))
                for cb in range(ncb):
                    c_sz = min(128, C - cb * 128)
                    if n >= 2:
                        t.wait_ge(sev.h, b_ev + n - 1)
                    mm = t.transpose(
                        pst[0:c_sz, (n % 2) * 1024:(n % 2) * 1024 + r_sz],
                        wrow[0:r_sz, (rb % 2) * C + cb * 128:(rb % 2) * C + cb * 128 + c_sz],
                        k.ident_bf[0:r_sz, 0:r_sz])
                    mm.then_inc(smm.h, 1)
                    smm.v += 1
                    n += 1

        @blk.scalar
        def _(sc_raw):
            sc = k.eng(sc_raw, "scalar")
            n = 0
            for rb in range(nrb):
                r_sz = min(128, R - rb * 128)
                for cb in range(ncb):
                    c_sz = min(128, C - cb * 128)
                    sc.wait(smm, b_mm + n + 1)
                    sc.op(sc_raw.copy,
                          dst_sb[0:c_sz, cb * R + rb * 128:cb * R + rb * 128 + r_sz],
                          pst[0:c_sz, (n % 2) * 1024:(n % 2) * 1024 + r_sz],
                          inc=(sev, 1))
                    n += 1


def emit_xz(k: K, x2T_sb, wint_hbm, xi_hbm, z_hbm, tag):
    nc = k.nc
    smm, sev = k.sem("t_mm"), k.sem("s_ev")
    slds = [k.sem("g_wld0"), k.sem("g_wld1")]
    ssts = [k.sem(f"s_st{i}") for i in range(4)]
    b_lds = [s.v for s in slds]
    b_sts = [s.v for s in ssts]
    b_mm, b_ev = smm.v, sev.v
    slab = k.sbuf(f"xz_slab_{tag}", [128, 2 * 1024], BF16)
    ev = k.sbuf(f"xz_ev_{tag}", [128, 4 * 512], F32)
    ps = k.psum(f"xz_ps_{tag}", [128, 4 * 512], F32)

    with nc.Block() as blk:
        @blk.gpsimd
        def _(g_raw):
            g = k.eng(g_raw, "gpsimd")
            for et in range(32):
                if et >= 2:
                    g_raw.wait_ge(smm.h, b_mm + 8 * (et - 1))
                g.dma(slab[:, (et % 2) * 1024:(et % 2) * 1024 + 1024],
                      wint_hbm[et], slds[et % 2])

        @blk.tensor
        def _(t):
            n = 0
            for et in range(32):
                t.wait_ge(slds[et % 2].h, b_lds[et % 2] + 16 * (et // 2 + 1))
                for tb in range(8):
                    if n >= 4:
                        t.wait_ge(sev.h, b_ev + n - 3)
                    for kt in range(8):
                        mm = t.matmul(
                            ps[:, (n % 4) * 512:(n % 4) * 512 + 512],
                            slab[:, (et % 2) * 1024 + kt * 128:(et % 2) * 1024 + kt * 128 + 128],
                            x2T_sb[:, kt * T + tb * 512:kt * T + tb * 512 + 512],
                            start=(kt == 0), stop=(kt == 7))
                    mm.then_inc(smm.h, 1)
                    smm.v += 1
                    n += 1

        @blk.scalar
        def _(sc_raw):
            sc = k.eng(sc_raw, "scalar")
            n = 0
            for et in range(32):
                for tb in range(8):
                    sc.wait(smm, b_mm + n + 1)
                    if n >= 4:
                        sc.wait(ssts[n % 4], b_sts[n % 4] + 16 * (n // 4))
                    sc.op(sc_raw.copy, ev[:, (n % 4) * 512:(n % 4) * 512 + 512],
                          ps[:, (n % 4) * 512:(n % 4) * 512 + 512], inc=(sev, 1))
                    n += 1

        @blk.sync
        def _(s_raw):
            s = k.eng(s_raw, "sync")
            n = 0
            for et in range(32):
                for tb in range(8):
                    s.wait(sev, b_ev + n + 1)
                    dst = xi_hbm[et] if et < 16 else z_hbm[et - 16]
                    s.dma(dst[:, tb * 512:tb * 512 + 512],
                          ev[:, (n % 4) * 512:(n % 4) * 512 + 512], ssts[n % 4])
                    n += 1
            for ss in ssts:
                s_raw.wait_ge(ss.h, ss.v)


def emit_conv(k: K, xi_hbm, cw_sb, cb_sb, xc_hbm, xcbf_hbm, tag):
    nc = k.nc
    ssc, svv = k.sem("s_ev"), k.sem("v_ev")
    slds = [k.sem("s_xld0"), k.sem("s_xld1")]
    ssts = [k.sem("g_st0"), k.sem("g_st1")]
    b_lds = [s.v for s in slds]
    b_sts = [s.v for s in ssts]
    b_sc, b_vv = ssc.v, svv.v
    PAD = 3
    xibuf = k.sbuf(f"cv_xi_{tag}", [128, 2 * (T + PAD)], F32)
    acc = k.sbuf(f"cv_acc_{tag}", [128, 2 * T], F32)
    tmp0 = k.sbuf(f"cv_t0_{tag}", [128, T], F32)
    tmp1 = k.sbuf(f"cv_t1_{tag}", [128, T], F32)
    xcf = k.sbuf(f"cv_xc_{tag}", [128, 2 * T], F32)
    xcb = k.sbuf(f"cv_xcb_{tag}", [128, 2 * T], BF16)

    with nc.Block() as blk:
        @blk.sync
        def _(s_raw):
            s = k.eng(s_raw, "sync")
            s_raw.wait_ge(svv.h, b_vv + 1)
            for dt in range(16):
                if dt >= 2:
                    s_raw.wait_ge(ssc.h, b_sc + 5 * (dt - 2) + 4)
                s.dma(xibuf[:, (dt % 2) * (T + PAD) + PAD:(dt % 2) * (T + PAD) + PAD + T],
                      xi_hbm[dt], slds[dt % 2])

        @blk.scalar
        def _(sc_raw):
            sc = k.eng(sc_raw, "scalar")
            for dt in range(16):
                sl = (dt % 2) * (T + PAD)
                ac = acc[:, (dt % 2) * T:(dt % 2) * T + T]
                sc.wait(slds[dt % 2], b_lds[dt % 2] + 16 * (dt // 2 + 1))
                sc.op(sc_raw.activation, ac, xibuf[:, sl + 3:sl + 3 + T], AF.Identity,
                      scale=cw_sb[:, dt * 4 + 3:dt * 4 + 4],
                      bias=cb_sb[:, dt:dt + 1], inc=(ssc, 1))
                if dt >= 1:
                    sc.wait(svv, b_vv + 1 + 4 * (dt - 1) + 3)
                sc.op(sc_raw.activation, tmp0[:, :], xibuf[:, sl + 2:sl + 2 + T],
                      AF.Identity, scale=cw_sb[:, dt * 4 + 2:dt * 4 + 3], inc=(ssc, 1))
                sc.op(sc_raw.activation, tmp1[:, :], xibuf[:, sl + 1:sl + 1 + T],
                      AF.Identity, scale=cw_sb[:, dt * 4 + 1:dt * 4 + 2], inc=(ssc, 1))
                sc.wait(svv, b_vv + 1 + 4 * dt + 1)
                sc.op(sc_raw.activation, tmp0[:, :], xibuf[:, sl + 0:sl + 0 + T],
                      AF.Identity, scale=cw_sb[:, dt * 4 + 0:dt * 4 + 1], inc=(ssc, 1))
                sc.wait(svv, b_vv + 1 + 4 * dt + 3)
                if dt >= 2:
                    sc.wait(svv, b_vv + 1 + 4 * (dt - 2) + 4)
                    sc.wait(ssts[dt % 2], b_sts[dt % 2] + 32 * (dt // 2))
                sc.op(sc_raw.activation, xcf[:, (dt % 2) * T:(dt % 2) * T + T],
                      ac, AF.Silu, inc=(ssc, 1))

        @blk.vector
        def _(v_raw):
            v = k.eng(v_raw, "vector")
            v.op(v_raw.memset, xibuf[:, 0:PAD], 0.0)
            v.op(v_raw.memset, xibuf[:, T + PAD:T + 2 * PAD], 0.0, inc=(svv, 1))
            for dt in range(16):
                ac = acc[:, (dt % 2) * T:(dt % 2) * T + T]
                v.wait(ssc, b_sc + 5 * dt + 2)
                v.op(v_raw.tensor_tensor, ac, ac, tmp0[:, :], ALU.add, inc=(svv, 1))
                v.wait(ssc, b_sc + 5 * dt + 3)
                v.op(v_raw.tensor_tensor, ac, ac, tmp1[:, :], ALU.add, inc=(svv, 1))
                v.wait(ssc, b_sc + 5 * dt + 4)
                v.op(v_raw.tensor_tensor, ac, ac, tmp0[:, :], ALU.add, inc=(svv, 1))
                v.wait(ssc, b_sc + 5 * dt + 5)
                if dt >= 2:
                    v.wait(ssts[dt % 2], b_sts[dt % 2] + 32 * (dt // 2))
                v.op(v_raw.tensor_copy, xcb[:, (dt % 2) * T:(dt % 2) * T + T],
                     xcf[:, (dt % 2) * T:(dt % 2) * T + T], inc=(svv, 1))

        @blk.gpsimd
        def _(g_raw):
            g = k.eng(g_raw, "gpsimd")
            for dt in range(16):
                g.wait(svv, b_vv + 1 + 4 * dt + 4)
                g.dma(xc_hbm[dt], xcf[:, (dt % 2) * T:(dt % 2) * T + T],
                      ssts[dt % 2])
                g.dma(xcbf_hbm[dt], xcb[:, (dt % 2) * T:(dt % 2) * T + T],
                      ssts[dt % 2])
            for ss in ssts:
                g_raw.wait_ge(ss.h, ss.v)


def emit_xdbl(k: K, xcbf_hbm, wxT_sb, xdbl_sb, tag):
    nc = k.nc
    smm, sev = k.sem("t_mm"), k.sem("s_ev")
    slds = [k.sem("s_xld0"), k.sem("s_xld1")]
    b_lds = [s.v for s in slds]
    b_mm, b_ev = smm.v, sev.v
    rbuf = k.sbuf(f"xd_r_{tag}", [128, 2 * 512], BF16)
    ps = k.psum(f"xd_ps_{tag}", [96, 2 * 512], F32)

    with nc.Block() as blk:
        @blk.sync
        def _(s_raw):
            s = k.eng(s_raw, "sync")
            n = 0
            for tb in range(8):
                for kt in range(16):
                    if n >= 2:
                        s_raw.wait_ge(smm.h, b_mm + n - 1)
                    s.dma(rbuf[:, (n % 2) * 512:(n % 2) * 512 + 512],
                          xcbf_hbm[kt][:, tb * 512:tb * 512 + 512], slds[n % 2])
                    n += 1

        @blk.tensor
        def _(t):
            n = 0
            for tb in range(8):
                for kt in range(16):
                    t.wait_ge(slds[n % 2].h, b_lds[n % 2] + 16 * (n // 2 + 1))
                    if kt == 0 and tb >= 2:
                        t.wait_ge(sev.h, b_ev + tb - 1)
                    mm = t.matmul(ps[:, (tb % 2) * 512:(tb % 2) * 512 + 512],
                                  wxT_sb[:, kt * 96:kt * 96 + 96],
                                  rbuf[:, (n % 2) * 512:(n % 2) * 512 + 512],
                                  start=(kt == 0), stop=(kt == 15))
                    mm.then_inc(smm.h, 1)
                    smm.v += 1
                    n += 1

        @blk.scalar
        def _(sc_raw):
            sc = k.eng(sc_raw, "scalar")
            for tb in range(8):
                sc.wait(smm, b_mm + 16 * (tb + 1))
                sc.op(sc_raw.copy, xdbl_sb[:, tb * 512:tb * 512 + 512],
                      ps[:, (tb % 2) * 512:(tb % 2) * 512 + 512], inc=(sev, 1))


def emit_dtproj(k: K, xdbl_sb, wdtT_sb, bdt_col, dtt_hbm, tag):
    nc = k.nc
    smm, sev, svv = k.sem("t_mm"), k.sem("s_ev"), k.sem("v_ev")
    ssts = [k.sem(f"g_st{i}") for i in range(4)]
    b_sts = [s.v for s in ssts]
    b_mm, b_ev, b_vv = smm.v, sev.v, svv.v
    dtr_bf = k.sbuf(f"dp_r_{tag}", [64, T], BF16)
    tab = k.sbuf(f"dp_ab_{tag}", [128, 2 * 512], F32)
    trl = k.sbuf(f"dp_rl_{tag}", [128, 2 * 512], F32)
    ev = k.sbuf(f"dp_ev_{tag}", [128, 4 * 512], F32)
    ps = k.psum(f"dp_ps_{tag}", [128, 2 * 512], F32)

    with nc.Block() as blk:
        @blk.vector
        def _(v_raw):
            v = k.eng(v_raw, "vector")
            v.op(v_raw.tensor_copy, dtr_bf[:, :], xdbl_sb[0:64, :], inc=(svv, 1))
            n = 0
            for dt in range(16):
                for tb in range(8):
                    v.wait(sev, b_ev + 2 * n + 2)
                    if n >= 4:
                        v.wait(ssts[n % 4], b_sts[n % 4] + 16 * (n // 4))
                    v.op(v_raw.tensor_tensor, ev[:, (n % 4) * 512:(n % 4) * 512 + 512],
                         trl[:, (n % 2) * 512:(n % 2) * 512 + 512],
                         tab[:, (n % 2) * 512:(n % 2) * 512 + 512],
                         ALU.add, inc=(svv, 1))
                    n += 1

        @blk.tensor
        def _(t):
            t.wait_ge(svv.h, b_vv + 1)
            n = 0
            for dt in range(16):
                for tb in range(8):
                    if n >= 2:
                        t.wait_ge(sev.h, b_ev + 2 * (n - 1))
                    mm = t.matmul(ps[:, (n % 2) * 512:(n % 2) * 512 + 512],
                                  wdtT_sb[:, dt * 128:dt * 128 + 128],
                                  dtr_bf[:, tb * 512:tb * 512 + 512],
                                  start=True, stop=True)
                    mm.then_inc(smm.h, 1)
                    smm.v += 1
                    n += 1

        @blk.scalar
        def _(sc_raw):
            sc = k.eng(sc_raw, "scalar")
            n = 0
            for dt in range(16):
                for tb in range(8):
                    sc.wait(smm, b_mm + n + 1)
                    if n >= 1:
                        if n >= 2:
                            sc.wait(svv, b_vv + 1 + n - 1)
                    sc.op(sc_raw.activation, tab[:, (n % 2) * 512:(n % 2) * 512 + 512],
                          ps[:, (n % 2) * 512:(n % 2) * 512 + 512], AF.Abs,
                          bias=bdt_col[:, dt:dt + 1])
                    sc.op(sc_raw.activation, trl[:, (n % 2) * 512:(n % 2) * 512 + 512],
                          ps[:, (n % 2) * 512:(n % 2) * 512 + 512], AF.Relu,
                          bias=bdt_col[:, dt:dt + 1], inc=(sev, 1))
                    sc.op(sc_raw.activation, tab[:, (n % 2) * 512:(n % 2) * 512 + 512],
                          tab[:, (n % 2) * 512:(n % 2) * 512 + 512], AF.Exp,
                          scale=-1.0)
                    sc.op(sc_raw.activation, tab[:, (n % 2) * 512:(n % 2) * 512 + 512],
                          tab[:, (n % 2) * 512:(n % 2) * 512 + 512], AF.Ln,
                          bias=1.0, inc=(sev, 1))
                    n += 1

        @blk.gpsimd
        def _(g_raw):
            g = k.eng(g_raw, "gpsimd")
            n = 0
            for dt in range(16):
                for tb in range(8):
                    g.wait(svv, b_vv + 1 + n + 1)
                    g.dma(dtt_hbm[dt][:, tb * 512:tb * 512 + 512],
                          ev[:, (n % 4) * 512:(n % 4) * 512 + 512], ssts[n % 4])
                    n += 1
            for ss in ssts:
                g_raw.wait_ge(ss.h, ss.v)


def emit_scan(k: K, dtt_hbm, xc_hbm, z_hbm, bc_sb, sel_sb, a_sb, dp_col,
              yt_hbm, tag):
    nc = k.nc
    s_dxs = [k.sem("s_dxld0"), k.sem("s_dxld1")]
    b_dxs = [s.v for s in s_dxs]
    s_zl = k.sem("s_zld")
    t_bc = k.sem("t_bc")
    s_dA = k.sem("s_dA")
    s_cc = k.sem("s_cc")
    v_scan = k.sem("v_scan")
    v_bc = k.sem("v_bc")
    s_t2 = k.sem("s_t2")
    s_sz = k.sem("s_sz")
    v_fin = k.sem("v_fin")
    g_add = k.sem("g_add")
    s_ysts = [k.sem("g_yst0"), k.sem("g_yst1")]
    b_ysts = [s.v for s in s_ysts]
    b_zl, b_bc, b_dA, b_cc = s_zl.v, t_bc.v, s_dA.v, s_cc.v
    b_vs, b_vb, b_t2, b_sz, b_fin = (v_scan.v, v_bc.v, s_t2.v, s_sz.v,
                                     v_fin.v)
    b_ga = g_add.v

    dtt = k.sbuf(f"sn_dt_{tag}", [128, 2 * T], F32)
    xct = k.sbuf(f"sn_xc_{tag}", [128, T], F32)
    ztt = k.sbuf(f"sn_z_{tag}", [128, T], F32)
    u_sb = k.sbuf(f"sn_u_{tag}", [128, T], F32)
    ya = k.sbuf(f"sn_ya_{tag}", [128, T], F32)
    sz = k.sbuf(f"sn_sz_{tag}", [128, T], F32)
    da = k.sbuf(f"sn_da_{tag}", [128, 2 * 512], F32)
    dbu = k.sbuf(f"sn_dbu_{tag}", [128, 2 * 512], F32)
    hh = k.sbuf(f"sn_h_{tag}", [128, 2 * 512], F32)
    tmp = k.sbuf(f"sn_tmp_{tag}", [128, 2 * 512], F32)
    carry = k.sbuf(f"sn_carry_{tag}", [128, 16], F32)
    ytb = k.sbuf(f"sn_yt_{tag}", [128, 2 * T], BF16)
    ps_bc = k.psum(f"sn_ps_{tag}", [128, 4 * 512], F32)

    NIT = 128
    add_id = {}
    _aid = 0
    for _dt in range(16):
        for _it in range(NIT):
            if _it % 16 != 0:
                add_id[_dt * NIT + _it] = _aid
                _aid += 1
    NADD_PER_DT = 120

    with nc.Block() as blk:
        @blk.sync
        def _(s_raw):
            s = k.eng(s_raw, "sync")
            for dt in range(16):
                if dt >= 2:
                    s_raw.wait_ge(s_ysts[dt % 2].h, b_ysts[dt % 2] + 16 * (dt // 2))
                s.dma(dtt[:, (dt % 2) * T:(dt % 2) * T + T], dtt_hbm[dt],
                      s_dxs[dt % 2])
                if dt >= 1:
                    s_raw.wait_ge(s_t2.h, b_t2 + dt)
                s.dma(xct[:, :], xc_hbm[dt], s_dxs[dt % 2])
                if dt >= 1:
                    s_raw.wait_ge(s_sz.h, b_sz + dt)
                s.dma(ztt[:, :], z_hbm[dt], s_zl)

        @blk.tensor
        def _(t):
            for dt in range(16):
                for it in range(NIT):
                    gi = dt * NIT + it
                    ch, n = it // 16, it % 16
                    slot = gi % 2
                    if gi >= 2:
                        t.wait_ge(v_bc.h, b_vb + gi - 1)
                    t.matmul(ps_bc[:, slot * 1024:slot * 1024 + 512],
                             sel_sb[:, n * 128:n * 128 + 128],
                             bc_sb[:, ch * 512:ch * 512 + 512],
                             start=True, stop=False, skip_group_check=True)
                    t.matmul(ps_bc[:, slot * 1024:slot * 1024 + 512],
                             sel_sb[:, n * 128:n * 128 + 128],
                             bc_sb[:, T + ch * 512:T + ch * 512 + 512],
                             start=False, stop=True, skip_group_check=True)
                    t.matmul(ps_bc[:, slot * 1024 + 512:slot * 1024 + 1024],
                             sel_sb[:, (16 + n) * 128:(16 + n) * 128 + 128],
                             bc_sb[:, ch * 512:ch * 512 + 512],
                             start=True, stop=False, skip_group_check=True)
                    mm = t.matmul(ps_bc[:, slot * 1024 + 512:slot * 1024 + 1024],
                                  sel_sb[:, (16 + n) * 128:(16 + n) * 128 + 128],
                                  bc_sb[:, T + ch * 512:T + ch * 512 + 512],
                                  start=False, stop=True, skip_group_check=True)
                    mm.then_inc(t_bc.h, 1)
                    t_bc.v += 1

        @blk.scalar
        def _(sc_raw):
            sc = k.eng(sc_raw, "scalar")
            def cc(gj):
                nj = gj % 16
                sc.wait(v_scan, b_vs + gj + 1)
                sc.op(sc_raw.copy, carry[:, nj:nj + 1],
                      hh[:, (gj % 2) * 512 + 511:(gj % 2) * 512 + 512],
                      inc=(s_cc, 1))
            for dt in range(16):
                sc.wait(s_dxs[dt % 2], b_dxs[dt % 2] + 32 * (dt // 2 + 1))
                for it in range(NIT):
                    gi = dt * NIT + it
                    ch, n = it // 16, it % 16
                    if gi >= 2:
                        sc.wait(v_scan, b_vs + gi - 1)
                    sc.op(sc_raw.activation, da[:, (gi % 2) * 512:(gi % 2) * 512 + 512],
                          dtt[:, (dt % 2) * T + ch * 512:(dt % 2) * T + ch * 512 + 512],
                          AF.Exp, scale=a_sb[:, dt * 16 + n:dt * 16 + n + 1],
                          inc=(s_dA, 1))
                    if gi >= 1:
                        cc(gi - 1)
                if dt == 15:
                    cc(16 * NIT - 1)
                sc.wait(v_bc, b_vb + NIT * (dt + 1))
                sc.op(sc_raw.activation, u_sb[:, :],
                      xct[:, :], AF.Identity,
                      scale=dp_col[:, dt:dt + 1], inc=(s_t2, 1))
                sc.wait(s_zl, b_zl + 16 * (dt + 1))
                if dt >= 1:
                    sc.wait(v_fin, b_fin + dt)
                sc.op(sc_raw.activation, sz[:, :], ztt[:, :], AF.Silu,
                      inc=(s_sz, 1))

        @blk.vector
        def _(v_raw):
            v = k.eng(v_raw, "vector")
            for dt in range(16):
                v.wait(s_dxs[dt % 2], b_dxs[dt % 2] + 32 * (dt // 2 + 1))
                v.wait(s_cc, b_cc + NIT * dt)
                v.op(v_raw.memset, carry[:, :], 0.0)
                v.op(v_raw.tensor_tensor, u_sb[:, :],
                     dtt[:, (dt % 2) * T:(dt % 2) * T + T],
                     xct[:, :], ALU.mult)
                for it in range(NIT):
                    gi = dt * NIT + it
                    ch, n = it // 16, it % 16
                    slot = gi % 2
                    cs = ch * 512
                    v.wait(t_bc, b_bc + gi + 1)
                    v.op(v_raw.tensor_tensor, dbu[:, slot * 512:slot * 512 + 512],
                         u_sb[:, cs:cs + 512],
                         ps_bc[:, slot * 1024:slot * 1024 + 512], ALU.mult)
                    v.wait(s_dA, b_dA + gi + 1)
                    if gi >= 2:
                        v.wait(s_cc, b_cc + gi - 1)
                    v.op(v_raw.tensor_tensor_scan,
                         hh[:, slot * 512:slot * 512 + 512],
                         da[:, slot * 512:slot * 512 + 512],
                         dbu[:, slot * 512:slot * 512 + 512],
                         carry[:, n:n + 1], ALU.mult, ALU.add,
                         inc=(v_scan, 1), dep=(ch == 0))
                    if n == 0:
                        v.op(v_raw.tensor_tensor, ya[:, cs:cs + 512],
                             hh[:, slot * 512:slot * 512 + 512],
                             ps_bc[:, slot * 1024 + 512:slot * 1024 + 1024],
                             ALU.mult, inc=(v_bc, 1), dep=True)
                    else:
                        a = add_id[gi]
                        if a >= 2:
                            v.wait(g_add, b_ga + a - 1)
                        v.op(v_raw.tensor_tensor, tmp[:, (a % 2) * 512:(a % 2) * 512 + 512],
                             hh[:, slot * 512:slot * 512 + 512],
                             ps_bc[:, slot * 1024 + 512:slot * 1024 + 1024],
                             ALU.mult, inc=(v_bc, 1), dep=True)
                v.wait(g_add, b_ga + NADD_PER_DT * (dt + 1))
                v.wait(s_t2, b_t2 + dt + 1)
                v.op(v_raw.tensor_tensor, ya[:, :], ya[:, :], u_sb[:, :], ALU.add)
                v.wait(s_sz, b_sz + dt + 1)
                if dt >= 2:
                    v.wait(s_ysts[dt % 2], b_ysts[dt % 2] + 16 * (dt // 2))
                v.op(v_raw.tensor_tensor, ytb[:, (dt % 2) * T:(dt % 2) * T + T],
                     ya[:, :], sz[:, :], ALU.mult, inc=(v_fin, 1))

        @blk.gpsimd
        def _(g_raw):
            g = k.eng(g_raw, "gpsimd")
            for dt in range(16):
                for it in range(NIT):
                    gi = dt * NIT + it
                    ch, n = it // 16, it % 16
                    if n == 0:
                        continue
                    a = add_id[gi]
                    cs = ch * 512
                    g.wait(v_bc, b_vb + gi + 1)
                    g.op(g_raw.tensor_tensor, ya[:, cs:cs + 512],
                         ya[:, cs:cs + 512],
                         tmp[:, (a % 2) * 512:(a % 2) * 512 + 512],
                         ALU.add, inc=(g_add, 1))
                g.wait(v_fin, b_fin + dt + 1)
                g.dma(yt_hbm[dt], ytb[:, (dt % 2) * T:(dt % 2) * T + T],
                      s_ysts[dt % 2])
            for ss in s_ysts:
                g_raw.wait_ge(ss.h, ss.v)


def emit_bc_split(k: K, xdbl_sb, bc_sb, tag):
    nc = k.nc
    bcf = k.sbuf(f"bcs_f_{tag}", [32, T], F32)
    with nc.Block() as blk:
        @blk.scalar
        def _(sc_raw):
            sc = k.eng(sc_raw, "scalar")
            sc.op(sc_raw.copy, bcf[:, :], xdbl_sb[64:96, :])
    with nc.Block() as blk:
        @blk.vector
        def _(v_raw):
            v = k.eng(v_raw, "vector")
            v.op(v_raw.tensor_copy, bc_sb[:, 0:T], bcf[:, :])
    with nc.Block() as blk:
        @blk.vector
        def _(v_raw):
            v = k.eng(v_raw, "vector")
            v.op(v_raw.tensor_tensor, bc_sb[:, T:2 * T], bcf[:, :],
                 bc_sb[:, 0:T], ALU.subtract)


def body_mamba(k, x2_in, win_in, cw_in, cb_in, wx_in, wdt_in, bdt_in,
               alog_in, dp_in, wout_in, sel_in, x3_out, pfx="mb"):
    nc = k.nc
    wint_hbm = k.dram(f"{pfx}_scr_wint", [32, 128, 1024], BF16)
    xi_hbm = k.dram(f"{pfx}_scr_xi", [16, 128, T], F32)
    z_hbm = k.dram(f"{pfx}_scr_z", [16, 128, T], F32)
    xc_hbm = k.dram(f"{pfx}_scr_xc", [16, 128, T], F32)
    xcbf_hbm = k.dram(f"{pfx}_scr_xcbf", [16, 128, T], BF16)
    dtt_hbm = k.dram(f"{pfx}_scr_dtt", [16, 128, T], F32)
    yt_hbm = k.dram(f"{pfx}_scr_yt", [16, 128, T], BF16)

    phm = contextlib.ExitStack()
    sel_sb = phm.enter_context(nc.sbuf_tensor("sel_sb", [32, 32 * 128], BF16))
    wxT_sb = phm.enter_context(nc.sbuf_tensor("wxT_sb", [128, 16 * 96], BF16))
    wdtT_sb = phm.enter_context(nc.sbuf_tensor("wdtT_sb", [64, 2048], BF16))
    a_sb = phm.enter_context(nc.sbuf_tensor("a_sb", [128, 256], F32))
    cw_sb = phm.enter_context(nc.sbuf_tensor("cw_sb", [128, 64], F32))
    cb_col = phm.enter_context(nc.sbuf_tensor("cb_col", [128, 16], F32))
    bdt_col = phm.enter_context(nc.sbuf_tensor("bdt_col", [128, 16], F32))
    dp_col = phm.enter_context(nc.sbuf_tensor("dp_col", [128, 16], F32))
    xdbl_sb = phm.enter_context(nc.sbuf_tensor("xdbl_sb", [96, T], F32))
    bc_sb = phm.enter_context(nc.sbuf_tensor("bc_sb", [32, 2 * T], BF16))

    s_ml = k.sem("s_mld")
    with nc.Block() as blk:
        @blk.sync
        def _(s_raw):
            s = k.eng(s_raw, "sync")
            s.dma(sel_sb[:, :], sel_in[:, :], s_ml)
            av = a_sb.rearrange("p (dt n) -> p dt n", n=16)
            for c in range(4):
                s.dma(av[:, 4 * c:4 * c + 4, :],
                      alog_in.rearrange("(dt p) n -> p dt n", p=128)[:, 4 * c:4 * c + 4, :],
                      s_ml)
            cwv = cw_sb.rearrange("p (dt kk) -> p dt kk", kk=4)
            for c in range(4):
                s.dma(cwv[:, 4 * c:4 * c + 4, :],
                      cw_in.rearrange("(dt p) kk -> p dt kk", p=128)[:, 4 * c:4 * c + 4, :],
                      s_ml)
            s_raw.wait_ge(s_ml.h, s_ml.v)
    with nc.Block() as blk:
        @blk.scalar
        def _(sc_raw):
            sc = k.eng(sc_raw, "scalar")
            sc.op(sc_raw.activation, a_sb[:, :], a_sb[:, :], AF.Exp)
    with nc.Block() as blk:
        @blk.vector
        def _(v_raw):
            v = k.eng(v_raw, "vector")
            v.op(v_raw.tensor_scalar, a_sb[:, :], a_sb[:, :], -1.0, None, ALU.mult)

    with k.scope():
        scale_cols(k, cb_in, 2048, cb_col, "cb")
    with k.scope():
        scale_cols(k, bdt_in, 2048, bdt_col, "bdt")
    with k.scope():
        scale_cols(k, dp_in, 2048, dp_col, "dp")
    with k.scope():
        tw_hbm(k, win_in[:, :], wint_hbm, 4096, 1024, "win")
    with k.scope():
        tw_sbuf_any(k, wx_in[:, :], wxT_sb, 96, 2048, "wx")
    with k.scope():
        tw_sbuf_any(k, wdt_in[:, :], wdtT_sb, 2048, 64, "wdt")

    with contextlib.ExitStack() as phx:
        x2T = phx.enter_context(nc.sbuf_tensor("x2T", [128, 8 * T], BF16))
        with k.scope():
            tx_sbuf(k, x2_in[:, :], x2T, "x2")
        with k.scope():
            emit_xz(k, x2T, wint_hbm, xi_hbm, z_hbm, "xz")
    with k.scope():
        emit_conv(k, xi_hbm, cw_sb, cb_col, xc_hbm, xcbf_hbm, "cv")
    with k.scope():
        emit_xdbl(k, xcbf_hbm, wxT_sb, xdbl_sb, "xd")
    with k.scope():
        emit_bc_split(k, xdbl_sb, bc_sb, "bc")
    with k.scope():
        emit_dtproj(k, xdbl_sb, wdtT_sb, bdt_col, dtt_hbm, "dp")
    with k.scope():
        emit_scan(k, dtt_hbm, xc_hbm, z_hbm, bc_sb, sel_sb, a_sb, dp_col,
                  yt_hbm, "sn")
    phm.close()

    with contextlib.ExitStack() as phw:
        woutT_sb = phw.enter_context(nc.sbuf_tensor("woutT_sb", [128, 16 * 1024], BF16))
        with k.scope():
            tw_sbuf(k, wout_in[:, :], woutT_sb, 1024, 2048, "wo2")

        def rhs_wout(kt, nb):
            return woutT_sb[:, kt * 1024 + nb * 512:kt * 1024 + nb * 512 + 512]

        with k.scope():
            emit_proj_residual_ln(k, yt_hbm.rearrange("a p t -> (a p) t"),
                                  rhs_wout, x3_out, "mm",
                                  resid_hbm=x2_in[:, :], ctx_rows=2048)


def build_mamba():
    k = K("mamba")
    x2_in = k.dram("x2_in", [T, D], F32, kind="ExternalInput")
    win_in = k.dram("win_in", [4096, 1024], F32, kind="ExternalInput")
    cw_in = k.dram("convw_in", [2048, 4], F32, kind="ExternalInput")
    cb_in = k.dram("convb_in", [2048], F32, kind="ExternalInput")
    wx_in = k.dram("wxproj_in", [96, 2048], F32, kind="ExternalInput")
    wdt_in = k.dram("wdt_in", [2048, 64], F32, kind="ExternalInput")
    bdt_in = k.dram("bdt_in", [2048], F32, kind="ExternalInput")
    alog_in = k.dram("alog_in", [2048, 16], F32, kind="ExternalInput")
    dp_in = k.dram("dp_in", [2048], F32, kind="ExternalInput")
    wout_in = k.dram("wout_in", [1024, 2048], F32, kind="ExternalInput")
    sel_in = k.dram("c_sel_bf", [32, 32 * 128], BF16, kind="ExternalInput")
    x3_out = k.dram("x3_out", [T, D], F32, kind="ExternalOutput")
    declare_and_load_consts(k)
    body_mamba(k, x2_in, win_in, cw_in, cb_in, wx_in, wdt_in, bdt_in,
               alog_in, dp_in, wout_in, sel_in, x3_out)
    return k


def host_sel_const():
    sel = np.zeros((32, 32 * 128), dtype=BF)
    for n in range(32):
        sel[n, n * 128:(n + 1) * 128] = 1.0
    return sel


def run_mamba(x2_full, W_in, conv_w, conv_b, W_xproj, W_dt, b_dt, A_log, Dp,
              W_out):
    k = build_mamba()
    consts = host_consts()
    consts["c_sel_bf"] = host_sel_const()
    in_maps = []
    for c in range(NCORES):
        m = dict(consts)
        m["x2_in"] = np.ascontiguousarray(x2_full[c])
        m["win_in"] = W_in
        m["convw_in"] = conv_w
        m["convb_in"] = conv_b
        m["wxproj_in"] = W_xproj
        m["wdt_in"] = W_dt
        m["bdt_in"] = b_dt
        m["alog_in"] = A_log
        m["dp_in"] = Dp
        m["wout_in"] = W_out
        in_maps.append(m)
    res = run_bass_kernel_spmd(k.nc, in_maps, core_ids=list(range(NCORES)))
    k.close()
    return np.stack([res.results[c]["x3_out"] for c in range(NCORES)])



def emit_ln_transpose(k: K, x_hbm, xT_sb, tag, ln_scale_bc=None,
                      ln_shift_bc=None, gate=None):
    nc = k.nc
    s_xls = [k.sem("s_xld0"), k.sem("s_xld1")]
    b_xls = [s.v for s in s_xls]
    s_lnv, s_lns = k.sem("ln_v"), k.sem("ln_s")
    v_ap = k.sem("v_apl")
    t_tr = k.sem("t_tr")
    s_eb = k.sem("s_evb")
    s_ef = k.sem("s_evf")
    t_gm = k.sem("t_gmm")
    v_tp = k.sem("v_top")
    s_sg = k.sem("s_sig")
    b_lnv, b_lns, b_ap = s_lnv.v, s_lns.v, v_ap.v
    b_tr, b_eb, b_ef, b_gm = t_tr.v, s_eb.v, s_ef.v, t_gm.v
    b_tp, b_sg = v_tp.v, s_sg.v

    import os
    _vdep = os.environ.get("LT_VDEP", "0") == "1"
    _sdep = os.environ.get("LT_SDEP", "0") == "1"
    xrow = k.sbuf(f"lt_x_{tag}", [128, 2 * 1024], F32)
    xln = k.sbuf(f"lt_ln_{tag}", [128, 2 * 1024], F32)
    sqs = k.sbuf(f"lt_sq_{tag}", [128, 1024], F32)
    stats = k.sbuf(f"lt_st_{tag}", [128, 16], F32)
    gsc = k.sbuf(f"lt_gs_{tag}", [128, 3 * 8], F32)
    xtf = k.sbuf(f"lt_xtf_{tag}", [128, 8 * 128], F32)
    pst = k.psum(f"lt_pst_{tag}", [128, 2 * 512], F32)
    psg = k.psum(f"lt_psg_{tag}", [128, 8], F32) if gate else None
    if gate:
        gw_w_sb, gw_sb = gate

    with nc.Block() as blk:
        @blk.sync
        def _(s_raw):
            s = k.eng(s_raw, "sync")
            for ti in range(32):
                if ti >= 2:
                    s_raw.wait_ge(v_ap.h, b_ap + ti - 1)
                s.dma(xrow[:, (ti % 2) * 1024:(ti % 2) * 1024 + 1024],
                      x_hbm[ti * 128:(ti + 1) * 128, :], s_xls[ti % 2])

        @blk.vector
        def _(v_raw):
            v = k.eng(v_raw, "vector")
            v.default_dep = _vdep
            for ti in range(32):
                xa = xrow[:, (ti % 2) * 1024:(ti % 2) * 1024 + 1024]
                la = xln[:, (ti % 2) * 1024:(ti % 2) * 1024 + 1024]
                v.wait(s_xls[ti % 2], b_xls[ti % 2] + 16 * (ti // 2 + 1))
                v.op(v_raw.reduce_sum, stats[:, 1:2], xa, axis=AX.X,
                     inc=(s_lnv, 1))
                v.wait(s_lns, b_lns + 2 * ti + 1)
                v.op(v_raw.tensor_scalar, stats[:, 2:3], stats[:, 1:2],
                     1.0 / D, None, ALU.mult, dep=True)
                v.op(v_raw.tensor_scalar, stats[:, 3:4], stats[:, 2:3],
                     stats[:, 2:3], None, ALU.mult, dep=True)
                v.op(v_raw.tensor_scalar, stats[:, 4:5], stats[:, 0:1],
                     1.0 / D, None, ALU.mult)
                v.op(v_raw.tensor_tensor, stats[:, 4:5], stats[:, 4:5],
                     stats[:, 3:4], ALU.subtract, inc=(s_lnv, 1), dep=True)
                v.wait(s_lns, b_lns + 2 * ti + 2)
                if ti >= 2:
                    v.wait(t_tr, b_tr + 8 * (ti - 1))
                v.op(v_raw.tensor_scalar, la, xa, stats[:, 2:3], stats[:, 5:6],
                     ALU.subtract, ALU.mult, dep=True)
                if ln_scale_bc is not None:
                    v.op(v_raw.tensor_tensor, la, la, ln_scale_bc, ALU.mult)
                    v.op(v_raw.tensor_tensor, la, la, ln_shift_bc, ALU.add,
                         inc=(v_ap, 1))
                else:
                    v.op(v_raw.tensor_copy, la, la, inc=(v_ap, 1))
                if gate:
                    v.wait(t_gm, b_gm + 8 * (ti + 1))
                    v.op(v_raw.reduce_max, stats[:, 8:9], psg[:, :], axis=AX.X)
                    v.op(v_raw.tensor_scalar, gsc[:, 0:8], psg[:, :],
                         stats[:, 8:9], None, ALU.is_ge, dep=True)
                    v.op(v_raw.scalar_tensor_tensor, gsc[:, 8:16], gsc[:, 0:8],
                         -1e30, psg[:, :], ALU.mult, ALU.add, dep=True)
                    v.op(v_raw.reduce_max, stats[:, 9:10], gsc[:, 8:16],
                         axis=AX.X, dep=True)
                    v.op(v_raw.tensor_scalar, gsc[:, 8:16], gsc[:, 8:16],
                         stats[:, 9:10], None, ALU.is_ge, dep=True)
                    v.op(v_raw.tensor_tensor, stats[:, 10:11], stats[:, 8:9],
                         stats[:, 9:10], ALU.subtract, inc=(v_tp, 1), dep=True)
                    v.wait(s_sg, b_sg + ti + 1)
                    v.op(v_raw.tensor_scalar, stats[:, 12:13], stats[:, 11:12],
                         -1.0, 1.0, ALU.mult, ALU.add, dep=True)
                    v.op(v_raw.tensor_scalar, gsc[:, 16:24], gsc[:, 0:8],
                         stats[:, 11:12], None, ALU.mult, dep=True)
                    v.op(v_raw.scalar_tensor_tensor, gw_sb[:, ti * 8:ti * 8 + 8],
                         gsc[:, 8:16], stats[:, 12:13], gsc[:, 16:24],
                         ALU.mult, ALU.add, inc=(v_tp, 1), dep=True)

        @blk.scalar
        def _(sc_raw):
            sc = k.eng(sc_raw, "scalar")
            sc.default_dep = _sdep
            for ti in range(32):
                xa = xrow[:, (ti % 2) * 1024:(ti % 2) * 1024 + 1024]
                sc.wait(s_lnv, b_lnv + 2 * ti + 1)
                sc.op(sc_raw.activation, sqs[:, :], xa, AF.Square,
                      accum_out=stats[:, 0:1], inc=(s_lns, 1))
                sc.wait(s_lnv, b_lnv + 2 * ti + 2)
                sc.op(sc_raw.activation, stats[:, 5:6], stats[:, 4:5], AF.Ln,
                      bias=LNEPS)
                sc.op(sc_raw.activation, stats[:, 5:6], stats[:, 5:6], AF.Exp,
                      scale=-0.5, inc=(s_lns, 1))
                for dc in range(8):
                    g = ti * 8 + dc
                    sc.wait(t_tr, b_tr + g + 1)
                    sc.op(sc_raw.copy,
                          xT_sb[:, dc * T + ti * 128:dc * T + ti * 128 + 128],
                          pst[:, (g % 2) * 512:(g % 2) * 512 + 128],
                          inc=(s_eb, 1))
                    if gate:
                        if dc == 0 and ti >= 1:
                            sc.wait(t_gm, b_gm + 8 * ti)
                        sc.op(sc_raw.copy, xtf[:, dc * 128:dc * 128 + 128],
                              pst[:, (g % 2) * 512:(g % 2) * 512 + 128],
                              inc=(s_ef, 1))
                if gate:
                    sc.wait(v_tp, b_tp + 2 * ti + 1)
                    sc.op(sc_raw.activation, stats[:, 11:12], stats[:, 10:11],
                          AF.Sigmoid, inc=(s_sg, 1))

        @blk.tensor
        def _(t):
            for ti in range(32):
                t.wait_ge(v_ap.h, b_ap + ti + 1)
                for dc in range(8):
                    g = ti * 8 + dc
                    if g >= 2:
                        t.wait_ge(s_eb.h, b_eb + g - 1)
                        if gate:
                            t.wait_ge(s_ef.h, b_ef + g - 1)
                    mm = t.transpose(
                        pst[:, (g % 2) * 512:(g % 2) * 512 + 128],
                        xln[:, (ti % 2) * 1024 + dc * 128:(ti % 2) * 1024 + dc * 128 + 128],
                        k.ident_f32[:, :])
                    mm.then_inc(t_tr.h, 1)
                    t_tr.v += 1
                if gate:
                    if ti >= 1:
                        t.wait_ge(v_tp.h, b_tp + 2 * ti)
                    for dc in range(8):
                        g = ti * 8 + dc
                        t.wait_ge(s_ef.h, b_ef + g + 1)
                        mm = t.matmul(psg[:, :],
                                      xtf[:, dc * 128:dc * 128 + 128],
                                      gw_w_sb[:, dc * 8:dc * 8 + 8],
                                      start=(dc == 0), stop=(dc == 7),
                                      skip_group_check=True)
                        mm.then_inc(t_gm.h, 1)
                        t_gm.v += 1


def emit_moe(k: K, xT_sb, gw_sb, w1_in, w2_in, w3_in, acc_hbm, tag):
    nc = k.nc
    w1_sb = k.sbuf(f"moe_w1_{tag}", [128, 8 * 2048], BF16)
    w2_sb = k.sbuf(f"moe_w2_{tag}", [128, 8 * 2048], BF16)
    w3_sb = k.sbuf(f"moe_w3_{tag}", [128, 16 * 1024], BF16)
    ht_sb = k.sbuf(f"moe_ht_{tag}", [128, 16 * 512], BF16)
    tsil = k.sbuf(f"moe_ts_{tag}", [128, 2 * 512], F32)
    accb = k.sbuf(f"moe_acc_{tag}", [128, 4 * 512], F32)
    ps12 = k.psum(f"moe_p12_{tag}", [128, 4 * 512], F32)
    ps3 = k.psum(f"moe_p3_{tag}", [128, 2 * 512], F32)

    for e in range(8):
        s_wl = k.sem("g_wld")
        s_als = [k.sem(f"s_accld{i}") for i in range(4)]
        t_m12 = k.sem("t_m12")
        t_m3 = k.sem("t_m3")
        s_ev = k.sem("s_ev")
        v_ht = k.sem("v_ht")
        v_ac = k.sem("v_acc")
        s_sts = [k.sem(f"s_st{i}") for i in range(4)]
        b_als = [s.v for s in s_als]
        b_sts = [s.v for s in s_sts]
        b_wl, b_12, b_3 = s_wl.v, t_m12.v, t_m3.v
        b_ev, b_ht, b_ac = s_ev.v, v_ht.v, v_ac.v

        with nc.Block() as blk:
            @blk.gpsimd
            def _(g_raw):
                g = k.eng(g_raw, "gpsimd")
                for c in range(8):
                    g.dma(w1_sb[:, c * 2048:(c + 1) * 2048],
                          w1_in[e, c * 128:(c + 1) * 128, :], s_wl)
                    g.dma(w2_sb[:, c * 2048:(c + 1) * 2048],
                          w2_in[e, c * 128:(c + 1) * 128, :], s_wl)
                w3v = w3_sb.rearrange("p (ht d) -> p ht d", d=1024)
                w3s = w3_in[e].rearrange("(ht p) d -> p ht d", p=128)
                for c in range(4):
                    g.dma(w3v[:, 4 * c:4 * c + 4, :], w3s[:, 4 * c:4 * c + 4, :],
                          s_wl)

            @blk.tensor
            def _(t):
                t.wait_ge(s_wl.h, s_wl.v)
                for tb in range(8):
                    for ht in range(16):
                        n = tb * 16 + ht
                        if n >= 2:
                            t.wait_ge(v_ht.h, b_ht + n - 1)
                        for kt in range(8):
                            t.matmul(ps12[:, (n % 2) * 512:(n % 2) * 512 + 512],
                                     w1_sb[:, kt * 2048 + ht * 128:kt * 2048 + ht * 128 + 128],
                                     xT_sb[:, kt * T + tb * 512:kt * T + tb * 512 + 512],
                                     start=(kt == 0), stop=(kt == 7),
                                     skip_group_check=True)
                        for kt in range(8):
                            mm = t.matmul(
                                ps12[:, (2 + n % 2) * 512:(2 + n % 2) * 512 + 512],
                                w2_sb[:, kt * 2048 + ht * 128:kt * 2048 + ht * 128 + 128],
                                xT_sb[:, kt * T + tb * 512:kt * T + tb * 512 + 512],
                                start=(kt == 0), stop=(kt == 7),
                                skip_group_check=True)
                        mm.then_inc(t_m12.h, 1)
                        t_m12.v += 1
                    t.wait_ge(v_ht.h, b_ht + 16 * (tb + 1))
                    for tt in range(4):
                        for db in range(2):
                            m = tb * 8 + tt * 2 + db
                            if m >= 2:
                                t.wait_ge(v_ac.h, b_ac + m - 1)
                            for ht in range(16):
                                mm = t.matmul(
                                    ps3[:, (m % 2) * 512:(m % 2) * 512 + 512],
                                    ht_sb[:, ht * 512 + tt * 128:ht * 512 + tt * 128 + 128],
                                    w3_sb[:, ht * 1024 + db * 512:ht * 1024 + db * 512 + 512],
                                    start=(ht == 0), stop=(ht == 15),
                                    skip_group_check=True)
                            mm.then_inc(t_m3.h, 1)
                            t_m3.v += 1

            @blk.scalar
            def _(sc_raw):
                sc = k.eng(sc_raw, "scalar")
                for tb in range(8):
                    for ht in range(16):
                        n = tb * 16 + ht
                        sc.wait(t_m12, b_12 + n + 1)
                        if n >= 2:
                            sc.wait(v_ht, b_ht + n - 1)
                        sc.op(sc_raw.activation,
                              tsil[:, (n % 2) * 512:(n % 2) * 512 + 512],
                              ps12[:, (n % 2) * 512:(n % 2) * 512 + 512],
                              AF.Silu, inc=(s_ev, 1))

            def acc_slice(m):
                tb2, r = divmod(m, 8)
                tt2, db2 = divmod(r, 2)
                tg2 = tb2 * 4 + tt2
                return acc_hbm[tg2 * 128:(tg2 + 1) * 128,
                               db2 * 512:db2 * 512 + 512]

            @blk.sync
            def _(s_raw):
                s = k.eng(s_raw, "sync")
                if e > 0:
                    for m in range(4):
                        s.dma(accb[:, (m % 4) * 512:(m % 4) * 512 + 512],
                              acc_slice(m), s_als[m % 4])
                for m in range(64):
                    s.wait(v_ac, b_ac + m + 1)
                    s.dma(acc_slice(m),
                          accb[:, (m % 4) * 512:(m % 4) * 512 + 512], s_sts[m % 4])
                    if e > 0 and m + 4 < 64:
                        s.wait(s_sts[m % 4], b_sts[m % 4] + 16 * (m // 4 + 1))
                        s.dma(accb[:, (m % 4) * 512:(m % 4) * 512 + 512],
                              acc_slice(m + 4), s_als[m % 4])
                for ss in s_sts:
                    s_raw.wait_ge(ss.h, ss.v)

            @blk.vector
            def _(v_raw):
                v = k.eng(v_raw, "vector")
                for tb in range(8):
                    for ht in range(16):
                        n = tb * 16 + ht
                        v.wait(s_ev, b_ev + n + 1)
                        if ht == 0 and tb >= 1:
                            v.wait(t_m3, b_3 + 8 * tb)
                        v.op(v_raw.tensor_tensor,
                             ht_sb[:, ht * 512:(ht + 1) * 512],
                             tsil[:, (n % 2) * 512:(n % 2) * 512 + 512],
                             ps12[:, (2 + n % 2) * 512:(2 + n % 2) * 512 + 512],
                             ALU.mult, inc=(v_ht, 1))
                    for tt in range(4):
                        for db in range(2):
                            m = tb * 8 + tt * 2 + db
                            tg = tb * 4 + tt
                            v.wait(t_m3, b_3 + m + 1)
                            if e == 0 and m >= 4:
                                v.wait(s_sts[m % 4], b_sts[m % 4] + 16 * (m // 4))
                            gcol = gw_sb[:, tg * 8 + e:tg * 8 + e + 1]
                            if e == 0:
                                v.op(v_raw.tensor_scalar,
                                     accb[:, (m % 4) * 512:(m % 4) * 512 + 512],
                                     ps3[:, (m % 2) * 512:(m % 2) * 512 + 512],
                                     gcol, None, ALU.mult, inc=(v_ac, 1))
                            else:
                                v.wait(s_als[m % 4], b_als[m % 4] + 16 * (m // 4 + 1))
                                v.op(v_raw.scalar_tensor_tensor,
                                     accb[:, (m % 4) * 512:(m % 4) * 512 + 512],
                                     ps3[:, (m % 2) * 512:(m % 2) * 512 + 512],
                                     gcol,
                                     accb[:, (m % 4) * 512:(m % 4) * 512 + 512],
                                     ALU.mult, ALU.add, inc=(v_ac, 1))



def emit_add3_ln(k: K, a_hbm, b_hbm, c_hbm, out_hbm, tag):
    nc = k.nc
    s_xls = [k.sem("s_xld0"), k.sem("s_xld1")]
    s_lnv, s_lns = k.sem("ln_v"), k.sem("ln_s")
    sout = k.sem("v_out")
    ssts = [k.sem("g_st0"), k.sem("g_st1")]
    b_xls = [s.v for s in s_xls]
    b_sts = [s.v for s in ssts]
    b_lnv, b_lns, b_out = s_lnv.v, s_lns.v, sout.v
    aa = k.sbuf(f"a3_a_{tag}", [128, 2 * 1024], F32)
    bb = k.sbuf(f"a3_b_{tag}", [128, 2 * 1024], F32)
    cc = k.sbuf(f"a3_c_{tag}", [128, 2 * 1024], F32)
    vt = k.sbuf(f"a3_v_{tag}", [128, 2 * 1024], F32)
    sqs = k.sbuf(f"a3_sq_{tag}", [128, 1024], F32)
    stats = k.sbuf(f"a3_st_{tag}", [128, 8], F32)
    outt = k.sbuf(f"a3_o_{tag}", [128, 2 * 1024], F32)

    with nc.Block() as blk:
        @blk.sync
        def _(s_raw):
            s = k.eng(s_raw, "sync")
            for ti in range(32):
                if ti >= 2:
                    s_raw.wait_ge(s_lnv.h, b_lnv + 2 * (ti - 2) + 1)
                sl = (ti % 2) * 1024
                s.dma(aa[:, sl:sl + 1024], a_hbm[ti * 128:(ti + 1) * 128, :],
                      s_xls[ti % 2])
                s.dma(bb[:, sl:sl + 1024], b_hbm[ti * 128:(ti + 1) * 128, :],
                      s_xls[ti % 2])
                s.dma(cc[:, sl:sl + 1024], c_hbm[ti * 128:(ti + 1) * 128, :],
                      s_xls[ti % 2])

        @blk.vector
        def _(v_raw):
            v = k.eng(v_raw, "vector")
            for ti in range(32):
                sl = (ti % 2) * 1024
                va = vt[:, sl:sl + 1024]
                v.wait(s_xls[ti % 2], b_xls[ti % 2] + 48 * (ti // 2 + 1))
                v.op(v_raw.tensor_tensor, va, aa[:, sl:sl + 1024],
                     bb[:, sl:sl + 1024], ALU.add)
                v.op(v_raw.tensor_tensor, va, va, cc[:, sl:sl + 1024], ALU.add)
                v.op(v_raw.reduce_sum, stats[:, 1:2], va, axis=AX.X,
                     inc=(s_lnv, 1))
                v.wait(s_lns, b_lns + 2 * ti + 1)
                v.op(v_raw.tensor_scalar, stats[:, 2:3], stats[:, 1:2],
                     1.0 / D, None, ALU.mult, dep=True)
                v.op(v_raw.tensor_scalar, stats[:, 3:4], stats[:, 2:3],
                     stats[:, 2:3], None, ALU.mult, dep=True)
                v.op(v_raw.tensor_scalar, stats[:, 4:5], stats[:, 0:1],
                     1.0 / D, None, ALU.mult)
                v.op(v_raw.tensor_tensor, stats[:, 4:5], stats[:, 4:5],
                     stats[:, 3:4], ALU.subtract, inc=(s_lnv, 1), dep=True)
                v.wait(s_lns, b_lns + 2 * ti + 2)
                if ti >= 2:
                    v.wait(ssts[ti % 2], b_sts[ti % 2] + 16 * (ti // 2))
                v.op(v_raw.tensor_scalar, outt[:, sl:sl + 1024], va,
                     stats[:, 2:3], stats[:, 5:6], ALU.subtract, ALU.mult,
                     inc=(sout, 1), dep=True)

        @blk.scalar
        def _(sc_raw):
            sc = k.eng(sc_raw, "scalar")
            for ti in range(32):
                sc.wait(s_lnv, b_lnv + 2 * ti + 1)
                sc.op(sc_raw.activation, sqs[:, :],
                      vt[:, (ti % 2) * 1024:(ti % 2) * 1024 + 1024],
                      AF.Square, accum_out=stats[:, 0:1], inc=(s_lns, 1))
                sc.wait(s_lnv, b_lnv + 2 * ti + 2)
                sc.op(sc_raw.activation, stats[:, 5:6], stats[:, 4:5], AF.Ln,
                      bias=LNEPS)
                sc.op(sc_raw.activation, stats[:, 5:6], stats[:, 5:6], AF.Exp,
                      scale=-0.5, inc=(s_lns, 1))

        @blk.gpsimd
        def _(g_raw):
            g = k.eng(g_raw, "gpsimd")
            for ti in range(32):
                g.wait(sout, b_out + ti + 1)
                g.dma(out_hbm[ti * 128:(ti + 1) * 128, :],
                      outt[:, (ti % 2) * 1024:(ti % 2) * 1024 + 1024],
                      ssts[ti % 2])
            for ss in ssts:
                g_raw.wait_ge(ss.h, ss.v)


def body_l0ffn(k, x1_in, wq_in, wk_in, wv_in, wo_in, bo_in, l1s_in, l1b_in,
               l2s_in, l2b_in, gw_in, w1_in, w2_in, w3_in, x2_out, pfx="tf"):
    nc = k.nc
    qt_hbm = k.dram(f"{pfx}_scr_qt", [D, T], BF16)
    kt_hbm = k.dram(f"{pfx}_scr_kt", [D, T], BF16)
    ctxT_hbm = k.dram(f"{pfx}_scr_ctxT", [D, T], BF16)
    u_hbm = k.dram(f"{pfx}_scr_u", [T, D], F32)
    acc_hbm = k.dram(f"{pfx}_scr_acc", [T, D], F32)

    with contextlib.ExitStack() as ph:
        ln1s_bc = ph.enter_context(nc.sbuf_tensor("ln1s_bc", [128, 1024], F32))
        ln1b_bc = ph.enter_context(nc.sbuf_tensor("ln1b_bc", [128, 1024], F32))
        with k.scope():
            bcast_rows(k, [
                (l1s_in.rearrange("(o d) -> o d", o=1), ln1s_bc),
                (l1b_in.rearrange("(o d) -> o d", o=1), ln1b_bc),
            ], "l1")
        v_sb = ph.enter_context(nc.sbuf_tensor("vaug1", [128, 32 * 1040], BF16))
        cbias = ph.enter_context(nc.sbuf_tensor("cbias1", [128, 16], F32))
        with contextlib.ExitStack() as phx:
            xT = phx.enter_context(nc.sbuf_tensor("xT1", [128, 8 * T], BF16))
            with k.scope():
                emit_ln_transpose(k, x1_in[:, :], xT, "l1",
                                  ln_scale_bc=ln1s_bc[:, :],
                                  ln_shift_bc=ln1b_bc[:, :])

            wq_sb = phx.enter_context(nc.sbuf_tensor("wq_sb", [128, 8 * D], BF16))
            wk_sb = phx.enter_context(nc.sbuf_tensor("wk_sb", [128, 8 * D], BF16))
            wv_sb = phx.enter_context(nc.sbuf_tensor("wv_sb", [128, 8 * D], BF16))
            s_wl = k.sem("g_wld")
            with nc.Block() as blk:
                @blk.gpsimd
                def _(g_raw):
                    g = k.eng(g_raw, "gpsimd")
                    for kt in range(8):
                        g.dma(wq_sb[:, kt * D:(kt + 1) * D],
                              wq_in[kt * 128:(kt + 1) * 128, :], s_wl)
                        g.dma(wk_sb[:, kt * D:(kt + 1) * D],
                              wk_in[kt * 128:(kt + 1) * 128, :], s_wl)
                        g.dma(wv_sb[:, kt * D:(kt + 1) * D],
                              wv_in[kt * 128:(kt + 1) * 128, :], s_wl)
                    g_raw.wait_ge(s_wl.h, s_wl.v)

            def lhsT_qk(kind, kt, mb):
                w = wq_sb if kind == "q" else wk_sb
                return w[:, kt * D + mb * 128:kt * D + mb * 128 + 128]

            def rhs_v(kt, nb):
                return wv_sb[:, kt * D + nb * 512:kt * D + nb * 512 + 512]

            with k.scope():
                emit_qkv(k, xT, lhsT_qk, rhs_v, qt_hbm, kt_hbm, v_sb, "a1")
        with k.scope():
            emit_att_bounds(k, qt_hbm, kt_hbm, cbias, "a1")
        with k.scope():
            emit_attention(k, qt_hbm, kt_hbm, v_sb, cbias, ctxT_hbm, "a1")

    with contextlib.ExitStack() as ph3:
        wo_sb = ph3.enter_context(nc.sbuf_tensor("wo_sb", [128, 8 * D], BF16))
        bo_bc = ph3.enter_context(nc.sbuf_tensor("bo_bc", [128, 1024], F32))
        with k.scope():
            bcast_rows(k, [(bo_in.rearrange("(o d) -> o d", o=1), bo_bc)], "bo")
        s_wl2 = k.sem("g_wld")
        with nc.Block() as blk:
            @blk.gpsimd
            def _(g_raw):
                g = k.eng(g_raw, "gpsimd")
                for kt in range(8):
                    g.dma(wo_sb[:, kt * D:(kt + 1) * D],
                          wo_in[kt * 128:(kt + 1) * 128, :], s_wl2)
                g_raw.wait_ge(s_wl2.h, s_wl2.v)

        def rhs_o(kt, nb):
            return wo_sb[:, kt * D + nb * 512:kt * D + nb * 512 + 512]

        with k.scope():
            emit_proj_residual_ln(k, ctxT_hbm, rhs_o, u_hbm, "a1",
                                  resid_hbm=x1_in[:, :], bias_bc=bo_bc[:, :],
                                  do_ln=False)

    with contextlib.ExitStack() as ph4:
        ln2s_bc = ph4.enter_context(nc.sbuf_tensor("ln2s_bc", [128, 1024], F32))
        ln2b_bc = ph4.enter_context(nc.sbuf_tensor("ln2b_bc", [128, 1024], F32))
        with k.scope():
            bcast_rows(k, [
                (l2s_in.rearrange("(o d) -> o d", o=1), ln2s_bc),
                (l2b_in.rearrange("(o d) -> o d", o=1), ln2b_bc),
            ], "l2")
        xT2 = ph4.enter_context(nc.sbuf_tensor("xT2", [128, 8 * T], BF16))
        gw_w_sb = ph4.enter_context(nc.sbuf_tensor("gw_w_sb", [128, 8 * 8], F32))
        gw_sb = ph4.enter_context(nc.sbuf_tensor("gw_sb", [128, 8 * 32], F32))
        s_gl = k.sem("s_gld")
        with nc.Block() as blk:
            @blk.sync
            def _(s_raw):
                s = k.eng(s_raw, "sync")
                gv = gw_w_sb.rearrange("p (kt e) -> p kt e", e=8)
                gs = gw_in.rearrange("(kt p) e -> p kt e", p=128)
                s.dma(gv[:, 0:4, :], gs[:, 0:4, :], s_gl)
                s.dma(gv[:, 4:8, :], gs[:, 4:8, :], s_gl)
                s_raw.wait_ge(s_gl.h, s_gl.v)
        with k.scope():
            emit_ln_transpose(k, u_hbm[:, :], xT2, "l2",
                              ln_scale_bc=ln2s_bc[:, :], ln_shift_bc=ln2b_bc[:, :],
                              gate=(gw_w_sb, gw_sb))
        with k.scope():
            emit_moe(k, xT2, gw_sb, w1_in, w2_in, w3_in, acc_hbm, "moe")

    with k.scope():
        emit_add3_ln(k, x1_in[:, :], u_hbm[:, :], acc_hbm[:, :], x2_out, "f")


def build_l0ffn():
    k = K("l0ffn")
    x1_in = k.dram("x1_in", [T, D], F32, kind="ExternalInput")
    wq_in = k.dram("wq_in", [D, D], F32, kind="ExternalInput")
    wk_in = k.dram("wk_in", [D, D], F32, kind="ExternalInput")
    wv_in = k.dram("wv_in", [D, D], F32, kind="ExternalInput")
    wo_in = k.dram("wo_in", [D, D], F32, kind="ExternalInput")
    bo_in = k.dram("bo_in", [D], F32, kind="ExternalInput")
    l1s_in = k.dram("ln1s_in", [D], F32, kind="ExternalInput")
    l1b_in = k.dram("ln1b_in", [D], F32, kind="ExternalInput")
    l2s_in = k.dram("ln2s_in", [D], F32, kind="ExternalInput")
    l2b_in = k.dram("ln2b_in", [D], F32, kind="ExternalInput")
    gw_in = k.dram("gatew_in", [D, 8], F32, kind="ExternalInput")
    w1_in = k.dram("moew1_in", [8, 1024, 2048], F32, kind="ExternalInput")
    w2_in = k.dram("moew2_in", [8, 1024, 2048], F32, kind="ExternalInput")
    w3_in = k.dram("moew3_in", [8, 2048, 1024], F32, kind="ExternalInput")
    x2_out = k.dram("x2_out", [T, D], F32, kind="ExternalOutput")
    declare_and_load_consts(k)
    body_l0ffn(k, x1_in, wq_in, wk_in, wv_in, wo_in, bo_in, l1s_in, l1b_in,
               l2s_in, l2b_in, gw_in, w1_in, w2_in, w3_in, x2_out)
    return k


def run_l0ffn(x1_full, wq, wk, wv, wo, bo, l1s, l1b, l2s, l2b, gw, w1, w2, w3):
    k = build_l0ffn()
    consts = host_consts()
    in_maps = []
    for c in range(NCORES):
        m = dict(consts)
        m["x1_in"] = np.ascontiguousarray(x1_full[c])
        m.update(wq_in=wq, wk_in=wk, wv_in=wv, wo_in=wo, bo_in=bo,
                 ln1s_in=l1s, ln1b_in=l1b, ln2s_in=l2s, ln2b_in=l2b,
                 gatew_in=gw, moew1_in=w1, moew2_in=w2, moew3_in=w3)
        in_maps.append(m)
    res = run_bass_kernel_spmd(k.nc, in_maps, core_ids=list(range(NCORES)))
    k.close()
    return np.stack([res.results[c]["x2_out"] for c in range(NCORES)])



def build_full():
    k = K("full")

    def fin(name, shape):
        return k.dram(name, shape, F32, kind="ExternalInput")

    x_in = fin("L0_attn_X", [T, D])
    a_q = fin("L0_attn_q_proj", [D, D])
    a_k = fin("L0_attn_k_proj", [D, D])
    a_v = fin("L0_attn_v_proj", [D, D])
    a_o = fin("L0_attn_o_proj", [D, D])
    f_l1s = fin("L0_ffn_ln1_scale", [D])
    f_l1b = fin("L0_ffn_ln1_shift", [D])
    f_wq = fin("L0_ffn_wq", [D, D])
    f_wk = fin("L0_ffn_wk", [D, D])
    f_wv = fin("L0_ffn_wv", [D, D])
    f_wo = fin("L0_ffn_wo", [D, D])
    f_bo = fin("L0_ffn_bo", [D])
    f_l2s = fin("L0_ffn_ln2_scale", [D])
    f_l2b = fin("L0_ffn_ln2_shift", [D])
    f_gw = fin("L0_ffn_gate_w", [D, 8])
    f_w1 = fin("L0_ffn_w1", [8, 1024, 2048])
    f_w2 = fin("L0_ffn_w2", [8, 1024, 2048])
    f_w3 = fin("L0_ffn_w3", [8, 2048, 1024])
    m_win = fin("L1_attn_W_in", [4096, 1024])
    m_cw = fin("L1_attn_conv_w", [2048, 4])
    m_cb = fin("L1_attn_conv_b", [2048])
    m_wx = fin("L1_attn_W_xproj", [96, 2048])
    m_wdt = fin("L1_attn_W_dt", [2048, 64])
    m_bdt = fin("L1_attn_b_dt", [2048])
    m_alog = fin("L1_attn_A_log", [2048, 16])
    m_dp = fin("L1_attn_Dp", [2048])
    m_wout = fin("L1_attn_W_out", [1024, 2048])
    t_w1 = fin("L1_ffn_w1_w", [4096, 1024])
    t_s1 = fin("L1_ffn_w1_s", [4096])
    t_w2 = fin("L1_ffn_w2_w", [4096, 1024])
    t_s2 = fin("L1_ffn_w2_s", [4096])
    t_h1 = fin("L1_ffn_h1_w", [4096, 4096])
    t_sh1 = fin("L1_ffn_h1_s", [4096])
    t_h2 = fin("L1_ffn_h2_w", [4096, 4096])
    t_sh2 = fin("L1_ffn_h2_s", [4096])
    t_w3 = fin("L1_ffn_w3_w", [1024, 4096])
    t_s3 = fin("L1_ffn_w3_s", [1024])
    sel_in = k.dram("c_sel_bf", [32, 32 * 128], BF16, kind="ExternalInput")
    y_out = k.dram("y_out", [T, D], F32, kind="ExternalOutput")

    x1 = k.dram("scr_x1", [T, D], F32)
    x2 = k.dram("scr_x2", [T, D], F32)
    x3 = k.dram("scr_x3", [T, D], F32)

    declare_and_load_consts(k)
    body_l0attn(k, x_in, a_q, a_k, a_v, a_o, x1)
    body_l0ffn(k, x1, f_wq, f_wk, f_wv, f_wo, f_bo, f_l1s, f_l1b, f_l2s,
               f_l2b, f_gw, f_w1, f_w2, f_w3, x2)
    body_mamba(k, x2, m_win, m_cw, m_cb, m_wx, m_wdt, m_bdt, m_alog, m_dp,
               m_wout, sel_in, x3)
    body_l1ffn(k, x3, t_w1, t_w2, t_h1, t_h2, t_w3, t_s1, t_s2, t_sh1, t_sh2,
               t_s3, y_out)
    return k


def kernel(**inputs):
    g = {n: _f32(v) for n, v in inputs.items()}
    k = build_full()
    consts = host_consts()
    consts["c_sel_bf"] = host_sel_const()
    in_maps = []
    for c in range(NCORES):
        m = dict(consts)
        for n, v in g.items():
            if n == "L0_attn_X":
                m[n] = np.ascontiguousarray(v[c])
            else:
                m[n] = v
        in_maps.append(m)
    res = run_bass_kernel_spmd(k.nc, in_maps, core_ids=list(range(NCORES)))
    k.close()
    return np.stack([res.results[c]["y_out"] for c in range(NCORES)])


def kernel_unfused(**inputs):
    g = {n: _f32(v) for n, v in inputs.items()}
    x1 = run_l0attn(g["L0_attn_X"], g["L0_attn_q_proj"], g["L0_attn_k_proj"],
                    g["L0_attn_v_proj"], g["L0_attn_o_proj"])
    x2 = run_l0ffn(x1, g["L0_ffn_wq"], g["L0_ffn_wk"], g["L0_ffn_wv"],
                   g["L0_ffn_wo"], g["L0_ffn_bo"], g["L0_ffn_ln1_scale"],
                   g["L0_ffn_ln1_shift"], g["L0_ffn_ln2_scale"],
                   g["L0_ffn_ln2_shift"], g["L0_ffn_gate_w"], g["L0_ffn_w1"],
                   g["L0_ffn_w2"], g["L0_ffn_w3"])
    x3 = run_mamba(x2, g["L1_attn_W_in"], g["L1_attn_conv_w"],
                   g["L1_attn_conv_b"], g["L1_attn_W_xproj"], g["L1_attn_W_dt"],
                   g["L1_attn_b_dt"], g["L1_attn_A_log"], g["L1_attn_Dp"],
                   g["L1_attn_W_out"])
    y = run_l1ffn(x3, g["L1_ffn_w1_w"], g["L1_ffn_w1_s"], g["L1_ffn_w2_w"],
                  g["L1_ffn_w2_s"], g["L1_ffn_h1_w"], g["L1_ffn_h1_s"],
                  g["L1_ffn_h2_w"], g["L1_ffn_h2_s"], g["L1_ffn_w3_w"],
                  g["L1_ffn_w3_s"])
    return y


# version 07
# speedup vs baseline: 1.3028x; 1.0566x over previous
import contextlib
import math

import numpy as np
import ml_dtypes

import concourse.bass as bass
import concourse.mybir as mybir
from concourse.bass_utils import run_bass_kernel_spmd

F32 = mybir.dt.float32
BF16 = mybir.dt.bfloat16
AF = mybir.ActivationFunctionType
ALU = mybir.AluOpType
AX = mybir.AxisListType

NCORES = 8
T = 4096
D = 1024
NH = 16
DK = 64
ATT_SCALE = 0.125
LNEPS = 1e-5

BF = ml_dtypes.bfloat16



class SemT:
    def __init__(self, handle, name):
        self.h = handle
        self.v = 0
        self.name = name


class Eng:

    def __init__(self, raw, self_sem, default_dep=False):
        self.e = raw
        self.sem = self_sem
        self.default_dep = default_dep
        self.last = None
        self.last_inc = None

    def _pre(self):
        if self.last is not None:
            if self.last_inc is None:
                self.sem.v += 1
                self.last.then_inc(self.sem.h, 1)
                self.last_inc = (self.sem.h, self.sem.v)
            h, v = self.last_inc
            self.e.wait_ge(h, v)
            self.last = None
            self.last_inc = None

    def op(self, f, *a, inc=None, dep=None, **kw):
        if dep is None:
            dep = self.default_dep
        if dep:
            self._pre()
        inst = f(*a, **kw)
        self.last = inst
        if inc is not None:
            st, n = inc
            inst.then_inc(st.h, n)
            st.v += n
            self.last_inc = (st.h, st.v)
        else:
            self.last_inc = None
        return inst

    def wait(self, st: SemT, v: int):
        self.e.wait_ge(st.h, v)

    def dma(self, dst, src, st: SemT, dep=False):
        if dep:
            self._pre()
        inst = self.e.dma_start(dst, src)
        inst.then_inc(st.h, 16)
        st.v += 16
        return inst


class K:

    def __init__(self, name):
        self.nc = bass.Bass("TRN2", target_bir_lowering=False)
        self.es = contextlib.ExitStack()
        self.cur = self.es
        self.name = name
        self.sems = {}

    @contextlib.contextmanager
    def scope(self):
        prev = self.cur
        st = contextlib.ExitStack()
        self.cur = st
        try:
            yield
        finally:
            self.cur = prev
            st.close()

    def sem(self, name) -> SemT:
        if name not in self.sems:
            h = self.es.enter_context(self.nc.semaphore(name))
            self.sems[name] = SemT(h, name)
        return self.sems[name]

    def sbuf(self, name, shape, dt):
        return self.cur.enter_context(self.nc.sbuf_tensor(name, shape, dt))

    def psum(self, name, shape, dt=F32):
        return self.cur.enter_context(self.nc.psum_tensor(name, shape, dt))

    def dram(self, name, shape, dt, kind=None):
        if kind is None:
            return self.nc.dram_tensor(name, shape, dt)
        return self.nc.dram_tensor(name, shape, dt, kind=kind)

    def eng(self, raw, which) -> Eng:
        return Eng(raw, self.sem("self_" + which),
                   default_dep=which.startswith("gpsimd"))

    def close(self):
        self.es.close()


def host_consts():
    ident_bf = np.eye(128, dtype=BF)
    ident_f32 = np.eye(128, dtype=np.float32)
    ones_f32 = np.ones((1, 128), np.float32)
    ones_col_bf = np.ones((128, 1), dtype=BF)
    masks = np.zeros((4, 128, 512), np.float32)
    s = np.arange(128)[:, None]
    q = np.arange(512)[None, :]
    for m in range(4):
        masks[m] = (q >= s + 128 * m).astype(np.float32)
    r0ones = np.zeros((128, 128), dtype=BF)
    r0ones[0, :] = 1.0
    return {
        "c_ident_bf": ident_bf,
        "c_ident_f32": ident_f32,
        "c_ones_f32": ones_f32,
        "c_ones_col_bf": ones_col_bf,
        "c_masks_bf": masks.astype(BF),
        "c_r0ones_bf": r0ones,
    }


def declare_and_load_consts(k: K):
    ci = k.dram("c_ident_bf", [128, 128], BF16, kind="ExternalInput")
    cif = k.dram("c_ident_f32", [128, 128], F32, kind="ExternalInput")
    co = k.dram("c_ones_f32", [1, 128], F32, kind="ExternalInput")
    cob = k.dram("c_ones_col_bf", [128, 1], BF16, kind="ExternalInput")
    cm = k.dram("c_masks_bf", [4, 128, 512], BF16, kind="ExternalInput")
    cr0 = k.dram("c_r0ones_bf", [128, 128], BF16, kind="ExternalInput")
    k.ident_bf = k.sbuf("ident_bf_sb", [128, 128], BF16)
    k.ident_f32 = k.sbuf("ident_f32_sb", [128, 128], F32)
    k.ones_f32 = k.sbuf("ones_f32_sb", [1, 128], F32)
    k.ones_col_bf = k.sbuf("ones_col_bf_sb", [128, 1], BF16)
    k.masks_bf = k.sbuf("masks_bf_sb", [128, 4 * 512], BF16)
    k.r0ones = k.sbuf("r0ones_sb", [128, 128], BF16)
    ACT_CONSTS = [LNEPS, -0.5, ATT_SCALE, -1.0, 0.5]
    k.cst_sb = k.sbuf("actcst_sb", [128, len(ACT_CONSTS)], F32)
    sc = k.sem("s_cst")
    with k.nc.Block() as blk:
        @blk.sync
        def _(s_raw):
            s = k.eng(s_raw, "sync")
            s.dma(k.ident_bf[:, :], ci[:, :], sc)
            s.dma(k.ident_f32[:, :], cif[:, :], sc)
            s.dma(k.ones_f32[:, :], co[:, :], sc)
            s.dma(k.ones_col_bf[:, :], cob[:, :], sc)
            s.dma(k.masks_bf.rearrange("p (m q) -> p m q", m=4),
                  cm.rearrange("m p q -> p m q"), sc)
            s.dma(k.r0ones[:, :], cr0[:, :], sc)
            s_raw.wait_ge(sc.h, sc.v)

        @blk.vector
        def _(v_raw):
            v = k.eng(v_raw, "vector")
            for i, val in enumerate(ACT_CONSTS):
                v.op(v_raw.memset, k.cst_sb[:, i:i + 1], val)
    for i, val in enumerate(ACT_CONSTS):
        k.nc.const_aps.register(F32, val, k.cst_sb[:, i:i + 1])


def bcast_rows(k: K, jobs, tag):
    nc = k.nc
    total = sum(src.shape[1] for src, dst in jobs)
    row_sb = k.sbuf(f"bcrow_{tag}", [1, total], F32)
    chunks = []
    off = 0
    loads = []
    for src, dst in jobs:
        N = src.shape[1]
        loads.append((row_sb[0:1, off:off + N], src))
        for c0 in range(0, N, 512):
            c1 = min(N, c0 + 512)
            chunks.append((row_sb[0:1, off + c0:off + c1], dst[:, c0:c1], c1 - c0))
        off += N
    ps = k.psum(f"bcps_{tag}", [128, 4 * 512], F32)
    sld, smm, sev = k.sem("s_xld"), k.sem("t_mm"), k.sem("s_ev")
    b_ld, b_mm, b_ev = sld.v, smm.v, sev.v
    with nc.Block() as blk:
        @blk.sync
        def _(s_raw):
            s = k.eng(s_raw, "sync")
            for dst, src in loads:
                s.dma(dst, src, sld)

        @blk.tensor
        def _(t):
            t.wait_ge(sld.h, b_ld + 16 * len(loads))
            for n, (src, dst, w) in enumerate(chunks):
                if n >= 4:
                    t.wait_ge(sev.h, b_ev + n - 3)
                mm = t.matmul(ps[:, (n % 4) * 512:(n % 4) * 512 + w],
                              k.ones_f32[0:1, :], src, start=True, stop=True)
                mm.then_inc(smm.h, 1)
                smm.v += 1

        @blk.scalar
        def _(sc_raw):
            sc = k.eng(sc_raw, "scalar")
            for n, (src, dst, w) in enumerate(chunks):
                sc.wait(smm, b_mm + n + 1)
                sc.op(sc_raw.copy, dst, ps[:, (n % 4) * 512:(n % 4) * 512 + w],
                      inc=(sev, 1))



def tw_sbuf(k: K, src_ap, dst_sb, R, C, tag):
    nc = k.nc
    smm, sev = k.sem("t_mm"), k.sem("s_ev")
    slds = [k.sem("g_wld0"), k.sem("g_wld1")]
    nrb, ncb = R // 128, C // 128
    wrow = k.sbuf(f"twrow_{tag}", [128, 2 * C], BF16)
    pst = k.psum(f"twps_{tag}", [128, 2 * 1024], BF16)
    b_lds = [s.v for s in slds]
    b_mm, b_ev = smm.v, sev.v

    with nc.Block() as blk:
        @blk.gpsimd
        def _(g_raw):
            g = k.eng(g_raw, "gpsimd")
            for rb in range(nrb):
                if rb >= 2:
                    g_raw.wait_ge(smm.h, b_mm + (rb - 1) * ncb)
                g.dma(wrow[:, (rb % 2) * C:(rb % 2) * C + C],
                      src_ap[rb * 128:(rb + 1) * 128, :], slds[rb % 2])

        @blk.tensor
        def _(t):
            n = 0
            for rb in range(nrb):
                t.wait_ge(slds[rb % 2].h, b_lds[rb % 2] + 16 * (rb // 2 + 1))
                for cb in range(ncb):
                    if n >= 2:
                        t.wait_ge(sev.h, b_ev + n - 1)
                    mm = t.transpose(
                        pst[:, (n % 2) * 1024:(n % 2) * 1024 + 128],
                        wrow[:, (rb % 2) * C + cb * 128:(rb % 2) * C + cb * 128 + 128],
                        k.ident_bf[:, :])
                    mm.then_inc(smm.h, 1)
                    smm.v += 1
                    n += 1

        @blk.scalar
        def _(sc_raw):
            sc = k.eng(sc_raw, "scalar")
            n = 0
            for rb in range(nrb):
                for cb in range(ncb):
                    sc.wait(smm, b_mm + n + 1)
                    sc.op(sc_raw.copy,
                          dst_sb[:, cb * R + rb * 128:cb * R + rb * 128 + 128],
                          pst[:, (n % 2) * 1024:(n % 2) * 1024 + 128],
                          inc=(sev, 1))
                    n += 1


def tx_sbuf(k: K, src_x_hbm, dst_sb, tag, Tn=T, Dn=D):
    nc = k.nc
    smm, sev, svv = k.sem("t_mm"), k.sem("s_ev"), k.sem("v_ev")
    slds = [k.sem("s_xld0"), k.sem("s_xld1")]
    ntt, ndt = Tn // 128, Dn // 128
    xrow = k.sbuf(f"txrow_{tag}", [128, 2 * Dn], F32)
    xrow_bf = k.sbuf(f"txrowb_{tag}", [128, 2 * Dn], BF16)
    pst = k.psum(f"txps_{tag}", [128, 2 * 1024], BF16)
    b_lds = [s.v for s in slds]
    b_mm, b_ev, b_vv = smm.v, sev.v, svv.v

    with nc.Block() as blk:
        @blk.sync
        def _(s_raw):
            s = k.eng(s_raw, "sync")
            for tt in range(ntt):
                if tt >= 2:
                    s_raw.wait_ge(svv.h, b_vv + tt - 1)
                s.dma(xrow[:, (tt % 2) * Dn:(tt % 2) * Dn + Dn],
                      src_x_hbm[tt * 128:(tt + 1) * 128, :], slds[tt % 2])

        @blk.vector
        def _(v_raw):
            v = k.eng(v_raw, "vector")
            for tt in range(ntt):
                v.wait(slds[tt % 2], b_lds[tt % 2] + 16 * (tt // 2 + 1))
                if tt >= 2:
                    v.wait(smm, b_mm + (tt - 1) * ndt)
                v.op(v_raw.tensor_copy,
                     xrow_bf[:, (tt % 2) * Dn:(tt % 2) * Dn + Dn],
                     xrow[:, (tt % 2) * Dn:(tt % 2) * Dn + Dn],
                     inc=(svv, 1))

        @blk.tensor
        def _(t):
            n = 0
            for tt in range(ntt):
                t.wait_ge(svv.h, b_vv + tt + 1)
                for dt in range(ndt):
                    if n >= 2:
                        t.wait_ge(sev.h, b_ev + n - 1)
                    mm = t.transpose(
                        pst[:, (n % 2) * 1024:(n % 2) * 1024 + 128],
                        xrow_bf[:, (tt % 2) * Dn + dt * 128:(tt % 2) * Dn + dt * 128 + 128],
                        k.ident_bf[:, :])
                    mm.then_inc(smm.h, 1)
                    smm.v += 1
                    n += 1

        @blk.scalar
        def _(sc_raw):
            sc = k.eng(sc_raw, "scalar")
            n = 0
            for tt in range(ntt):
                for dt in range(ndt):
                    sc.wait(smm, b_mm + n + 1)
                    sc.op(sc_raw.copy,
                          dst_sb[:, dt * Tn + tt * 128:dt * Tn + tt * 128 + 128],
                          pst[:, (n % 2) * 1024:(n % 2) * 1024 + 128],
                          inc=(sev, 1))
                    n += 1



def emit_qkv(k: K, xT_sb, lhsT_tile, rhs_tile_v, qt_hbm, kt_hbm, v_sb, tag):
    nc = k.nc
    smm, sev = k.sem("t_mm"), k.sem("s_ev")
    ssts = [k.sem(f"s_st{i}") for i in range(4)]
    svv = k.sem("v_ev")
    b_mm, b_ev, b_vv = smm.v, sev.v, svv.v
    b_sts = [s.v for s in ssts]
    pq = k.psum(f"qkvps_{tag}", [128, 4 * 512], F32)
    ev = k.sbuf(f"qkvev_{tag}", [128, 4 * 512], BF16)

    jobs = [("v", st, nb) for st in range(32) for nb in range(2)]
    jobs += [(kind, mb, nb) for kind in ("q", "k") for mb in range(8) for nb in range(8)]
    nstores_before = []
    ns = 0
    for kind, a, nb in jobs:
        nstores_before.append(ns)
        if kind != "v":
            ns += 1

    with nc.Block() as blk:
        @blk.vector
        def _(v_raw):
            v = k.eng(v_raw, "vector")
            v.op(v_raw.memset, v_sb[:, :], 1.0, inc=(svv, 1))

        @blk.tensor
        def _(t):
            t.wait_ge(svv.h, b_vv + 1)
            for n, (kind, a, nb) in enumerate(jobs):
                bank = n % 4
                if n >= 4:
                    t.wait_ge(sev.h, b_ev + n - 3)
                for kt in range(8):
                    if kind == "v":
                        mm = t.matmul(pq[:, bank * 512:bank * 512 + 512],
                                      xT_sb[:, kt * T + a * 128:kt * T + a * 128 + 128],
                                      rhs_tile_v(kt, nb),
                                      start=(kt == 0), stop=(kt == 7))
                    else:
                        mm = t.matmul(pq[:, bank * 512:bank * 512 + 512],
                                      lhsT_tile(kind, kt, a),
                                      xT_sb[:, kt * T + nb * 512:kt * T + nb * 512 + 512],
                                      start=(kt == 0), stop=(kt == 7))
                mm.then_inc(smm.h, 1)
                smm.v += 1

        @blk.scalar
        def _(sc_raw):
            sc = k.eng(sc_raw, "scalar")
            for n, (kind, a, nb) in enumerate(jobs):
                bank = n % 4
                sc.wait(smm, b_mm + n + 1)
                if kind == "v":
                    out = v_sb.rearrange("p (st h dv) -> p st h dv", st=32, dv=65)[
                        :, a, nb * 8:nb * 8 + 8, 0:64]
                    sc.op(sc_raw.copy, out, pq[:, bank * 512:bank * 512 + 512],
                          inc=(sev, 1))
                else:
                    j = nstores_before[n]
                    if j >= 4:
                        sc.wait(ssts[j % 4], b_sts[j % 4] + 16 * (j // 4))
                    sc.op(sc_raw.copy, ev[:, bank * 512:bank * 512 + 512],
                          pq[:, bank * 512:bank * 512 + 512], inc=(sev, 1))

        @blk.sync
        def _(s_raw):
            s = k.eng(s_raw, "sync")
            j = 0
            for n, (kind, a, nb) in enumerate(jobs):
                if kind == "v":
                    continue
                bank = n % 4
                s.wait(sev, b_ev + n + 1)
                dst = qt_hbm if kind == "q" else kt_hbm
                s.dma(dst[a * 128:(a + 1) * 128, nb * 512:(nb + 1) * 512],
                      ev[:, bank * 512:bank * 512 + 512], ssts[j % 4])
                j += 1
            for ss in ssts:
                s_raw.wait_ge(ss.h, ss.v)



def emit_att_bounds(k: K, qt_hbm, kt_hbm, cbias_sb, tag):
    nc = k.nc
    smm, svv, ssc = k.sem("t_mm"), k.sem("v_ev"), k.sem("s_ev")
    slds = [k.sem("s_xld0"), k.sem("s_xld1")]
    b_lds = [s.v for s in slds]
    b_mm, b_vv, b_sc = smm.v, svv.v, ssc.v
    slab = k.sbuf(f"bnd_slab_{tag}", [128, 2 * T], BF16)
    sq = k.sbuf(f"bnd_sq_{tag}", [128, 2 * T], BF16)
    nmax = k.sbuf(f"bnd_nmax_{tag}", [1, 33], F32)
    cneg = k.sbuf(f"bnd_cneg_{tag}", [1, 16], F32)
    ps_n = k.psum(f"bnd_ps_{tag}", [1, 2 * 512], F32)
    ps_c = k.psum(f"bnd_psc_{tag}", [128, 16], F32)
    n_chunk = T // 512

    with nc.Block() as blk:
        @blk.sync
        def _(s_raw):
            s = k.eng(s_raw, "sync")
            for i in range(16):
                src = qt_hbm if i < 8 else kt_hbm
                r0 = (i % 8) * 128
                if i >= 2:
                    s_raw.wait_ge(ssc.h, b_sc + (i - 1))
                s.dma(slab[:, (i % 2) * T:(i % 2) * T + T], src[r0:r0 + 128, :],
                      slds[i % 2])

        @blk.scalar
        def _(sc_raw):
            sc = k.eng(sc_raw, "scalar")
            for i in range(16):
                sc.wait(slds[i % 2], b_lds[i % 2] + 16 * (i // 2 + 1))
                if i >= 2:
                    sc.wait(smm, b_mm + (i - 1) * 2 * n_chunk)
                sc.op(sc_raw.activation, sq[:, (i % 2) * T:(i % 2) * T + T],
                      slab[:, (i % 2) * T:(i % 2) * T + T], AF.Square,
                      inc=(ssc, 1))

        @blk.tensor
        def _(t):
            n = 0
            for i in range(16):
                t.wait_ge(ssc.h, b_sc + i + 1)
                for half in range(2):
                    for ch in range(n_chunk):
                        if n >= 2:
                            t.wait_ge(svv.h, b_vv + 2 * (n - 1))
                        mm = t.matmul(
                            ps_n[:, (n % 2) * 512:(n % 2) * 512 + 512],
                            k.ones_col_bf[half * 64:half * 64 + 64, 0:1],
                            sq[half * 64:half * 64 + 64,
                               (i % 2) * T + ch * 512:(i % 2) * T + ch * 512 + 512],
                            start=True, stop=True)
                        mm.then_inc(smm.h, 1)
                        smm.v += 1
                        n += 1

        @blk.vector
        def _(v_raw):
            v = k.eng(v_raw, "vector")
            v.op(v_raw.memset, nmax[:, :], 0.0)
            n = 0
            for i in range(16):
                for half in range(2):
                    hcol = (0 if i < 8 else 16) + (i % 8) * 2 + half
                    for ch in range(n_chunk):
                        v.wait(smm, b_mm + n + 1)
                        v.op(v_raw.reduce_max, nmax[0:1, 32:33],
                             ps_n[0:1, (n % 2) * 512:(n % 2) * 512 + 512],
                             axis=AX.X)
                        v.op(v_raw.tensor_tensor, nmax[0:1, hcol:hcol + 1],
                             nmax[0:1, hcol:hcol + 1], nmax[0:1, 32:33],
                             ALU.max, inc=(svv, 2), dep=True)
                        n += 1

    with nc.Block() as blk:
        @blk.vector
        def _(v_raw):
            v = k.eng(v_raw, "vector")
            v.op(v_raw.tensor_tensor, cneg[0:1, 0:16], nmax[0:1, 0:16],
                 nmax[0:1, 16:32], ALU.mult)

    with nc.Block() as blk:
        @blk.scalar
        def _(sc_raw):
            sc = k.eng(sc_raw, "scalar")
            sc.op(sc_raw.activation, cneg[0:1, :], cneg[0:1, :], AF.Sqrt)

    with nc.Block() as blk:
        @blk.tensor
        def _(t):
            t.matmul(ps_c[:, 0:16], k.ones_f32[0:1, :], cneg[0:1, 0:16],
                     start=True, stop=True).then_inc(smm.h, 1)
            smm.v += 1

    with nc.Block() as blk:
        @blk.vector
        def _(v_raw):
            v = k.eng(v_raw, "vector")
            v.op(v_raw.tensor_scalar, cbias_sb[:, 0:16], ps_c[:, 0:16],
                 -1.02 * ATT_SCALE, None, ALU.mult)



def emit_attention(k: K, qt_hbm, kt_hbm, v_sb, cbias_sb, ctxT_hbm, tag):
    nc = k.nc
    s_qks = [k.sem("s_qkld0"), k.sem("s_qkld1")]
    b_qks = [s.v for s in s_qks]
    s_sc = k.sem("t_sc")
    s_ex = k.sem("s_exp")
    s_mk = k.sem("v_msk")
    s_av = k.sem("t_av")
    s_rc = k.sem("v_rec")
    s_rb = k.sem("t_rbc")
    s_re = k.sem("s_rev")
    s_cm = k.sem("v_cmul")
    s_sts = [k.sem(f"g_cst{i}") for i in range(4)]
    b_sts = [s.v for s in s_sts]
    b_sc, b_ex, b_mk, b_av = s_sc.v, s_ex.v, s_mk.v, s_av.v
    b_rc, b_rb, b_re, b_cm = s_rc.v, s_rb.v, s_re.v, s_cm.v

    qkt = k.sbuf(f"qkt_{tag}", [128, 4 * T], BF16)
    expp = k.sbuf(f"expp_{tag}", [128, 4 * 512], BF16)
    rrec = k.sbuf(f"rrec_{tag}", [128, 2 * 512], BF16)
    rrecf = k.sbuf(f"rrecf_{tag}", [1, 2 * 512], F32)
    rbc = k.sbuf(f"rbc_{tag}", [64, 2 * 512], F32)
    ctxo = k.sbuf(f"ctxo_{tag}", [64, 4 * 512], BF16)
    ps_sc = k.psum(f"ps_sc_{tag}", [128, 4 * 512], F32)
    ps_cx = k.psum(f"ps_cx_{tag}", [65, 2 * 512], F32)
    ps_rb = k.psum(f"ps_rb_{tag}", [128, 2 * 512], F32)
    z_qk = k.sem("v_zqk")
    b_zq = z_qk.v

    def ntiles(qb):
        return 4 * qb + 4

    TPH = sum(ntiles(qb) for qb in range(8))
    wlist = [(h, qb) for h in range(NH) for qb in range(8)]
    cumtiles = [0]
    for (h, qb) in wlist:
        cumtiles.append(cumtiles[-1] + ntiles(qb))
    pm_of = {}
    pm = 0
    for w, (h, qb) in enumerate(wlist):
        for st in range(ntiles(qb)):
            if st >= 4 * qb:
                pm_of[(w, st)] = pm
                pm += 1
    NPART = pm

    v_aug = v_sb.rearrange("p (st h dv) -> p st h dv", st=32, dv=65)

    with nc.Block() as blk:
        @blk.sync
        def _(s_raw):
            s = k.eng(s_raw, "sync")
            for h in range(NH):
                if h >= 2:
                    s_raw.wait_ge(s_av.h, b_av + cumtiles[(h - 1) * 8])
                s.dma(qkt[0:64, (h % 2) * 2 * T:(h % 2) * 2 * T + T],
                      qt_hbm[h * 64:(h + 1) * 64, :], s_qks[h % 2])
                s.dma(qkt[0:64, (h % 2) * 2 * T + T:(h % 2) * 2 * T + 2 * T],
                      kt_hbm[h * 64:(h + 1) * 64, :], s_qks[h % 2])

        @blk.tensor
        def _(t):
            t.wait_ge(z_qk.h, b_zq + 1)
            for w, (h, qb) in enumerate(wlist):
                hb = h % 2
                if qb == 0:
                    t.wait_ge(s_qks[h % 2].h, b_qks[h % 2] + 32 * (h // 2 + 1))
                g0 = cumtiles[w]
                LA = 2
                nt = ntiles(qb)
                for j in range(nt + LA):
                    if j < nt:
                        st = j
                        g = g0 + st
                        if g >= 4:
                            t.wait_ge(s_ex.h, b_ex + g - 3)
                        mm = t.matmul(
                            ps_sc[:, (g % 4) * 512:(g % 4) * 512 + 512],
                            qkt[:, hb * 2 * T + T + st * 128:hb * 2 * T + T + st * 128 + 128],
                            qkt[:, hb * 2 * T + qb * 512:hb * 2 * T + qb * 512 + 512],
                            start=True, stop=True, skip_group_check=True)
                        mm.then_inc(s_sc.h, 1)
                        s_sc.v += 1
                    if j >= LA:
                        st = j - LA
                        g = g0 + st
                        if st == 0 and w >= 2:
                            t.wait_ge(s_cm.h, b_cm + w - 1)
                        if st >= 4 * qb:
                            t.wait_ge(s_mk.h, b_mk + pm_of[(w, st)] + 1)
                        else:
                            t.wait_ge(s_ex.h, b_ex + g + 1)
                        mm = t.matmul(
                            ps_cx[:, (w % 2) * 512:(w % 2) * 512 + 512],
                            v_aug[:, st, h, :],
                            expp[:, (g % 4) * 512:(g % 4) * 512 + 512],
                            start=(st == 0), stop=(st == nt - 1),
                            skip_group_check=True)
                        mm.then_inc(s_av.h, 1)
                        s_av.v += 1
                if w >= 1:
                    t.wait_ge(s_rc.h, b_rc + w)
                    if w >= 3:
                        t.wait_ge(s_re.h, b_re + w - 2)
                    wb = (w - 1) % 2
                    mm = t.matmul(ps_rb[:, wb * 512:wb * 512 + 512],
                                  k.r0ones[:, :],
                                  rrec[:, wb * 512:wb * 512 + 512],
                                  start=True, stop=True, skip_group_check=True)
                    mm.then_inc(s_rb.h, 1)
                    s_rb.v += 1
            t.wait_ge(s_rc.h, b_rc + 128)
            t.wait_ge(s_re.h, b_re + 126)
            wb = 127 % 2
            mm = t.matmul(ps_rb[:, wb * 512:wb * 512 + 512],
                          k.r0ones[:, :],
                          rrec[:, wb * 512:wb * 512 + 512],
                          start=True, stop=True, skip_group_check=True)
            mm.then_inc(s_rb.h, 1)
            s_rb.v += 1

        @blk.scalar
        def _(sc_raw):
            sc = k.eng(sc_raw, "scalar")
            for w, (h, qb) in enumerate(wlist):
                g0 = cumtiles[w]
                for st in range(ntiles(qb)):
                    g = g0 + st
                    sc.wait(s_sc, b_sc + g + 1)
                    if g >= 4:
                        sc.wait(s_av, b_av + g - 3)
                    sc.op(sc_raw.activation,
                          expp[:, (g % 4) * 512:(g % 4) * 512 + 512],
                          ps_sc[:, (g % 4) * 512:(g % 4) * 512 + 512],
                          AF.Exp, bias=cbias_sb[:, h:h + 1], scale=ATT_SCALE,
                          inc=(s_ex, 1))
                if w >= 1:
                    sc.wait(s_rb, b_rb + w)
                    if w >= 3:
                        sc.wait(s_cm, b_cm + w - 2)
                    wb = (w - 1) % 2
                    sc.op(sc_raw.copy, rbc[:, wb * 512:wb * 512 + 512],
                          ps_rb[0:64, wb * 512:wb * 512 + 512], inc=(s_re, 1))
            sc.wait(s_rb, b_rb + 128)
            sc.wait(s_cm, b_cm + 126)
            wb = 127 % 2
            sc.op(sc_raw.copy, rbc[:, wb * 512:wb * 512 + 512],
                  ps_rb[0:64, wb * 512:wb * 512 + 512], inc=(s_re, 1))

        @blk.vector
        def _(v_raw):
            v = k.eng(v_raw, "vector")
            v.op(v_raw.memset, qkt[64:128, :], 0.0)
            v.op(v_raw.memset, rrec[:, :], 0.0, inc=(z_qk, 1))

            def do_mult(w):
                wb = w % 2
                (h, qb) = wlist[w]
                v.wait(s_re, b_re + w + 1)
                if w >= 4:
                    v.wait(s_sts[w % 4], b_sts[w % 4] + 16 * (w // 4))
                v.op(v_raw.tensor_tensor,
                     ctxo[:, (w % 4) * 512:(w % 4) * 512 + 512],
                     ps_cx[:, wb * 512:wb * 512 + 512].rearrange(
                         "p n -> p n")[0:64, :] if False else
                     ps_cx[0:64, wb * 512:wb * 512 + 512],
                     rbc[:, wb * 512:wb * 512 + 512],
                     ALU.mult, inc=(s_cm, 1))

            def do_recip(w):
                wb = w % 2
                v.wait(s_av, b_av + cumtiles[w + 1])
                v.op(v_raw.reciprocal, rrecf[0:1, wb * 512:wb * 512 + 512],
                     ps_cx[64:65, wb * 512:wb * 512 + 512])
                v.op(v_raw.tensor_copy, rrec[0:1, wb * 512:wb * 512 + 512],
                     rrecf[0:1, wb * 512:wb * 512 + 512], inc=(s_rc, 1),
                     dep=True)

            for w, (h, qb) in enumerate(wlist):
                if w >= 2:
                    do_mult(w - 2)
                if w >= 1:
                    do_recip(w - 1)
                g0 = cumtiles[w]
                for st in range(4 * qb, ntiles(qb)):
                    g = g0 + st
                    m = st - 4 * qb
                    v.wait(s_ex, b_ex + g + 1)
                    v.op(v_raw.tensor_tensor,
                         expp[:, (g % 4) * 512:(g % 4) * 512 + 512],
                         expp[:, (g % 4) * 512:(g % 4) * 512 + 512],
                         k.masks_bf[:, m * 512:m * 512 + 512],
                         ALU.mult, inc=(s_mk, 1))
            do_mult(126)
            do_recip(127)
            do_mult(127)

        @blk.gpsimd
        def _(g_raw):
            g = k.eng(g_raw, "gpsimd")
            for w, (h, qb) in enumerate(wlist):
                g.wait(s_cm, b_cm + w + 1)
                g.dma(ctxT_hbm[h * 64:(h + 1) * 64, qb * 512:(qb + 1) * 512],
                      ctxo[:, (w % 4) * 512:(w % 4) * 512 + 512], s_sts[w % 4])
            for ss in s_sts:
                g_raw.wait_ge(ss.h, ss.v)




def emit_proj_residual_ln(k: K, ctxT_hbm, rhs_tile, out_hbm, tag,
                          resid_hbm=None, resid2_hbm=None, bias_bc=None,
                          ln_scale_bc=None, ln_shift_bc=None,
                          ctx_rows=1024, do_ln=True):
    nc = k.nc
    s_cl = k.sem("s_ctxld")
    s_xls = [k.sem("s_xld2a"), k.sem("s_xld2b")]
    smm, sacc, sout = k.sem("t_mm"), k.sem("v_acc"), k.sem("v_out")
    ssts = [k.sem("g_st0"), k.sem("g_st1")]
    s_lnv, s_lns = k.sem("ln_v"), k.sem("ln_s")
    nkt = ctx_rows // 128
    ctx_sb = k.sbuf(f"proj_ctx_{tag}", [128, nkt * T], BF16)

    with nc.Block() as blk:
        @blk.sync
        def _(s_raw):
            s = k.eng(s_raw, "sync")
            for r in range(nkt):
                s.dma(ctx_sb[:, r * T:(r + 1) * T],
                      ctxT_hbm[r * 128:(r + 1) * 128, :], s_cl)
            s_raw.wait_ge(s_cl.h, s_cl.v)

    b_mm, b_acc, b_out = smm.v, sacc.v, sout.v
    b_sts = [s.v for s in ssts]
    b_xls = [s.v for s in s_xls]
    b_lnv, b_lns = s_lnv.v, s_lns.v
    ps = k.psum(f"proj_ps_{tag}", [128, 4 * 512], F32)
    vt = k.sbuf(f"proj_v_{tag}", [128, 2 * 1024], F32)
    res = k.sbuf(f"proj_res_{tag}", [128, 2 * 1024], F32)
    res2 = (k.sbuf(f"proj_res2_{tag}", [128, 2 * 1024], F32)
            if resid2_hbm is not None else None)
    sqs = k.sbuf(f"proj_sq_{tag}", [128, 1024], F32)
    stats = k.sbuf(f"proj_stats_{tag}", [128, 8], F32)
    outt = k.sbuf(f"proj_out_{tag}", [128, 2 * 1024], F32)
    nres_dma = (1 if resid_hbm is not None else 0) + (1 if resid2_hbm is not None else 0)

    with nc.Block() as blk:
        @blk.sync
        def _(s_raw):
            s = k.eng(s_raw, "sync")
            for qt in range(32):
                if qt >= 2 and nres_dma:
                    s_raw.wait_ge(sacc.h, b_acc + qt - 1)
                if resid_hbm is not None:
                    s.dma(res[:, (qt % 2) * 1024:(qt % 2) * 1024 + 1024],
                          resid_hbm[qt * 128:(qt + 1) * 128, :], s_xls[qt % 2])
                if resid2_hbm is not None:
                    s.dma(res2[:, (qt % 2) * 1024:(qt % 2) * 1024 + 1024],
                          resid2_hbm[qt * 128:(qt + 1) * 128, :], s_xls[qt % 2])

        @blk.tensor
        def _(t):
            for qt in range(32):
                for nb in range(2):
                    n = qt * 2 + nb
                    if n >= 4:
                        t.wait_ge(sacc.h, b_acc + qt - 1)
                    for kt in range(nkt):
                        mm = t.matmul(ps[:, (n % 4) * 512:(n % 4) * 512 + 512],
                                      ctx_sb[:, kt * T + qt * 128:kt * T + qt * 128 + 128],
                                      rhs_tile(kt, nb),
                                      start=(kt == 0), stop=(kt == nkt - 1))
                    mm.then_inc(smm.h, 1)
                    smm.v += 1

        @blk.vector
        def _(v_raw):
            v = k.eng(v_raw, "vector")
            for qt in range(32):
                v.wait(smm, b_mm + 2 * qt + 2)
                if nres_dma:
                    v.wait(s_xls[qt % 2],
                           b_xls[qt % 2] + 16 * nres_dma * (qt // 2 + 1))
                vt_ap = vt[:, (qt % 2) * 1024:(qt % 2) * 1024 + 1024]
                pb = (2 * qt) % 4
                ps_pair = ps[:, pb * 512:pb * 512 + 1024]
                accs = []
                if resid_hbm is not None:
                    accs.append(("tt", res[:, (qt % 2) * 1024:(qt % 2) * 1024 + 1024]))
                if resid2_hbm is not None:
                    accs.append(("tt", res2[:, (qt % 2) * 1024:(qt % 2) * 1024 + 1024]))
                if bias_bc is not None:
                    accs.append(("tt", bias_bc))
                if not accs:
                    v.op(v_raw.tensor_copy, vt_ap, ps_pair, inc=(sacc, 1))
                else:
                    first = accs[0][1]
                    last = (len(accs) == 1)
                    v.op(v_raw.tensor_tensor, vt_ap, ps_pair, first, ALU.add,
                         inc=(sacc, 1) if last else None)
                    for i, (_, other) in enumerate(accs[1:]):
                        last = (i == len(accs) - 2)
                        v.op(v_raw.tensor_tensor, vt_ap, vt_ap, other, ALU.add,
                             inc=(sacc, 1) if last else None)
                out_ap = outt[:, (qt % 2) * 1024:(qt % 2) * 1024 + 1024]
                if qt >= 2:
                    v.wait(ssts[qt % 2], b_sts[qt % 2] + 16 * (qt // 2))
                if do_ln:
                    v.op(v_raw.reduce_sum, stats[:, 1:2], vt_ap, axis=AX.X,
                         inc=(s_lnv, 1))
                    v.wait(s_lns, b_lns + 2 * qt + 1)
                    v.op(v_raw.tensor_scalar, stats[:, 2:3], stats[:, 1:2],
                         1.0 / D, None, ALU.mult, dep=True)
                    v.op(v_raw.tensor_scalar, stats[:, 3:4], stats[:, 2:3],
                         stats[:, 2:3], None, ALU.mult, dep=True)
                    v.op(v_raw.tensor_scalar, stats[:, 4:5], stats[:, 0:1],
                         1.0 / D, None, ALU.mult)
                    v.op(v_raw.tensor_tensor, stats[:, 4:5], stats[:, 4:5],
                         stats[:, 3:4], ALU.subtract, inc=(s_lnv, 1), dep=True)
                    v.wait(s_lns, b_lns + 2 * qt + 2)
                    if ln_scale_bc is not None:
                        v.op(v_raw.tensor_scalar, out_ap, vt_ap, stats[:, 2:3],
                             stats[:, 5:6], ALU.subtract, ALU.mult, dep=True)
                        v.op(v_raw.tensor_tensor, out_ap, out_ap, ln_scale_bc,
                             ALU.mult)
                        v.op(v_raw.tensor_tensor, out_ap, out_ap, ln_shift_bc,
                             ALU.add, inc=(sout, 1))
                    else:
                        v.op(v_raw.tensor_scalar, out_ap, vt_ap, stats[:, 2:3],
                             stats[:, 5:6], ALU.subtract, ALU.mult,
                             inc=(sout, 1), dep=True)
                else:
                    v.op(v_raw.tensor_copy, out_ap, vt_ap, inc=(sout, 1))

        @blk.scalar
        def _(sc_raw):
            sc = k.eng(sc_raw, "scalar")
            if not do_ln:
                return
            for qt in range(32):
                sc.wait(s_lnv, b_lnv + 2 * qt + 1)
                sc.op(sc_raw.activation, sqs[:, :],
                      vt[:, (qt % 2) * 1024:(qt % 2) * 1024 + 1024],
                      AF.Square, accum_out=stats[:, 0:1], inc=(s_lns, 1))
                sc.wait(s_lnv, b_lnv + 2 * qt + 2)
                sc.op(sc_raw.activation, stats[:, 5:6], stats[:, 4:5], AF.Ln,
                      bias=LNEPS)
                sc.op(sc_raw.activation, stats[:, 5:6], stats[:, 5:6], AF.Exp,
                      scale=-0.5, inc=(s_lns, 1))

        @blk.gpsimd
        def _(g_raw):
            g = k.eng(g_raw, "gpsimd")
            for qt in range(32):
                g.wait(sout, b_out + qt + 1)
                g.dma(out_hbm[qt * 128:(qt + 1) * 128, :],
                      outt[:, (qt % 2) * 1024:(qt % 2) * 1024 + 1024],
                      ssts[qt % 2])
            for ss in ssts:
                g_raw.wait_ge(ss.h, ss.v)




def body_l0attn(k, x_in, wq_in, wk_in, wv_in, wo_in, x1_out, pfx="a0"):
    nc = k.nc
    qt_hbm = k.dram(f"{pfx}_scr_qt", [D, T], BF16)
    kt_hbm = k.dram(f"{pfx}_scr_kt", [D, T], BF16)
    ctxT_hbm = k.dram(f"{pfx}_scr_ctxT", [D, T], BF16)

    with contextlib.ExitStack() as ph:
        wqT = ph.enter_context(nc.sbuf_tensor("wqT", [128, 8 * D], BF16))
        wkT = ph.enter_context(nc.sbuf_tensor("wkT", [128, 8 * D], BF16))
        wvT = ph.enter_context(nc.sbuf_tensor("wvT", [128, 8 * D], BF16))
        with k.scope():
            tw_sbuf(k, wq_in[:, :], wqT, D, D, "wq")
        with k.scope():
            tw_sbuf(k, wk_in[:, :], wkT, D, D, "wk")
        with k.scope():
            tw_sbuf(k, wv_in[:, :], wvT, D, D, "wv")

        v_sb = ph.enter_context(nc.sbuf_tensor("vaug0", [128, 32 * 1040], BF16))
        cbias = ph.enter_context(nc.sbuf_tensor("cbias0", [128, 16], F32))

        with contextlib.ExitStack() as phx:
            xT = phx.enter_context(nc.sbuf_tensor("xT0", [128, 8 * T], BF16))
            with k.scope():
                tx_sbuf(k, x_in[:, :], xT, "x0")

            def lhsT_qk(kind, kt, mb):
                w = wqT if kind == "q" else wkT
                return w[:, kt * D + mb * 128:kt * D + mb * 128 + 128]

            def rhs_v(kt, nb):
                return wvT[:, kt * D + nb * 512:kt * D + nb * 512 + 512]

            with k.scope():
                emit_qkv(k, xT, lhsT_qk, rhs_v, qt_hbm, kt_hbm, v_sb, "a0")

        with k.scope():
            emit_att_bounds(k, qt_hbm, kt_hbm, cbias, "a0")
        with k.scope():
            emit_attention(k, qt_hbm, kt_hbm, v_sb, cbias, ctxT_hbm, "a0")

    with contextlib.ExitStack() as ph3:
        woT = ph3.enter_context(nc.sbuf_tensor("woT", [128, 8 * D], BF16))
        with k.scope():
            tw_sbuf(k, wo_in[:, :], woT, D, D, "wo")

        def rhs_o(kt, nb):
            return woT[:, kt * D + nb * 512:kt * D + nb * 512 + 512]

        with k.scope():
            emit_proj_residual_ln(k, ctxT_hbm, rhs_o, x1_out, "a0",
                                  resid_hbm=x_in[:, :])


def build_l0attn():
    k = K("l0attn")
    x_in = k.dram("x_in", [T, D], F32, kind="ExternalInput")
    wq_in = k.dram("wq_in", [D, D], F32, kind="ExternalInput")
    wk_in = k.dram("wk_in", [D, D], F32, kind="ExternalInput")
    wv_in = k.dram("wv_in", [D, D], F32, kind="ExternalInput")
    wo_in = k.dram("wo_in", [D, D], F32, kind="ExternalInput")
    x1_out = k.dram("x1_out", [T, D], F32, kind="ExternalOutput")
    declare_and_load_consts(k)
    body_l0attn(k, x_in, wq_in, wk_in, wv_in, wo_in, x1_out)
    return k


def run_l0attn(x_full, q_proj, k_proj, v_proj, o_proj):
    k = build_l0attn()
    consts = host_consts()
    in_maps = []
    for c in range(NCORES):
        m = dict(consts)
        m["x_in"] = np.ascontiguousarray(x_full[c])
        m["wq_in"] = q_proj
        m["wk_in"] = k_proj
        m["wv_in"] = v_proj
        m["wo_in"] = o_proj
        in_maps.append(m)
    res = run_bass_kernel_spmd(k.nc, in_maps, core_ids=list(range(NCORES)))
    k.close()
    return np.stack([res.results[c]["x1_out"] for c in range(NCORES)])


def _f32(a):
    return np.ascontiguousarray(np.asarray(a, dtype=np.float32))




def tw_hbm(k: K, src_ap, dst_hbm, R, C, tag):
    nc = k.nc
    smm, sev = k.sem("t_mm"), k.sem("s_ev")
    slds = [k.sem("g_wld0"), k.sem("g_wld1")]
    ssts = [k.sem("s_st0"), k.sem("s_st1")]
    nrb, ncb = R // 128, C // 128
    wrow = k.sbuf(f"throw_{tag}", [128, 2 * C], BF16)
    stage = k.sbuf(f"thstg_{tag}", [128, 2 * C], BF16)
    pst = k.psum(f"thps_{tag}", [128, 2 * 1024], BF16)
    b_lds = [s.v for s in slds]
    b_sts = [s.v for s in ssts]
    b_mm, b_ev = smm.v, sev.v

    with nc.Block() as blk:
        @blk.gpsimd
        def _(g_raw):
            g = k.eng(g_raw, "gpsimd")
            for rb in range(nrb):
                if rb >= 2:
                    g_raw.wait_ge(smm.h, b_mm + (rb - 1) * ncb)
                half = C // 2 if C > 2048 else C
                for c0 in range(0, C, half):
                    g.dma(wrow[:, (rb % 2) * C + c0:(rb % 2) * C + c0 + half],
                          src_ap[rb * 128:(rb + 1) * 128, c0:c0 + half],
                          slds[rb % 2])

        nld_per_rb = 2 if C > 2048 else 1

        @blk.tensor
        def _(t):
            n = 0
            for rb in range(nrb):
                t.wait_ge(slds[rb % 2].h,
                          b_lds[rb % 2] + 16 * nld_per_rb * (rb // 2 + 1))
                for cb in range(ncb):
                    if n >= 2:
                        t.wait_ge(sev.h, b_ev + n - 1)
                    mm = t.transpose(
                        pst[:, (n % 2) * 1024:(n % 2) * 1024 + 128],
                        wrow[:, (rb % 2) * C + cb * 128:(rb % 2) * C + cb * 128 + 128],
                        k.ident_bf[:, :])
                    mm.then_inc(smm.h, 1)
                    smm.v += 1
                    n += 1

        @blk.scalar
        def _(sc_raw):
            sc = k.eng(sc_raw, "scalar")
            n = 0
            for rb in range(nrb):
                if rb >= 2:
                    sc.wait(ssts[rb % 2], b_sts[rb % 2] + 16 * (rb // 2))
                for cb in range(ncb):
                    sc.wait(smm, b_mm + n + 1)
                    sc.op(sc_raw.copy,
                          stage[:, (rb % 2) * C + cb * 128:(rb % 2) * C + cb * 128 + 128],
                          pst[:, (n % 2) * 1024:(n % 2) * 1024 + 128],
                          inc=(sev, 1))
                    n += 1

        @blk.sync
        def _(s_raw):
            s = k.eng(s_raw, "sync")
            for rb in range(nrb):
                s.wait(sev, b_ev + (rb + 1) * ncb)
                s.dma(dst_hbm[rb], stage[:, (rb % 2) * C:(rb % 2) * C + C],
                      ssts[rb % 2])
            for ss in ssts:
                s_raw.wait_ge(ss.h, ss.v)


def scale_cols(k: K, src_row_dram, n, dst_sb, tag):
    nc = k.nc
    nj = n // 128
    tmp = k.sbuf(f"sc_tmp_{tag}", [128, 128], F32)
    pst = k.psum(f"sc_ps_{tag}", [128, 128], F32)
    sld = k.sem("s_xld")
    b_ld = sld.v
    with nc.Block() as blk:
        @blk.sync
        def _(s_raw):
            s = k.eng(s_raw, "sync")
            s.dma(tmp[0:nj, :], src_row_dram.rearrange("(j p) -> j p", p=128), sld)
            s_raw.wait_ge(sld.h, sld.v)
    with nc.Block() as blk:
        @blk.tensor
        def _(t):
            t.transpose(pst[:, :], tmp[:, :], k.ident_f32[:, :])
    with nc.Block() as blk:
        @blk.scalar
        def _(sc_raw):
            sc = k.eng(sc_raw, "scalar")
            sc.op(sc_raw.copy, dst_sb[:, 0:nj], pst[:, 0:nj])


def emit_gw12(k: K, x3T_sb, w1t_hbm, w2t_hbm, s1col, s2col, at_hbm, tag):
    nc = k.nc
    smm, sev, svv = k.sem("t_mm"), k.sem("s_ev"), k.sem("v_ev")
    slds = [k.sem("s_xld0"), k.sem("s_xld1")]
    ssts = [k.sem(f"g_st{i}") for i in range(4)]
    b_lds = [s.v for s in slds]
    b_sts = [s.v for s in ssts]
    b_mm, b_ev, b_vv = smm.v, sev.v, svv.v
    slab = k.sbuf(f"gw_slab_{tag}", [128, 4 * 1024], BF16)
    tsil = k.sbuf(f"gw_tsil_{tag}", [128, 2 * 512], F32)
    aev = k.sbuf(f"gw_aev_{tag}", [128, 4 * 512], BF16)
    ps = k.psum(f"gw_ps_{tag}", [128, 4 * 512], F32)

    with nc.Block() as blk:
        @blk.sync
        def _(s_raw):
            s = k.eng(s_raw, "sync")
            for mb in range(32):
                if mb >= 2:
                    s_raw.wait_ge(smm.h, b_mm + 8 * (mb - 1))
                s.dma(slab[:, (mb % 2) * 2048:(mb % 2) * 2048 + 1024],
                      w1t_hbm[mb], slds[mb % 2])
                s.dma(slab[:, (mb % 2) * 2048 + 1024:(mb % 2) * 2048 + 2048],
                      w2t_hbm[mb], slds[mb % 2])

        @blk.tensor
        def _(t):
            n = 0
            for mb in range(32):
                t.wait_ge(slds[mb % 2].h, b_lds[mb % 2] + 32 * (mb // 2 + 1))
                for tb in range(8):
                    if n >= 2:
                        t.wait_ge(svv.h, b_vv + n - 1)
                    for kt in range(8):
                        mm = t.matmul(
                            ps[:, (2 * n % 4) * 512:(2 * n % 4) * 512 + 512],
                            slab[:, (mb % 2) * 2048 + kt * 128:(mb % 2) * 2048 + kt * 128 + 128],
                            x3T_sb[:, kt * T + tb * 512:kt * T + tb * 512 + 512],
                            start=(kt == 0), stop=(kt == 7))
                    for kt in range(8):
                        mm = t.matmul(
                            ps[:, ((2 * n + 1) % 4) * 512:((2 * n + 1) % 4) * 512 + 512],
                            slab[:, (mb % 2) * 2048 + 1024 + kt * 128:(mb % 2) * 2048 + 1024 + kt * 128 + 128],
                            x3T_sb[:, kt * T + tb * 512:kt * T + tb * 512 + 512],
                            start=(kt == 0), stop=(kt == 7))
                    mm.then_inc(smm.h, 1)
                    smm.v += 1
                    n += 1

        @blk.scalar
        def _(sc_raw):
            sc = k.eng(sc_raw, "scalar")
            n = 0
            for mb in range(32):
                for tb in range(8):
                    sc.wait(smm, b_mm + n + 1)
                    if n >= 2:
                        sc.wait(svv, b_vv + n - 1)
                    sc.op(sc_raw.activation, tsil[:, (n % 2) * 512:(n % 2) * 512 + 512],
                          ps[:, (2 * n % 4) * 512:(2 * n % 4) * 512 + 512],
                          AF.Silu, scale=s1col[:, mb:mb + 1], inc=(sev, 1))
                    n += 1

        @blk.vector
        def _(v_raw):
            v = k.eng(v_raw, "vector")
            n = 0
            for mb in range(32):
                for tb in range(8):
                    v.wait(sev, b_ev + n + 1)
                    if n >= 4:
                        v.wait(ssts[n % 4], b_sts[n % 4] + 16 * (n // 4))
                    v.op(v_raw.scalar_tensor_tensor,
                         aev[:, (n % 4) * 512:(n % 4) * 512 + 512],
                         ps[:, ((2 * n + 1) % 4) * 512:((2 * n + 1) % 4) * 512 + 512],
                         s2col[:, mb:mb + 1],
                         tsil[:, (n % 2) * 512:(n % 2) * 512 + 512],
                         ALU.mult, ALU.mult, inc=(svv, 1))
                    n += 1

        @blk.gpsimd
        def _(g_raw):
            g = k.eng(g_raw, "gpsimd")
            n = 0
            for mb in range(32):
                for tb in range(8):
                    g.wait(svv, b_vv + n + 1)
                    g.dma(at_hbm[mb][:, tb * 512:tb * 512 + 512],
                          aev[:, (n % 4) * 512:(n % 4) * 512 + 512], ssts[n % 4])
                    n += 1
            for ss in ssts:
                g_raw.wait_ge(ss.h, ss.v)


def emit_gh_resident(k: K, rT_sb, ht_hbm, scol, out_hbm, tag):
    nc = k.nc
    smm, sev = k.sem("t_mm"), k.sem("s_ev")
    slds = [k.sem("g_wld0"), k.sem("g_wld1")]
    ssts = [k.sem(f"s_st{i}") for i in range(4)]
    b_lds = [s.v for s in slds]
    b_sts = [s.v for s in ssts]
    b_mm, b_ev = smm.v, sev.v
    slab = k.sbuf(f"gh_slab_{tag}", [128, 2 * 4096], BF16)
    ev = k.sbuf(f"gh_ev_{tag}", [128, 4 * 512], BF16)
    ps = k.psum(f"gh_ps_{tag}", [128, 4 * 512], F32)

    with nc.Block() as blk:
        @blk.gpsimd
        def _(g_raw):
            g = k.eng(g_raw, "gpsimd")
            for ot in range(32):
                if ot >= 2:
                    g_raw.wait_ge(smm.h, b_mm + 8 * (ot - 1))
                g.dma(slab[:, (ot % 2) * 4096:(ot % 2) * 4096 + 4096],
                      ht_hbm[ot], slds[ot % 2])

        @blk.tensor
        def _(t):
            n = 0
            for ot in range(32):
                t.wait_ge(slds[ot % 2].h, b_lds[ot % 2] + 16 * (ot // 2 + 1))
                for tb in range(8):
                    if n >= 4:
                        t.wait_ge(sev.h, b_ev + n - 3)
                    for kt in range(32):
                        mm = t.matmul(
                            ps[:, (n % 4) * 512:(n % 4) * 512 + 512],
                            slab[:, (ot % 2) * 4096 + kt * 128:(ot % 2) * 4096 + kt * 128 + 128],
                            rT_sb[:, kt * T + tb * 512:kt * T + tb * 512 + 512],
                            start=(kt == 0), stop=(kt == 31))
                    mm.then_inc(smm.h, 1)
                    smm.v += 1
                    n += 1

        @blk.scalar
        def _(sc_raw):
            sc = k.eng(sc_raw, "scalar")
            n = 0
            for ot in range(32):
                for tb in range(8):
                    sc.wait(smm, b_mm + n + 1)
                    if n >= 4:
                        sc.wait(ssts[n % 4], b_sts[n % 4] + 16 * (n // 4))
                    sc.op(sc_raw.activation, ev[:, (n % 4) * 512:(n % 4) * 512 + 512],
                          ps[:, (n % 4) * 512:(n % 4) * 512 + 512],
                          AF.Gelu, scale=scol[:, ot:ot + 1], inc=(sev, 1))
                    n += 1

        @blk.sync
        def _(s_raw):
            s = k.eng(s_raw, "sync")
            n = 0
            for ot in range(32):
                for tb in range(8):
                    s.wait(sev, b_ev + n + 1)
                    s.dma(out_hbm[ot][:, tb * 512:(tb + 1) * 512],
                          ev[:, (n % 4) * 512:(n % 4) * 512 + 512], ssts[n % 4])
                    n += 1
            for ss in ssts:
                s_raw.wait_ge(ss.h, ss.v)


def emit_gh_blocked(k: K, rT_hbm, ht_hbm, scol, out_hbm, tag):
    nc = k.nc
    s_rl = k.sem("s_rhld")
    smm, sev = k.sem("t_mm"), k.sem("s_ev")
    slds = [k.sem("g_wld0"), k.sem("g_wld1")]
    ssts = [k.sem(f"s_st{i}") for i in range(4)]
    b_lds = [s.v for s in slds]
    b_sts = [s.v for s in ssts]
    b_rl, b_mm, b_ev = s_rl.v, smm.v, sev.v
    rblk = k.sbuf(f"ghb_r_{tag}", [128, 32 * 1024], BF16)
    slab = k.sbuf(f"ghb_slab_{tag}", [128, 2 * 4096], BF16)
    ev = k.sbuf(f"ghb_ev_{tag}", [128, 4 * 512], BF16)
    ps = k.psum(f"ghb_ps_{tag}", [128, 4 * 512], F32)

    with nc.Block() as blk:
        @blk.sync
        def _(s_raw):
            s = k.eng(s_raw, "sync")
            for tbb in range(4):
                if tbb >= 1:
                    s_raw.wait_ge(smm.h, b_mm + 64 * tbb)
                for kt in range(32):
                    s.dma(rblk[:, kt * 1024:kt * 1024 + 1024],
                          rT_hbm[kt][:, tbb * 1024:tbb * 1024 + 1024], s_rl)

        @blk.gpsimd
        def _(g_raw):
            g = k.eng(g_raw, "gpsimd")
            for tbb in range(4):
                for ot in range(32):
                    gi = tbb * 32 + ot
                    if gi >= 2:
                        g_raw.wait_ge(smm.h, b_mm + 2 * (gi - 1))
                    g.dma(slab[:, (gi % 2) * 4096:(gi % 2) * 4096 + 4096],
                          ht_hbm[ot], slds[gi % 2])

        @blk.tensor
        def _(t):
            n = 0
            for tbb in range(4):
                t.wait_ge(s_rl.h, b_rl + 16 * 32 * (tbb + 1))
                for ot in range(32):
                    gi = tbb * 32 + ot
                    t.wait_ge(slds[gi % 2].h, b_lds[gi % 2] + 16 * (gi // 2 + 1))
                    for tb in range(2):
                        if n >= 4:
                            t.wait_ge(sev.h, b_ev + n - 3)
                        for kt in range(32):
                            mm = t.matmul(
                                ps[:, (n % 4) * 512:(n % 4) * 512 + 512],
                                slab[:, (gi % 2) * 4096 + kt * 128:(gi % 2) * 4096 + kt * 128 + 128],
                                rblk[:, kt * 1024 + tb * 512:kt * 1024 + tb * 512 + 512],
                                start=(kt == 0), stop=(kt == 31))
                        mm.then_inc(smm.h, 1)
                        smm.v += 1
                        n += 1

        @blk.scalar
        def _(sc_raw):
            sc = k.eng(sc_raw, "scalar")
            n = 0
            for tbb in range(4):
                for ot in range(32):
                    for tb in range(2):
                        sc.wait(smm, b_mm + n + 1)
                        if n >= 4:
                            sc.wait(ssts[n % 4], b_sts[n % 4] + 16 * (n // 4))
                        sc.op(sc_raw.activation,
                              ev[:, (n % 4) * 512:(n % 4) * 512 + 512],
                              ps[:, (n % 4) * 512:(n % 4) * 512 + 512],
                              AF.Gelu, scale=scol[:, ot:ot + 1], inc=(sev, 1))
                        sc.dma(out_hbm[ot][:, tbb * 1024 + tb * 512:tbb * 1024 + tb * 512 + 512],
                               ev[:, (n % 4) * 512:(n % 4) * 512 + 512],
                               ssts[n % 4], dep=True)
                        n += 1
            for ss in ssts:
                sc_raw.wait_ge(ss.h, ss.v)


def emit_w3_ln(k: K, ct_hbm, w3t_sb, s3col, x3_hbm, y_hbm, tag):
    nc = k.nc
    s_ct = k.sem("s_ctld")
    s_x3s = [k.sem("s_xld2a"), k.sem("s_xld2b")]
    smm, str_, se1, se2 = k.sem("t_mm"), k.sem("t_tr"), k.sem("s_e1"), k.sem("s_e2")
    s_lnv, s_lns = k.sem("ln_v"), k.sem("ln_s")
    sout = k.sem("v_out")
    ssts = [k.sem("g_st0"), k.sem("g_st1")]
    b_x3s = [s.v for s in s_x3s]
    b_sts = [s.v for s in ssts]
    b_ct, b_mm, b_tr = s_ct.v, smm.v, str_.v
    b_e1, b_e2, b_lnv, b_lns = se1.v, se2.v, s_lnv.v, s_lns.v
    b_out = sout.v

    ctblk = k.sbuf(f"w3_ct_{tag}", [128, 32 * 512], BF16)
    o4t = k.sbuf(f"w3_o4t_{tag}", [128, 2 * 512], F32)
    obuf = k.sbuf(f"w3_ob_{tag}", [128, 2 * 4096], F32)
    res = k.sbuf(f"w3_res_{tag}", [128, 2 * 1024], F32)
    outt = k.sbuf(f"w3_out_{tag}", [128, 2 * 1024], F32)
    sqs = k.sbuf(f"w3_sq_{tag}", [128, 1024], F32)
    stats = k.sbuf(f"w3_stats_{tag}", [128, 8], F32)
    ps1 = k.psum(f"w3_ps1_{tag}", [128, 2 * 512], F32)
    ps2 = k.psum(f"w3_ps2_{tag}", [128, 2 * 512], F32)

    with nc.Block() as blk:
        @blk.sync
        def _(s_raw):
            s = k.eng(s_raw, "sync")
            for tb in range(8):
                if tb >= 1:
                    s_raw.wait_ge(smm.h, b_mm + 8 * tb)
                for ht in range(32):
                    s.dma(ctblk[:, ht * 512:ht * 512 + 512],
                          ct_hbm[ht][:, tb * 512:tb * 512 + 512], s_ct)
                for c in range(4):
                    w = tb * 4 + c
                    if w >= 2:
                        s_raw.wait_ge(s_lnv.h, b_lnv + 2 * (w - 2) + 1)
                    s.dma(res[:, (w % 2) * 1024:(w % 2) * 1024 + 1024],
                          x3_hbm[w * 128:(w + 1) * 128, :], s_x3s[w % 2])

        @blk.tensor
        def _(t):
            for tb in range(8):
                t.wait_ge(s_ct.h, b_ct + 16 * 32 * (tb + 1))
                for db in range(8):
                    g = tb * 8 + db
                    if g >= 2:
                        t.wait_ge(se1.h, b_e1 + g - 1)
                    for ht in range(32):
                        mm = t.matmul(
                            ps1[:, (g % 2) * 512:(g % 2) * 512 + 512],
                            w3t_sb[:, db * 4096 + ht * 128:db * 4096 + ht * 128 + 128],
                            ctblk[:, ht * 512:ht * 512 + 512],
                            start=(ht == 0), stop=(ht == 31))
                    mm.then_inc(smm.h, 1)
                    smm.v += 1
                    t.wait_ge(se1.h, b_e1 + g + 1)
                    for c in range(4):
                        trg = g * 4 + c
                        if trg >= 2:
                            t.wait_ge(se2.h, b_e2 + trg - 1)
                        mm = t.transpose(
                            ps2[:, (trg % 2) * 512:(trg % 2) * 512 + 128],
                            o4t[:, (g % 2) * 512 + c * 128:(g % 2) * 512 + c * 128 + 128],
                            k.ident_f32[:, :])
                        mm.then_inc(str_.h, 1)
                        str_.v += 1

        @blk.scalar
        def _(sc_raw):
            sc = k.eng(sc_raw, "scalar")
            for tb in range(8):
                for db in range(8):
                    g = tb * 8 + db
                    sc.wait(smm, b_mm + g + 1)
                    if g >= 2:
                        sc.wait(str_, b_tr + 4 * (g - 1))
                    sc.op(sc_raw.activation, o4t[:, (g % 2) * 512:(g % 2) * 512 + 512],
                          ps1[:, (g % 2) * 512:(g % 2) * 512 + 512],
                          AF.Identity, scale=s3col[:, db:db + 1], inc=(se1, 1))
                    if tb >= 2 and db == 0:
                        sc.wait(sout, b_out + 4 * (tb - 1))
                    for c in range(4):
                        trg = g * 4 + c
                        sc.wait(str_, b_tr + trg + 1)
                        sc.op(sc_raw.copy,
                              obuf[:, (tb % 2) * 4096 + c * 1024 + db * 128:
                                   (tb % 2) * 4096 + c * 1024 + db * 128 + 128],
                              ps2[:, (trg % 2) * 512:(trg % 2) * 512 + 128],
                              inc=(se2, 1))
                for c in range(4):
                    w = tb * 4 + c
                    sc.wait(s_lnv, b_lnv + 2 * w + 1)
                    sc.op(sc_raw.activation, sqs[:, :],
                          obuf[:, (tb % 2) * 4096 + c * 1024:(tb % 2) * 4096 + c * 1024 + 1024],
                          AF.Square, accum_out=stats[:, 0:1], inc=(s_lns, 1))
                    sc.wait(s_lnv, b_lnv + 2 * w + 2)
                    sc.op(sc_raw.activation, stats[:, 5:6], stats[:, 4:5], AF.Ln,
                          bias=LNEPS)
                    sc.op(sc_raw.activation, stats[:, 5:6], stats[:, 5:6], AF.Exp,
                          scale=-0.5, inc=(s_lns, 1))

        @blk.vector
        def _(v_raw):
            v = k.eng(v_raw, "vector")
            for tb in range(8):
                v.wait(se2, b_e2 + 32 * (tb + 1))
                for c in range(4):
                    w = tb * 4 + c
                    ob_ap = obuf[:, (tb % 2) * 4096 + c * 1024:(tb % 2) * 4096 + c * 1024 + 1024]
                    v.wait(s_x3s[w % 2], b_x3s[w % 2] + 16 * (w // 2 + 1))
                    v.op(v_raw.tensor_tensor, ob_ap, ob_ap,
                         res[:, (w % 2) * 1024:(w % 2) * 1024 + 1024], ALU.add)
                    v.op(v_raw.reduce_sum, stats[:, 1:2], ob_ap, axis=AX.X,
                         inc=(s_lnv, 1))
                    v.wait(s_lns, b_lns + 2 * w + 1)
                    v.op(v_raw.tensor_scalar, stats[:, 2:3], stats[:, 1:2],
                         1.0 / D, None, ALU.mult, dep=True)
                    v.op(v_raw.tensor_scalar, stats[:, 3:4], stats[:, 2:3],
                         stats[:, 2:3], None, ALU.mult, dep=True)
                    v.op(v_raw.tensor_scalar, stats[:, 4:5], stats[:, 0:1],
                         1.0 / D, None, ALU.mult)
                    v.op(v_raw.tensor_tensor, stats[:, 4:5], stats[:, 4:5],
                         stats[:, 3:4], ALU.subtract, inc=(s_lnv, 1), dep=True)
                    v.wait(s_lns, b_lns + 2 * w + 2)
                    if w >= 2:
                        v.wait(ssts[w % 2], b_sts[w % 2] + 16 * (w // 2))
                    v.op(v_raw.tensor_scalar,
                         outt[:, (w % 2) * 1024:(w % 2) * 1024 + 1024],
                         ob_ap, stats[:, 2:3], stats[:, 5:6],
                         ALU.subtract, ALU.mult, inc=(sout, 1), dep=True)

        @blk.gpsimd
        def _(g_raw):
            g = k.eng(g_raw, "gpsimd")
            for w in range(32):
                g.wait(sout, b_out + w + 1)
                g.dma(y_hbm[w * 128:(w + 1) * 128, :],
                      outt[:, (w % 2) * 1024:(w % 2) * 1024 + 1024], ssts[w % 2])
            for ss in ssts:
                g_raw.wait_ge(ss.h, ss.v)


def body_l1ffn(k, x3_in, w1_in, w2_in, h1_in, h2_in, w3_in, s1_in, s2_in,
               sh1_in, sh2_in, s3_in, y_out, pfx="lf"):
    nc = k.nc
    w1t_hbm = k.dram(f"{pfx}_scr_w1t", [32, 128, 1024], BF16)
    w2t_hbm = k.dram(f"{pfx}_scr_w2t", [32, 128, 1024], BF16)
    h1t_hbm = k.dram(f"{pfx}_scr_h1t", [32, 128, 4096], BF16)
    h2t_hbm = k.dram(f"{pfx}_scr_h2t", [32, 128, 4096], BF16)
    w3t_hbm = k.dram(f"{pfx}_scr_w3t", [8, 128, 4096], BF16)
    bt_hbm = k.dram(f"{pfx}_scr_bt", [32, 128, T], BF16)
    ct_hbm = k.dram(f"{pfx}_scr_ct", [32, 128, T], BF16)
    at_hbm = k.dram(f"{pfx}_scr_at", [32, 128, T], BF16)

    with k.scope():
        tw_hbm(k, w1_in[:, :], w1t_hbm, 4096, 1024, "w1")
    with k.scope():
        tw_hbm(k, w2_in[:, :], w2t_hbm, 4096, 1024, "w2")
    with k.scope():
        tw_hbm(k, h1_in[:, :], h1t_hbm, 4096, 4096, "h1")
    with k.scope():
        tw_hbm(k, h2_in[:, :], h2t_hbm, 4096, 4096, "h2")
    with k.scope():
        tw_hbm(k, w3_in[:, :], w3t_hbm, 1024, 4096, "w3")

    s1c = k.sbuf("s1c", [128, 32], F32)
    s2c = k.sbuf("s2c", [128, 32], F32)
    sh1c = k.sbuf("sh1c", [128, 32], F32)
    sh2c = k.sbuf("sh2c", [128, 32], F32)
    s3c = k.sbuf("s3c", [128, 8], F32)
    with k.scope():
        scale_cols(k, s1_in, 4096, s1c, "s1")
    with k.scope():
        scale_cols(k, s2_in, 4096, s2c, "s2")
    with k.scope():
        scale_cols(k, sh1_in, 4096, sh1c, "sh1")
    with k.scope():
        scale_cols(k, sh2_in, 4096, sh2c, "sh2")
    with k.scope():
        scale_cols(k, s3_in, 1024, s3c, "s3")

    with contextlib.ExitStack() as phx:
        x3T = phx.enter_context(nc.sbuf_tensor("x3T", [128, 8 * T], BF16))
        with k.scope():
            tx_sbuf(k, x3_in[:, :], x3T, "x3")
        with k.scope():
            emit_gw12(k, x3T, w1t_hbm, w2t_hbm, s1c, s2c, at_hbm, "gw")
    with k.scope():
        emit_gh_blocked(k, at_hbm, h1t_hbm, sh1c, bt_hbm, "h1")
    with k.scope():
        emit_gh_blocked(k, bt_hbm, h2t_hbm, sh2c, ct_hbm, "h2")
    with contextlib.ExitStack() as phw:
        w3t_sb = phw.enter_context(nc.sbuf_tensor("w3t_sb", [128, 8 * 4096], BF16))
        s_wl = k.sem("s_w3ld")
        with nc.Block() as blk:
            @blk.sync
            def _(s_raw):
                s = k.eng(s_raw, "sync")
                for dt in range(8):
                    s.dma(w3t_sb[:, dt * 4096:(dt + 1) * 4096], w3t_hbm[dt], s_wl)
                s_raw.wait_ge(s_wl.h, s_wl.v)
        with k.scope():
            emit_w3_ln(k, ct_hbm, w3t_sb, s3c, x3_in[:, :], y_out, "w3")


def build_l1ffn():
    k = K("l1ffn")
    x3_in = k.dram("x3_in", [T, D], F32, kind="ExternalInput")
    w1_in = k.dram("w1w_in", [4096, 1024], F32, kind="ExternalInput")
    w2_in = k.dram("w2w_in", [4096, 1024], F32, kind="ExternalInput")
    h1_in = k.dram("h1w_in", [4096, 4096], F32, kind="ExternalInput")
    h2_in = k.dram("h2w_in", [4096, 4096], F32, kind="ExternalInput")
    w3_in = k.dram("w3w_in", [1024, 4096], F32, kind="ExternalInput")
    s1_in = k.dram("s1_in", [4096], F32, kind="ExternalInput")
    s2_in = k.dram("s2_in", [4096], F32, kind="ExternalInput")
    sh1_in = k.dram("sh1_in", [4096], F32, kind="ExternalInput")
    sh2_in = k.dram("sh2_in", [4096], F32, kind="ExternalInput")
    s3_in = k.dram("s3_in", [1024], F32, kind="ExternalInput")
    y_out = k.dram("y_out", [T, D], F32, kind="ExternalOutput")
    declare_and_load_consts(k)
    body_l1ffn(k, x3_in, w1_in, w2_in, h1_in, h2_in, w3_in, s1_in, s2_in,
               sh1_in, sh2_in, s3_in, y_out)
    return k


def run_l1ffn(x3_full, w1w, w1s, w2w, w2s, h1w, h1s, h2w, h2s, w3w, w3s):
    k = build_l1ffn()
    consts = host_consts()
    in_maps = []
    for c in range(NCORES):
        m = dict(consts)
        m["x3_in"] = np.ascontiguousarray(x3_full[c])
        m["w1w_in"] = w1w
        m["w2w_in"] = w2w
        m["h1w_in"] = h1w
        m["h2w_in"] = h2w
        m["w3w_in"] = w3w
        m["s1_in"] = w1s
        m["s2_in"] = w2s
        m["sh1_in"] = h1s
        m["sh2_in"] = h2s
        m["s3_in"] = w3s
        in_maps.append(m)
    res = run_bass_kernel_spmd(k.nc, in_maps, core_ids=list(range(NCORES)))
    k.close()
    return np.stack([res.results[c]["y_out"] for c in range(NCORES)])



def tw_sbuf_any(k: K, src_ap, dst_sb, R, C, tag):
    nc = k.nc
    smm, sev = k.sem("t_mm"), k.sem("s_ev")
    slds = [k.sem("g_wld0"), k.sem("g_wld1")]
    b_lds = [s.v for s in slds]
    nrb = (R + 127) // 128
    ncb = (C + 127) // 128
    wrow = k.sbuf(f"tsa_row_{tag}", [128, 2 * C], BF16)
    pst = k.psum(f"tsa_ps_{tag}", [128, 2 * 1024], BF16)
    b_mm, b_ev = smm.v, sev.v

    with nc.Block() as blk:
        @blk.gpsimd
        def _(g_raw):
            g = k.eng(g_raw, "gpsimd")
            for rb in range(nrb):
                r_sz = min(128, R - rb * 128)
                if rb >= 2:
                    g_raw.wait_ge(smm.h, b_mm + (rb - 1) * ncb)
                g.dma(wrow[0:r_sz, (rb % 2) * C:(rb % 2) * C + C],
                      src_ap[rb * 128:rb * 128 + r_sz, :], slds[rb % 2])

        @blk.tensor
        def _(t):
            n = 0
            for rb in range(nrb):
                r_sz = min(128, R - rb * 128)
                t.wait_ge(slds[rb % 2].h, b_lds[rb % 2] + 16 * (rb // 2 + 1))
                for cb in range(ncb):
                    c_sz = min(128, C - cb * 128)
                    if n >= 2:
                        t.wait_ge(sev.h, b_ev + n - 1)
                    mm = t.transpose(
                        pst[0:c_sz, (n % 2) * 1024:(n % 2) * 1024 + r_sz],
                        wrow[0:r_sz, (rb % 2) * C + cb * 128:(rb % 2) * C + cb * 128 + c_sz],
                        k.ident_bf[0:r_sz, 0:r_sz])
                    mm.then_inc(smm.h, 1)
                    smm.v += 1
                    n += 1

        @blk.scalar
        def _(sc_raw):
            sc = k.eng(sc_raw, "scalar")
            n = 0
            for rb in range(nrb):
                r_sz = min(128, R - rb * 128)
                for cb in range(ncb):
                    c_sz = min(128, C - cb * 128)
                    sc.wait(smm, b_mm + n + 1)
                    sc.op(sc_raw.copy,
                          dst_sb[0:c_sz, cb * R + rb * 128:cb * R + rb * 128 + r_sz],
                          pst[0:c_sz, (n % 2) * 1024:(n % 2) * 1024 + r_sz],
                          inc=(sev, 1))
                    n += 1


def emit_xz(k: K, x2T_sb, wint_hbm, xi_hbm, z_hbm, tag):
    nc = k.nc
    smm, sev = k.sem("t_mm"), k.sem("s_ev")
    slds = [k.sem("g_wld0"), k.sem("g_wld1")]
    ssts = [k.sem(f"s_st{i}") for i in range(4)]
    b_lds = [s.v for s in slds]
    b_sts = [s.v for s in ssts]
    b_mm, b_ev = smm.v, sev.v
    slab = k.sbuf(f"xz_slab_{tag}", [128, 2 * 1024], BF16)
    ev = k.sbuf(f"xz_ev_{tag}", [128, 4 * 512], F32)
    ps = k.psum(f"xz_ps_{tag}", [128, 4 * 512], F32)

    with nc.Block() as blk:
        @blk.gpsimd
        def _(g_raw):
            g = k.eng(g_raw, "gpsimd")
            for et in range(32):
                if et >= 2:
                    g_raw.wait_ge(smm.h, b_mm + 8 * (et - 1))
                g.dma(slab[:, (et % 2) * 1024:(et % 2) * 1024 + 1024],
                      wint_hbm[et], slds[et % 2])

        @blk.tensor
        def _(t):
            n = 0
            for et in range(32):
                t.wait_ge(slds[et % 2].h, b_lds[et % 2] + 16 * (et // 2 + 1))
                for tb in range(8):
                    if n >= 4:
                        t.wait_ge(sev.h, b_ev + n - 3)
                    for kt in range(8):
                        mm = t.matmul(
                            ps[:, (n % 4) * 512:(n % 4) * 512 + 512],
                            slab[:, (et % 2) * 1024 + kt * 128:(et % 2) * 1024 + kt * 128 + 128],
                            x2T_sb[:, kt * T + tb * 512:kt * T + tb * 512 + 512],
                            start=(kt == 0), stop=(kt == 7))
                    mm.then_inc(smm.h, 1)
                    smm.v += 1
                    n += 1

        @blk.scalar
        def _(sc_raw):
            sc = k.eng(sc_raw, "scalar")
            n = 0
            for et in range(32):
                for tb in range(8):
                    sc.wait(smm, b_mm + n + 1)
                    if n >= 4:
                        sc.wait(ssts[n % 4], b_sts[n % 4] + 16 * (n // 4))
                    sc.op(sc_raw.copy, ev[:, (n % 4) * 512:(n % 4) * 512 + 512],
                          ps[:, (n % 4) * 512:(n % 4) * 512 + 512], inc=(sev, 1))
                    n += 1

        @blk.sync
        def _(s_raw):
            s = k.eng(s_raw, "sync")
            n = 0
            for et in range(32):
                for tb in range(8):
                    s.wait(sev, b_ev + n + 1)
                    dst = xi_hbm[et] if et < 16 else z_hbm[et - 16]
                    s.dma(dst[:, tb * 512:tb * 512 + 512],
                          ev[:, (n % 4) * 512:(n % 4) * 512 + 512], ssts[n % 4])
                    n += 1
            for ss in ssts:
                s_raw.wait_ge(ss.h, ss.v)


def emit_conv(k: K, xi_hbm, cw_sb, cb_sb, xc_hbm, xcbf_hbm, tag):
    nc = k.nc
    ssc, svv = k.sem("s_ev"), k.sem("v_ev")
    slds = [k.sem("s_xld0"), k.sem("s_xld1")]
    ssts = [k.sem("g_st0"), k.sem("g_st1")]
    b_lds = [s.v for s in slds]
    b_sts = [s.v for s in ssts]
    b_sc, b_vv = ssc.v, svv.v
    PAD = 3
    xibuf = k.sbuf(f"cv_xi_{tag}", [128, 2 * (T + PAD)], F32)
    acc = k.sbuf(f"cv_acc_{tag}", [128, 2 * T], F32)
    tmp0 = k.sbuf(f"cv_t0_{tag}", [128, T], F32)
    tmp1 = k.sbuf(f"cv_t1_{tag}", [128, T], F32)
    xcf = k.sbuf(f"cv_xc_{tag}", [128, 2 * T], F32)
    xcb = k.sbuf(f"cv_xcb_{tag}", [128, 2 * T], BF16)

    with nc.Block() as blk:
        @blk.sync
        def _(s_raw):
            s = k.eng(s_raw, "sync")
            s_raw.wait_ge(svv.h, b_vv + 1)
            for dt in range(16):
                if dt >= 2:
                    s_raw.wait_ge(ssc.h, b_sc + 5 * (dt - 2) + 4)
                s.dma(xibuf[:, (dt % 2) * (T + PAD) + PAD:(dt % 2) * (T + PAD) + PAD + T],
                      xi_hbm[dt], slds[dt % 2])

        @blk.scalar
        def _(sc_raw):
            sc = k.eng(sc_raw, "scalar")
            for dt in range(16):
                sl = (dt % 2) * (T + PAD)
                ac = acc[:, (dt % 2) * T:(dt % 2) * T + T]
                sc.wait(slds[dt % 2], b_lds[dt % 2] + 16 * (dt // 2 + 1))
                sc.op(sc_raw.activation, ac, xibuf[:, sl + 3:sl + 3 + T], AF.Identity,
                      scale=cw_sb[:, dt * 4 + 3:dt * 4 + 4],
                      bias=cb_sb[:, dt:dt + 1], inc=(ssc, 1))
                if dt >= 1:
                    sc.wait(svv, b_vv + 1 + 4 * (dt - 1) + 3)
                sc.op(sc_raw.activation, tmp0[:, :], xibuf[:, sl + 2:sl + 2 + T],
                      AF.Identity, scale=cw_sb[:, dt * 4 + 2:dt * 4 + 3], inc=(ssc, 1))
                sc.op(sc_raw.activation, tmp1[:, :], xibuf[:, sl + 1:sl + 1 + T],
                      AF.Identity, scale=cw_sb[:, dt * 4 + 1:dt * 4 + 2], inc=(ssc, 1))
                sc.wait(svv, b_vv + 1 + 4 * dt + 1)
                sc.op(sc_raw.activation, tmp0[:, :], xibuf[:, sl + 0:sl + 0 + T],
                      AF.Identity, scale=cw_sb[:, dt * 4 + 0:dt * 4 + 1], inc=(ssc, 1))
                sc.wait(svv, b_vv + 1 + 4 * dt + 3)
                if dt >= 2:
                    sc.wait(svv, b_vv + 1 + 4 * (dt - 2) + 4)
                    sc.wait(ssts[dt % 2], b_sts[dt % 2] + 32 * (dt // 2))
                sc.op(sc_raw.activation, xcf[:, (dt % 2) * T:(dt % 2) * T + T],
                      ac, AF.Silu, inc=(ssc, 1))

        @blk.vector
        def _(v_raw):
            v = k.eng(v_raw, "vector")
            v.op(v_raw.memset, xibuf[:, 0:PAD], 0.0)
            v.op(v_raw.memset, xibuf[:, T + PAD:T + 2 * PAD], 0.0, inc=(svv, 1))
            for dt in range(16):
                ac = acc[:, (dt % 2) * T:(dt % 2) * T + T]
                v.wait(ssc, b_sc + 5 * dt + 2)
                v.op(v_raw.tensor_tensor, ac, ac, tmp0[:, :], ALU.add, inc=(svv, 1))
                v.wait(ssc, b_sc + 5 * dt + 3)
                v.op(v_raw.tensor_tensor, ac, ac, tmp1[:, :], ALU.add, inc=(svv, 1))
                v.wait(ssc, b_sc + 5 * dt + 4)
                v.op(v_raw.tensor_tensor, ac, ac, tmp0[:, :], ALU.add, inc=(svv, 1))
                v.wait(ssc, b_sc + 5 * dt + 5)
                if dt >= 2:
                    v.wait(ssts[dt % 2], b_sts[dt % 2] + 32 * (dt // 2))
                v.op(v_raw.tensor_copy, xcb[:, (dt % 2) * T:(dt % 2) * T + T],
                     xcf[:, (dt % 2) * T:(dt % 2) * T + T], inc=(svv, 1))

        @blk.gpsimd
        def _(g_raw):
            g = k.eng(g_raw, "gpsimd")
            for dt in range(16):
                g.wait(svv, b_vv + 1 + 4 * dt + 4)
                g.dma(xc_hbm[dt], xcf[:, (dt % 2) * T:(dt % 2) * T + T],
                      ssts[dt % 2])
                g.dma(xcbf_hbm[dt], xcb[:, (dt % 2) * T:(dt % 2) * T + T],
                      ssts[dt % 2])
            for ss in ssts:
                g_raw.wait_ge(ss.h, ss.v)


def emit_xdbl(k: K, xcbf_hbm, wxT_sb, xdbl_sb, tag):
    nc = k.nc
    smm, sev = k.sem("t_mm"), k.sem("s_ev")
    slds = [k.sem("s_xld0"), k.sem("s_xld1")]
    b_lds = [s.v for s in slds]
    b_mm, b_ev = smm.v, sev.v
    rbuf = k.sbuf(f"xd_r_{tag}", [128, 2 * 512], BF16)
    ps = k.psum(f"xd_ps_{tag}", [96, 2 * 512], F32)

    with nc.Block() as blk:
        @blk.sync
        def _(s_raw):
            s = k.eng(s_raw, "sync")
            n = 0
            for tb in range(8):
                for kt in range(16):
                    if n >= 2:
                        s_raw.wait_ge(smm.h, b_mm + n - 1)
                    s.dma(rbuf[:, (n % 2) * 512:(n % 2) * 512 + 512],
                          xcbf_hbm[kt][:, tb * 512:tb * 512 + 512], slds[n % 2])
                    n += 1

        @blk.tensor
        def _(t):
            n = 0
            for tb in range(8):
                for kt in range(16):
                    t.wait_ge(slds[n % 2].h, b_lds[n % 2] + 16 * (n // 2 + 1))
                    if kt == 0 and tb >= 2:
                        t.wait_ge(sev.h, b_ev + tb - 1)
                    mm = t.matmul(ps[:, (tb % 2) * 512:(tb % 2) * 512 + 512],
                                  wxT_sb[:, kt * 96:kt * 96 + 96],
                                  rbuf[:, (n % 2) * 512:(n % 2) * 512 + 512],
                                  start=(kt == 0), stop=(kt == 15))
                    mm.then_inc(smm.h, 1)
                    smm.v += 1
                    n += 1

        @blk.scalar
        def _(sc_raw):
            sc = k.eng(sc_raw, "scalar")
            for tb in range(8):
                sc.wait(smm, b_mm + 16 * (tb + 1))
                sc.op(sc_raw.copy, xdbl_sb[:, tb * 512:tb * 512 + 512],
                      ps[:, (tb % 2) * 512:(tb % 2) * 512 + 512], inc=(sev, 1))


def emit_dtproj(k: K, xdbl_sb, wdtT_sb, bdt_col, dtt_hbm, tag):
    nc = k.nc
    smm, sev, svv = k.sem("t_mm"), k.sem("s_ev"), k.sem("v_ev")
    ssts = [k.sem(f"g_st{i}") for i in range(4)]
    b_sts = [s.v for s in ssts]
    b_mm, b_ev, b_vv = smm.v, sev.v, svv.v
    dtr_bf = k.sbuf(f"dp_r_{tag}", [64, T], BF16)
    tab = k.sbuf(f"dp_ab_{tag}", [128, 2 * 512], F32)
    trl = k.sbuf(f"dp_rl_{tag}", [128, 2 * 512], F32)
    ev = k.sbuf(f"dp_ev_{tag}", [128, 4 * 512], F32)
    ps = k.psum(f"dp_ps_{tag}", [128, 2 * 512], F32)

    with nc.Block() as blk:
        @blk.vector
        def _(v_raw):
            v = k.eng(v_raw, "vector")
            v.op(v_raw.tensor_copy, dtr_bf[:, :], xdbl_sb[0:64, :], inc=(svv, 1))
            n = 0
            for dt in range(16):
                for tb in range(8):
                    v.wait(sev, b_ev + 2 * n + 2)
                    if n >= 4:
                        v.wait(ssts[n % 4], b_sts[n % 4] + 16 * (n // 4))
                    v.op(v_raw.tensor_tensor, ev[:, (n % 4) * 512:(n % 4) * 512 + 512],
                         trl[:, (n % 2) * 512:(n % 2) * 512 + 512],
                         tab[:, (n % 2) * 512:(n % 2) * 512 + 512],
                         ALU.add, inc=(svv, 1))
                    n += 1

        @blk.tensor
        def _(t):
            t.wait_ge(svv.h, b_vv + 1)
            n = 0
            for dt in range(16):
                for tb in range(8):
                    if n >= 2:
                        t.wait_ge(sev.h, b_ev + 2 * (n - 1))
                    mm = t.matmul(ps[:, (n % 2) * 512:(n % 2) * 512 + 512],
                                  wdtT_sb[:, dt * 128:dt * 128 + 128],
                                  dtr_bf[:, tb * 512:tb * 512 + 512],
                                  start=True, stop=True)
                    mm.then_inc(smm.h, 1)
                    smm.v += 1
                    n += 1

        @blk.scalar
        def _(sc_raw):
            sc = k.eng(sc_raw, "scalar")
            n = 0
            for dt in range(16):
                for tb in range(8):
                    sc.wait(smm, b_mm + n + 1)
                    if n >= 1:
                        if n >= 2:
                            sc.wait(svv, b_vv + 1 + n - 1)
                    sc.op(sc_raw.activation, tab[:, (n % 2) * 512:(n % 2) * 512 + 512],
                          ps[:, (n % 2) * 512:(n % 2) * 512 + 512], AF.Abs,
                          bias=bdt_col[:, dt:dt + 1])
                    sc.op(sc_raw.activation, trl[:, (n % 2) * 512:(n % 2) * 512 + 512],
                          ps[:, (n % 2) * 512:(n % 2) * 512 + 512], AF.Relu,
                          bias=bdt_col[:, dt:dt + 1], inc=(sev, 1))
                    sc.op(sc_raw.activation, tab[:, (n % 2) * 512:(n % 2) * 512 + 512],
                          tab[:, (n % 2) * 512:(n % 2) * 512 + 512], AF.Exp,
                          scale=-1.0)
                    sc.op(sc_raw.activation, tab[:, (n % 2) * 512:(n % 2) * 512 + 512],
                          tab[:, (n % 2) * 512:(n % 2) * 512 + 512], AF.Ln,
                          bias=1.0, inc=(sev, 1))
                    n += 1

        @blk.gpsimd
        def _(g_raw):
            g = k.eng(g_raw, "gpsimd")
            n = 0
            for dt in range(16):
                for tb in range(8):
                    g.wait(svv, b_vv + 1 + n + 1)
                    g.dma(dtt_hbm[dt][:, tb * 512:tb * 512 + 512],
                          ev[:, (n % 4) * 512:(n % 4) * 512 + 512], ssts[n % 4])
                    n += 1
            for ss in ssts:
                g_raw.wait_ge(ss.h, ss.v)


def emit_scan(k: K, dtt_hbm, xc_hbm, z_hbm, bc_sb, sel_sb, a_sb, dp_col,
              yt_hbm, tag):
    nc = k.nc
    s_dxs = [k.sem("s_dxld0"), k.sem("s_dxld1")]
    b_dxs = [s.v for s in s_dxs]
    s_zl = k.sem("s_zld")
    t_bc = k.sem("t_bc")
    s_dA = k.sem("s_dA")
    s_cc = k.sem("s_cc")
    v_scan = k.sem("v_scan")
    v_bc = k.sem("v_bc")
    s_t2 = k.sem("s_t2")
    s_sz = k.sem("s_sz")
    v_fin = k.sem("v_fin")
    g_add = k.sem("g_add")
    s_ysts = [k.sem("g_yst0"), k.sem("g_yst1")]
    b_ysts = [s.v for s in s_ysts]
    b_zl, b_bc, b_dA, b_cc = s_zl.v, t_bc.v, s_dA.v, s_cc.v
    b_vs, b_vb, b_t2, b_sz, b_fin = (v_scan.v, v_bc.v, s_t2.v, s_sz.v,
                                     v_fin.v)
    b_ga = g_add.v

    dtt = k.sbuf(f"sn_dt_{tag}", [128, 2 * T], F32)
    xct = k.sbuf(f"sn_xc_{tag}", [128, T], F32)
    ztt = k.sbuf(f"sn_z_{tag}", [128, T], F32)
    u_sb = k.sbuf(f"sn_u_{tag}", [128, T], F32)
    ya = k.sbuf(f"sn_ya_{tag}", [128, T], F32)
    sz = k.sbuf(f"sn_sz_{tag}", [128, T], F32)
    da = k.sbuf(f"sn_da_{tag}", [128, 2 * 512], F32)
    dbu = k.sbuf(f"sn_dbu_{tag}", [128, 2 * 512], F32)
    hh = k.sbuf(f"sn_h_{tag}", [128, 2 * 512], F32)
    tmp = k.sbuf(f"sn_tmp_{tag}", [128, 2 * 512], F32)
    carry = k.sbuf(f"sn_carry_{tag}", [128, 16], F32)
    ytb = k.sbuf(f"sn_yt_{tag}", [128, 2 * T], BF16)
    ps_bc = k.psum(f"sn_ps_{tag}", [128, 4 * 512], F32)

    NIT = 128
    add_id = {}
    _aid = 0
    for _dt in range(16):
        for _it in range(NIT):
            if _it % 16 != 0:
                add_id[_dt * NIT + _it] = _aid
                _aid += 1
    NADD_PER_DT = 120

    with nc.Block() as blk:
        @blk.sync
        def _(s_raw):
            s = k.eng(s_raw, "sync")
            for dt in range(16):
                if dt >= 2:
                    s_raw.wait_ge(s_ysts[dt % 2].h, b_ysts[dt % 2] + 16 * (dt // 2))
                s.dma(dtt[:, (dt % 2) * T:(dt % 2) * T + T], dtt_hbm[dt],
                      s_dxs[dt % 2])
                if dt >= 1:
                    s_raw.wait_ge(s_t2.h, b_t2 + dt)
                s.dma(xct[:, :], xc_hbm[dt], s_dxs[dt % 2])
                if dt >= 1:
                    s_raw.wait_ge(s_sz.h, b_sz + dt)
                s.dma(ztt[:, :], z_hbm[dt], s_zl)

        @blk.tensor
        def _(t):
            for dt in range(16):
                for it in range(NIT):
                    gi = dt * NIT + it
                    ch, n = it // 16, it % 16
                    slot = gi % 2
                    if gi >= 2:
                        t.wait_ge(v_bc.h, b_vb + gi - 1)
                    t.matmul(ps_bc[:, slot * 1024:slot * 1024 + 512],
                             sel_sb[:, n * 128:n * 128 + 128],
                             bc_sb[:, ch * 512:ch * 512 + 512],
                             start=True, stop=False, skip_group_check=True)
                    t.matmul(ps_bc[:, slot * 1024:slot * 1024 + 512],
                             sel_sb[:, n * 128:n * 128 + 128],
                             bc_sb[:, T + ch * 512:T + ch * 512 + 512],
                             start=False, stop=True, skip_group_check=True)
                    t.matmul(ps_bc[:, slot * 1024 + 512:slot * 1024 + 1024],
                             sel_sb[:, (16 + n) * 128:(16 + n) * 128 + 128],
                             bc_sb[:, ch * 512:ch * 512 + 512],
                             start=True, stop=False, skip_group_check=True)
                    mm = t.matmul(ps_bc[:, slot * 1024 + 512:slot * 1024 + 1024],
                                  sel_sb[:, (16 + n) * 128:(16 + n) * 128 + 128],
                                  bc_sb[:, T + ch * 512:T + ch * 512 + 512],
                                  start=False, stop=True, skip_group_check=True)
                    mm.then_inc(t_bc.h, 1)
                    t_bc.v += 1

        @blk.scalar
        def _(sc_raw):
            sc = k.eng(sc_raw, "scalar")
            def cc(gj):
                nj = gj % 16
                sc.wait(v_scan, b_vs + gj + 1)
                sc.op(sc_raw.copy, carry[:, nj:nj + 1],
                      hh[:, (gj % 2) * 512 + 511:(gj % 2) * 512 + 512],
                      inc=(s_cc, 1))
            for dt in range(16):
                sc.wait(s_dxs[dt % 2], b_dxs[dt % 2] + 32 * (dt // 2 + 1))
                for it in range(NIT):
                    gi = dt * NIT + it
                    ch, n = it // 16, it % 16
                    if gi >= 2:
                        sc.wait(v_scan, b_vs + gi - 1)
                    sc.op(sc_raw.activation, da[:, (gi % 2) * 512:(gi % 2) * 512 + 512],
                          dtt[:, (dt % 2) * T + ch * 512:(dt % 2) * T + ch * 512 + 512],
                          AF.Exp, scale=a_sb[:, dt * 16 + n:dt * 16 + n + 1],
                          inc=(s_dA, 1))
                    if gi >= 1:
                        cc(gi - 1)
                if dt == 15:
                    cc(16 * NIT - 1)
                sc.wait(v_bc, b_vb + NIT * (dt + 1))
                sc.op(sc_raw.activation, u_sb[:, :],
                      xct[:, :], AF.Identity,
                      scale=dp_col[:, dt:dt + 1], inc=(s_t2, 1))
                sc.wait(s_zl, b_zl + 16 * (dt + 1))
                if dt >= 1:
                    sc.wait(v_fin, b_fin + dt)
                sc.op(sc_raw.activation, sz[:, :], ztt[:, :], AF.Silu,
                      inc=(s_sz, 1))

        @blk.vector
        def _(v_raw):
            v = k.eng(v_raw, "vector")
            for dt in range(16):
                v.wait(s_dxs[dt % 2], b_dxs[dt % 2] + 32 * (dt // 2 + 1))
                v.wait(s_cc, b_cc + NIT * dt)
                v.op(v_raw.memset, carry[:, :], 0.0)
                v.op(v_raw.tensor_tensor, u_sb[:, :],
                     dtt[:, (dt % 2) * T:(dt % 2) * T + T],
                     xct[:, :], ALU.mult)
                for it in range(NIT):
                    gi = dt * NIT + it
                    ch, n = it // 16, it % 16
                    slot = gi % 2
                    cs = ch * 512
                    v.wait(t_bc, b_bc + gi + 1)
                    v.op(v_raw.tensor_tensor, dbu[:, slot * 512:slot * 512 + 512],
                         u_sb[:, cs:cs + 512],
                         ps_bc[:, slot * 1024:slot * 1024 + 512], ALU.mult)
                    v.wait(s_dA, b_dA + gi + 1)
                    if gi >= 2:
                        v.wait(s_cc, b_cc + gi - 1)
                    v.op(v_raw.tensor_tensor_scan,
                         hh[:, slot * 512:slot * 512 + 512],
                         da[:, slot * 512:slot * 512 + 512],
                         dbu[:, slot * 512:slot * 512 + 512],
                         carry[:, n:n + 1], ALU.mult, ALU.add,
                         inc=(v_scan, 1), dep=(ch == 0))
                    if n == 0:
                        v.op(v_raw.tensor_tensor, ya[:, cs:cs + 512],
                             hh[:, slot * 512:slot * 512 + 512],
                             ps_bc[:, slot * 1024 + 512:slot * 1024 + 1024],
                             ALU.mult, inc=(v_bc, 1), dep=True)
                    else:
                        a = add_id[gi]
                        if a >= 2:
                            v.wait(g_add, b_ga + a - 1)
                        v.op(v_raw.tensor_tensor, tmp[:, (a % 2) * 512:(a % 2) * 512 + 512],
                             hh[:, slot * 512:slot * 512 + 512],
                             ps_bc[:, slot * 1024 + 512:slot * 1024 + 1024],
                             ALU.mult, inc=(v_bc, 1), dep=True)
                v.wait(g_add, b_ga + NADD_PER_DT * (dt + 1))
                v.wait(s_t2, b_t2 + dt + 1)
                v.op(v_raw.tensor_tensor, ya[:, :], ya[:, :], u_sb[:, :], ALU.add)
                v.wait(s_sz, b_sz + dt + 1)
                if dt >= 2:
                    v.wait(s_ysts[dt % 2], b_ysts[dt % 2] + 16 * (dt // 2))
                v.op(v_raw.tensor_tensor, ytb[:, (dt % 2) * T:(dt % 2) * T + T],
                     ya[:, :], sz[:, :], ALU.mult, inc=(v_fin, 1))

        @blk.gpsimd
        def _(g_raw):
            g = k.eng(g_raw, "gpsimd")
            for dt in range(16):
                for it in range(NIT):
                    gi = dt * NIT + it
                    ch, n = it // 16, it % 16
                    if n == 0:
                        continue
                    a = add_id[gi]
                    cs = ch * 512
                    g.wait(v_bc, b_vb + gi + 1)
                    g.op(g_raw.tensor_tensor, ya[:, cs:cs + 512],
                         ya[:, cs:cs + 512],
                         tmp[:, (a % 2) * 512:(a % 2) * 512 + 512],
                         ALU.add, inc=(g_add, 1))
                g.wait(v_fin, b_fin + dt + 1)
                g.dma(yt_hbm[dt], ytb[:, (dt % 2) * T:(dt % 2) * T + T],
                      s_ysts[dt % 2])
            for ss in s_ysts:
                g_raw.wait_ge(ss.h, ss.v)


def emit_bc_split(k: K, xdbl_sb, bc_sb, tag):
    nc = k.nc
    bcf = k.sbuf(f"bcs_f_{tag}", [32, T], F32)
    with nc.Block() as blk:
        @blk.scalar
        def _(sc_raw):
            sc = k.eng(sc_raw, "scalar")
            sc.op(sc_raw.copy, bcf[:, :], xdbl_sb[64:96, :])
    with nc.Block() as blk:
        @blk.vector
        def _(v_raw):
            v = k.eng(v_raw, "vector")
            v.op(v_raw.tensor_copy, bc_sb[:, 0:T], bcf[:, :])
    with nc.Block() as blk:
        @blk.vector
        def _(v_raw):
            v = k.eng(v_raw, "vector")
            v.op(v_raw.tensor_tensor, bc_sb[:, T:2 * T], bcf[:, :],
                 bc_sb[:, 0:T], ALU.subtract)


def body_mamba(k, x2_in, win_in, cw_in, cb_in, wx_in, wdt_in, bdt_in,
               alog_in, dp_in, wout_in, sel_in, x3_out, pfx="mb"):
    nc = k.nc
    wint_hbm = k.dram(f"{pfx}_scr_wint", [32, 128, 1024], BF16)
    xi_hbm = k.dram(f"{pfx}_scr_xi", [16, 128, T], F32)
    z_hbm = k.dram(f"{pfx}_scr_z", [16, 128, T], F32)
    xc_hbm = k.dram(f"{pfx}_scr_xc", [16, 128, T], F32)
    xcbf_hbm = k.dram(f"{pfx}_scr_xcbf", [16, 128, T], BF16)
    dtt_hbm = k.dram(f"{pfx}_scr_dtt", [16, 128, T], F32)
    yt_hbm = k.dram(f"{pfx}_scr_yt", [16, 128, T], BF16)

    phm = contextlib.ExitStack()
    sel_sb = phm.enter_context(nc.sbuf_tensor("sel_sb", [32, 32 * 128], BF16))
    wxT_sb = phm.enter_context(nc.sbuf_tensor("wxT_sb", [128, 16 * 96], BF16))
    wdtT_sb = phm.enter_context(nc.sbuf_tensor("wdtT_sb", [64, 2048], BF16))
    a_sb = phm.enter_context(nc.sbuf_tensor("a_sb", [128, 256], F32))
    cw_sb = phm.enter_context(nc.sbuf_tensor("cw_sb", [128, 64], F32))
    cb_col = phm.enter_context(nc.sbuf_tensor("cb_col", [128, 16], F32))
    bdt_col = phm.enter_context(nc.sbuf_tensor("bdt_col", [128, 16], F32))
    dp_col = phm.enter_context(nc.sbuf_tensor("dp_col", [128, 16], F32))
    xdbl_sb = phm.enter_context(nc.sbuf_tensor("xdbl_sb", [96, T], F32))
    bc_sb = phm.enter_context(nc.sbuf_tensor("bc_sb", [32, 2 * T], BF16))

    s_ml = k.sem("s_mld")
    with nc.Block() as blk:
        @blk.sync
        def _(s_raw):
            s = k.eng(s_raw, "sync")
            s.dma(sel_sb[:, :], sel_in[:, :], s_ml)
            av = a_sb.rearrange("p (dt n) -> p dt n", n=16)
            for c in range(4):
                s.dma(av[:, 4 * c:4 * c + 4, :],
                      alog_in.rearrange("(dt p) n -> p dt n", p=128)[:, 4 * c:4 * c + 4, :],
                      s_ml)
            cwv = cw_sb.rearrange("p (dt kk) -> p dt kk", kk=4)
            for c in range(4):
                s.dma(cwv[:, 4 * c:4 * c + 4, :],
                      cw_in.rearrange("(dt p) kk -> p dt kk", p=128)[:, 4 * c:4 * c + 4, :],
                      s_ml)
            s_raw.wait_ge(s_ml.h, s_ml.v)
    with nc.Block() as blk:
        @blk.scalar
        def _(sc_raw):
            sc = k.eng(sc_raw, "scalar")
            sc.op(sc_raw.activation, a_sb[:, :], a_sb[:, :], AF.Exp)
    with nc.Block() as blk:
        @blk.vector
        def _(v_raw):
            v = k.eng(v_raw, "vector")
            v.op(v_raw.tensor_scalar, a_sb[:, :], a_sb[:, :], -1.0, None, ALU.mult)

    with k.scope():
        scale_cols(k, cb_in, 2048, cb_col, "cb")
    with k.scope():
        scale_cols(k, bdt_in, 2048, bdt_col, "bdt")
    with k.scope():
        scale_cols(k, dp_in, 2048, dp_col, "dp")
    with k.scope():
        tw_hbm(k, win_in[:, :], wint_hbm, 4096, 1024, "win")
    with k.scope():
        tw_sbuf_any(k, wx_in[:, :], wxT_sb, 96, 2048, "wx")
    with k.scope():
        tw_sbuf_any(k, wdt_in[:, :], wdtT_sb, 2048, 64, "wdt")

    with contextlib.ExitStack() as phx:
        x2T = phx.enter_context(nc.sbuf_tensor("x2T", [128, 8 * T], BF16))
        with k.scope():
            tx_sbuf(k, x2_in[:, :], x2T, "x2")
        with k.scope():
            emit_xz(k, x2T, wint_hbm, xi_hbm, z_hbm, "xz")
    with k.scope():
        emit_conv(k, xi_hbm, cw_sb, cb_col, xc_hbm, xcbf_hbm, "cv")
    with k.scope():
        emit_xdbl(k, xcbf_hbm, wxT_sb, xdbl_sb, "xd")
    with k.scope():
        emit_bc_split(k, xdbl_sb, bc_sb, "bc")
    with k.scope():
        emit_dtproj(k, xdbl_sb, wdtT_sb, bdt_col, dtt_hbm, "dp")
    with k.scope():
        emit_scan(k, dtt_hbm, xc_hbm, z_hbm, bc_sb, sel_sb, a_sb, dp_col,
                  yt_hbm, "sn")
    phm.close()

    with contextlib.ExitStack() as phw:
        woutT_sb = phw.enter_context(nc.sbuf_tensor("woutT_sb", [128, 16 * 1024], BF16))
        with k.scope():
            tw_sbuf(k, wout_in[:, :], woutT_sb, 1024, 2048, "wo2")

        def rhs_wout(kt, nb):
            return woutT_sb[:, kt * 1024 + nb * 512:kt * 1024 + nb * 512 + 512]

        with k.scope():
            emit_proj_residual_ln(k, yt_hbm.rearrange("a p t -> (a p) t"),
                                  rhs_wout, x3_out, "mm",
                                  resid_hbm=x2_in[:, :], ctx_rows=2048)


def build_mamba():
    k = K("mamba")
    x2_in = k.dram("x2_in", [T, D], F32, kind="ExternalInput")
    win_in = k.dram("win_in", [4096, 1024], F32, kind="ExternalInput")
    cw_in = k.dram("convw_in", [2048, 4], F32, kind="ExternalInput")
    cb_in = k.dram("convb_in", [2048], F32, kind="ExternalInput")
    wx_in = k.dram("wxproj_in", [96, 2048], F32, kind="ExternalInput")
    wdt_in = k.dram("wdt_in", [2048, 64], F32, kind="ExternalInput")
    bdt_in = k.dram("bdt_in", [2048], F32, kind="ExternalInput")
    alog_in = k.dram("alog_in", [2048, 16], F32, kind="ExternalInput")
    dp_in = k.dram("dp_in", [2048], F32, kind="ExternalInput")
    wout_in = k.dram("wout_in", [1024, 2048], F32, kind="ExternalInput")
    sel_in = k.dram("c_sel_bf", [32, 32 * 128], BF16, kind="ExternalInput")
    x3_out = k.dram("x3_out", [T, D], F32, kind="ExternalOutput")
    declare_and_load_consts(k)
    body_mamba(k, x2_in, win_in, cw_in, cb_in, wx_in, wdt_in, bdt_in,
               alog_in, dp_in, wout_in, sel_in, x3_out)
    return k


def host_sel_const():
    sel = np.zeros((32, 32 * 128), dtype=BF)
    for n in range(32):
        sel[n, n * 128:(n + 1) * 128] = 1.0
    return sel


def run_mamba(x2_full, W_in, conv_w, conv_b, W_xproj, W_dt, b_dt, A_log, Dp,
              W_out):
    k = build_mamba()
    consts = host_consts()
    consts["c_sel_bf"] = host_sel_const()
    in_maps = []
    for c in range(NCORES):
        m = dict(consts)
        m["x2_in"] = np.ascontiguousarray(x2_full[c])
        m["win_in"] = W_in
        m["convw_in"] = conv_w
        m["convb_in"] = conv_b
        m["wxproj_in"] = W_xproj
        m["wdt_in"] = W_dt
        m["bdt_in"] = b_dt
        m["alog_in"] = A_log
        m["dp_in"] = Dp
        m["wout_in"] = W_out
        in_maps.append(m)
    res = run_bass_kernel_spmd(k.nc, in_maps, core_ids=list(range(NCORES)))
    k.close()
    return np.stack([res.results[c]["x3_out"] for c in range(NCORES)])



def emit_ln_transpose(k: K, x_hbm, xT_sb, tag, ln_scale_bc=None,
                      ln_shift_bc=None, gate=None):
    nc = k.nc
    s_xls = [k.sem("s_xld0"), k.sem("s_xld1")]
    b_xls = [s.v for s in s_xls]
    s_lnv, s_lns = k.sem("ln_v"), k.sem("ln_s")
    v_ap = k.sem("v_apl")
    t_tr = k.sem("t_tr")
    s_eb = k.sem("s_evb")
    s_ef = k.sem("s_evf")
    t_gm = k.sem("t_gmm")
    v_tp = k.sem("v_top")
    s_sg = k.sem("s_sig")
    b_lnv, b_lns, b_ap = s_lnv.v, s_lns.v, v_ap.v
    b_tr, b_eb, b_ef, b_gm = t_tr.v, s_eb.v, s_ef.v, t_gm.v
    b_tp, b_sg = v_tp.v, s_sg.v

    import os
    _vdep = os.environ.get("LT_VDEP", "0") == "1"
    _sdep = os.environ.get("LT_SDEP", "0") == "1"
    xrow = k.sbuf(f"lt_x_{tag}", [128, 2 * 1024], F32)
    xln = k.sbuf(f"lt_ln_{tag}", [128, 2 * 1024], F32)
    sqs = k.sbuf(f"lt_sq_{tag}", [128, 1024], F32)
    stats = k.sbuf(f"lt_st_{tag}", [128, 16], F32)
    gsc = k.sbuf(f"lt_gs_{tag}", [128, 3 * 8], F32)
    xtf = k.sbuf(f"lt_xtf_{tag}", [128, 8 * 128], F32)
    pst = k.psum(f"lt_pst_{tag}", [128, 2 * 512], F32)
    psg = k.psum(f"lt_psg_{tag}", [128, 8], F32) if gate else None
    if gate:
        gw_w_sb, gw_sb = gate

    with nc.Block() as blk:
        @blk.sync
        def _(s_raw):
            s = k.eng(s_raw, "sync")
            for ti in range(32):
                if ti >= 2:
                    s_raw.wait_ge(v_ap.h, b_ap + ti - 1)
                s.dma(xrow[:, (ti % 2) * 1024:(ti % 2) * 1024 + 1024],
                      x_hbm[ti * 128:(ti + 1) * 128, :], s_xls[ti % 2])

        @blk.vector
        def _(v_raw):
            v = k.eng(v_raw, "vector")
            v.default_dep = _vdep
            for ti in range(32):
                xa = xrow[:, (ti % 2) * 1024:(ti % 2) * 1024 + 1024]
                la = xln[:, (ti % 2) * 1024:(ti % 2) * 1024 + 1024]
                v.wait(s_xls[ti % 2], b_xls[ti % 2] + 16 * (ti // 2 + 1))
                v.op(v_raw.reduce_sum, stats[:, 1:2], xa, axis=AX.X,
                     inc=(s_lnv, 1))
                v.wait(s_lns, b_lns + 2 * ti + 1)
                v.op(v_raw.tensor_scalar, stats[:, 2:3], stats[:, 1:2],
                     1.0 / D, None, ALU.mult, dep=True)
                v.op(v_raw.tensor_scalar, stats[:, 3:4], stats[:, 2:3],
                     stats[:, 2:3], None, ALU.mult, dep=True)
                v.op(v_raw.tensor_scalar, stats[:, 4:5], stats[:, 0:1],
                     1.0 / D, None, ALU.mult)
                v.op(v_raw.tensor_tensor, stats[:, 4:5], stats[:, 4:5],
                     stats[:, 3:4], ALU.subtract, inc=(s_lnv, 1), dep=True)
                v.wait(s_lns, b_lns + 2 * ti + 2)
                if ti >= 2:
                    v.wait(t_tr, b_tr + 8 * (ti - 1))
                v.op(v_raw.tensor_scalar, la, xa, stats[:, 2:3], stats[:, 5:6],
                     ALU.subtract, ALU.mult, dep=True)
                if ln_scale_bc is not None:
                    v.op(v_raw.tensor_tensor, la, la, ln_scale_bc, ALU.mult)
                    v.op(v_raw.tensor_tensor, la, la, ln_shift_bc, ALU.add,
                         inc=(v_ap, 1))
                else:
                    v.op(v_raw.tensor_copy, la, la, inc=(v_ap, 1))
                if gate:
                    v.wait(t_gm, b_gm + 8 * (ti + 1))
                    v.op(v_raw.reduce_max, stats[:, 8:9], psg[:, :], axis=AX.X)
                    v.op(v_raw.tensor_scalar, gsc[:, 0:8], psg[:, :],
                         stats[:, 8:9], None, ALU.is_ge, dep=True)
                    v.op(v_raw.scalar_tensor_tensor, gsc[:, 8:16], gsc[:, 0:8],
                         -1e30, psg[:, :], ALU.mult, ALU.add, dep=True)
                    v.op(v_raw.reduce_max, stats[:, 9:10], gsc[:, 8:16],
                         axis=AX.X, dep=True)
                    v.op(v_raw.tensor_scalar, gsc[:, 8:16], gsc[:, 8:16],
                         stats[:, 9:10], None, ALU.is_ge, dep=True)
                    v.op(v_raw.tensor_tensor, stats[:, 10:11], stats[:, 8:9],
                         stats[:, 9:10], ALU.subtract, inc=(v_tp, 1), dep=True)
                    v.wait(s_sg, b_sg + ti + 1)
                    v.op(v_raw.tensor_scalar, stats[:, 12:13], stats[:, 11:12],
                         -1.0, 1.0, ALU.mult, ALU.add, dep=True)
                    v.op(v_raw.tensor_scalar, gsc[:, 16:24], gsc[:, 0:8],
                         stats[:, 11:12], None, ALU.mult, dep=True)
                    v.op(v_raw.scalar_tensor_tensor, gw_sb[:, ti * 8:ti * 8 + 8],
                         gsc[:, 8:16], stats[:, 12:13], gsc[:, 16:24],
                         ALU.mult, ALU.add, inc=(v_tp, 1), dep=True)

        @blk.scalar
        def _(sc_raw):
            sc = k.eng(sc_raw, "scalar")
            sc.default_dep = _sdep
            for ti in range(32):
                xa = xrow[:, (ti % 2) * 1024:(ti % 2) * 1024 + 1024]
                sc.wait(s_lnv, b_lnv + 2 * ti + 1)
                sc.op(sc_raw.activation, sqs[:, :], xa, AF.Square,
                      accum_out=stats[:, 0:1], inc=(s_lns, 1))
                sc.wait(s_lnv, b_lnv + 2 * ti + 2)
                sc.op(sc_raw.activation, stats[:, 5:6], stats[:, 4:5], AF.Ln,
                      bias=LNEPS)
                sc.op(sc_raw.activation, stats[:, 5:6], stats[:, 5:6], AF.Exp,
                      scale=-0.5, inc=(s_lns, 1))
                for dc in range(8):
                    g = ti * 8 + dc
                    sc.wait(t_tr, b_tr + g + 1)
                    sc.op(sc_raw.copy,
                          xT_sb[:, dc * T + ti * 128:dc * T + ti * 128 + 128],
                          pst[:, (g % 2) * 512:(g % 2) * 512 + 128],
                          inc=(s_eb, 1))
                    if gate:
                        if dc == 0 and ti >= 1:
                            sc.wait(t_gm, b_gm + 8 * ti)
                        sc.op(sc_raw.copy, xtf[:, dc * 128:dc * 128 + 128],
                              pst[:, (g % 2) * 512:(g % 2) * 512 + 128],
                              inc=(s_ef, 1))
                if gate:
                    sc.wait(v_tp, b_tp + 2 * ti + 1)
                    sc.op(sc_raw.activation, stats[:, 11:12], stats[:, 10:11],
                          AF.Sigmoid, inc=(s_sg, 1))

        @blk.tensor
        def _(t):
            for ti in range(32):
                t.wait_ge(v_ap.h, b_ap + ti + 1)
                for dc in range(8):
                    g = ti * 8 + dc
                    if g >= 2:
                        t.wait_ge(s_eb.h, b_eb + g - 1)
                        if gate:
                            t.wait_ge(s_ef.h, b_ef + g - 1)
                    mm = t.transpose(
                        pst[:, (g % 2) * 512:(g % 2) * 512 + 128],
                        xln[:, (ti % 2) * 1024 + dc * 128:(ti % 2) * 1024 + dc * 128 + 128],
                        k.ident_f32[:, :])
                    mm.then_inc(t_tr.h, 1)
                    t_tr.v += 1
                if gate:
                    if ti >= 1:
                        t.wait_ge(v_tp.h, b_tp + 2 * ti)
                    for dc in range(8):
                        g = ti * 8 + dc
                        t.wait_ge(s_ef.h, b_ef + g + 1)
                        mm = t.matmul(psg[:, :],
                                      xtf[:, dc * 128:dc * 128 + 128],
                                      gw_w_sb[:, dc * 8:dc * 8 + 8],
                                      start=(dc == 0), stop=(dc == 7),
                                      skip_group_check=True)
                        mm.then_inc(t_gm.h, 1)
                        t_gm.v += 1


def emit_moe(k: K, xT_sb, gw_sb, w1_in, w2_in, w3_in, acc_hbm, tag):
    nc = k.nc
    w1_sb = k.sbuf(f"moe_w1_{tag}", [128, 8 * 2048], BF16)
    w2_sb = k.sbuf(f"moe_w2_{tag}", [128, 8 * 2048], BF16)
    w3_sb = k.sbuf(f"moe_w3_{tag}", [128, 16 * 1024], BF16)
    ht_sb = k.sbuf(f"moe_ht_{tag}", [128, 16 * 512], BF16)
    tsil = k.sbuf(f"moe_ts_{tag}", [128, 2 * 512], F32)
    accb = k.sbuf(f"moe_acc_{tag}", [128, 4 * 512], F32)
    ps12 = k.psum(f"moe_p12_{tag}", [128, 4 * 512], F32)
    ps3 = k.psum(f"moe_p3_{tag}", [128, 2 * 512], F32)

    for e in range(8):
        s_wl = k.sem("g_wld")
        s_als = [k.sem(f"s_accld{i}") for i in range(4)]
        t_m12 = k.sem("t_m12")
        t_m3 = k.sem("t_m3")
        s_ev = k.sem("s_ev")
        v_ht = k.sem("v_ht")
        v_ac = k.sem("v_acc")
        s_sts = [k.sem(f"s_st{i}") for i in range(4)]
        b_als = [s.v for s in s_als]
        b_sts = [s.v for s in s_sts]
        b_wl, b_12, b_3 = s_wl.v, t_m12.v, t_m3.v
        b_ev, b_ht, b_ac = s_ev.v, v_ht.v, v_ac.v

        with nc.Block() as blk:
            @blk.gpsimd
            def _(g_raw):
                g = k.eng(g_raw, "gpsimd")
                for c in range(8):
                    g.dma(w1_sb[:, c * 2048:(c + 1) * 2048],
                          w1_in[e, c * 128:(c + 1) * 128, :], s_wl)
                    g.dma(w2_sb[:, c * 2048:(c + 1) * 2048],
                          w2_in[e, c * 128:(c + 1) * 128, :], s_wl)
                w3v = w3_sb.rearrange("p (ht d) -> p ht d", d=1024)
                w3s = w3_in[e].rearrange("(ht p) d -> p ht d", p=128)
                for c in range(4):
                    g.dma(w3v[:, 4 * c:4 * c + 4, :], w3s[:, 4 * c:4 * c + 4, :],
                          s_wl)

            @blk.tensor
            def _(t):
                t.wait_ge(s_wl.h, s_wl.v)
                for tb in range(8):
                    for ht in range(16):
                        n = tb * 16 + ht
                        if n >= 2:
                            t.wait_ge(v_ht.h, b_ht + n - 1)
                        for kt in range(8):
                            t.matmul(ps12[:, (n % 2) * 512:(n % 2) * 512 + 512],
                                     w1_sb[:, kt * 2048 + ht * 128:kt * 2048 + ht * 128 + 128],
                                     xT_sb[:, kt * T + tb * 512:kt * T + tb * 512 + 512],
                                     start=(kt == 0), stop=(kt == 7),
                                     skip_group_check=True)
                        for kt in range(8):
                            mm = t.matmul(
                                ps12[:, (2 + n % 2) * 512:(2 + n % 2) * 512 + 512],
                                w2_sb[:, kt * 2048 + ht * 128:kt * 2048 + ht * 128 + 128],
                                xT_sb[:, kt * T + tb * 512:kt * T + tb * 512 + 512],
                                start=(kt == 0), stop=(kt == 7),
                                skip_group_check=True)
                        mm.then_inc(t_m12.h, 1)
                        t_m12.v += 1
                    t.wait_ge(v_ht.h, b_ht + 16 * (tb + 1))
                    for tt in range(4):
                        for db in range(2):
                            m = tb * 8 + tt * 2 + db
                            if m >= 2:
                                t.wait_ge(v_ac.h, b_ac + m - 1)
                            for ht in range(16):
                                mm = t.matmul(
                                    ps3[:, (m % 2) * 512:(m % 2) * 512 + 512],
                                    ht_sb[:, ht * 512 + tt * 128:ht * 512 + tt * 128 + 128],
                                    w3_sb[:, ht * 1024 + db * 512:ht * 1024 + db * 512 + 512],
                                    start=(ht == 0), stop=(ht == 15),
                                    skip_group_check=True)
                            mm.then_inc(t_m3.h, 1)
                            t_m3.v += 1

            @blk.scalar
            def _(sc_raw):
                sc = k.eng(sc_raw, "scalar")
                for tb in range(8):
                    for ht in range(16):
                        n = tb * 16 + ht
                        sc.wait(t_m12, b_12 + n + 1)
                        if n >= 2:
                            sc.wait(v_ht, b_ht + n - 1)
                        sc.op(sc_raw.activation,
                              tsil[:, (n % 2) * 512:(n % 2) * 512 + 512],
                              ps12[:, (n % 2) * 512:(n % 2) * 512 + 512],
                              AF.Silu, inc=(s_ev, 1))

            def acc_slice(m):
                tb2, r = divmod(m, 8)
                tt2, db2 = divmod(r, 2)
                tg2 = tb2 * 4 + tt2
                return acc_hbm[tg2 * 128:(tg2 + 1) * 128,
                               db2 * 512:db2 * 512 + 512]

            @blk.sync
            def _(s_raw):
                s = k.eng(s_raw, "sync")
                if e > 0:
                    for m in range(4):
                        s.dma(accb[:, (m % 4) * 512:(m % 4) * 512 + 512],
                              acc_slice(m), s_als[m % 4])
                for m in range(64):
                    s.wait(v_ac, b_ac + m + 1)
                    s.dma(acc_slice(m),
                          accb[:, (m % 4) * 512:(m % 4) * 512 + 512], s_sts[m % 4])
                    if e > 0 and m + 4 < 64:
                        s.wait(s_sts[m % 4], b_sts[m % 4] + 16 * (m // 4 + 1))
                        s.dma(accb[:, (m % 4) * 512:(m % 4) * 512 + 512],
                              acc_slice(m + 4), s_als[m % 4])
                for ss in s_sts:
                    s_raw.wait_ge(ss.h, ss.v)

            @blk.vector
            def _(v_raw):
                v = k.eng(v_raw, "vector")
                for tb in range(8):
                    for ht in range(16):
                        n = tb * 16 + ht
                        v.wait(s_ev, b_ev + n + 1)
                        if ht == 0 and tb >= 1:
                            v.wait(t_m3, b_3 + 8 * tb)
                        v.op(v_raw.tensor_tensor,
                             ht_sb[:, ht * 512:(ht + 1) * 512],
                             tsil[:, (n % 2) * 512:(n % 2) * 512 + 512],
                             ps12[:, (2 + n % 2) * 512:(2 + n % 2) * 512 + 512],
                             ALU.mult, inc=(v_ht, 1))
                    for tt in range(4):
                        for db in range(2):
                            m = tb * 8 + tt * 2 + db
                            tg = tb * 4 + tt
                            v.wait(t_m3, b_3 + m + 1)
                            if e == 0 and m >= 4:
                                v.wait(s_sts[m % 4], b_sts[m % 4] + 16 * (m // 4))
                            gcol = gw_sb[:, tg * 8 + e:tg * 8 + e + 1]
                            if e == 0:
                                v.op(v_raw.tensor_scalar,
                                     accb[:, (m % 4) * 512:(m % 4) * 512 + 512],
                                     ps3[:, (m % 2) * 512:(m % 2) * 512 + 512],
                                     gcol, None, ALU.mult, inc=(v_ac, 1))
                            else:
                                v.wait(s_als[m % 4], b_als[m % 4] + 16 * (m // 4 + 1))
                                v.op(v_raw.scalar_tensor_tensor,
                                     accb[:, (m % 4) * 512:(m % 4) * 512 + 512],
                                     ps3[:, (m % 2) * 512:(m % 2) * 512 + 512],
                                     gcol,
                                     accb[:, (m % 4) * 512:(m % 4) * 512 + 512],
                                     ALU.mult, ALU.add, inc=(v_ac, 1))



def emit_add3_ln(k: K, a_hbm, b_hbm, c_hbm, out_hbm, tag):
    nc = k.nc
    s_xls = [k.sem("s_xld0"), k.sem("s_xld1")]
    s_lnv, s_lns = k.sem("ln_v"), k.sem("ln_s")
    sout = k.sem("v_out")
    ssts = [k.sem("g_st0"), k.sem("g_st1")]
    b_xls = [s.v for s in s_xls]
    b_sts = [s.v for s in ssts]
    b_lnv, b_lns, b_out = s_lnv.v, s_lns.v, sout.v
    aa = k.sbuf(f"a3_a_{tag}", [128, 2 * 1024], F32)
    bb = k.sbuf(f"a3_b_{tag}", [128, 2 * 1024], F32)
    cc = k.sbuf(f"a3_c_{tag}", [128, 2 * 1024], F32)
    vt = k.sbuf(f"a3_v_{tag}", [128, 2 * 1024], F32)
    sqs = k.sbuf(f"a3_sq_{tag}", [128, 1024], F32)
    stats = k.sbuf(f"a3_st_{tag}", [128, 8], F32)
    outt = k.sbuf(f"a3_o_{tag}", [128, 2 * 1024], F32)

    with nc.Block() as blk:
        @blk.sync
        def _(s_raw):
            s = k.eng(s_raw, "sync")
            for ti in range(32):
                if ti >= 2:
                    s_raw.wait_ge(s_lnv.h, b_lnv + 2 * (ti - 2) + 1)
                sl = (ti % 2) * 1024
                s.dma(aa[:, sl:sl + 1024], a_hbm[ti * 128:(ti + 1) * 128, :],
                      s_xls[ti % 2])
                s.dma(bb[:, sl:sl + 1024], b_hbm[ti * 128:(ti + 1) * 128, :],
                      s_xls[ti % 2])
                s.dma(cc[:, sl:sl + 1024], c_hbm[ti * 128:(ti + 1) * 128, :],
                      s_xls[ti % 2])

        @blk.vector
        def _(v_raw):
            v = k.eng(v_raw, "vector")
            for ti in range(32):
                sl = (ti % 2) * 1024
                va = vt[:, sl:sl + 1024]
                v.wait(s_xls[ti % 2], b_xls[ti % 2] + 48 * (ti // 2 + 1))
                v.op(v_raw.tensor_tensor, va, aa[:, sl:sl + 1024],
                     bb[:, sl:sl + 1024], ALU.add)
                v.op(v_raw.tensor_tensor, va, va, cc[:, sl:sl + 1024], ALU.add)
                v.op(v_raw.reduce_sum, stats[:, 1:2], va, axis=AX.X,
                     inc=(s_lnv, 1))
                v.wait(s_lns, b_lns + 2 * ti + 1)
                v.op(v_raw.tensor_scalar, stats[:, 2:3], stats[:, 1:2],
                     1.0 / D, None, ALU.mult, dep=True)
                v.op(v_raw.tensor_scalar, stats[:, 3:4], stats[:, 2:3],
                     stats[:, 2:3], None, ALU.mult, dep=True)
                v.op(v_raw.tensor_scalar, stats[:, 4:5], stats[:, 0:1],
                     1.0 / D, None, ALU.mult)
                v.op(v_raw.tensor_tensor, stats[:, 4:5], stats[:, 4:5],
                     stats[:, 3:4], ALU.subtract, inc=(s_lnv, 1), dep=True)
                v.wait(s_lns, b_lns + 2 * ti + 2)
                if ti >= 2:
                    v.wait(ssts[ti % 2], b_sts[ti % 2] + 16 * (ti // 2))
                v.op(v_raw.tensor_scalar, outt[:, sl:sl + 1024], va,
                     stats[:, 2:3], stats[:, 5:6], ALU.subtract, ALU.mult,
                     inc=(sout, 1), dep=True)

        @blk.scalar
        def _(sc_raw):
            sc = k.eng(sc_raw, "scalar")
            for ti in range(32):
                sc.wait(s_lnv, b_lnv + 2 * ti + 1)
                sc.op(sc_raw.activation, sqs[:, :],
                      vt[:, (ti % 2) * 1024:(ti % 2) * 1024 + 1024],
                      AF.Square, accum_out=stats[:, 0:1], inc=(s_lns, 1))
                sc.wait(s_lnv, b_lnv + 2 * ti + 2)
                sc.op(sc_raw.activation, stats[:, 5:6], stats[:, 4:5], AF.Ln,
                      bias=LNEPS)
                sc.op(sc_raw.activation, stats[:, 5:6], stats[:, 5:6], AF.Exp,
                      scale=-0.5, inc=(s_lns, 1))

        @blk.gpsimd
        def _(g_raw):
            g = k.eng(g_raw, "gpsimd")
            for ti in range(32):
                g.wait(sout, b_out + ti + 1)
                g.dma(out_hbm[ti * 128:(ti + 1) * 128, :],
                      outt[:, (ti % 2) * 1024:(ti % 2) * 1024 + 1024],
                      ssts[ti % 2])
            for ss in ssts:
                g_raw.wait_ge(ss.h, ss.v)


def body_l0ffn(k, x1_in, wq_in, wk_in, wv_in, wo_in, bo_in, l1s_in, l1b_in,
               l2s_in, l2b_in, gw_in, w1_in, w2_in, w3_in, x2_out, pfx="tf"):
    nc = k.nc
    qt_hbm = k.dram(f"{pfx}_scr_qt", [D, T], BF16)
    kt_hbm = k.dram(f"{pfx}_scr_kt", [D, T], BF16)
    ctxT_hbm = k.dram(f"{pfx}_scr_ctxT", [D, T], BF16)
    u_hbm = k.dram(f"{pfx}_scr_u", [T, D], F32)
    acc_hbm = k.dram(f"{pfx}_scr_acc", [T, D], F32)

    with contextlib.ExitStack() as ph:
        ln1s_bc = ph.enter_context(nc.sbuf_tensor("ln1s_bc", [128, 1024], F32))
        ln1b_bc = ph.enter_context(nc.sbuf_tensor("ln1b_bc", [128, 1024], F32))
        with k.scope():
            bcast_rows(k, [
                (l1s_in.rearrange("(o d) -> o d", o=1), ln1s_bc),
                (l1b_in.rearrange("(o d) -> o d", o=1), ln1b_bc),
            ], "l1")
        v_sb = ph.enter_context(nc.sbuf_tensor("vaug1", [128, 32 * 1040], BF16))
        cbias = ph.enter_context(nc.sbuf_tensor("cbias1", [128, 16], F32))
        with contextlib.ExitStack() as phx:
            xT = phx.enter_context(nc.sbuf_tensor("xT1", [128, 8 * T], BF16))
            with k.scope():
                emit_ln_transpose(k, x1_in[:, :], xT, "l1",
                                  ln_scale_bc=ln1s_bc[:, :],
                                  ln_shift_bc=ln1b_bc[:, :])

            wq_sb = phx.enter_context(nc.sbuf_tensor("wq_sb", [128, 8 * D], BF16))
            wk_sb = phx.enter_context(nc.sbuf_tensor("wk_sb", [128, 8 * D], BF16))
            wv_sb = phx.enter_context(nc.sbuf_tensor("wv_sb", [128, 8 * D], BF16))
            s_wl = k.sem("g_wld")
            with nc.Block() as blk:
                @blk.gpsimd
                def _(g_raw):
                    g = k.eng(g_raw, "gpsimd")
                    for kt in range(8):
                        g.dma(wq_sb[:, kt * D:(kt + 1) * D],
                              wq_in[kt * 128:(kt + 1) * 128, :], s_wl)
                        g.dma(wk_sb[:, kt * D:(kt + 1) * D],
                              wk_in[kt * 128:(kt + 1) * 128, :], s_wl)
                        g.dma(wv_sb[:, kt * D:(kt + 1) * D],
                              wv_in[kt * 128:(kt + 1) * 128, :], s_wl)
                    g_raw.wait_ge(s_wl.h, s_wl.v)

            def lhsT_qk(kind, kt, mb):
                w = wq_sb if kind == "q" else wk_sb
                return w[:, kt * D + mb * 128:kt * D + mb * 128 + 128]

            def rhs_v(kt, nb):
                return wv_sb[:, kt * D + nb * 512:kt * D + nb * 512 + 512]

            with k.scope():
                emit_qkv(k, xT, lhsT_qk, rhs_v, qt_hbm, kt_hbm, v_sb, "a1")
        with k.scope():
            emit_att_bounds(k, qt_hbm, kt_hbm, cbias, "a1")
        with k.scope():
            emit_attention(k, qt_hbm, kt_hbm, v_sb, cbias, ctxT_hbm, "a1")

    with contextlib.ExitStack() as ph3:
        wo_sb = ph3.enter_context(nc.sbuf_tensor("wo_sb", [128, 8 * D], BF16))
        bo_bc = ph3.enter_context(nc.sbuf_tensor("bo_bc", [128, 1024], F32))
        with k.scope():
            bcast_rows(k, [(bo_in.rearrange("(o d) -> o d", o=1), bo_bc)], "bo")
        s_wl2 = k.sem("g_wld")
        with nc.Block() as blk:
            @blk.gpsimd
            def _(g_raw):
                g = k.eng(g_raw, "gpsimd")
                for kt in range(8):
                    g.dma(wo_sb[:, kt * D:(kt + 1) * D],
                          wo_in[kt * 128:(kt + 1) * 128, :], s_wl2)
                g_raw.wait_ge(s_wl2.h, s_wl2.v)

        def rhs_o(kt, nb):
            return wo_sb[:, kt * D + nb * 512:kt * D + nb * 512 + 512]

        with k.scope():
            emit_proj_residual_ln(k, ctxT_hbm, rhs_o, u_hbm, "a1",
                                  resid_hbm=x1_in[:, :], bias_bc=bo_bc[:, :],
                                  do_ln=False)

    with contextlib.ExitStack() as ph4:
        ln2s_bc = ph4.enter_context(nc.sbuf_tensor("ln2s_bc", [128, 1024], F32))
        ln2b_bc = ph4.enter_context(nc.sbuf_tensor("ln2b_bc", [128, 1024], F32))
        with k.scope():
            bcast_rows(k, [
                (l2s_in.rearrange("(o d) -> o d", o=1), ln2s_bc),
                (l2b_in.rearrange("(o d) -> o d", o=1), ln2b_bc),
            ], "l2")
        xT2 = ph4.enter_context(nc.sbuf_tensor("xT2", [128, 8 * T], BF16))
        gw_w_sb = ph4.enter_context(nc.sbuf_tensor("gw_w_sb", [128, 8 * 8], F32))
        gw_sb = ph4.enter_context(nc.sbuf_tensor("gw_sb", [128, 8 * 32], F32))
        s_gl = k.sem("s_gld")
        with nc.Block() as blk:
            @blk.sync
            def _(s_raw):
                s = k.eng(s_raw, "sync")
                gv = gw_w_sb.rearrange("p (kt e) -> p kt e", e=8)
                gs = gw_in.rearrange("(kt p) e -> p kt e", p=128)
                s.dma(gv[:, 0:4, :], gs[:, 0:4, :], s_gl)
                s.dma(gv[:, 4:8, :], gs[:, 4:8, :], s_gl)
                s_raw.wait_ge(s_gl.h, s_gl.v)
        with k.scope():
            emit_ln_transpose(k, u_hbm[:, :], xT2, "l2",
                              ln_scale_bc=ln2s_bc[:, :], ln_shift_bc=ln2b_bc[:, :],
                              gate=(gw_w_sb, gw_sb))
        with k.scope():
            emit_moe(k, xT2, gw_sb, w1_in, w2_in, w3_in, acc_hbm, "moe")

    with k.scope():
        emit_add3_ln(k, x1_in[:, :], u_hbm[:, :], acc_hbm[:, :], x2_out, "f")


def build_l0ffn():
    k = K("l0ffn")
    x1_in = k.dram("x1_in", [T, D], F32, kind="ExternalInput")
    wq_in = k.dram("wq_in", [D, D], F32, kind="ExternalInput")
    wk_in = k.dram("wk_in", [D, D], F32, kind="ExternalInput")
    wv_in = k.dram("wv_in", [D, D], F32, kind="ExternalInput")
    wo_in = k.dram("wo_in", [D, D], F32, kind="ExternalInput")
    bo_in = k.dram("bo_in", [D], F32, kind="ExternalInput")
    l1s_in = k.dram("ln1s_in", [D], F32, kind="ExternalInput")
    l1b_in = k.dram("ln1b_in", [D], F32, kind="ExternalInput")
    l2s_in = k.dram("ln2s_in", [D], F32, kind="ExternalInput")
    l2b_in = k.dram("ln2b_in", [D], F32, kind="ExternalInput")
    gw_in = k.dram("gatew_in", [D, 8], F32, kind="ExternalInput")
    w1_in = k.dram("moew1_in", [8, 1024, 2048], F32, kind="ExternalInput")
    w2_in = k.dram("moew2_in", [8, 1024, 2048], F32, kind="ExternalInput")
    w3_in = k.dram("moew3_in", [8, 2048, 1024], F32, kind="ExternalInput")
    x2_out = k.dram("x2_out", [T, D], F32, kind="ExternalOutput")
    declare_and_load_consts(k)
    body_l0ffn(k, x1_in, wq_in, wk_in, wv_in, wo_in, bo_in, l1s_in, l1b_in,
               l2s_in, l2b_in, gw_in, w1_in, w2_in, w3_in, x2_out)
    return k


def run_l0ffn(x1_full, wq, wk, wv, wo, bo, l1s, l1b, l2s, l2b, gw, w1, w2, w3):
    k = build_l0ffn()
    consts = host_consts()
    in_maps = []
    for c in range(NCORES):
        m = dict(consts)
        m["x1_in"] = np.ascontiguousarray(x1_full[c])
        m.update(wq_in=wq, wk_in=wk, wv_in=wv, wo_in=wo, bo_in=bo,
                 ln1s_in=l1s, ln1b_in=l1b, ln2s_in=l2s, ln2b_in=l2b,
                 gatew_in=gw, moew1_in=w1, moew2_in=w2, moew3_in=w3)
        in_maps.append(m)
    res = run_bass_kernel_spmd(k.nc, in_maps, core_ids=list(range(NCORES)))
    k.close()
    return np.stack([res.results[c]["x2_out"] for c in range(NCORES)])



def build_full():
    k = K("full")

    def fin(name, shape):
        return k.dram(name, shape, F32, kind="ExternalInput")

    x_in = fin("L0_attn_X", [T, D])
    a_q = fin("L0_attn_q_proj", [D, D])
    a_k = fin("L0_attn_k_proj", [D, D])
    a_v = fin("L0_attn_v_proj", [D, D])
    a_o = fin("L0_attn_o_proj", [D, D])
    f_l1s = fin("L0_ffn_ln1_scale", [D])
    f_l1b = fin("L0_ffn_ln1_shift", [D])
    f_wq = fin("L0_ffn_wq", [D, D])
    f_wk = fin("L0_ffn_wk", [D, D])
    f_wv = fin("L0_ffn_wv", [D, D])
    f_wo = fin("L0_ffn_wo", [D, D])
    f_bo = fin("L0_ffn_bo", [D])
    f_l2s = fin("L0_ffn_ln2_scale", [D])
    f_l2b = fin("L0_ffn_ln2_shift", [D])
    f_gw = fin("L0_ffn_gate_w", [D, 8])
    f_w1 = fin("L0_ffn_w1", [8, 1024, 2048])
    f_w2 = fin("L0_ffn_w2", [8, 1024, 2048])
    f_w3 = fin("L0_ffn_w3", [8, 2048, 1024])
    m_win = fin("L1_attn_W_in", [4096, 1024])
    m_cw = fin("L1_attn_conv_w", [2048, 4])
    m_cb = fin("L1_attn_conv_b", [2048])
    m_wx = fin("L1_attn_W_xproj", [96, 2048])
    m_wdt = fin("L1_attn_W_dt", [2048, 64])
    m_bdt = fin("L1_attn_b_dt", [2048])
    m_alog = fin("L1_attn_A_log", [2048, 16])
    m_dp = fin("L1_attn_Dp", [2048])
    m_wout = fin("L1_attn_W_out", [1024, 2048])
    t_w1 = fin("L1_ffn_w1_w", [4096, 1024])
    t_s1 = fin("L1_ffn_w1_s", [4096])
    t_w2 = fin("L1_ffn_w2_w", [4096, 1024])
    t_s2 = fin("L1_ffn_w2_s", [4096])
    t_h1 = fin("L1_ffn_h1_w", [4096, 4096])
    t_sh1 = fin("L1_ffn_h1_s", [4096])
    t_h2 = fin("L1_ffn_h2_w", [4096, 4096])
    t_sh2 = fin("L1_ffn_h2_s", [4096])
    t_w3 = fin("L1_ffn_w3_w", [1024, 4096])
    t_s3 = fin("L1_ffn_w3_s", [1024])
    sel_in = k.dram("c_sel_bf", [32, 32 * 128], BF16, kind="ExternalInput")
    y_out = k.dram("y_out", [T, D], F32, kind="ExternalOutput")

    x1 = k.dram("scr_x1", [T, D], F32)
    x2 = k.dram("scr_x2", [T, D], F32)
    x3 = k.dram("scr_x3", [T, D], F32)

    declare_and_load_consts(k)
    body_l0attn(k, x_in, a_q, a_k, a_v, a_o, x1)
    body_l0ffn(k, x1, f_wq, f_wk, f_wv, f_wo, f_bo, f_l1s, f_l1b, f_l2s,
               f_l2b, f_gw, f_w1, f_w2, f_w3, x2)
    body_mamba(k, x2, m_win, m_cw, m_cb, m_wx, m_wdt, m_bdt, m_alog, m_dp,
               m_wout, sel_in, x3)
    body_l1ffn(k, x3, t_w1, t_w2, t_h1, t_h2, t_w3, t_s1, t_s2, t_sh1, t_sh2,
               t_s3, y_out)
    return k


def kernel(**inputs):
    g = {n: _f32(v) for n, v in inputs.items()}
    k = build_full()
    consts = host_consts()
    consts["c_sel_bf"] = host_sel_const()
    in_maps = []
    for c in range(NCORES):
        m = dict(consts)
        for n, v in g.items():
            if n == "L0_attn_X":
                m[n] = np.ascontiguousarray(v[c])
            else:
                m[n] = v
        in_maps.append(m)
    res = run_bass_kernel_spmd(k.nc, in_maps, core_ids=list(range(NCORES)))
    k.close()
    return np.stack([res.results[c]["y_out"] for c in range(NCORES)])


def kernel_unfused(**inputs):
    g = {n: _f32(v) for n, v in inputs.items()}
    x1 = run_l0attn(g["L0_attn_X"], g["L0_attn_q_proj"], g["L0_attn_k_proj"],
                    g["L0_attn_v_proj"], g["L0_attn_o_proj"])
    x2 = run_l0ffn(x1, g["L0_ffn_wq"], g["L0_ffn_wk"], g["L0_ffn_wv"],
                   g["L0_ffn_wo"], g["L0_ffn_bo"], g["L0_ffn_ln1_scale"],
                   g["L0_ffn_ln1_shift"], g["L0_ffn_ln2_scale"],
                   g["L0_ffn_ln2_shift"], g["L0_ffn_gate_w"], g["L0_ffn_w1"],
                   g["L0_ffn_w2"], g["L0_ffn_w3"])
    x3 = run_mamba(x2, g["L1_attn_W_in"], g["L1_attn_conv_w"],
                   g["L1_attn_conv_b"], g["L1_attn_W_xproj"], g["L1_attn_W_dt"],
                   g["L1_attn_b_dt"], g["L1_attn_A_log"], g["L1_attn_Dp"],
                   g["L1_attn_W_out"])
    y = run_l1ffn(x3, g["L1_ffn_w1_w"], g["L1_ffn_w1_s"], g["L1_ffn_w2_w"],
                  g["L1_ffn_w2_s"], g["L1_ffn_h1_w"], g["L1_ffn_h1_s"],
                  g["L1_ffn_h2_w"], g["L1_ffn_h2_s"], g["L1_ffn_w3_w"],
                  g["L1_ffn_w3_s"])
    return y


# version 08 (final)
# speedup vs baseline: 1.4036x; 1.0774x over previous
import contextlib
import math

import numpy as np
import ml_dtypes

import concourse.bass as bass
import concourse.mybir as mybir
from concourse.bass_utils import run_bass_kernel_spmd

F32 = mybir.dt.float32
BF16 = mybir.dt.bfloat16
AF = mybir.ActivationFunctionType
ALU = mybir.AluOpType
AX = mybir.AxisListType

NCORES = 8
T = 4096
D = 1024
NH = 16
DK = 64
ATT_SCALE = 0.125
LNEPS = 1e-5

BF = ml_dtypes.bfloat16



class SemT:
    def __init__(self, handle, name):
        self.h = handle
        self.v = 0
        self.name = name


class Eng:

    def __init__(self, raw, self_sem, default_dep=False):
        self.e = raw
        self.sem = self_sem
        self.default_dep = default_dep
        self.last = None
        self.last_inc = None

    def _pre(self):
        if self.last is not None:
            if self.last_inc is None:
                self.sem.v += 1
                self.last.then_inc(self.sem.h, 1)
                self.last_inc = (self.sem.h, self.sem.v)
            h, v = self.last_inc
            self.e.wait_ge(h, v)
            self.last = None
            self.last_inc = None

    def op(self, f, *a, inc=None, dep=None, **kw):
        if dep is None:
            dep = self.default_dep
        if dep:
            self._pre()
        inst = f(*a, **kw)
        self.last = inst
        if inc is not None:
            st, n = inc
            inst.then_inc(st.h, n)
            st.v += n
            self.last_inc = (st.h, st.v)
        else:
            self.last_inc = None
        return inst

    def wait(self, st: SemT, v: int):
        self.e.wait_ge(st.h, v)

    def dma(self, dst, src, st: SemT, dep=False):
        if dep:
            self._pre()
        inst = self.e.dma_start(dst, src)
        inst.then_inc(st.h, 16)
        st.v += 16
        return inst


class K:

    def __init__(self, name):
        self.nc = bass.Bass("TRN2", target_bir_lowering=False)
        self.es = contextlib.ExitStack()
        self.cur = self.es
        self.name = name
        self.sems = {}

    @contextlib.contextmanager
    def scope(self):
        prev = self.cur
        st = contextlib.ExitStack()
        self.cur = st
        try:
            yield
        finally:
            self.cur = prev
            st.close()

    def sem(self, name) -> SemT:
        if name not in self.sems:
            h = self.es.enter_context(self.nc.semaphore(name))
            self.sems[name] = SemT(h, name)
        return self.sems[name]

    def sbuf(self, name, shape, dt):
        return self.cur.enter_context(self.nc.sbuf_tensor(name, shape, dt))

    def psum(self, name, shape, dt=F32):
        return self.cur.enter_context(self.nc.psum_tensor(name, shape, dt))

    def dram(self, name, shape, dt, kind=None):
        if kind is None:
            return self.nc.dram_tensor(name, shape, dt)
        return self.nc.dram_tensor(name, shape, dt, kind=kind)

    def eng(self, raw, which) -> Eng:
        return Eng(raw, self.sem("self_" + which),
                   default_dep=which.startswith("gpsimd"))

    def close(self):
        self.es.close()


def host_consts():
    ident_bf = np.eye(128, dtype=BF)
    ident_f32 = np.eye(128, dtype=np.float32)
    ones_f32 = np.ones((1, 128), np.float32)
    ones_col_bf = np.ones((128, 1), dtype=BF)
    masks = np.zeros((4, 128, 512), np.float32)
    s = np.arange(128)[:, None]
    q = np.arange(512)[None, :]
    for m in range(4):
        masks[m] = (q >= s + 128 * m).astype(np.float32)
    r0ones = np.zeros((128, 128), dtype=BF)
    r0ones[0, :] = 1.0
    return {
        "c_ident_bf": ident_bf,
        "c_ident_f32": ident_f32,
        "c_ones_f32": ones_f32,
        "c_ones_col_bf": ones_col_bf,
        "c_masks_bf": masks.astype(BF),
        "c_r0ones_bf": r0ones,
    }


def declare_and_load_consts(k: K):
    ci = k.dram("c_ident_bf", [128, 128], BF16, kind="ExternalInput")
    cif = k.dram("c_ident_f32", [128, 128], F32, kind="ExternalInput")
    co = k.dram("c_ones_f32", [1, 128], F32, kind="ExternalInput")
    cob = k.dram("c_ones_col_bf", [128, 1], BF16, kind="ExternalInput")
    cm = k.dram("c_masks_bf", [4, 128, 512], BF16, kind="ExternalInput")
    cr0 = k.dram("c_r0ones_bf", [128, 128], BF16, kind="ExternalInput")
    k.ident_bf = k.sbuf("ident_bf_sb", [128, 128], BF16)
    k.ident_f32 = k.sbuf("ident_f32_sb", [128, 128], F32)
    k.ones_f32 = k.sbuf("ones_f32_sb", [1, 128], F32)
    k.ones_col_bf = k.sbuf("ones_col_bf_sb", [128, 1], BF16)
    k.masks_bf = k.sbuf("masks_bf_sb", [128, 4 * 512], BF16)
    k.r0ones = k.sbuf("r0ones_sb", [128, 128], BF16)
    ACT_CONSTS = [LNEPS, -0.5, ATT_SCALE, -1.0, 0.5]
    k.cst_sb = k.sbuf("actcst_sb", [128, len(ACT_CONSTS)], F32)
    sc = k.sem("s_cst")
    with k.nc.Block() as blk:
        @blk.sync
        def _(s_raw):
            s = k.eng(s_raw, "sync")
            s.dma(k.ident_bf[:, :], ci[:, :], sc)
            s.dma(k.ident_f32[:, :], cif[:, :], sc)
            s.dma(k.ones_f32[:, :], co[:, :], sc)
            s.dma(k.ones_col_bf[:, :], cob[:, :], sc)
            s.dma(k.masks_bf.rearrange("p (m q) -> p m q", m=4),
                  cm.rearrange("m p q -> p m q"), sc)
            s.dma(k.r0ones[:, :], cr0[:, :], sc)
            s_raw.wait_ge(sc.h, sc.v)

        @blk.vector
        def _(v_raw):
            v = k.eng(v_raw, "vector")
            for i, val in enumerate(ACT_CONSTS):
                v.op(v_raw.memset, k.cst_sb[:, i:i + 1], val)
    for i, val in enumerate(ACT_CONSTS):
        k.nc.const_aps.register(F32, val, k.cst_sb[:, i:i + 1])


def bcast_rows(k: K, jobs, tag):
    nc = k.nc
    total = sum(src.shape[1] for src, dst in jobs)
    row_sb = k.sbuf(f"bcrow_{tag}", [1, total], F32)
    chunks = []
    off = 0
    loads = []
    for src, dst in jobs:
        N = src.shape[1]
        loads.append((row_sb[0:1, off:off + N], src))
        for c0 in range(0, N, 512):
            c1 = min(N, c0 + 512)
            chunks.append((row_sb[0:1, off + c0:off + c1], dst[:, c0:c1], c1 - c0))
        off += N
    ps = k.psum(f"bcps_{tag}", [128, 4 * 512], F32)
    sld, smm, sev = k.sem("s_xld"), k.sem("t_mm"), k.sem("s_ev")
    b_ld, b_mm, b_ev = sld.v, smm.v, sev.v
    with nc.Block() as blk:
        @blk.sync
        def _(s_raw):
            s = k.eng(s_raw, "sync")
            for dst, src in loads:
                s.dma(dst, src, sld)

        @blk.tensor
        def _(t):
            t.wait_ge(sld.h, b_ld + 16 * len(loads))
            for n, (src, dst, w) in enumerate(chunks):
                if n >= 4:
                    t.wait_ge(sev.h, b_ev + n - 3)
                mm = t.matmul(ps[:, (n % 4) * 512:(n % 4) * 512 + w],
                              k.ones_f32[0:1, :], src, start=True, stop=True)
                mm.then_inc(smm.h, 1)
                smm.v += 1

        @blk.scalar
        def _(sc_raw):
            sc = k.eng(sc_raw, "scalar")
            for n, (src, dst, w) in enumerate(chunks):
                sc.wait(smm, b_mm + n + 1)
                sc.op(sc_raw.copy, dst, ps[:, (n % 4) * 512:(n % 4) * 512 + w],
                      inc=(sev, 1))



def tw_sbuf(k: K, src_ap, dst_sb, R, C, tag):
    nc = k.nc
    smm, sev = k.sem("t_mm"), k.sem("s_ev")
    slds = [k.sem("g_wld0"), k.sem("g_wld1")]
    nrb, ncb = R // 128, C // 128
    wrow = k.sbuf(f"twrow_{tag}", [128, 2 * C], BF16)
    pst = k.psum(f"twps_{tag}", [128, 2 * 1024], BF16)
    b_lds = [s.v for s in slds]
    b_mm, b_ev = smm.v, sev.v

    with nc.Block() as blk:
        @blk.gpsimd
        def _(g_raw):
            g = k.eng(g_raw, "gpsimd")
            for rb in range(nrb):
                if rb >= 2:
                    g_raw.wait_ge(smm.h, b_mm + (rb - 1) * ncb)
                g.dma(wrow[:, (rb % 2) * C:(rb % 2) * C + C],
                      src_ap[rb * 128:(rb + 1) * 128, :], slds[rb % 2])

        @blk.tensor
        def _(t):
            n = 0
            for rb in range(nrb):
                t.wait_ge(slds[rb % 2].h, b_lds[rb % 2] + 16 * (rb // 2 + 1))
                for cb in range(ncb):
                    if n >= 2:
                        t.wait_ge(sev.h, b_ev + n - 1)
                    mm = t.transpose(
                        pst[:, (n % 2) * 1024:(n % 2) * 1024 + 128],
                        wrow[:, (rb % 2) * C + cb * 128:(rb % 2) * C + cb * 128 + 128],
                        k.ident_bf[:, :])
                    mm.then_inc(smm.h, 1)
                    smm.v += 1
                    n += 1

        @blk.scalar
        def _(sc_raw):
            sc = k.eng(sc_raw, "scalar")
            n = 0
            for rb in range(nrb):
                for cb in range(ncb):
                    sc.wait(smm, b_mm + n + 1)
                    sc.op(sc_raw.copy,
                          dst_sb[:, cb * R + rb * 128:cb * R + rb * 128 + 128],
                          pst[:, (n % 2) * 1024:(n % 2) * 1024 + 128],
                          inc=(sev, 1))
                    n += 1


def tx_sbuf(k: K, src_x_hbm, dst_sb, tag, Tn=T, Dn=D):
    nc = k.nc
    smm, sev, svv = k.sem("t_mm"), k.sem("s_ev"), k.sem("v_ev")
    slds = [k.sem("s_xld0"), k.sem("s_xld1")]
    ntt, ndt = Tn // 128, Dn // 128
    xrow = k.sbuf(f"txrow_{tag}", [128, 2 * Dn], F32)
    xrow_bf = k.sbuf(f"txrowb_{tag}", [128, 2 * Dn], BF16)
    pst = k.psum(f"txps_{tag}", [128, 2 * 1024], BF16)
    b_lds = [s.v for s in slds]
    b_mm, b_ev, b_vv = smm.v, sev.v, svv.v

    with nc.Block() as blk:
        @blk.sync
        def _(s_raw):
            s = k.eng(s_raw, "sync")
            for tt in range(ntt):
                if tt >= 2:
                    s_raw.wait_ge(svv.h, b_vv + tt - 1)
                s.dma(xrow[:, (tt % 2) * Dn:(tt % 2) * Dn + Dn],
                      src_x_hbm[tt * 128:(tt + 1) * 128, :], slds[tt % 2])

        @blk.vector
        def _(v_raw):
            v = k.eng(v_raw, "vector")
            for tt in range(ntt):
                v.wait(slds[tt % 2], b_lds[tt % 2] + 16 * (tt // 2 + 1))
                if tt >= 2:
                    v.wait(smm, b_mm + (tt - 1) * ndt)
                v.op(v_raw.tensor_copy,
                     xrow_bf[:, (tt % 2) * Dn:(tt % 2) * Dn + Dn],
                     xrow[:, (tt % 2) * Dn:(tt % 2) * Dn + Dn],
                     inc=(svv, 1))

        @blk.tensor
        def _(t):
            n = 0
            for tt in range(ntt):
                t.wait_ge(svv.h, b_vv + tt + 1)
                for dt in range(ndt):
                    if n >= 2:
                        t.wait_ge(sev.h, b_ev + n - 1)
                    mm = t.transpose(
                        pst[:, (n % 2) * 1024:(n % 2) * 1024 + 128],
                        xrow_bf[:, (tt % 2) * Dn + dt * 128:(tt % 2) * Dn + dt * 128 + 128],
                        k.ident_bf[:, :])
                    mm.then_inc(smm.h, 1)
                    smm.v += 1
                    n += 1

        @blk.scalar
        def _(sc_raw):
            sc = k.eng(sc_raw, "scalar")
            n = 0
            for tt in range(ntt):
                for dt in range(ndt):
                    sc.wait(smm, b_mm + n + 1)
                    sc.op(sc_raw.copy,
                          dst_sb[:, dt * Tn + tt * 128:dt * Tn + tt * 128 + 128],
                          pst[:, (n % 2) * 1024:(n % 2) * 1024 + 128],
                          inc=(sev, 1))
                    n += 1



def emit_qkv(k: K, xT_sb, lhsT_tile, rhs_tile_v, qt_hbm, kt_hbm, v_sb, tag):
    nc = k.nc
    smm, sev = k.sem("t_mm"), k.sem("s_ev")
    ssts = [k.sem(f"s_st{i}") for i in range(4)]
    svv = k.sem("v_ev")
    b_mm, b_ev, b_vv = smm.v, sev.v, svv.v
    b_sts = [s.v for s in ssts]
    pq = k.psum(f"qkvps_{tag}", [128, 4 * 512], F32)
    ev = k.sbuf(f"qkvev_{tag}", [128, 4 * 512], BF16)

    jobs = [("v", st, nb) for st in range(32) for nb in range(2)]
    jobs += [(kind, mb, nb) for kind in ("q", "k") for mb in range(8) for nb in range(8)]
    nstores_before = []
    ns = 0
    for kind, a, nb in jobs:
        nstores_before.append(ns)
        if kind != "v":
            ns += 1

    with nc.Block() as blk:
        @blk.vector
        def _(v_raw):
            v = k.eng(v_raw, "vector")
            v.op(v_raw.memset, v_sb[:, :], 1.0, inc=(svv, 1))

        @blk.tensor
        def _(t):
            t.wait_ge(svv.h, b_vv + 1)
            for n, (kind, a, nb) in enumerate(jobs):
                bank = n % 4
                if n >= 4:
                    t.wait_ge(sev.h, b_ev + n - 3)
                for kt in range(8):
                    if kind == "v":
                        mm = t.matmul(pq[:, bank * 512:bank * 512 + 512],
                                      xT_sb[:, kt * T + a * 128:kt * T + a * 128 + 128],
                                      rhs_tile_v(kt, nb),
                                      start=(kt == 0), stop=(kt == 7))
                    else:
                        mm = t.matmul(pq[:, bank * 512:bank * 512 + 512],
                                      lhsT_tile(kind, kt, a),
                                      xT_sb[:, kt * T + nb * 512:kt * T + nb * 512 + 512],
                                      start=(kt == 0), stop=(kt == 7))
                mm.then_inc(smm.h, 1)
                smm.v += 1

        @blk.scalar
        def _(sc_raw):
            sc = k.eng(sc_raw, "scalar")
            for n, (kind, a, nb) in enumerate(jobs):
                bank = n % 4
                sc.wait(smm, b_mm + n + 1)
                if kind == "v":
                    out = v_sb.rearrange("p (st h dv) -> p st h dv", st=32, dv=65)[
                        :, a, nb * 8:nb * 8 + 8, 0:64]
                    sc.op(sc_raw.copy, out, pq[:, bank * 512:bank * 512 + 512],
                          inc=(sev, 1))
                else:
                    j = nstores_before[n]
                    if j >= 4:
                        sc.wait(ssts[j % 4], b_sts[j % 4] + 16 * (j // 4))
                    sc.op(sc_raw.copy, ev[:, bank * 512:bank * 512 + 512],
                          pq[:, bank * 512:bank * 512 + 512], inc=(sev, 1))

        @blk.sync
        def _(s_raw):
            s = k.eng(s_raw, "sync")
            j = 0
            for n, (kind, a, nb) in enumerate(jobs):
                if kind == "v":
                    continue
                bank = n % 4
                s.wait(sev, b_ev + n + 1)
                dst = qt_hbm if kind == "q" else kt_hbm
                s.dma(dst[a * 128:(a + 1) * 128, nb * 512:(nb + 1) * 512],
                      ev[:, bank * 512:bank * 512 + 512], ssts[j % 4])
                j += 1
            for ss in ssts:
                s_raw.wait_ge(ss.h, ss.v)



def emit_att_bounds(k: K, qt_hbm, kt_hbm, cbias_sb, tag):
    nc = k.nc
    smm, svv, ssc = k.sem("t_mm"), k.sem("v_ev"), k.sem("s_ev")
    slds = [k.sem("s_xld0"), k.sem("s_xld1")]
    b_lds = [s.v for s in slds]
    b_mm, b_vv, b_sc = smm.v, svv.v, ssc.v
    slab = k.sbuf(f"bnd_slab_{tag}", [128, 2 * T], BF16)
    sq = k.sbuf(f"bnd_sq_{tag}", [128, 2 * T], BF16)
    nmax = k.sbuf(f"bnd_nmax_{tag}", [1, 33], F32)
    cneg = k.sbuf(f"bnd_cneg_{tag}", [1, 16], F32)
    ps_n = k.psum(f"bnd_ps_{tag}", [1, 2 * 512], F32)
    ps_c = k.psum(f"bnd_psc_{tag}", [128, 16], F32)
    n_chunk = T // 512

    with nc.Block() as blk:
        @blk.sync
        def _(s_raw):
            s = k.eng(s_raw, "sync")
            for i in range(16):
                src = qt_hbm if i < 8 else kt_hbm
                r0 = (i % 8) * 128
                if i >= 2:
                    s_raw.wait_ge(ssc.h, b_sc + (i - 1))
                s.dma(slab[:, (i % 2) * T:(i % 2) * T + T], src[r0:r0 + 128, :],
                      slds[i % 2])

        @blk.scalar
        def _(sc_raw):
            sc = k.eng(sc_raw, "scalar")
            for i in range(16):
                sc.wait(slds[i % 2], b_lds[i % 2] + 16 * (i // 2 + 1))
                if i >= 2:
                    sc.wait(smm, b_mm + (i - 1) * 2 * n_chunk)
                sc.op(sc_raw.activation, sq[:, (i % 2) * T:(i % 2) * T + T],
                      slab[:, (i % 2) * T:(i % 2) * T + T], AF.Square,
                      inc=(ssc, 1))

        @blk.tensor
        def _(t):
            n = 0
            for i in range(16):
                t.wait_ge(ssc.h, b_sc + i + 1)
                for half in range(2):
                    for ch in range(n_chunk):
                        if n >= 2:
                            t.wait_ge(svv.h, b_vv + 2 * (n - 1))
                        mm = t.matmul(
                            ps_n[:, (n % 2) * 512:(n % 2) * 512 + 512],
                            k.ones_col_bf[half * 64:half * 64 + 64, 0:1],
                            sq[half * 64:half * 64 + 64,
                               (i % 2) * T + ch * 512:(i % 2) * T + ch * 512 + 512],
                            start=True, stop=True)
                        mm.then_inc(smm.h, 1)
                        smm.v += 1
                        n += 1

        @blk.vector
        def _(v_raw):
            v = k.eng(v_raw, "vector")
            v.op(v_raw.memset, nmax[:, :], 0.0)
            n = 0
            for i in range(16):
                for half in range(2):
                    hcol = (0 if i < 8 else 16) + (i % 8) * 2 + half
                    for ch in range(n_chunk):
                        v.wait(smm, b_mm + n + 1)
                        v.op(v_raw.reduce_max, nmax[0:1, 32:33],
                             ps_n[0:1, (n % 2) * 512:(n % 2) * 512 + 512],
                             axis=AX.X)
                        v.op(v_raw.tensor_tensor, nmax[0:1, hcol:hcol + 1],
                             nmax[0:1, hcol:hcol + 1], nmax[0:1, 32:33],
                             ALU.max, inc=(svv, 2), dep=True)
                        n += 1

    with nc.Block() as blk:
        @blk.vector
        def _(v_raw):
            v = k.eng(v_raw, "vector")
            v.op(v_raw.tensor_tensor, cneg[0:1, 0:16], nmax[0:1, 0:16],
                 nmax[0:1, 16:32], ALU.mult)

    with nc.Block() as blk:
        @blk.scalar
        def _(sc_raw):
            sc = k.eng(sc_raw, "scalar")
            sc.op(sc_raw.activation, cneg[0:1, :], cneg[0:1, :], AF.Sqrt)

    with nc.Block() as blk:
        @blk.tensor
        def _(t):
            t.matmul(ps_c[:, 0:16], k.ones_f32[0:1, :], cneg[0:1, 0:16],
                     start=True, stop=True).then_inc(smm.h, 1)
            smm.v += 1

    with nc.Block() as blk:
        @blk.vector
        def _(v_raw):
            v = k.eng(v_raw, "vector")
            v.op(v_raw.tensor_scalar, cbias_sb[:, 0:16], ps_c[:, 0:16],
                 -1.02 * ATT_SCALE, None, ALU.mult)



def emit_attention(k: K, qt_hbm, kt_hbm, v_sb, cbias_sb, ctxT_hbm, tag):
    nc = k.nc
    s_qks = [k.sem("s_qkld0"), k.sem("s_qkld1")]
    b_qks = [s.v for s in s_qks]
    s_sc = k.sem("t_sc")
    s_ex = k.sem("s_exp")
    s_mk = k.sem("v_msk")
    s_av = k.sem("t_av")
    s_rc = k.sem("v_rec")
    s_rb = k.sem("t_rbc")
    s_re = k.sem("s_rev")
    s_cm = k.sem("v_cmul")
    s_sts = [k.sem(f"g_cst{i}") for i in range(4)]
    b_sts = [s.v for s in s_sts]
    b_sc, b_ex, b_mk, b_av = s_sc.v, s_ex.v, s_mk.v, s_av.v
    b_rc, b_rb, b_re, b_cm = s_rc.v, s_rb.v, s_re.v, s_cm.v

    qkt = k.sbuf(f"qkt_{tag}", [128, 4 * T], BF16)
    expp = k.sbuf(f"expp_{tag}", [128, 4 * 512], BF16)
    rrec = k.sbuf(f"rrec_{tag}", [128, 2 * 512], BF16)
    rrecf = k.sbuf(f"rrecf_{tag}", [1, 2 * 512], F32)
    rbc = k.sbuf(f"rbc_{tag}", [64, 2 * 512], F32)
    ctxo = k.sbuf(f"ctxo_{tag}", [64, 4 * 512], BF16)
    ps_sc = k.psum(f"ps_sc_{tag}", [128, 4 * 512], F32)
    ps_cx = k.psum(f"ps_cx_{tag}", [65, 2 * 512], F32)
    ps_rb = k.psum(f"ps_rb_{tag}", [128, 2 * 512], F32)
    z_qk = k.sem("v_zqk")
    b_zq = z_qk.v

    def ntiles(qb):
        return 4 * qb + 4

    TPH = sum(ntiles(qb) for qb in range(8))
    wlist = [(h, qb) for h in range(NH) for qb in range(8)]
    cumtiles = [0]
    for (h, qb) in wlist:
        cumtiles.append(cumtiles[-1] + ntiles(qb))
    pm_of = {}
    pm = 0
    for w, (h, qb) in enumerate(wlist):
        for st in range(ntiles(qb)):
            if st >= 4 * qb:
                pm_of[(w, st)] = pm
                pm += 1
    NPART = pm

    v_aug = v_sb.rearrange("p (st h dv) -> p st h dv", st=32, dv=65)

    with nc.Block() as blk:
        @blk.sync
        def _(s_raw):
            s = k.eng(s_raw, "sync")
            for h in range(NH):
                if h >= 2:
                    s_raw.wait_ge(s_av.h, b_av + cumtiles[(h - 1) * 8])
                s.dma(qkt[0:64, (h % 2) * 2 * T:(h % 2) * 2 * T + T],
                      qt_hbm[h * 64:(h + 1) * 64, :], s_qks[h % 2])
                s.dma(qkt[0:64, (h % 2) * 2 * T + T:(h % 2) * 2 * T + 2 * T],
                      kt_hbm[h * 64:(h + 1) * 64, :], s_qks[h % 2])

        @blk.tensor
        def _(t):
            t.wait_ge(z_qk.h, b_zq + 1)
            for w, (h, qb) in enumerate(wlist):
                hb = h % 2
                if qb == 0:
                    t.wait_ge(s_qks[h % 2].h, b_qks[h % 2] + 32 * (h // 2 + 1))
                g0 = cumtiles[w]
                LA = 2
                nt = ntiles(qb)
                for j in range(nt + LA):
                    if j < nt:
                        st = j
                        g = g0 + st
                        if g >= 4:
                            t.wait_ge(s_ex.h, b_ex + g - 3)
                        mm = t.matmul(
                            ps_sc[:, (g % 4) * 512:(g % 4) * 512 + 512],
                            qkt[:, hb * 2 * T + T + st * 128:hb * 2 * T + T + st * 128 + 128],
                            qkt[:, hb * 2 * T + qb * 512:hb * 2 * T + qb * 512 + 512],
                            start=True, stop=True, skip_group_check=True)
                        mm.then_inc(s_sc.h, 1)
                        s_sc.v += 1
                    if j >= LA:
                        st = j - LA
                        g = g0 + st
                        if st == 0 and w >= 2:
                            t.wait_ge(s_cm.h, b_cm + w - 1)
                        if st >= 4 * qb:
                            t.wait_ge(s_mk.h, b_mk + pm_of[(w, st)] + 1)
                        else:
                            t.wait_ge(s_ex.h, b_ex + g + 1)
                        mm = t.matmul(
                            ps_cx[:, (w % 2) * 512:(w % 2) * 512 + 512],
                            v_aug[:, st, h, :],
                            expp[:, (g % 4) * 512:(g % 4) * 512 + 512],
                            start=(st == 0), stop=(st == nt - 1),
                            skip_group_check=True)
                        mm.then_inc(s_av.h, 1)
                        s_av.v += 1
                if w >= 1:
                    t.wait_ge(s_rc.h, b_rc + w)
                    if w >= 3:
                        t.wait_ge(s_re.h, b_re + w - 2)
                    wb = (w - 1) % 2
                    mm = t.matmul(ps_rb[:, wb * 512:wb * 512 + 512],
                                  k.r0ones[:, :],
                                  rrec[:, wb * 512:wb * 512 + 512],
                                  start=True, stop=True, skip_group_check=True)
                    mm.then_inc(s_rb.h, 1)
                    s_rb.v += 1
            t.wait_ge(s_rc.h, b_rc + 128)
            t.wait_ge(s_re.h, b_re + 126)
            wb = 127 % 2
            mm = t.matmul(ps_rb[:, wb * 512:wb * 512 + 512],
                          k.r0ones[:, :],
                          rrec[:, wb * 512:wb * 512 + 512],
                          start=True, stop=True, skip_group_check=True)
            mm.then_inc(s_rb.h, 1)
            s_rb.v += 1

        @blk.scalar
        def _(sc_raw):
            sc = k.eng(sc_raw, "scalar")
            for w, (h, qb) in enumerate(wlist):
                g0 = cumtiles[w]
                for st in range(ntiles(qb)):
                    g = g0 + st
                    sc.wait(s_sc, b_sc + g + 1)
                    if g >= 4:
                        sc.wait(s_av, b_av + g - 3)
                    sc.op(sc_raw.activation,
                          expp[:, (g % 4) * 512:(g % 4) * 512 + 512],
                          ps_sc[:, (g % 4) * 512:(g % 4) * 512 + 512],
                          AF.Exp, bias=cbias_sb[:, h:h + 1], scale=ATT_SCALE,
                          inc=(s_ex, 1))
                if w >= 1:
                    sc.wait(s_rb, b_rb + w)
                    if w >= 3:
                        sc.wait(s_cm, b_cm + w - 2)
                    wb = (w - 1) % 2
                    sc.op(sc_raw.copy, rbc[:, wb * 512:wb * 512 + 512],
                          ps_rb[0:64, wb * 512:wb * 512 + 512], inc=(s_re, 1))
            sc.wait(s_rb, b_rb + 128)
            sc.wait(s_cm, b_cm + 126)
            wb = 127 % 2
            sc.op(sc_raw.copy, rbc[:, wb * 512:wb * 512 + 512],
                  ps_rb[0:64, wb * 512:wb * 512 + 512], inc=(s_re, 1))

        @blk.vector
        def _(v_raw):
            v = k.eng(v_raw, "vector")
            v.op(v_raw.memset, qkt[64:128, :], 0.0)
            v.op(v_raw.memset, rrec[:, :], 0.0, inc=(z_qk, 1))

            def do_mult(w):
                wb = w % 2
                (h, qb) = wlist[w]
                v.wait(s_re, b_re + w + 1)
                if w >= 4:
                    v.wait(s_sts[w % 4], b_sts[w % 4] + 16 * (w // 4))
                v.op(v_raw.tensor_tensor,
                     ctxo[:, (w % 4) * 512:(w % 4) * 512 + 512],
                     ps_cx[:, wb * 512:wb * 512 + 512].rearrange(
                         "p n -> p n")[0:64, :] if False else
                     ps_cx[0:64, wb * 512:wb * 512 + 512],
                     rbc[:, wb * 512:wb * 512 + 512],
                     ALU.mult, inc=(s_cm, 1))

            def do_recip(w):
                wb = w % 2
                v.wait(s_av, b_av + cumtiles[w + 1])
                v.op(v_raw.reciprocal, rrecf[0:1, wb * 512:wb * 512 + 512],
                     ps_cx[64:65, wb * 512:wb * 512 + 512])
                v.op(v_raw.tensor_copy, rrec[0:1, wb * 512:wb * 512 + 512],
                     rrecf[0:1, wb * 512:wb * 512 + 512], inc=(s_rc, 1),
                     dep=True)

            for w, (h, qb) in enumerate(wlist):
                if w >= 2:
                    do_mult(w - 2)
                if w >= 1:
                    do_recip(w - 1)
                g0 = cumtiles[w]
                for st in range(4 * qb, ntiles(qb)):
                    g = g0 + st
                    m = st - 4 * qb
                    v.wait(s_ex, b_ex + g + 1)
                    v.op(v_raw.tensor_tensor,
                         expp[:, (g % 4) * 512:(g % 4) * 512 + 512],
                         expp[:, (g % 4) * 512:(g % 4) * 512 + 512],
                         k.masks_bf[:, m * 512:m * 512 + 512],
                         ALU.mult, inc=(s_mk, 1))
            do_mult(126)
            do_recip(127)
            do_mult(127)

        @blk.gpsimd
        def _(g_raw):
            g = k.eng(g_raw, "gpsimd")
            for w, (h, qb) in enumerate(wlist):
                g.wait(s_cm, b_cm + w + 1)
                g.dma(ctxT_hbm[h * 64:(h + 1) * 64, qb * 512:(qb + 1) * 512],
                      ctxo[:, (w % 4) * 512:(w % 4) * 512 + 512], s_sts[w % 4])
            for ss in s_sts:
                g_raw.wait_ge(ss.h, ss.v)




def emit_proj_residual_ln(k: K, ctxT_hbm, rhs_tile, out_hbm, tag,
                          resid_hbm=None, resid2_hbm=None, bias_bc=None,
                          ln_scale_bc=None, ln_shift_bc=None,
                          ctx_rows=1024, do_ln=True):
    nc = k.nc
    s_cl = k.sem("s_ctxld")
    s_xls = [k.sem("s_xld2a"), k.sem("s_xld2b")]
    smm, sacc, sout = k.sem("t_mm"), k.sem("v_acc"), k.sem("v_out")
    ssts = [k.sem("g_st0"), k.sem("g_st1")]
    s_lnv, s_lns = k.sem("ln_v"), k.sem("ln_s")
    nkt = ctx_rows // 128
    ctx_sb = k.sbuf(f"proj_ctx_{tag}", [128, nkt * T], BF16)

    with nc.Block() as blk:
        @blk.sync
        def _(s_raw):
            s = k.eng(s_raw, "sync")
            for r in range(nkt):
                s.dma(ctx_sb[:, r * T:(r + 1) * T],
                      ctxT_hbm[r * 128:(r + 1) * 128, :], s_cl)
            s_raw.wait_ge(s_cl.h, s_cl.v)

    b_mm, b_acc, b_out = smm.v, sacc.v, sout.v
    b_sts = [s.v for s in ssts]
    b_xls = [s.v for s in s_xls]
    b_lnv, b_lns = s_lnv.v, s_lns.v
    ps = k.psum(f"proj_ps_{tag}", [128, 4 * 512], F32)
    vt = k.sbuf(f"proj_v_{tag}", [128, 2 * 1024], F32)
    res = k.sbuf(f"proj_res_{tag}", [128, 2 * 1024], F32)
    res2 = (k.sbuf(f"proj_res2_{tag}", [128, 2 * 1024], F32)
            if resid2_hbm is not None else None)
    sqs = k.sbuf(f"proj_sq_{tag}", [128, 1024], F32)
    stats = k.sbuf(f"proj_stats_{tag}", [128, 8], F32)
    outt = k.sbuf(f"proj_out_{tag}", [128, 2 * 1024], F32)
    nres_dma = (1 if resid_hbm is not None else 0) + (1 if resid2_hbm is not None else 0)

    with nc.Block() as blk:
        @blk.sync
        def _(s_raw):
            s = k.eng(s_raw, "sync")
            for qt in range(32):
                if qt >= 2 and nres_dma:
                    s_raw.wait_ge(sacc.h, b_acc + qt - 1)
                if resid_hbm is not None:
                    s.dma(res[:, (qt % 2) * 1024:(qt % 2) * 1024 + 1024],
                          resid_hbm[qt * 128:(qt + 1) * 128, :], s_xls[qt % 2])
                if resid2_hbm is not None:
                    s.dma(res2[:, (qt % 2) * 1024:(qt % 2) * 1024 + 1024],
                          resid2_hbm[qt * 128:(qt + 1) * 128, :], s_xls[qt % 2])

        @blk.tensor
        def _(t):
            for qt in range(32):
                for nb in range(2):
                    n = qt * 2 + nb
                    if n >= 4:
                        t.wait_ge(sacc.h, b_acc + qt - 1)
                    for kt in range(nkt):
                        mm = t.matmul(ps[:, (n % 4) * 512:(n % 4) * 512 + 512],
                                      ctx_sb[:, kt * T + qt * 128:kt * T + qt * 128 + 128],
                                      rhs_tile(kt, nb),
                                      start=(kt == 0), stop=(kt == nkt - 1))
                    mm.then_inc(smm.h, 1)
                    smm.v += 1

        @blk.vector
        def _(v_raw):
            v = k.eng(v_raw, "vector")
            for qt in range(32):
                v.wait(smm, b_mm + 2 * qt + 2)
                if nres_dma:
                    v.wait(s_xls[qt % 2],
                           b_xls[qt % 2] + 16 * nres_dma * (qt // 2 + 1))
                vt_ap = vt[:, (qt % 2) * 1024:(qt % 2) * 1024 + 1024]
                pb = (2 * qt) % 4
                ps_pair = ps[:, pb * 512:pb * 512 + 1024]
                accs = []
                if resid_hbm is not None:
                    accs.append(("tt", res[:, (qt % 2) * 1024:(qt % 2) * 1024 + 1024]))
                if resid2_hbm is not None:
                    accs.append(("tt", res2[:, (qt % 2) * 1024:(qt % 2) * 1024 + 1024]))
                if bias_bc is not None:
                    accs.append(("tt", bias_bc))
                if not accs:
                    v.op(v_raw.tensor_copy, vt_ap, ps_pair, inc=(sacc, 1))
                else:
                    first = accs[0][1]
                    last = (len(accs) == 1)
                    v.op(v_raw.tensor_tensor, vt_ap, ps_pair, first, ALU.add,
                         inc=(sacc, 1) if last else None)
                    for i, (_, other) in enumerate(accs[1:]):
                        last = (i == len(accs) - 2)
                        v.op(v_raw.tensor_tensor, vt_ap, vt_ap, other, ALU.add,
                             inc=(sacc, 1) if last else None)
                out_ap = outt[:, (qt % 2) * 1024:(qt % 2) * 1024 + 1024]
                if qt >= 2:
                    v.wait(ssts[qt % 2], b_sts[qt % 2] + 16 * (qt // 2))
                if do_ln:
                    v.op(v_raw.reduce_sum, stats[:, 1:2], vt_ap, axis=AX.X,
                         inc=(s_lnv, 1))
                    v.wait(s_lns, b_lns + 2 * qt + 1)
                    v.op(v_raw.tensor_scalar, stats[:, 2:3], stats[:, 1:2],
                         1.0 / D, None, ALU.mult, dep=True)
                    v.op(v_raw.tensor_scalar, stats[:, 3:4], stats[:, 2:3],
                         stats[:, 2:3], None, ALU.mult, dep=True)
                    v.op(v_raw.tensor_scalar, stats[:, 4:5], stats[:, 0:1],
                         1.0 / D, None, ALU.mult)
                    v.op(v_raw.tensor_tensor, stats[:, 4:5], stats[:, 4:5],
                         stats[:, 3:4], ALU.subtract, inc=(s_lnv, 1), dep=True)
                    v.wait(s_lns, b_lns + 2 * qt + 2)
                    if ln_scale_bc is not None:
                        v.op(v_raw.tensor_scalar, out_ap, vt_ap, stats[:, 2:3],
                             stats[:, 5:6], ALU.subtract, ALU.mult, dep=True)
                        v.op(v_raw.tensor_tensor, out_ap, out_ap, ln_scale_bc,
                             ALU.mult)
                        v.op(v_raw.tensor_tensor, out_ap, out_ap, ln_shift_bc,
                             ALU.add, inc=(sout, 1))
                    else:
                        v.op(v_raw.tensor_scalar, out_ap, vt_ap, stats[:, 2:3],
                             stats[:, 5:6], ALU.subtract, ALU.mult,
                             inc=(sout, 1), dep=True)
                else:
                    v.op(v_raw.tensor_copy, out_ap, vt_ap, inc=(sout, 1))

        @blk.scalar
        def _(sc_raw):
            sc = k.eng(sc_raw, "scalar")
            if not do_ln:
                return
            for qt in range(32):
                sc.wait(s_lnv, b_lnv + 2 * qt + 1)
                sc.op(sc_raw.activation, sqs[:, :],
                      vt[:, (qt % 2) * 1024:(qt % 2) * 1024 + 1024],
                      AF.Square, accum_out=stats[:, 0:1], inc=(s_lns, 1))
                sc.wait(s_lnv, b_lnv + 2 * qt + 2)
                sc.op(sc_raw.activation, stats[:, 5:6], stats[:, 4:5], AF.Ln,
                      bias=LNEPS)
                sc.op(sc_raw.activation, stats[:, 5:6], stats[:, 5:6], AF.Exp,
                      scale=-0.5, inc=(s_lns, 1))

        @blk.gpsimd
        def _(g_raw):
            g = k.eng(g_raw, "gpsimd")
            for qt in range(32):
                g.wait(sout, b_out + qt + 1)
                g.dma(out_hbm[qt * 128:(qt + 1) * 128, :],
                      outt[:, (qt % 2) * 1024:(qt % 2) * 1024 + 1024],
                      ssts[qt % 2])
            for ss in ssts:
                g_raw.wait_ge(ss.h, ss.v)




def body_l0attn(k, x_in, wq_in, wk_in, wv_in, wo_in, x1_out, pfx="a0"):
    nc = k.nc
    qt_hbm = k.dram(f"{pfx}_scr_qt", [D, T], BF16)
    kt_hbm = k.dram(f"{pfx}_scr_kt", [D, T], BF16)
    ctxT_hbm = k.dram(f"{pfx}_scr_ctxT", [D, T], BF16)

    with contextlib.ExitStack() as ph:
        wqT = ph.enter_context(nc.sbuf_tensor("wqT", [128, 8 * D], BF16))
        wkT = ph.enter_context(nc.sbuf_tensor("wkT", [128, 8 * D], BF16))
        wvT = ph.enter_context(nc.sbuf_tensor("wvT", [128, 8 * D], BF16))
        with k.scope():
            tw_sbuf(k, wq_in[:, :], wqT, D, D, "wq")
        with k.scope():
            tw_sbuf(k, wk_in[:, :], wkT, D, D, "wk")
        with k.scope():
            tw_sbuf(k, wv_in[:, :], wvT, D, D, "wv")

        v_sb = ph.enter_context(nc.sbuf_tensor("vaug0", [128, 32 * 1040], BF16))
        cbias = ph.enter_context(nc.sbuf_tensor("cbias0", [128, 16], F32))

        with contextlib.ExitStack() as phx:
            xT = phx.enter_context(nc.sbuf_tensor("xT0", [128, 8 * T], BF16))
            with k.scope():
                tx_sbuf(k, x_in[:, :], xT, "x0")

            def lhsT_qk(kind, kt, mb):
                w = wqT if kind == "q" else wkT
                return w[:, kt * D + mb * 128:kt * D + mb * 128 + 128]

            def rhs_v(kt, nb):
                return wvT[:, kt * D + nb * 512:kt * D + nb * 512 + 512]

            with k.scope():
                emit_qkv(k, xT, lhsT_qk, rhs_v, qt_hbm, kt_hbm, v_sb, "a0")

        with k.scope():
            emit_att_bounds(k, qt_hbm, kt_hbm, cbias, "a0")
        with k.scope():
            emit_attention(k, qt_hbm, kt_hbm, v_sb, cbias, ctxT_hbm, "a0")

    with contextlib.ExitStack() as ph3:
        woT = ph3.enter_context(nc.sbuf_tensor("woT", [128, 8 * D], BF16))
        with k.scope():
            tw_sbuf(k, wo_in[:, :], woT, D, D, "wo")

        def rhs_o(kt, nb):
            return woT[:, kt * D + nb * 512:kt * D + nb * 512 + 512]

        with k.scope():
            emit_proj_residual_ln(k, ctxT_hbm, rhs_o, x1_out, "a0",
                                  resid_hbm=x_in[:, :])


def build_l0attn():
    k = K("l0attn")
    x_in = k.dram("x_in", [T, D], F32, kind="ExternalInput")
    wq_in = k.dram("wq_in", [D, D], F32, kind="ExternalInput")
    wk_in = k.dram("wk_in", [D, D], F32, kind="ExternalInput")
    wv_in = k.dram("wv_in", [D, D], F32, kind="ExternalInput")
    wo_in = k.dram("wo_in", [D, D], F32, kind="ExternalInput")
    x1_out = k.dram("x1_out", [T, D], F32, kind="ExternalOutput")
    declare_and_load_consts(k)
    body_l0attn(k, x_in, wq_in, wk_in, wv_in, wo_in, x1_out)
    return k


def run_l0attn(x_full, q_proj, k_proj, v_proj, o_proj):
    k = build_l0attn()
    consts = host_consts()
    in_maps = []
    for c in range(NCORES):
        m = dict(consts)
        m["x_in"] = np.ascontiguousarray(x_full[c])
        m["wq_in"] = q_proj
        m["wk_in"] = k_proj
        m["wv_in"] = v_proj
        m["wo_in"] = o_proj
        in_maps.append(m)
    res = run_bass_kernel_spmd(k.nc, in_maps, core_ids=list(range(NCORES)))
    k.close()
    return np.stack([res.results[c]["x1_out"] for c in range(NCORES)])


def _f32(a):
    return np.ascontiguousarray(np.asarray(a, dtype=np.float32))




def tw_hbm(k: K, src_ap, dst_hbm, R, C, tag):
    nc = k.nc
    smm, sev = k.sem("t_mm"), k.sem("s_ev")
    slds = [k.sem("g_wld0"), k.sem("g_wld1")]
    ssts = [k.sem("s_st0"), k.sem("s_st1")]
    nrb, ncb = R // 128, C // 128
    wrow = k.sbuf(f"throw_{tag}", [128, 2 * C], BF16)
    stage = k.sbuf(f"thstg_{tag}", [128, 2 * C], BF16)
    pst = k.psum(f"thps_{tag}", [128, 2 * 1024], BF16)
    b_lds = [s.v for s in slds]
    b_sts = [s.v for s in ssts]
    b_mm, b_ev = smm.v, sev.v

    with nc.Block() as blk:
        @blk.gpsimd
        def _(g_raw):
            g = k.eng(g_raw, "gpsimd")
            for rb in range(nrb):
                if rb >= 2:
                    g_raw.wait_ge(smm.h, b_mm + (rb - 1) * ncb)
                half = C // 2 if C > 2048 else C
                for c0 in range(0, C, half):
                    g.dma(wrow[:, (rb % 2) * C + c0:(rb % 2) * C + c0 + half],
                          src_ap[rb * 128:(rb + 1) * 128, c0:c0 + half],
                          slds[rb % 2])

        nld_per_rb = 2 if C > 2048 else 1

        @blk.tensor
        def _(t):
            n = 0
            for rb in range(nrb):
                t.wait_ge(slds[rb % 2].h,
                          b_lds[rb % 2] + 16 * nld_per_rb * (rb // 2 + 1))
                for cb in range(ncb):
                    if n >= 2:
                        t.wait_ge(sev.h, b_ev + n - 1)
                    mm = t.transpose(
                        pst[:, (n % 2) * 1024:(n % 2) * 1024 + 128],
                        wrow[:, (rb % 2) * C + cb * 128:(rb % 2) * C + cb * 128 + 128],
                        k.ident_bf[:, :])
                    mm.then_inc(smm.h, 1)
                    smm.v += 1
                    n += 1

        @blk.scalar
        def _(sc_raw):
            sc = k.eng(sc_raw, "scalar")
            n = 0
            for rb in range(nrb):
                if rb >= 2:
                    sc.wait(ssts[rb % 2], b_sts[rb % 2] + 16 * (rb // 2))
                for cb in range(ncb):
                    sc.wait(smm, b_mm + n + 1)
                    sc.op(sc_raw.copy,
                          stage[:, (rb % 2) * C + cb * 128:(rb % 2) * C + cb * 128 + 128],
                          pst[:, (n % 2) * 1024:(n % 2) * 1024 + 128],
                          inc=(sev, 1))
                    n += 1

        @blk.sync
        def _(s_raw):
            s = k.eng(s_raw, "sync")
            for rb in range(nrb):
                s.wait(sev, b_ev + (rb + 1) * ncb)
                s.dma(dst_hbm[rb], stage[:, (rb % 2) * C:(rb % 2) * C + C],
                      ssts[rb % 2])
            for ss in ssts:
                s_raw.wait_ge(ss.h, ss.v)


def scale_cols(k: K, src_row_dram, n, dst_sb, tag):
    nc = k.nc
    nj = n // 128
    tmp = k.sbuf(f"sc_tmp_{tag}", [128, 128], F32)
    pst = k.psum(f"sc_ps_{tag}", [128, 128], F32)
    sld = k.sem("s_xld")
    b_ld = sld.v
    with nc.Block() as blk:
        @blk.sync
        def _(s_raw):
            s = k.eng(s_raw, "sync")
            s.dma(tmp[0:nj, :], src_row_dram.rearrange("(j p) -> j p", p=128), sld)
            s_raw.wait_ge(sld.h, sld.v)
    with nc.Block() as blk:
        @blk.tensor
        def _(t):
            t.transpose(pst[:, :], tmp[:, :], k.ident_f32[:, :])
    with nc.Block() as blk:
        @blk.scalar
        def _(sc_raw):
            sc = k.eng(sc_raw, "scalar")
            sc.op(sc_raw.copy, dst_sb[:, 0:nj], pst[:, 0:nj])


def emit_gw12(k: K, x3T_sb, w1t_hbm, w2t_hbm, s1col, s2col, at_hbm, tag):
    nc = k.nc
    smm, sev, svv = k.sem("t_mm"), k.sem("s_ev"), k.sem("v_ev")
    slds = [k.sem("s_xld0"), k.sem("s_xld1")]
    ssts = [k.sem(f"g_st{i}") for i in range(4)]
    b_lds = [s.v for s in slds]
    b_sts = [s.v for s in ssts]
    b_mm, b_ev, b_vv = smm.v, sev.v, svv.v
    slab = k.sbuf(f"gw_slab_{tag}", [128, 4 * 1024], BF16)
    tsil = k.sbuf(f"gw_tsil_{tag}", [128, 2 * 512], F32)
    aev = k.sbuf(f"gw_aev_{tag}", [128, 4 * 512], BF16)
    ps = k.psum(f"gw_ps_{tag}", [128, 4 * 512], F32)

    with nc.Block() as blk:
        @blk.sync
        def _(s_raw):
            s = k.eng(s_raw, "sync")
            for mb in range(32):
                if mb >= 2:
                    s_raw.wait_ge(smm.h, b_mm + 8 * (mb - 1))
                s.dma(slab[:, (mb % 2) * 2048:(mb % 2) * 2048 + 1024],
                      w1t_hbm[mb], slds[mb % 2])
                s.dma(slab[:, (mb % 2) * 2048 + 1024:(mb % 2) * 2048 + 2048],
                      w2t_hbm[mb], slds[mb % 2])

        @blk.tensor
        def _(t):
            n = 0
            for mb in range(32):
                t.wait_ge(slds[mb % 2].h, b_lds[mb % 2] + 32 * (mb // 2 + 1))
                for tb in range(8):
                    if n >= 2:
                        t.wait_ge(svv.h, b_vv + n - 1)
                    for kt in range(8):
                        mm = t.matmul(
                            ps[:, (2 * n % 4) * 512:(2 * n % 4) * 512 + 512],
                            slab[:, (mb % 2) * 2048 + kt * 128:(mb % 2) * 2048 + kt * 128 + 128],
                            x3T_sb[:, kt * T + tb * 512:kt * T + tb * 512 + 512],
                            start=(kt == 0), stop=(kt == 7))
                    for kt in range(8):
                        mm = t.matmul(
                            ps[:, ((2 * n + 1) % 4) * 512:((2 * n + 1) % 4) * 512 + 512],
                            slab[:, (mb % 2) * 2048 + 1024 + kt * 128:(mb % 2) * 2048 + 1024 + kt * 128 + 128],
                            x3T_sb[:, kt * T + tb * 512:kt * T + tb * 512 + 512],
                            start=(kt == 0), stop=(kt == 7))
                    mm.then_inc(smm.h, 1)
                    smm.v += 1
                    n += 1

        @blk.scalar
        def _(sc_raw):
            sc = k.eng(sc_raw, "scalar")
            n = 0
            for mb in range(32):
                for tb in range(8):
                    sc.wait(smm, b_mm + n + 1)
                    if n >= 2:
                        sc.wait(svv, b_vv + n - 1)
                    sc.op(sc_raw.activation, tsil[:, (n % 2) * 512:(n % 2) * 512 + 512],
                          ps[:, (2 * n % 4) * 512:(2 * n % 4) * 512 + 512],
                          AF.Silu, scale=s1col[:, mb:mb + 1], inc=(sev, 1))
                    n += 1

        @blk.vector
        def _(v_raw):
            v = k.eng(v_raw, "vector")
            n = 0
            for mb in range(32):
                for tb in range(8):
                    v.wait(sev, b_ev + n + 1)
                    if n >= 4:
                        v.wait(ssts[n % 4], b_sts[n % 4] + 16 * (n // 4))
                    v.op(v_raw.scalar_tensor_tensor,
                         aev[:, (n % 4) * 512:(n % 4) * 512 + 512],
                         ps[:, ((2 * n + 1) % 4) * 512:((2 * n + 1) % 4) * 512 + 512],
                         s2col[:, mb:mb + 1],
                         tsil[:, (n % 2) * 512:(n % 2) * 512 + 512],
                         ALU.mult, ALU.mult, inc=(svv, 1))
                    n += 1

        @blk.gpsimd
        def _(g_raw):
            g = k.eng(g_raw, "gpsimd")
            n = 0
            for mb in range(32):
                for tb in range(8):
                    g.wait(svv, b_vv + n + 1)
                    g.dma(at_hbm[mb][:, tb * 512:tb * 512 + 512],
                          aev[:, (n % 4) * 512:(n % 4) * 512 + 512], ssts[n % 4])
                    n += 1
            for ss in ssts:
                g_raw.wait_ge(ss.h, ss.v)


def emit_gh_resident(k: K, rT_sb, ht_hbm, scol, out_hbm, tag):
    nc = k.nc
    smm, sev = k.sem("t_mm"), k.sem("s_ev")
    slds = [k.sem("g_wld0"), k.sem("g_wld1")]
    ssts = [k.sem(f"s_st{i}") for i in range(4)]
    b_lds = [s.v for s in slds]
    b_sts = [s.v for s in ssts]
    b_mm, b_ev = smm.v, sev.v
    slab = k.sbuf(f"gh_slab_{tag}", [128, 2 * 4096], BF16)
    ev = k.sbuf(f"gh_ev_{tag}", [128, 4 * 512], BF16)
    ps = k.psum(f"gh_ps_{tag}", [128, 4 * 512], F32)

    with nc.Block() as blk:
        @blk.gpsimd
        def _(g_raw):
            g = k.eng(g_raw, "gpsimd")
            for ot in range(32):
                if ot >= 2:
                    g_raw.wait_ge(smm.h, b_mm + 8 * (ot - 1))
                g.dma(slab[:, (ot % 2) * 4096:(ot % 2) * 4096 + 4096],
                      ht_hbm[ot], slds[ot % 2])

        @blk.tensor
        def _(t):
            n = 0
            for ot in range(32):
                t.wait_ge(slds[ot % 2].h, b_lds[ot % 2] + 16 * (ot // 2 + 1))
                for tb in range(8):
                    if n >= 4:
                        t.wait_ge(sev.h, b_ev + n - 3)
                    for kt in range(32):
                        mm = t.matmul(
                            ps[:, (n % 4) * 512:(n % 4) * 512 + 512],
                            slab[:, (ot % 2) * 4096 + kt * 128:(ot % 2) * 4096 + kt * 128 + 128],
                            rT_sb[:, kt * T + tb * 512:kt * T + tb * 512 + 512],
                            start=(kt == 0), stop=(kt == 31))
                    mm.then_inc(smm.h, 1)
                    smm.v += 1
                    n += 1

        @blk.scalar
        def _(sc_raw):
            sc = k.eng(sc_raw, "scalar")
            n = 0
            for ot in range(32):
                for tb in range(8):
                    sc.wait(smm, b_mm + n + 1)
                    if n >= 4:
                        sc.wait(ssts[n % 4], b_sts[n % 4] + 16 * (n // 4))
                    sc.op(sc_raw.activation, ev[:, (n % 4) * 512:(n % 4) * 512 + 512],
                          ps[:, (n % 4) * 512:(n % 4) * 512 + 512],
                          AF.Gelu, scale=scol[:, ot:ot + 1], inc=(sev, 1))
                    n += 1

        @blk.sync
        def _(s_raw):
            s = k.eng(s_raw, "sync")
            n = 0
            for ot in range(32):
                for tb in range(8):
                    s.wait(sev, b_ev + n + 1)
                    s.dma(out_hbm[ot][:, tb * 512:(tb + 1) * 512],
                          ev[:, (n % 4) * 512:(n % 4) * 512 + 512], ssts[n % 4])
                    n += 1
            for ss in ssts:
                s_raw.wait_ge(ss.h, ss.v)


def emit_gh_blocked(k: K, rT_hbm, ht_hbm, scol, out_hbm, tag):
    nc = k.nc
    s_rl = k.sem("s_rhld")
    smm, sev = k.sem("t_mm"), k.sem("s_ev")
    slds = [k.sem("g_wld0"), k.sem("g_wld1")]
    ssts = [k.sem(f"s_st{i}") for i in range(4)]
    b_lds = [s.v for s in slds]
    b_sts = [s.v for s in ssts]
    b_rl, b_mm, b_ev = s_rl.v, smm.v, sev.v
    rblk = k.sbuf(f"ghb_r_{tag}", [128, 32 * 1024], BF16)
    slab = k.sbuf(f"ghb_slab_{tag}", [128, 2 * 4096], BF16)
    ev = k.sbuf(f"ghb_ev_{tag}", [128, 4 * 512], BF16)
    ps = k.psum(f"ghb_ps_{tag}", [128, 4 * 512], F32)

    with nc.Block() as blk:
        @blk.sync
        def _(s_raw):
            s = k.eng(s_raw, "sync")
            for tbb in range(4):
                if tbb >= 1:
                    s_raw.wait_ge(smm.h, b_mm + 64 * tbb)
                for kt in range(32):
                    s.dma(rblk[:, kt * 1024:kt * 1024 + 1024],
                          rT_hbm[kt][:, tbb * 1024:tbb * 1024 + 1024], s_rl)

        @blk.gpsimd
        def _(g_raw):
            g = k.eng(g_raw, "gpsimd")
            for tbb in range(4):
                for ot in range(32):
                    gi = tbb * 32 + ot
                    if gi >= 2:
                        g_raw.wait_ge(smm.h, b_mm + 2 * (gi - 1))
                    g.dma(slab[:, (gi % 2) * 4096:(gi % 2) * 4096 + 4096],
                          ht_hbm[ot], slds[gi % 2])

        @blk.tensor
        def _(t):
            n = 0
            for tbb in range(4):
                t.wait_ge(s_rl.h, b_rl + 16 * 32 * (tbb + 1))
                for ot in range(32):
                    gi = tbb * 32 + ot
                    t.wait_ge(slds[gi % 2].h, b_lds[gi % 2] + 16 * (gi // 2 + 1))
                    for tb in range(2):
                        if n >= 4:
                            t.wait_ge(sev.h, b_ev + n - 3)
                        for kt in range(32):
                            mm = t.matmul(
                                ps[:, (n % 4) * 512:(n % 4) * 512 + 512],
                                slab[:, (gi % 2) * 4096 + kt * 128:(gi % 2) * 4096 + kt * 128 + 128],
                                rblk[:, kt * 1024 + tb * 512:kt * 1024 + tb * 512 + 512],
                                start=(kt == 0), stop=(kt == 31))
                        mm.then_inc(smm.h, 1)
                        smm.v += 1
                        n += 1

        @blk.scalar
        def _(sc_raw):
            sc = k.eng(sc_raw, "scalar")
            n = 0
            for tbb in range(4):
                for ot in range(32):
                    for tb in range(2):
                        sc.wait(smm, b_mm + n + 1)
                        if n >= 4:
                            sc.wait(ssts[n % 4], b_sts[n % 4] + 16 * (n // 4))
                        sc.op(sc_raw.activation,
                              ev[:, (n % 4) * 512:(n % 4) * 512 + 512],
                              ps[:, (n % 4) * 512:(n % 4) * 512 + 512],
                              AF.Gelu, scale=scol[:, ot:ot + 1], inc=(sev, 1))
                        sc.dma(out_hbm[ot][:, tbb * 1024 + tb * 512:tbb * 1024 + tb * 512 + 512],
                               ev[:, (n % 4) * 512:(n % 4) * 512 + 512],
                               ssts[n % 4], dep=True)
                        n += 1
            for ss in ssts:
                sc_raw.wait_ge(ss.h, ss.v)


def emit_w3_ln(k: K, ct_hbm, w3t_sb, s3col, x3_hbm, y_hbm, tag):
    nc = k.nc
    s_ct = k.sem("s_ctld")
    s_x3s = [k.sem("s_xld2a"), k.sem("s_xld2b")]
    smm, str_, se1, se2 = k.sem("t_mm"), k.sem("t_tr"), k.sem("s_e1"), k.sem("s_e2")
    s_lnv, s_lns = k.sem("ln_v"), k.sem("ln_s")
    sout = k.sem("v_out")
    ssts = [k.sem("g_st0"), k.sem("g_st1")]
    b_x3s = [s.v for s in s_x3s]
    b_sts = [s.v for s in ssts]
    b_ct, b_mm, b_tr = s_ct.v, smm.v, str_.v
    b_e1, b_e2, b_lnv, b_lns = se1.v, se2.v, s_lnv.v, s_lns.v
    b_out = sout.v

    ctblk = k.sbuf(f"w3_ct_{tag}", [128, 32 * 512], BF16)
    o4t = k.sbuf(f"w3_o4t_{tag}", [128, 2 * 512], F32)
    obuf = k.sbuf(f"w3_ob_{tag}", [128, 2 * 4096], F32)
    res = k.sbuf(f"w3_res_{tag}", [128, 2 * 1024], F32)
    outt = k.sbuf(f"w3_out_{tag}", [128, 2 * 1024], F32)
    sqs = k.sbuf(f"w3_sq_{tag}", [128, 1024], F32)
    stats = k.sbuf(f"w3_stats_{tag}", [128, 8], F32)
    ps1 = k.psum(f"w3_ps1_{tag}", [128, 2 * 512], F32)
    ps2 = k.psum(f"w3_ps2_{tag}", [128, 2 * 512], F32)

    with nc.Block() as blk:
        @blk.sync
        def _(s_raw):
            s = k.eng(s_raw, "sync")
            for tb in range(8):
                if tb >= 1:
                    s_raw.wait_ge(smm.h, b_mm + 8 * tb)
                for ht in range(32):
                    s.dma(ctblk[:, ht * 512:ht * 512 + 512],
                          ct_hbm[ht][:, tb * 512:tb * 512 + 512], s_ct)
                for c in range(4):
                    w = tb * 4 + c
                    if w >= 2:
                        s_raw.wait_ge(s_lnv.h, b_lnv + 2 * (w - 2) + 1)
                    s.dma(res[:, (w % 2) * 1024:(w % 2) * 1024 + 1024],
                          x3_hbm[w * 128:(w + 1) * 128, :], s_x3s[w % 2])

        @blk.tensor
        def _(t):
            for tb in range(8):
                t.wait_ge(s_ct.h, b_ct + 16 * 32 * (tb + 1))
                for db in range(8):
                    g = tb * 8 + db
                    if g >= 2:
                        t.wait_ge(se1.h, b_e1 + g - 1)
                    for ht in range(32):
                        mm = t.matmul(
                            ps1[:, (g % 2) * 512:(g % 2) * 512 + 512],
                            w3t_sb[:, db * 4096 + ht * 128:db * 4096 + ht * 128 + 128],
                            ctblk[:, ht * 512:ht * 512 + 512],
                            start=(ht == 0), stop=(ht == 31))
                    mm.then_inc(smm.h, 1)
                    smm.v += 1
                    t.wait_ge(se1.h, b_e1 + g + 1)
                    for c in range(4):
                        trg = g * 4 + c
                        if trg >= 2:
                            t.wait_ge(se2.h, b_e2 + trg - 1)
                        mm = t.transpose(
                            ps2[:, (trg % 2) * 512:(trg % 2) * 512 + 128],
                            o4t[:, (g % 2) * 512 + c * 128:(g % 2) * 512 + c * 128 + 128],
                            k.ident_f32[:, :])
                        mm.then_inc(str_.h, 1)
                        str_.v += 1

        @blk.scalar
        def _(sc_raw):
            sc = k.eng(sc_raw, "scalar")
            for tb in range(8):
                for db in range(8):
                    g = tb * 8 + db
                    sc.wait(smm, b_mm + g + 1)
                    if g >= 2:
                        sc.wait(str_, b_tr + 4 * (g - 1))
                    sc.op(sc_raw.activation, o4t[:, (g % 2) * 512:(g % 2) * 512 + 512],
                          ps1[:, (g % 2) * 512:(g % 2) * 512 + 512],
                          AF.Identity, scale=s3col[:, db:db + 1], inc=(se1, 1))
                    if tb >= 2 and db == 0:
                        sc.wait(sout, b_out + 4 * (tb - 1))
                    for c in range(4):
                        trg = g * 4 + c
                        sc.wait(str_, b_tr + trg + 1)
                        sc.op(sc_raw.copy,
                              obuf[:, (tb % 2) * 4096 + c * 1024 + db * 128:
                                   (tb % 2) * 4096 + c * 1024 + db * 128 + 128],
                              ps2[:, (trg % 2) * 512:(trg % 2) * 512 + 128],
                              inc=(se2, 1))
                for c in range(4):
                    w = tb * 4 + c
                    sc.wait(s_lnv, b_lnv + 2 * w + 1)
                    sc.op(sc_raw.activation, sqs[:, :],
                          obuf[:, (tb % 2) * 4096 + c * 1024:(tb % 2) * 4096 + c * 1024 + 1024],
                          AF.Square, accum_out=stats[:, 0:1], inc=(s_lns, 1))
                    sc.wait(s_lnv, b_lnv + 2 * w + 2)
                    sc.op(sc_raw.activation, stats[:, 5:6], stats[:, 4:5], AF.Ln,
                          bias=LNEPS)
                    sc.op(sc_raw.activation, stats[:, 5:6], stats[:, 5:6], AF.Exp,
                          scale=-0.5, inc=(s_lns, 1))

        @blk.vector
        def _(v_raw):
            v = k.eng(v_raw, "vector")
            for tb in range(8):
                v.wait(se2, b_e2 + 32 * (tb + 1))
                for c in range(4):
                    w = tb * 4 + c
                    ob_ap = obuf[:, (tb % 2) * 4096 + c * 1024:(tb % 2) * 4096 + c * 1024 + 1024]
                    v.wait(s_x3s[w % 2], b_x3s[w % 2] + 16 * (w // 2 + 1))
                    v.op(v_raw.tensor_tensor, ob_ap, ob_ap,
                         res[:, (w % 2) * 1024:(w % 2) * 1024 + 1024], ALU.add)
                    v.op(v_raw.reduce_sum, stats[:, 1:2], ob_ap, axis=AX.X,
                         inc=(s_lnv, 1))
                    v.wait(s_lns, b_lns + 2 * w + 1)
                    v.op(v_raw.tensor_scalar, stats[:, 2:3], stats[:, 1:2],
                         1.0 / D, None, ALU.mult, dep=True)
                    v.op(v_raw.tensor_scalar, stats[:, 3:4], stats[:, 2:3],
                         stats[:, 2:3], None, ALU.mult, dep=True)
                    v.op(v_raw.tensor_scalar, stats[:, 4:5], stats[:, 0:1],
                         1.0 / D, None, ALU.mult)
                    v.op(v_raw.tensor_tensor, stats[:, 4:5], stats[:, 4:5],
                         stats[:, 3:4], ALU.subtract, inc=(s_lnv, 1), dep=True)
                    v.wait(s_lns, b_lns + 2 * w + 2)
                    if w >= 2:
                        v.wait(ssts[w % 2], b_sts[w % 2] + 16 * (w // 2))
                    v.op(v_raw.tensor_scalar,
                         outt[:, (w % 2) * 1024:(w % 2) * 1024 + 1024],
                         ob_ap, stats[:, 2:3], stats[:, 5:6],
                         ALU.subtract, ALU.mult, inc=(sout, 1), dep=True)

        @blk.gpsimd
        def _(g_raw):
            g = k.eng(g_raw, "gpsimd")
            for w in range(32):
                g.wait(sout, b_out + w + 1)
                g.dma(y_hbm[w * 128:(w + 1) * 128, :],
                      outt[:, (w % 2) * 1024:(w % 2) * 1024 + 1024], ssts[w % 2])
            for ss in ssts:
                g_raw.wait_ge(ss.h, ss.v)


def body_l1ffn(k, x3_in, w1_in, w2_in, h1_in, h2_in, w3_in, s1_in, s2_in,
               sh1_in, sh2_in, s3_in, y_out, pfx="lf", pre_w=None):
    nc = k.nc
    if pre_w is not None:
        w1t_hbm, w2t_hbm, h1t_hbm, h2t_hbm, w3t_hbm = pre_w
    else:
        w1t_hbm = k.dram(f"{pfx}_scr_w1t", [32, 128, 1024], BF16)
        w2t_hbm = k.dram(f"{pfx}_scr_w2t", [32, 128, 1024], BF16)
        h1t_hbm = k.dram(f"{pfx}_scr_h1t", [32, 128, 4096], BF16)
        h2t_hbm = k.dram(f"{pfx}_scr_h2t", [32, 128, 4096], BF16)
        w3t_hbm = k.dram(f"{pfx}_scr_w3t", [8, 128, 4096], BF16)
    bt_hbm = k.dram(f"{pfx}_scr_bt", [32, 128, T], BF16)
    ct_hbm = k.dram(f"{pfx}_scr_ct", [32, 128, T], BF16)
    at_hbm = k.dram(f"{pfx}_scr_at", [32, 128, T], BF16)

    if pre_w is None:
        with k.scope():
            tw_hbm(k, w1_in[:, :], w1t_hbm, 4096, 1024, "w1")
        with k.scope():
            tw_hbm(k, w2_in[:, :], w2t_hbm, 4096, 1024, "w2")
        with k.scope():
            tw_hbm(k, h1_in[:, :], h1t_hbm, 4096, 4096, "h1")
        with k.scope():
            tw_hbm(k, h2_in[:, :], h2t_hbm, 4096, 4096, "h2")
        with k.scope():
            tw_hbm(k, w3_in[:, :], w3t_hbm, 1024, 4096, "w3")

    s1c = k.sbuf("s1c", [128, 32], F32)
    s2c = k.sbuf("s2c", [128, 32], F32)
    sh1c = k.sbuf("sh1c", [128, 32], F32)
    sh2c = k.sbuf("sh2c", [128, 32], F32)
    s3c = k.sbuf("s3c", [128, 8], F32)
    with k.scope():
        scale_cols(k, s1_in, 4096, s1c, "s1")
    with k.scope():
        scale_cols(k, s2_in, 4096, s2c, "s2")
    with k.scope():
        scale_cols(k, sh1_in, 4096, sh1c, "sh1")
    with k.scope():
        scale_cols(k, sh2_in, 4096, sh2c, "sh2")
    with k.scope():
        scale_cols(k, s3_in, 1024, s3c, "s3")

    with contextlib.ExitStack() as phx:
        x3T = phx.enter_context(nc.sbuf_tensor("x3T", [128, 8 * T], BF16))
        with k.scope():
            tx_sbuf(k, x3_in[:, :], x3T, "x3")
        with k.scope():
            emit_gw12(k, x3T, w1t_hbm, w2t_hbm, s1c, s2c, at_hbm, "gw")
    with k.scope():
        emit_gh_blocked(k, at_hbm, h1t_hbm, sh1c, bt_hbm, "h1")
    with k.scope():
        emit_gh_blocked(k, bt_hbm, h2t_hbm, sh2c, ct_hbm, "h2")
    with contextlib.ExitStack() as phw:
        w3t_sb = phw.enter_context(nc.sbuf_tensor("w3t_sb", [128, 8 * 4096], BF16))
        s_wl = k.sem("s_w3ld")
        with nc.Block() as blk:
            @blk.sync
            def _(s_raw):
                s = k.eng(s_raw, "sync")
                for dt in range(8):
                    s.dma(w3t_sb[:, dt * 4096:(dt + 1) * 4096], w3t_hbm[dt], s_wl)
                s_raw.wait_ge(s_wl.h, s_wl.v)
        with k.scope():
            emit_w3_ln(k, ct_hbm, w3t_sb, s3c, x3_in[:, :], y_out, "w3")


def build_l1ffn():
    k = K("l1ffn")
    x3_in = k.dram("x3_in", [T, D], F32, kind="ExternalInput")
    w1_in = k.dram("w1w_in", [4096, 1024], F32, kind="ExternalInput")
    w2_in = k.dram("w2w_in", [4096, 1024], F32, kind="ExternalInput")
    h1_in = k.dram("h1w_in", [4096, 4096], F32, kind="ExternalInput")
    h2_in = k.dram("h2w_in", [4096, 4096], F32, kind="ExternalInput")
    w3_in = k.dram("w3w_in", [1024, 4096], F32, kind="ExternalInput")
    s1_in = k.dram("s1_in", [4096], F32, kind="ExternalInput")
    s2_in = k.dram("s2_in", [4096], F32, kind="ExternalInput")
    sh1_in = k.dram("sh1_in", [4096], F32, kind="ExternalInput")
    sh2_in = k.dram("sh2_in", [4096], F32, kind="ExternalInput")
    s3_in = k.dram("s3_in", [1024], F32, kind="ExternalInput")
    y_out = k.dram("y_out", [T, D], F32, kind="ExternalOutput")
    declare_and_load_consts(k)
    body_l1ffn(k, x3_in, w1_in, w2_in, h1_in, h2_in, w3_in, s1_in, s2_in,
               sh1_in, sh2_in, s3_in, y_out)
    return k


def run_l1ffn(x3_full, w1w, w1s, w2w, w2s, h1w, h1s, h2w, h2s, w3w, w3s):
    k = build_l1ffn()
    consts = host_consts()
    in_maps = []
    for c in range(NCORES):
        m = dict(consts)
        m["x3_in"] = np.ascontiguousarray(x3_full[c])
        m["w1w_in"] = w1w
        m["w2w_in"] = w2w
        m["h1w_in"] = h1w
        m["h2w_in"] = h2w
        m["w3w_in"] = w3w
        m["s1_in"] = w1s
        m["s2_in"] = w2s
        m["sh1_in"] = h1s
        m["sh2_in"] = h2s
        m["s3_in"] = w3s
        in_maps.append(m)
    res = run_bass_kernel_spmd(k.nc, in_maps, core_ids=list(range(NCORES)))
    k.close()
    return np.stack([res.results[c]["y_out"] for c in range(NCORES)])



def tw_sbuf_any(k: K, src_ap, dst_sb, R, C, tag):
    nc = k.nc
    smm, sev = k.sem("t_mm"), k.sem("s_ev")
    slds = [k.sem("g_wld0"), k.sem("g_wld1")]
    b_lds = [s.v for s in slds]
    nrb = (R + 127) // 128
    ncb = (C + 127) // 128
    wrow = k.sbuf(f"tsa_row_{tag}", [128, 2 * C], BF16)
    pst = k.psum(f"tsa_ps_{tag}", [128, 2 * 1024], BF16)
    b_mm, b_ev = smm.v, sev.v

    with nc.Block() as blk:
        @blk.gpsimd
        def _(g_raw):
            g = k.eng(g_raw, "gpsimd")
            for rb in range(nrb):
                r_sz = min(128, R - rb * 128)
                if rb >= 2:
                    g_raw.wait_ge(smm.h, b_mm + (rb - 1) * ncb)
                g.dma(wrow[0:r_sz, (rb % 2) * C:(rb % 2) * C + C],
                      src_ap[rb * 128:rb * 128 + r_sz, :], slds[rb % 2])

        @blk.tensor
        def _(t):
            n = 0
            for rb in range(nrb):
                r_sz = min(128, R - rb * 128)
                t.wait_ge(slds[rb % 2].h, b_lds[rb % 2] + 16 * (rb // 2 + 1))
                for cb in range(ncb):
                    c_sz = min(128, C - cb * 128)
                    if n >= 2:
                        t.wait_ge(sev.h, b_ev + n - 1)
                    mm = t.transpose(
                        pst[0:c_sz, (n % 2) * 1024:(n % 2) * 1024 + r_sz],
                        wrow[0:r_sz, (rb % 2) * C + cb * 128:(rb % 2) * C + cb * 128 + c_sz],
                        k.ident_bf[0:r_sz, 0:r_sz])
                    mm.then_inc(smm.h, 1)
                    smm.v += 1
                    n += 1

        @blk.scalar
        def _(sc_raw):
            sc = k.eng(sc_raw, "scalar")
            n = 0
            for rb in range(nrb):
                r_sz = min(128, R - rb * 128)
                for cb in range(ncb):
                    c_sz = min(128, C - cb * 128)
                    sc.wait(smm, b_mm + n + 1)
                    sc.op(sc_raw.copy,
                          dst_sb[0:c_sz, cb * R + rb * 128:cb * R + rb * 128 + r_sz],
                          pst[0:c_sz, (n % 2) * 1024:(n % 2) * 1024 + r_sz],
                          inc=(sev, 1))
                    n += 1


def emit_xz(k: K, x2T_sb, wint_hbm, xi_hbm, z_hbm, tag):
    nc = k.nc
    smm, sev = k.sem("t_mm"), k.sem("s_ev")
    slds = [k.sem("g_wld0"), k.sem("g_wld1")]
    ssts = [k.sem(f"s_st{i}") for i in range(4)]
    b_lds = [s.v for s in slds]
    b_sts = [s.v for s in ssts]
    b_mm, b_ev = smm.v, sev.v
    slab = k.sbuf(f"xz_slab_{tag}", [128, 2 * 1024], BF16)
    ev = k.sbuf(f"xz_ev_{tag}", [128, 4 * 512], F32)
    ps = k.psum(f"xz_ps_{tag}", [128, 4 * 512], F32)

    with nc.Block() as blk:
        @blk.gpsimd
        def _(g_raw):
            g = k.eng(g_raw, "gpsimd")
            for et in range(32):
                if et >= 2:
                    g_raw.wait_ge(smm.h, b_mm + 8 * (et - 1))
                g.dma(slab[:, (et % 2) * 1024:(et % 2) * 1024 + 1024],
                      wint_hbm[et], slds[et % 2])

        @blk.tensor
        def _(t):
            n = 0
            for et in range(32):
                t.wait_ge(slds[et % 2].h, b_lds[et % 2] + 16 * (et // 2 + 1))
                for tb in range(8):
                    if n >= 4:
                        t.wait_ge(sev.h, b_ev + n - 3)
                    for kt in range(8):
                        mm = t.matmul(
                            ps[:, (n % 4) * 512:(n % 4) * 512 + 512],
                            slab[:, (et % 2) * 1024 + kt * 128:(et % 2) * 1024 + kt * 128 + 128],
                            x2T_sb[:, kt * T + tb * 512:kt * T + tb * 512 + 512],
                            start=(kt == 0), stop=(kt == 7))
                    mm.then_inc(smm.h, 1)
                    smm.v += 1
                    n += 1

        @blk.scalar
        def _(sc_raw):
            sc = k.eng(sc_raw, "scalar")
            n = 0
            for et in range(32):
                for tb in range(8):
                    sc.wait(smm, b_mm + n + 1)
                    if n >= 4:
                        sc.wait(ssts[n % 4], b_sts[n % 4] + 16 * (n // 4))
                    sc.op(sc_raw.copy, ev[:, (n % 4) * 512:(n % 4) * 512 + 512],
                          ps[:, (n % 4) * 512:(n % 4) * 512 + 512], inc=(sev, 1))
                    n += 1

        @blk.sync
        def _(s_raw):
            s = k.eng(s_raw, "sync")
            n = 0
            for et in range(32):
                for tb in range(8):
                    s.wait(sev, b_ev + n + 1)
                    dst = xi_hbm[et] if et < 16 else z_hbm[et - 16]
                    s.dma(dst[:, tb * 512:tb * 512 + 512],
                          ev[:, (n % 4) * 512:(n % 4) * 512 + 512], ssts[n % 4])
                    n += 1
            for ss in ssts:
                s_raw.wait_ge(ss.h, ss.v)


def emit_conv(k: K, xi_hbm, cw_sb, cb_sb, xc_hbm, xcbf_hbm, tag):
    nc = k.nc
    ssc, svv = k.sem("s_ev"), k.sem("v_ev")
    slds = [k.sem("s_xld0"), k.sem("s_xld1")]
    ssts = [k.sem("g_st0"), k.sem("g_st1")]
    b_lds = [s.v for s in slds]
    b_sts = [s.v for s in ssts]
    b_sc, b_vv = ssc.v, svv.v
    PAD = 3
    xibuf = k.sbuf(f"cv_xi_{tag}", [128, 2 * (T + PAD)], F32)
    acc = k.sbuf(f"cv_acc_{tag}", [128, 2 * T], F32)
    tmp0 = k.sbuf(f"cv_t0_{tag}", [128, T], F32)
    tmp1 = k.sbuf(f"cv_t1_{tag}", [128, T], F32)
    xcf = k.sbuf(f"cv_xc_{tag}", [128, 2 * T], F32)
    xcb = k.sbuf(f"cv_xcb_{tag}", [128, 2 * T], BF16)

    with nc.Block() as blk:
        @blk.sync
        def _(s_raw):
            s = k.eng(s_raw, "sync")
            s_raw.wait_ge(svv.h, b_vv + 1)
            for dt in range(16):
                if dt >= 2:
                    s_raw.wait_ge(ssc.h, b_sc + 5 * (dt - 2) + 4)
                s.dma(xibuf[:, (dt % 2) * (T + PAD) + PAD:(dt % 2) * (T + PAD) + PAD + T],
                      xi_hbm[dt], slds[dt % 2])

        @blk.scalar
        def _(sc_raw):
            sc = k.eng(sc_raw, "scalar")
            for dt in range(16):
                sl = (dt % 2) * (T + PAD)
                ac = acc[:, (dt % 2) * T:(dt % 2) * T + T]
                sc.wait(slds[dt % 2], b_lds[dt % 2] + 16 * (dt // 2 + 1))
                sc.op(sc_raw.activation, ac, xibuf[:, sl + 3:sl + 3 + T], AF.Identity,
                      scale=cw_sb[:, dt * 4 + 3:dt * 4 + 4],
                      bias=cb_sb[:, dt:dt + 1], inc=(ssc, 1))
                if dt >= 1:
                    sc.wait(svv, b_vv + 1 + 4 * (dt - 1) + 3)
                sc.op(sc_raw.activation, tmp0[:, :], xibuf[:, sl + 2:sl + 2 + T],
                      AF.Identity, scale=cw_sb[:, dt * 4 + 2:dt * 4 + 3], inc=(ssc, 1))
                sc.op(sc_raw.activation, tmp1[:, :], xibuf[:, sl + 1:sl + 1 + T],
                      AF.Identity, scale=cw_sb[:, dt * 4 + 1:dt * 4 + 2], inc=(ssc, 1))
                sc.wait(svv, b_vv + 1 + 4 * dt + 1)
                sc.op(sc_raw.activation, tmp0[:, :], xibuf[:, sl + 0:sl + 0 + T],
                      AF.Identity, scale=cw_sb[:, dt * 4 + 0:dt * 4 + 1], inc=(ssc, 1))
                sc.wait(svv, b_vv + 1 + 4 * dt + 3)
                if dt >= 2:
                    sc.wait(svv, b_vv + 1 + 4 * (dt - 2) + 4)
                    sc.wait(ssts[dt % 2], b_sts[dt % 2] + 32 * (dt // 2))
                sc.op(sc_raw.activation, xcf[:, (dt % 2) * T:(dt % 2) * T + T],
                      ac, AF.Silu, inc=(ssc, 1))

        @blk.vector
        def _(v_raw):
            v = k.eng(v_raw, "vector")
            v.op(v_raw.memset, xibuf[:, 0:PAD], 0.0)
            v.op(v_raw.memset, xibuf[:, T + PAD:T + 2 * PAD], 0.0, inc=(svv, 1))
            for dt in range(16):
                ac = acc[:, (dt % 2) * T:(dt % 2) * T + T]
                v.wait(ssc, b_sc + 5 * dt + 2)
                v.op(v_raw.tensor_tensor, ac, ac, tmp0[:, :], ALU.add, inc=(svv, 1))
                v.wait(ssc, b_sc + 5 * dt + 3)
                v.op(v_raw.tensor_tensor, ac, ac, tmp1[:, :], ALU.add, inc=(svv, 1))
                v.wait(ssc, b_sc + 5 * dt + 4)
                v.op(v_raw.tensor_tensor, ac, ac, tmp0[:, :], ALU.add, inc=(svv, 1))
                v.wait(ssc, b_sc + 5 * dt + 5)
                if dt >= 2:
                    v.wait(ssts[dt % 2], b_sts[dt % 2] + 32 * (dt // 2))
                v.op(v_raw.tensor_copy, xcb[:, (dt % 2) * T:(dt % 2) * T + T],
                     xcf[:, (dt % 2) * T:(dt % 2) * T + T], inc=(svv, 1))

        @blk.gpsimd
        def _(g_raw):
            g = k.eng(g_raw, "gpsimd")
            for dt in range(16):
                g.wait(svv, b_vv + 1 + 4 * dt + 4)
                g.dma(xc_hbm[dt], xcf[:, (dt % 2) * T:(dt % 2) * T + T],
                      ssts[dt % 2])
                g.dma(xcbf_hbm[dt], xcb[:, (dt % 2) * T:(dt % 2) * T + T],
                      ssts[dt % 2])
            for ss in ssts:
                g_raw.wait_ge(ss.h, ss.v)


def emit_xdbl(k: K, xcbf_hbm, wxT_sb, xdbl_sb, tag):
    nc = k.nc
    smm, sev = k.sem("t_mm"), k.sem("s_ev")
    slds = [k.sem("s_xld0"), k.sem("s_xld1")]
    b_lds = [s.v for s in slds]
    b_mm, b_ev = smm.v, sev.v
    rbuf = k.sbuf(f"xd_r_{tag}", [128, 2 * 512], BF16)
    ps = k.psum(f"xd_ps_{tag}", [96, 2 * 512], F32)

    with nc.Block() as blk:
        @blk.sync
        def _(s_raw):
            s = k.eng(s_raw, "sync")
            n = 0
            for tb in range(8):
                for kt in range(16):
                    if n >= 2:
                        s_raw.wait_ge(smm.h, b_mm + n - 1)
                    s.dma(rbuf[:, (n % 2) * 512:(n % 2) * 512 + 512],
                          xcbf_hbm[kt][:, tb * 512:tb * 512 + 512], slds[n % 2])
                    n += 1

        @blk.tensor
        def _(t):
            n = 0
            for tb in range(8):
                for kt in range(16):
                    t.wait_ge(slds[n % 2].h, b_lds[n % 2] + 16 * (n // 2 + 1))
                    if kt == 0 and tb >= 2:
                        t.wait_ge(sev.h, b_ev + tb - 1)
                    mm = t.matmul(ps[:, (tb % 2) * 512:(tb % 2) * 512 + 512],
                                  wxT_sb[:, kt * 96:kt * 96 + 96],
                                  rbuf[:, (n % 2) * 512:(n % 2) * 512 + 512],
                                  start=(kt == 0), stop=(kt == 15))
                    mm.then_inc(smm.h, 1)
                    smm.v += 1
                    n += 1

        @blk.scalar
        def _(sc_raw):
            sc = k.eng(sc_raw, "scalar")
            for tb in range(8):
                sc.wait(smm, b_mm + 16 * (tb + 1))
                sc.op(sc_raw.copy, xdbl_sb[:, tb * 512:tb * 512 + 512],
                      ps[:, (tb % 2) * 512:(tb % 2) * 512 + 512], inc=(sev, 1))


def emit_dtproj(k: K, xdbl_sb, wdtT_sb, bdt_col, dtt_hbm, tag):
    nc = k.nc
    smm, sev, svv = k.sem("t_mm"), k.sem("s_ev"), k.sem("v_ev")
    ssts = [k.sem(f"g_st{i}") for i in range(4)]
    b_sts = [s.v for s in ssts]
    b_mm, b_ev, b_vv = smm.v, sev.v, svv.v
    dtr_bf = k.sbuf(f"dp_r_{tag}", [64, T], BF16)
    tab = k.sbuf(f"dp_ab_{tag}", [128, 2 * 512], F32)
    trl = k.sbuf(f"dp_rl_{tag}", [128, 2 * 512], F32)
    ev = k.sbuf(f"dp_ev_{tag}", [128, 4 * 512], F32)
    ps = k.psum(f"dp_ps_{tag}", [128, 2 * 512], F32)

    with nc.Block() as blk:
        @blk.vector
        def _(v_raw):
            v = k.eng(v_raw, "vector")
            v.op(v_raw.tensor_copy, dtr_bf[:, :], xdbl_sb[0:64, :], inc=(svv, 1))
            n = 0
            for dt in range(16):
                for tb in range(8):
                    v.wait(sev, b_ev + 2 * n + 2)
                    if n >= 4:
                        v.wait(ssts[n % 4], b_sts[n % 4] + 16 * (n // 4))
                    v.op(v_raw.tensor_tensor, ev[:, (n % 4) * 512:(n % 4) * 512 + 512],
                         trl[:, (n % 2) * 512:(n % 2) * 512 + 512],
                         tab[:, (n % 2) * 512:(n % 2) * 512 + 512],
                         ALU.add, inc=(svv, 1))
                    n += 1

        @blk.tensor
        def _(t):
            t.wait_ge(svv.h, b_vv + 1)
            n = 0
            for dt in range(16):
                for tb in range(8):
                    if n >= 2:
                        t.wait_ge(sev.h, b_ev + 2 * (n - 1))
                    mm = t.matmul(ps[:, (n % 2) * 512:(n % 2) * 512 + 512],
                                  wdtT_sb[:, dt * 128:dt * 128 + 128],
                                  dtr_bf[:, tb * 512:tb * 512 + 512],
                                  start=True, stop=True)
                    mm.then_inc(smm.h, 1)
                    smm.v += 1
                    n += 1

        @blk.scalar
        def _(sc_raw):
            sc = k.eng(sc_raw, "scalar")
            n = 0
            for dt in range(16):
                for tb in range(8):
                    sc.wait(smm, b_mm + n + 1)
                    if n >= 1:
                        if n >= 2:
                            sc.wait(svv, b_vv + 1 + n - 1)
                    sc.op(sc_raw.activation, tab[:, (n % 2) * 512:(n % 2) * 512 + 512],
                          ps[:, (n % 2) * 512:(n % 2) * 512 + 512], AF.Abs,
                          bias=bdt_col[:, dt:dt + 1])
                    sc.op(sc_raw.activation, trl[:, (n % 2) * 512:(n % 2) * 512 + 512],
                          ps[:, (n % 2) * 512:(n % 2) * 512 + 512], AF.Relu,
                          bias=bdt_col[:, dt:dt + 1], inc=(sev, 1))
                    sc.op(sc_raw.activation, tab[:, (n % 2) * 512:(n % 2) * 512 + 512],
                          tab[:, (n % 2) * 512:(n % 2) * 512 + 512], AF.Exp,
                          scale=-1.0)
                    sc.op(sc_raw.activation, tab[:, (n % 2) * 512:(n % 2) * 512 + 512],
                          tab[:, (n % 2) * 512:(n % 2) * 512 + 512], AF.Ln,
                          bias=1.0, inc=(sev, 1))
                    n += 1

        @blk.gpsimd
        def _(g_raw):
            g = k.eng(g_raw, "gpsimd")
            n = 0
            for dt in range(16):
                for tb in range(8):
                    g.wait(svv, b_vv + 1 + n + 1)
                    g.dma(dtt_hbm[dt][:, tb * 512:tb * 512 + 512],
                          ev[:, (n % 4) * 512:(n % 4) * 512 + 512], ssts[n % 4])
                    n += 1
            for ss in ssts:
                g_raw.wait_ge(ss.h, ss.v)


def emit_scan(k: K, dtt_hbm, xc_hbm, z_hbm, bc_sb, sel_sb, a_sb, dp_col,
              yt_hbm, tag, wprep=None):
    nc = k.nc
    s_dxs = [k.sem("s_dxld0"), k.sem("s_dxld1")]
    b_dxs = [s.v for s in s_dxs]
    s_zl = k.sem("s_zld")
    t_bc = k.sem("t_bc")
    s_dA = k.sem("s_dA")
    s_cc = k.sem("s_cc")
    v_scan = k.sem("v_scan")
    v_bc = k.sem("v_bc")
    s_t2 = k.sem("s_t2")
    s_sz = k.sem("s_sz")
    v_fin = k.sem("v_fin")
    g_add = k.sem("g_add")
    s_ysts = [k.sem("g_yst0"), k.sem("g_yst1")]
    b_ysts = [s.v for s in s_ysts]
    b_zl, b_bc, b_dA, b_cc = s_zl.v, t_bc.v, s_dA.v, s_cc.v
    b_vs, b_vb, b_t2, b_sz, b_fin = (v_scan.v, v_bc.v, s_t2.v, s_sz.v,
                                     v_fin.v)
    b_ga = g_add.v

    dtt = k.sbuf(f"sn_dt_{tag}", [128, 2 * T], F32)
    xct = k.sbuf(f"sn_xc_{tag}", [128, T], F32)
    ztt = k.sbuf(f"sn_z_{tag}", [128, T], F32)
    u_sb = k.sbuf(f"sn_u_{tag}", [128, T], F32)
    ya = k.sbuf(f"sn_ya_{tag}", [128, T], F32)
    da = k.sbuf(f"sn_da_{tag}", [128, 2 * 512], F32)
    dbu = k.sbuf(f"sn_dbu_{tag}", [128, 2 * 512], F32)
    hh = k.sbuf(f"sn_h_{tag}", [128, 2 * 512], F32)
    tmp = k.sbuf(f"sn_tmp_{tag}", [128, 2 * 512], F32)
    carry = k.sbuf(f"sn_carry_{tag}", [128, 16], F32)
    ytb = k.sbuf(f"sn_yt_{tag}", [128, 2 * T], BF16)
    ps_bc = k.psum(f"sn_ps_{tag}", [128, 4 * 512], F32)

    NIT = 128
    NGI = 16 * NIT
    add_id = {}
    _aid = 0
    for _dt in range(16):
        for _it in range(NIT):
            if _it % 16 != 0:
                add_id[_dt * NIT + _it] = _aid
                _aid += 1
    NADD_PER_DT = 120

    wp_tr_by_gi = {}
    wp_ld_by_gi = {}
    wp_ev_by_gi = {}
    if wprep:
        wp_lds = [k.sem(f"wp_ld{i}") for i in range(3)]
        wp_mm = k.sem("wp_mm")
        wp_ev = k.sem("wp_ev")
        wp_sts = [k.sem(f"wp_st{i}") for i in range(2)]
        b_wlds = [s.v for s in wp_lds]
        b_wsts = [s.v for s in wp_sts]
        b_wmm, b_wev = wp_mm.v, wp_ev.v
        CMAX = max(j[3] for j in wprep)
        wrow = k.sbuf(f"wp_row_{tag}", [128, 3 * CMAX], BF16)
        wstg = k.sbuf(f"wp_stg_{tag}", [128, 2 * CMAX], BF16)
        wpst = k.psum(f"wp_ps_{tag}", [128, 4 * 1024], BF16)

        rb_jobs = []
        tr_jobs = []
        for (src, dst, R, C) in wprep:
            nld = 2 if C > 2048 else 1
            for rb in range(R // 128):
                tr0 = len(tr_jobs)
                for cb in range(C // 128):
                    tr_jobs.append((len(rb_jobs), cb))
                rb_jobs.append((src, dst, rb, C, nld, tr0, C // 128))
        NTR = len(tr_jobs)
        NRB = len(rb_jobs)
        ld_t16 = [0] * NRB
        slot_c = [0, 0, 0]
        for grb in range(NRB):
            slot_c[grb % 3] += rb_jobs[grb][4]
            ld_t16[grb] = 16 * slot_c[grb % 3]
        tr_gi = [min(NGI - 1, (i // 4) * 4 * NGI // NTR) for i in range(NTR)]
        first_tr = [tr_gi[rb_jobs[g][5]] for g in range(NRB)]
        end_tr = [tr_gi[rb_jobs[g][5] + rb_jobs[g][6] - 1] for g in range(NRB)]
        ld_gi = [0] * NRB
        for grb in range(NRB):
            lo = (end_tr[grb - 3] + 4) if grb >= 3 else 0
            want = max(0, first_tr[grb] - 12)
            g = max(lo, want)
            if grb:
                g = max(g, ld_gi[grb - 1])
            ld_gi[grb] = min(NGI - 1, g)
        for grb in range(3, NRB):
            assert ld_gi[grb] < first_tr[grb], (grb, ld_gi[grb], first_tr[grb])
        for i in range(NTR):
            wp_tr_by_gi.setdefault(tr_gi[i], []).append(i)
            wp_ev_by_gi.setdefault(min(NGI - 1, tr_gi[i] + 2), []).append(i)
        for grb in range(NRB):
            wp_ld_by_gi.setdefault(ld_gi[grb], []).append(grb)

        def wp_tensor(t, gi):
            for i in wp_tr_by_gi.get(gi, ()):
                grb, cb = tr_jobs[i]
                src, dst, rb, C, nld, tr0, ntr = rb_jobs[grb]
                if cb == 0:
                    t.wait_ge(wp_lds[grb % 3].h, b_wlds[grb % 3] + ld_t16[grb])
                if i >= 4:
                    t.wait_ge(wp_ev.h, b_wev + i - 3)
                mm = t.transpose(
                    wpst[:, (i % 4) * 1024:(i % 4) * 1024 + 128],
                    wrow[:, (grb % 3) * CMAX + cb * 128:(grb % 3) * CMAX + cb * 128 + 128],
                    k.ident_bf[:, :])
                mm.then_inc(wp_mm.h, 1)
                wp_mm.v += 1

        def wp_gpsimd(g, gi):
            for grb in wp_ld_by_gi.get(gi, ()):
                src, dst, rb, C, nld, tr0, ntr = rb_jobs[grb]
                if grb >= 3:
                    pg = grb - 3
                    g.wait(wp_mm, b_wmm + rb_jobs[pg][5] + rb_jobs[pg][6])
                half = C // nld
                for c0 in range(0, C, half):
                    g.dma(wrow[:, (grb % 3) * CMAX + c0:(grb % 3) * CMAX + c0 + half],
                          src[rb * 128:rb * 128 + 128, c0:c0 + half],
                          wp_lds[grb % 3])

        def wp_scalar(sc, gi):
            for i in wp_ev_by_gi.get(gi, ()):
                grb, cb = tr_jobs[i]
                src, dst, rb, C, nld, tr0, ntr = rb_jobs[grb]
                sc.wait(wp_mm, b_wmm + i + 1)
                if cb == 0 and grb >= 2:
                    pg = grb - 2
                    sc.wait(wp_sts[pg % 2], b_wsts[pg % 2] + 16 * (pg // 2 + 1))
                sc.op(sc.e.copy,
                      wstg[:, (grb % 2) * CMAX + cb * 128:(grb % 2) * CMAX + cb * 128 + 128],
                      wpst[:, (i % 4) * 1024:(i % 4) * 1024 + 128],
                      inc=(wp_ev, 1))
                if cb == ntr - 1:
                    sc.dma(dst[rb], wstg[:, (grb % 2) * CMAX:(grb % 2) * CMAX + C],
                           wp_sts[grb % 2], dep=True)
    else:
        def wp_tensor(t, gi):
            pass

        def wp_gpsimd(g, gi):
            pass

        def wp_scalar(sc, gi):
            pass

    with nc.Block() as blk:
        @blk.sync
        def _(s_raw):
            s = k.eng(s_raw, "sync")
            for dt in range(16):
                if dt >= 2:
                    s_raw.wait_ge(s_ysts[dt % 2].h, b_ysts[dt % 2] + 16 * (dt // 2))
                s.dma(dtt[:, (dt % 2) * T:(dt % 2) * T + T], dtt_hbm[dt],
                      s_dxs[dt % 2])
                if dt >= 1:
                    s_raw.wait_ge(s_t2.h, b_t2 + dt)
                s.dma(xct[:, :], xc_hbm[dt], s_dxs[dt % 2])
                if dt >= 1:
                    s_raw.wait_ge(v_fin.h, b_fin + dt)
                s.dma(ztt[:, :], z_hbm[dt], s_zl)

        @blk.tensor
        def _(t):
            for dt in range(16):
                for it in range(NIT):
                    gi = dt * NIT + it
                    ch, n = it // 16, it % 16
                    slot = gi % 2
                    if gi >= 2:
                        t.wait_ge(v_bc.h, b_vb + gi - 1)
                    t.matmul(ps_bc[:, slot * 1024:slot * 1024 + 512],
                             sel_sb[:, n * 128:n * 128 + 128],
                             bc_sb[:, ch * 512:ch * 512 + 512],
                             start=True, stop=False, skip_group_check=True)
                    t.matmul(ps_bc[:, slot * 1024:slot * 1024 + 512],
                             sel_sb[:, n * 128:n * 128 + 128],
                             bc_sb[:, T + ch * 512:T + ch * 512 + 512],
                             start=False, stop=True, skip_group_check=True)
                    t.matmul(ps_bc[:, slot * 1024 + 512:slot * 1024 + 1024],
                             sel_sb[:, (16 + n) * 128:(16 + n) * 128 + 128],
                             bc_sb[:, ch * 512:ch * 512 + 512],
                             start=True, stop=False, skip_group_check=True)
                    mm = t.matmul(ps_bc[:, slot * 1024 + 512:slot * 1024 + 1024],
                                  sel_sb[:, (16 + n) * 128:(16 + n) * 128 + 128],
                                  bc_sb[:, T + ch * 512:T + ch * 512 + 512],
                                  start=False, stop=True, skip_group_check=True)
                    mm.then_inc(t_bc.h, 1)
                    t_bc.v += 1
                    wp_tensor(t, gi)

        @blk.scalar
        def _(sc_raw):
            sc = k.eng(sc_raw, "scalar")
            def cc(gj):
                nj = gj % 16
                sc.wait(v_scan, b_vs + gj + 1)
                sc.op(sc_raw.copy, carry[:, nj:nj + 1],
                      hh[:, (gj % 2) * 512 + 511:(gj % 2) * 512 + 512],
                      inc=(s_cc, 1))
            for dt in range(16):
                sc.wait(s_dxs[dt % 2], b_dxs[dt % 2] + 32 * (dt // 2 + 1))
                for it in range(NIT):
                    gi = dt * NIT + it
                    ch, n = it // 16, it % 16
                    if gi >= 2:
                        sc.wait(v_scan, b_vs + gi - 1)
                    sc.op(sc_raw.activation, da[:, (gi % 2) * 512:(gi % 2) * 512 + 512],
                          dtt[:, (dt % 2) * T + ch * 512:(dt % 2) * T + ch * 512 + 512],
                          AF.Exp, scale=a_sb[:, dt * 16 + n:dt * 16 + n + 1],
                          inc=(s_dA, 1))
                    if gi >= 1:
                        cc(gi - 1)
                    wp_scalar(sc, gi)
                if dt == 15:
                    cc(16 * NIT - 1)
                sc.wait(v_bc, b_vb + NIT * (dt + 1))
                sc.op(sc_raw.activation, u_sb[:, :],
                      xct[:, :], AF.Identity,
                      scale=dp_col[:, dt:dt + 1], inc=(s_t2, 1))
                sc.wait(s_zl, b_zl + 16 * (dt + 1))
                if dt >= 1:
                    sc.wait(v_fin, b_fin + dt)
                sc.op(sc_raw.activation, ztt[:, :], ztt[:, :], AF.Silu,
                      inc=(s_sz, 1))
            if wprep:
                for ss in wp_sts:
                    sc_raw.wait_ge(ss.h, ss.v)

        @blk.vector
        def _(v_raw):
            v = k.eng(v_raw, "vector")
            for dt in range(16):
                v.wait(s_dxs[dt % 2], b_dxs[dt % 2] + 32 * (dt // 2 + 1))
                v.wait(s_cc, b_cc + NIT * dt)
                v.op(v_raw.memset, carry[:, :], 0.0)
                v.op(v_raw.tensor_tensor, u_sb[:, :],
                     dtt[:, (dt % 2) * T:(dt % 2) * T + T],
                     xct[:, :], ALU.mult)
                for it in range(NIT):
                    gi = dt * NIT + it
                    ch, n = it // 16, it % 16
                    slot = gi % 2
                    cs = ch * 512
                    v.wait(t_bc, b_bc + gi + 1)
                    v.op(v_raw.tensor_tensor, dbu[:, slot * 512:slot * 512 + 512],
                         u_sb[:, cs:cs + 512],
                         ps_bc[:, slot * 1024:slot * 1024 + 512], ALU.mult)
                    v.wait(s_dA, b_dA + gi + 1)
                    if gi >= 2:
                        v.wait(s_cc, b_cc + gi - 1)
                    v.op(v_raw.tensor_tensor_scan,
                         hh[:, slot * 512:slot * 512 + 512],
                         da[:, slot * 512:slot * 512 + 512],
                         dbu[:, slot * 512:slot * 512 + 512],
                         carry[:, n:n + 1], ALU.mult, ALU.add,
                         inc=(v_scan, 1), dep=(ch == 0))
                    if n == 0:
                        v.op(v_raw.tensor_tensor, ya[:, cs:cs + 512],
                             hh[:, slot * 512:slot * 512 + 512],
                             ps_bc[:, slot * 1024 + 512:slot * 1024 + 1024],
                             ALU.mult, inc=(v_bc, 1), dep=True)
                    else:
                        a = add_id[gi]
                        if a >= 2:
                            v.wait(g_add, b_ga + a - 1)
                        v.op(v_raw.tensor_tensor, tmp[:, (a % 2) * 512:(a % 2) * 512 + 512],
                             hh[:, slot * 512:slot * 512 + 512],
                             ps_bc[:, slot * 1024 + 512:slot * 1024 + 1024],
                             ALU.mult, inc=(v_bc, 1), dep=True)
                v.wait(g_add, b_ga + NADD_PER_DT * (dt + 1))
                v.wait(s_t2, b_t2 + dt + 1)
                v.op(v_raw.tensor_tensor, ya[:, :], ya[:, :], u_sb[:, :], ALU.add)
                v.wait(s_sz, b_sz + dt + 1)
                if dt >= 2:
                    v.wait(s_ysts[dt % 2], b_ysts[dt % 2] + 16 * (dt // 2))
                v.op(v_raw.tensor_tensor, ytb[:, (dt % 2) * T:(dt % 2) * T + T],
                     ya[:, :], ztt[:, :], ALU.mult, inc=(v_fin, 1))

        @blk.gpsimd
        def _(g_raw):
            g = k.eng(g_raw, "gpsimd")
            for dt in range(16):
                for it in range(NIT):
                    gi = dt * NIT + it
                    ch, n = it // 16, it % 16
                    wp_gpsimd(g, gi)
                    if n == 0:
                        continue
                    a = add_id[gi]
                    cs = ch * 512
                    g.wait(v_bc, b_vb + gi + 1)
                    g.op(g_raw.tensor_tensor, ya[:, cs:cs + 512],
                         ya[:, cs:cs + 512],
                         tmp[:, (a % 2) * 512:(a % 2) * 512 + 512],
                         ALU.add, inc=(g_add, 1))
                g.wait(v_fin, b_fin + dt + 1)
                g.dma(yt_hbm[dt], ytb[:, (dt % 2) * T:(dt % 2) * T + T],
                      s_ysts[dt % 2])
            for ss in s_ysts:
                g_raw.wait_ge(ss.h, ss.v)


def emit_bc_split(k: K, xdbl_sb, bc_sb, tag):
    nc = k.nc
    bcf = k.sbuf(f"bcs_f_{tag}", [32, T], F32)
    with nc.Block() as blk:
        @blk.scalar
        def _(sc_raw):
            sc = k.eng(sc_raw, "scalar")
            sc.op(sc_raw.copy, bcf[:, :], xdbl_sb[64:96, :])
    with nc.Block() as blk:
        @blk.vector
        def _(v_raw):
            v = k.eng(v_raw, "vector")
            v.op(v_raw.tensor_copy, bc_sb[:, 0:T], bcf[:, :])
    with nc.Block() as blk:
        @blk.vector
        def _(v_raw):
            v = k.eng(v_raw, "vector")
            v.op(v_raw.tensor_tensor, bc_sb[:, T:2 * T], bcf[:, :],
                 bc_sb[:, 0:T], ALU.subtract)


def body_mamba(k, x2_in, win_in, cw_in, cb_in, wx_in, wdt_in, bdt_in,
               alog_in, dp_in, wout_in, sel_in, x3_out, pfx="mb", wprep=None):
    nc = k.nc
    wint_hbm = k.dram(f"{pfx}_scr_wint", [32, 128, 1024], BF16)
    xi_hbm = k.dram(f"{pfx}_scr_xi", [16, 128, T], F32)
    z_hbm = k.dram(f"{pfx}_scr_z", [16, 128, T], F32)
    xc_hbm = k.dram(f"{pfx}_scr_xc", [16, 128, T], F32)
    xcbf_hbm = k.dram(f"{pfx}_scr_xcbf", [16, 128, T], BF16)
    dtt_hbm = k.dram(f"{pfx}_scr_dtt", [16, 128, T], F32)
    yt_hbm = k.dram(f"{pfx}_scr_yt", [16, 128, T], BF16)

    phm = contextlib.ExitStack()
    sel_sb = phm.enter_context(nc.sbuf_tensor("sel_sb", [32, 32 * 128], BF16))
    a_sb = phm.enter_context(nc.sbuf_tensor("a_sb", [128, 256], F32))
    dp_col = phm.enter_context(nc.sbuf_tensor("dp_col", [128, 16], F32))
    bc_sb = phm.enter_context(nc.sbuf_tensor("bc_sb", [32, 2 * T], BF16))
    phm2 = contextlib.ExitStack()
    wxT_sb = phm2.enter_context(nc.sbuf_tensor("wxT_sb", [128, 16 * 96], BF16))
    wdtT_sb = phm2.enter_context(nc.sbuf_tensor("wdtT_sb", [64, 2048], BF16))
    cw_sb = phm2.enter_context(nc.sbuf_tensor("cw_sb", [128, 64], F32))
    cb_col = phm2.enter_context(nc.sbuf_tensor("cb_col", [128, 16], F32))
    bdt_col = phm2.enter_context(nc.sbuf_tensor("bdt_col", [128, 16], F32))
    xdbl_sb = phm2.enter_context(nc.sbuf_tensor("xdbl_sb", [96, T], F32))

    s_ml = k.sem("s_mld")
    with nc.Block() as blk:
        @blk.sync
        def _(s_raw):
            s = k.eng(s_raw, "sync")
            s.dma(sel_sb[:, :], sel_in[:, :], s_ml)
            av = a_sb.rearrange("p (dt n) -> p dt n", n=16)
            for c in range(4):
                s.dma(av[:, 4 * c:4 * c + 4, :],
                      alog_in.rearrange("(dt p) n -> p dt n", p=128)[:, 4 * c:4 * c + 4, :],
                      s_ml)
            cwv = cw_sb.rearrange("p (dt kk) -> p dt kk", kk=4)
            for c in range(4):
                s.dma(cwv[:, 4 * c:4 * c + 4, :],
                      cw_in.rearrange("(dt p) kk -> p dt kk", p=128)[:, 4 * c:4 * c + 4, :],
                      s_ml)
            s_raw.wait_ge(s_ml.h, s_ml.v)
    with nc.Block() as blk:
        @blk.scalar
        def _(sc_raw):
            sc = k.eng(sc_raw, "scalar")
            sc.op(sc_raw.activation, a_sb[:, :], a_sb[:, :], AF.Exp)
    with nc.Block() as blk:
        @blk.vector
        def _(v_raw):
            v = k.eng(v_raw, "vector")
            v.op(v_raw.tensor_scalar, a_sb[:, :], a_sb[:, :], -1.0, None, ALU.mult)

    with k.scope():
        scale_cols(k, cb_in, 2048, cb_col, "cb")
    with k.scope():
        scale_cols(k, bdt_in, 2048, bdt_col, "bdt")
    with k.scope():
        scale_cols(k, dp_in, 2048, dp_col, "dp")
    with k.scope():
        tw_hbm(k, win_in[:, :], wint_hbm, 4096, 1024, "win")
    with k.scope():
        tw_sbuf_any(k, wx_in[:, :], wxT_sb, 96, 2048, "wx")
    with k.scope():
        tw_sbuf_any(k, wdt_in[:, :], wdtT_sb, 2048, 64, "wdt")

    with contextlib.ExitStack() as phx:
        x2T = phx.enter_context(nc.sbuf_tensor("x2T", [128, 8 * T], BF16))
        with k.scope():
            tx_sbuf(k, x2_in[:, :], x2T, "x2")
        with k.scope():
            emit_xz(k, x2T, wint_hbm, xi_hbm, z_hbm, "xz")
    with k.scope():
        emit_conv(k, xi_hbm, cw_sb, cb_col, xc_hbm, xcbf_hbm, "cv")
    with k.scope():
        emit_xdbl(k, xcbf_hbm, wxT_sb, xdbl_sb, "xd")
    with k.scope():
        emit_bc_split(k, xdbl_sb, bc_sb, "bc")
    with k.scope():
        emit_dtproj(k, xdbl_sb, wdtT_sb, bdt_col, dtt_hbm, "dp")
    phm2.close()
    with k.scope():
        emit_scan(k, dtt_hbm, xc_hbm, z_hbm, bc_sb, sel_sb, a_sb, dp_col,
                  yt_hbm, "sn", wprep=wprep)
    phm.close()

    with contextlib.ExitStack() as phw:
        woutT_sb = phw.enter_context(nc.sbuf_tensor("woutT_sb", [128, 16 * 1024], BF16))
        with k.scope():
            tw_sbuf(k, wout_in[:, :], woutT_sb, 1024, 2048, "wo2")

        def rhs_wout(kt, nb):
            return woutT_sb[:, kt * 1024 + nb * 512:kt * 1024 + nb * 512 + 512]

        with k.scope():
            emit_proj_residual_ln(k, yt_hbm.rearrange("a p t -> (a p) t"),
                                  rhs_wout, x3_out, "mm",
                                  resid_hbm=x2_in[:, :], ctx_rows=2048)


def build_mamba():
    k = K("mamba")
    x2_in = k.dram("x2_in", [T, D], F32, kind="ExternalInput")
    win_in = k.dram("win_in", [4096, 1024], F32, kind="ExternalInput")
    cw_in = k.dram("convw_in", [2048, 4], F32, kind="ExternalInput")
    cb_in = k.dram("convb_in", [2048], F32, kind="ExternalInput")
    wx_in = k.dram("wxproj_in", [96, 2048], F32, kind="ExternalInput")
    wdt_in = k.dram("wdt_in", [2048, 64], F32, kind="ExternalInput")
    bdt_in = k.dram("bdt_in", [2048], F32, kind="ExternalInput")
    alog_in = k.dram("alog_in", [2048, 16], F32, kind="ExternalInput")
    dp_in = k.dram("dp_in", [2048], F32, kind="ExternalInput")
    wout_in = k.dram("wout_in", [1024, 2048], F32, kind="ExternalInput")
    sel_in = k.dram("c_sel_bf", [32, 32 * 128], BF16, kind="ExternalInput")
    x3_out = k.dram("x3_out", [T, D], F32, kind="ExternalOutput")
    declare_and_load_consts(k)
    body_mamba(k, x2_in, win_in, cw_in, cb_in, wx_in, wdt_in, bdt_in,
               alog_in, dp_in, wout_in, sel_in, x3_out)
    return k


def host_sel_const():
    sel = np.zeros((32, 32 * 128), dtype=BF)
    for n in range(32):
        sel[n, n * 128:(n + 1) * 128] = 1.0
    return sel


def run_mamba(x2_full, W_in, conv_w, conv_b, W_xproj, W_dt, b_dt, A_log, Dp,
              W_out):
    k = build_mamba()
    consts = host_consts()
    consts["c_sel_bf"] = host_sel_const()
    in_maps = []
    for c in range(NCORES):
        m = dict(consts)
        m["x2_in"] = np.ascontiguousarray(x2_full[c])
        m["win_in"] = W_in
        m["convw_in"] = conv_w
        m["convb_in"] = conv_b
        m["wxproj_in"] = W_xproj
        m["wdt_in"] = W_dt
        m["bdt_in"] = b_dt
        m["alog_in"] = A_log
        m["dp_in"] = Dp
        m["wout_in"] = W_out
        in_maps.append(m)
    res = run_bass_kernel_spmd(k.nc, in_maps, core_ids=list(range(NCORES)))
    k.close()
    return np.stack([res.results[c]["x3_out"] for c in range(NCORES)])



def emit_ln_transpose(k: K, x_hbm, xT_sb, tag, ln_scale_bc=None,
                      ln_shift_bc=None, gate=None):
    nc = k.nc
    s_xls = [k.sem("s_xld0"), k.sem("s_xld1")]
    b_xls = [s.v for s in s_xls]
    s_lnv, s_lns = k.sem("ln_v"), k.sem("ln_s")
    v_ap = k.sem("v_apl")
    t_tr = k.sem("t_tr")
    s_eb = k.sem("s_evb")
    s_ef = k.sem("s_evf")
    t_gm = k.sem("t_gmm")
    v_tp = k.sem("v_top")
    s_sg = k.sem("s_sig")
    b_lnv, b_lns, b_ap = s_lnv.v, s_lns.v, v_ap.v
    b_tr, b_eb, b_ef, b_gm = t_tr.v, s_eb.v, s_ef.v, t_gm.v
    b_tp, b_sg = v_tp.v, s_sg.v

    import os
    _vdep = os.environ.get("LT_VDEP", "0") == "1"
    _sdep = os.environ.get("LT_SDEP", "0") == "1"
    xrow = k.sbuf(f"lt_x_{tag}", [128, 2 * 1024], F32)
    xln = k.sbuf(f"lt_ln_{tag}", [128, 2 * 1024], F32)
    sqs = k.sbuf(f"lt_sq_{tag}", [128, 1024], F32)
    stats = k.sbuf(f"lt_st_{tag}", [128, 16], F32)
    gsc = k.sbuf(f"lt_gs_{tag}", [128, 3 * 8], F32)
    xtf = k.sbuf(f"lt_xtf_{tag}", [128, 8 * 128], F32)
    pst = k.psum(f"lt_pst_{tag}", [128, 2 * 512], F32)
    psg = k.psum(f"lt_psg_{tag}", [128, 8], F32) if gate else None
    if gate:
        gw_w_sb, gw_sb = gate

    with nc.Block() as blk:
        @blk.sync
        def _(s_raw):
            s = k.eng(s_raw, "sync")
            for ti in range(32):
                if ti >= 2:
                    s_raw.wait_ge(v_ap.h, b_ap + ti - 1)
                s.dma(xrow[:, (ti % 2) * 1024:(ti % 2) * 1024 + 1024],
                      x_hbm[ti * 128:(ti + 1) * 128, :], s_xls[ti % 2])

        @blk.vector
        def _(v_raw):
            v = k.eng(v_raw, "vector")
            v.default_dep = _vdep
            for ti in range(32):
                xa = xrow[:, (ti % 2) * 1024:(ti % 2) * 1024 + 1024]
                la = xln[:, (ti % 2) * 1024:(ti % 2) * 1024 + 1024]
                v.wait(s_xls[ti % 2], b_xls[ti % 2] + 16 * (ti // 2 + 1))
                v.op(v_raw.reduce_sum, stats[:, 1:2], xa, axis=AX.X,
                     inc=(s_lnv, 1))
                v.wait(s_lns, b_lns + 2 * ti + 1)
                v.op(v_raw.tensor_scalar, stats[:, 2:3], stats[:, 1:2],
                     1.0 / D, None, ALU.mult, dep=True)
                v.op(v_raw.tensor_scalar, stats[:, 3:4], stats[:, 2:3],
                     stats[:, 2:3], None, ALU.mult, dep=True)
                v.op(v_raw.tensor_scalar, stats[:, 4:5], stats[:, 0:1],
                     1.0 / D, None, ALU.mult)
                v.op(v_raw.tensor_tensor, stats[:, 4:5], stats[:, 4:5],
                     stats[:, 3:4], ALU.subtract, inc=(s_lnv, 1), dep=True)
                v.wait(s_lns, b_lns + 2 * ti + 2)
                if ti >= 2:
                    v.wait(t_tr, b_tr + 8 * (ti - 1))
                v.op(v_raw.tensor_scalar, la, xa, stats[:, 2:3], stats[:, 5:6],
                     ALU.subtract, ALU.mult, dep=True)
                if ln_scale_bc is not None:
                    v.op(v_raw.tensor_tensor, la, la, ln_scale_bc, ALU.mult)
                    v.op(v_raw.tensor_tensor, la, la, ln_shift_bc, ALU.add,
                         inc=(v_ap, 1))
                else:
                    v.op(v_raw.tensor_copy, la, la, inc=(v_ap, 1))
                if gate:
                    v.wait(t_gm, b_gm + 8 * (ti + 1))
                    v.op(v_raw.reduce_max, stats[:, 8:9], psg[:, :], axis=AX.X)
                    v.op(v_raw.tensor_scalar, gsc[:, 0:8], psg[:, :],
                         stats[:, 8:9], None, ALU.is_ge, dep=True)
                    v.op(v_raw.scalar_tensor_tensor, gsc[:, 8:16], gsc[:, 0:8],
                         -1e30, psg[:, :], ALU.mult, ALU.add, dep=True)
                    v.op(v_raw.reduce_max, stats[:, 9:10], gsc[:, 8:16],
                         axis=AX.X, dep=True)
                    v.op(v_raw.tensor_scalar, gsc[:, 8:16], gsc[:, 8:16],
                         stats[:, 9:10], None, ALU.is_ge, dep=True)
                    v.op(v_raw.tensor_tensor, stats[:, 10:11], stats[:, 8:9],
                         stats[:, 9:10], ALU.subtract, inc=(v_tp, 1), dep=True)
                    v.wait(s_sg, b_sg + ti + 1)
                    v.op(v_raw.tensor_scalar, stats[:, 12:13], stats[:, 11:12],
                         -1.0, 1.0, ALU.mult, ALU.add, dep=True)
                    v.op(v_raw.tensor_scalar, gsc[:, 16:24], gsc[:, 0:8],
                         stats[:, 11:12], None, ALU.mult, dep=True)
                    v.op(v_raw.scalar_tensor_tensor, gw_sb[:, ti * 8:ti * 8 + 8],
                         gsc[:, 8:16], stats[:, 12:13], gsc[:, 16:24],
                         ALU.mult, ALU.add, inc=(v_tp, 1), dep=True)

        @blk.scalar
        def _(sc_raw):
            sc = k.eng(sc_raw, "scalar")
            sc.default_dep = _sdep
            for ti in range(32):
                xa = xrow[:, (ti % 2) * 1024:(ti % 2) * 1024 + 1024]
                sc.wait(s_lnv, b_lnv + 2 * ti + 1)
                sc.op(sc_raw.activation, sqs[:, :], xa, AF.Square,
                      accum_out=stats[:, 0:1], inc=(s_lns, 1))
                sc.wait(s_lnv, b_lnv + 2 * ti + 2)
                sc.op(sc_raw.activation, stats[:, 5:6], stats[:, 4:5], AF.Ln,
                      bias=LNEPS)
                sc.op(sc_raw.activation, stats[:, 5:6], stats[:, 5:6], AF.Exp,
                      scale=-0.5, inc=(s_lns, 1))
                for dc in range(8):
                    g = ti * 8 + dc
                    sc.wait(t_tr, b_tr + g + 1)
                    sc.op(sc_raw.copy,
                          xT_sb[:, dc * T + ti * 128:dc * T + ti * 128 + 128],
                          pst[:, (g % 2) * 512:(g % 2) * 512 + 128],
                          inc=(s_eb, 1))
                    if gate:
                        if dc == 0 and ti >= 1:
                            sc.wait(t_gm, b_gm + 8 * ti)
                        sc.op(sc_raw.copy, xtf[:, dc * 128:dc * 128 + 128],
                              pst[:, (g % 2) * 512:(g % 2) * 512 + 128],
                              inc=(s_ef, 1))
                if gate:
                    sc.wait(v_tp, b_tp + 2 * ti + 1)
                    sc.op(sc_raw.activation, stats[:, 11:12], stats[:, 10:11],
                          AF.Sigmoid, inc=(s_sg, 1))

        @blk.tensor
        def _(t):
            for ti in range(32):
                t.wait_ge(v_ap.h, b_ap + ti + 1)
                for dc in range(8):
                    g = ti * 8 + dc
                    if g >= 2:
                        t.wait_ge(s_eb.h, b_eb + g - 1)
                        if gate:
                            t.wait_ge(s_ef.h, b_ef + g - 1)
                    mm = t.transpose(
                        pst[:, (g % 2) * 512:(g % 2) * 512 + 128],
                        xln[:, (ti % 2) * 1024 + dc * 128:(ti % 2) * 1024 + dc * 128 + 128],
                        k.ident_f32[:, :])
                    mm.then_inc(t_tr.h, 1)
                    t_tr.v += 1
                if gate:
                    if ti >= 1:
                        t.wait_ge(v_tp.h, b_tp + 2 * ti)
                    for dc in range(8):
                        g = ti * 8 + dc
                        t.wait_ge(s_ef.h, b_ef + g + 1)
                        mm = t.matmul(psg[:, :],
                                      xtf[:, dc * 128:dc * 128 + 128],
                                      gw_w_sb[:, dc * 8:dc * 8 + 8],
                                      start=(dc == 0), stop=(dc == 7),
                                      skip_group_check=True)
                        mm.then_inc(t_gm.h, 1)
                        t_gm.v += 1


def emit_moe(k: K, xT_sb, gw_sb, w1_in, w2_in, w3_in, acc_hbm, tag):
    nc = k.nc
    w1_sb = k.sbuf(f"moe_w1_{tag}", [128, 8 * 2048], BF16)
    w2_sb = k.sbuf(f"moe_w2_{tag}", [128, 8 * 2048], BF16)
    w3_sb = k.sbuf(f"moe_w3_{tag}", [128, 16 * 1024], BF16)
    ht_sb = k.sbuf(f"moe_ht_{tag}", [128, 16 * 512], BF16)
    tsil = k.sbuf(f"moe_ts_{tag}", [128, 2 * 512], F32)
    accb = k.sbuf(f"moe_acc_{tag}", [128, 4 * 512], F32)
    ps12 = k.psum(f"moe_p12_{tag}", [128, 4 * 512], F32)
    ps3 = k.psum(f"moe_p3_{tag}", [128, 2 * 512], F32)

    for e in range(8):
        s_wl = k.sem("g_wld")
        s_als = [k.sem(f"s_accld{i}") for i in range(4)]
        t_m12 = k.sem("t_m12")
        t_m3 = k.sem("t_m3")
        s_ev = k.sem("s_ev")
        v_ht = k.sem("v_ht")
        v_ac = k.sem("v_acc")
        s_sts = [k.sem(f"s_st{i}") for i in range(4)]
        b_als = [s.v for s in s_als]
        b_sts = [s.v for s in s_sts]
        b_wl, b_12, b_3 = s_wl.v, t_m12.v, t_m3.v
        b_ev, b_ht, b_ac = s_ev.v, v_ht.v, v_ac.v

        with nc.Block() as blk:
            @blk.gpsimd
            def _(g_raw):
                g = k.eng(g_raw, "gpsimd")
                for c in range(8):
                    g.dma(w1_sb[:, c * 2048:(c + 1) * 2048],
                          w1_in[e, c * 128:(c + 1) * 128, :], s_wl)
                    g.dma(w2_sb[:, c * 2048:(c + 1) * 2048],
                          w2_in[e, c * 128:(c + 1) * 128, :], s_wl)
                w3v = w3_sb.rearrange("p (ht d) -> p ht d", d=1024)
                w3s = w3_in[e].rearrange("(ht p) d -> p ht d", p=128)
                for c in range(4):
                    g.dma(w3v[:, 4 * c:4 * c + 4, :], w3s[:, 4 * c:4 * c + 4, :],
                          s_wl)

            @blk.tensor
            def _(t):
                t.wait_ge(s_wl.h, s_wl.v)
                for tb in range(8):
                    for ht in range(16):
                        n = tb * 16 + ht
                        if n >= 2:
                            t.wait_ge(v_ht.h, b_ht + n - 1)
                        for kt in range(8):
                            t.matmul(ps12[:, (n % 2) * 512:(n % 2) * 512 + 512],
                                     w1_sb[:, kt * 2048 + ht * 128:kt * 2048 + ht * 128 + 128],
                                     xT_sb[:, kt * T + tb * 512:kt * T + tb * 512 + 512],
                                     start=(kt == 0), stop=(kt == 7),
                                     skip_group_check=True)
                        for kt in range(8):
                            mm = t.matmul(
                                ps12[:, (2 + n % 2) * 512:(2 + n % 2) * 512 + 512],
                                w2_sb[:, kt * 2048 + ht * 128:kt * 2048 + ht * 128 + 128],
                                xT_sb[:, kt * T + tb * 512:kt * T + tb * 512 + 512],
                                start=(kt == 0), stop=(kt == 7),
                                skip_group_check=True)
                        mm.then_inc(t_m12.h, 1)
                        t_m12.v += 1
                    t.wait_ge(v_ht.h, b_ht + 16 * (tb + 1))
                    for tt in range(4):
                        for db in range(2):
                            m = tb * 8 + tt * 2 + db
                            if m >= 2:
                                t.wait_ge(v_ac.h, b_ac + m - 1)
                            for ht in range(16):
                                mm = t.matmul(
                                    ps3[:, (m % 2) * 512:(m % 2) * 512 + 512],
                                    ht_sb[:, ht * 512 + tt * 128:ht * 512 + tt * 128 + 128],
                                    w3_sb[:, ht * 1024 + db * 512:ht * 1024 + db * 512 + 512],
                                    start=(ht == 0), stop=(ht == 15),
                                    skip_group_check=True)
                            mm.then_inc(t_m3.h, 1)
                            t_m3.v += 1

            @blk.scalar
            def _(sc_raw):
                sc = k.eng(sc_raw, "scalar")
                for tb in range(8):
                    for ht in range(16):
                        n = tb * 16 + ht
                        sc.wait(t_m12, b_12 + n + 1)
                        if n >= 2:
                            sc.wait(v_ht, b_ht + n - 1)
                        sc.op(sc_raw.activation,
                              tsil[:, (n % 2) * 512:(n % 2) * 512 + 512],
                              ps12[:, (n % 2) * 512:(n % 2) * 512 + 512],
                              AF.Silu, inc=(s_ev, 1))

            def acc_slice(m):
                tb2, r = divmod(m, 8)
                tt2, db2 = divmod(r, 2)
                tg2 = tb2 * 4 + tt2
                return acc_hbm[tg2 * 128:(tg2 + 1) * 128,
                               db2 * 512:db2 * 512 + 512]

            @blk.sync
            def _(s_raw):
                s = k.eng(s_raw, "sync")
                if e > 0:
                    for m in range(4):
                        s.dma(accb[:, (m % 4) * 512:(m % 4) * 512 + 512],
                              acc_slice(m), s_als[m % 4])
                for m in range(64):
                    s.wait(v_ac, b_ac + m + 1)
                    s.dma(acc_slice(m),
                          accb[:, (m % 4) * 512:(m % 4) * 512 + 512], s_sts[m % 4])
                    if e > 0 and m + 4 < 64:
                        s.wait(s_sts[m % 4], b_sts[m % 4] + 16 * (m // 4 + 1))
                        s.dma(accb[:, (m % 4) * 512:(m % 4) * 512 + 512],
                              acc_slice(m + 4), s_als[m % 4])
                for ss in s_sts:
                    s_raw.wait_ge(ss.h, ss.v)

            @blk.vector
            def _(v_raw):
                v = k.eng(v_raw, "vector")
                for tb in range(8):
                    for ht in range(16):
                        n = tb * 16 + ht
                        v.wait(s_ev, b_ev + n + 1)
                        if ht == 0 and tb >= 1:
                            v.wait(t_m3, b_3 + 8 * tb)
                        v.op(v_raw.tensor_tensor,
                             ht_sb[:, ht * 512:(ht + 1) * 512],
                             tsil[:, (n % 2) * 512:(n % 2) * 512 + 512],
                             ps12[:, (2 + n % 2) * 512:(2 + n % 2) * 512 + 512],
                             ALU.mult, inc=(v_ht, 1))
                    for tt in range(4):
                        for db in range(2):
                            m = tb * 8 + tt * 2 + db
                            tg = tb * 4 + tt
                            v.wait(t_m3, b_3 + m + 1)
                            if e == 0 and m >= 4:
                                v.wait(s_sts[m % 4], b_sts[m % 4] + 16 * (m // 4))
                            gcol = gw_sb[:, tg * 8 + e:tg * 8 + e + 1]
                            if e == 0:
                                v.op(v_raw.tensor_scalar,
                                     accb[:, (m % 4) * 512:(m % 4) * 512 + 512],
                                     ps3[:, (m % 2) * 512:(m % 2) * 512 + 512],
                                     gcol, None, ALU.mult, inc=(v_ac, 1))
                            else:
                                v.wait(s_als[m % 4], b_als[m % 4] + 16 * (m // 4 + 1))
                                v.op(v_raw.scalar_tensor_tensor,
                                     accb[:, (m % 4) * 512:(m % 4) * 512 + 512],
                                     ps3[:, (m % 2) * 512:(m % 2) * 512 + 512],
                                     gcol,
                                     accb[:, (m % 4) * 512:(m % 4) * 512 + 512],
                                     ALU.mult, ALU.add, inc=(v_ac, 1))



def emit_add3_ln(k: K, a_hbm, b_hbm, c_hbm, out_hbm, tag):
    nc = k.nc
    s_xls = [k.sem("s_xld0"), k.sem("s_xld1")]
    s_lnv, s_lns = k.sem("ln_v"), k.sem("ln_s")
    sout = k.sem("v_out")
    ssts = [k.sem("g_st0"), k.sem("g_st1")]
    b_xls = [s.v for s in s_xls]
    b_sts = [s.v for s in ssts]
    b_lnv, b_lns, b_out = s_lnv.v, s_lns.v, sout.v
    aa = k.sbuf(f"a3_a_{tag}", [128, 2 * 1024], F32)
    bb = k.sbuf(f"a3_b_{tag}", [128, 2 * 1024], F32)
    cc = k.sbuf(f"a3_c_{tag}", [128, 2 * 1024], F32)
    vt = k.sbuf(f"a3_v_{tag}", [128, 2 * 1024], F32)
    sqs = k.sbuf(f"a3_sq_{tag}", [128, 1024], F32)
    stats = k.sbuf(f"a3_st_{tag}", [128, 8], F32)
    outt = k.sbuf(f"a3_o_{tag}", [128, 2 * 1024], F32)

    with nc.Block() as blk:
        @blk.sync
        def _(s_raw):
            s = k.eng(s_raw, "sync")
            for ti in range(32):
                if ti >= 2:
                    s_raw.wait_ge(s_lnv.h, b_lnv + 2 * (ti - 2) + 1)
                sl = (ti % 2) * 1024
                s.dma(aa[:, sl:sl + 1024], a_hbm[ti * 128:(ti + 1) * 128, :],
                      s_xls[ti % 2])
                s.dma(bb[:, sl:sl + 1024], b_hbm[ti * 128:(ti + 1) * 128, :],
                      s_xls[ti % 2])
                s.dma(cc[:, sl:sl + 1024], c_hbm[ti * 128:(ti + 1) * 128, :],
                      s_xls[ti % 2])

        @blk.vector
        def _(v_raw):
            v = k.eng(v_raw, "vector")
            for ti in range(32):
                sl = (ti % 2) * 1024
                va = vt[:, sl:sl + 1024]
                v.wait(s_xls[ti % 2], b_xls[ti % 2] + 48 * (ti // 2 + 1))
                v.op(v_raw.tensor_tensor, va, aa[:, sl:sl + 1024],
                     bb[:, sl:sl + 1024], ALU.add)
                v.op(v_raw.tensor_tensor, va, va, cc[:, sl:sl + 1024], ALU.add)
                v.op(v_raw.reduce_sum, stats[:, 1:2], va, axis=AX.X,
                     inc=(s_lnv, 1))
                v.wait(s_lns, b_lns + 2 * ti + 1)
                v.op(v_raw.tensor_scalar, stats[:, 2:3], stats[:, 1:2],
                     1.0 / D, None, ALU.mult, dep=True)
                v.op(v_raw.tensor_scalar, stats[:, 3:4], stats[:, 2:3],
                     stats[:, 2:3], None, ALU.mult, dep=True)
                v.op(v_raw.tensor_scalar, stats[:, 4:5], stats[:, 0:1],
                     1.0 / D, None, ALU.mult)
                v.op(v_raw.tensor_tensor, stats[:, 4:5], stats[:, 4:5],
                     stats[:, 3:4], ALU.subtract, inc=(s_lnv, 1), dep=True)
                v.wait(s_lns, b_lns + 2 * ti + 2)
                if ti >= 2:
                    v.wait(ssts[ti % 2], b_sts[ti % 2] + 16 * (ti // 2))
                v.op(v_raw.tensor_scalar, outt[:, sl:sl + 1024], va,
                     stats[:, 2:3], stats[:, 5:6], ALU.subtract, ALU.mult,
                     inc=(sout, 1), dep=True)

        @blk.scalar
        def _(sc_raw):
            sc = k.eng(sc_raw, "scalar")
            for ti in range(32):
                sc.wait(s_lnv, b_lnv + 2 * ti + 1)
                sc.op(sc_raw.activation, sqs[:, :],
                      vt[:, (ti % 2) * 1024:(ti % 2) * 1024 + 1024],
                      AF.Square, accum_out=stats[:, 0:1], inc=(s_lns, 1))
                sc.wait(s_lnv, b_lnv + 2 * ti + 2)
                sc.op(sc_raw.activation, stats[:, 5:6], stats[:, 4:5], AF.Ln,
                      bias=LNEPS)
                sc.op(sc_raw.activation, stats[:, 5:6], stats[:, 5:6], AF.Exp,
                      scale=-0.5, inc=(s_lns, 1))

        @blk.gpsimd
        def _(g_raw):
            g = k.eng(g_raw, "gpsimd")
            for ti in range(32):
                g.wait(sout, b_out + ti + 1)
                g.dma(out_hbm[ti * 128:(ti + 1) * 128, :],
                      outt[:, (ti % 2) * 1024:(ti % 2) * 1024 + 1024],
                      ssts[ti % 2])
            for ss in ssts:
                g_raw.wait_ge(ss.h, ss.v)


def body_l0ffn(k, x1_in, wq_in, wk_in, wv_in, wo_in, bo_in, l1s_in, l1b_in,
               l2s_in, l2b_in, gw_in, w1_in, w2_in, w3_in, x2_out, pfx="tf"):
    nc = k.nc
    qt_hbm = k.dram(f"{pfx}_scr_qt", [D, T], BF16)
    kt_hbm = k.dram(f"{pfx}_scr_kt", [D, T], BF16)
    ctxT_hbm = k.dram(f"{pfx}_scr_ctxT", [D, T], BF16)
    u_hbm = k.dram(f"{pfx}_scr_u", [T, D], F32)
    acc_hbm = k.dram(f"{pfx}_scr_acc", [T, D], F32)

    with contextlib.ExitStack() as ph:
        ln1s_bc = ph.enter_context(nc.sbuf_tensor("ln1s_bc", [128, 1024], F32))
        ln1b_bc = ph.enter_context(nc.sbuf_tensor("ln1b_bc", [128, 1024], F32))
        with k.scope():
            bcast_rows(k, [
                (l1s_in.rearrange("(o d) -> o d", o=1), ln1s_bc),
                (l1b_in.rearrange("(o d) -> o d", o=1), ln1b_bc),
            ], "l1")
        v_sb = ph.enter_context(nc.sbuf_tensor("vaug1", [128, 32 * 1040], BF16))
        cbias = ph.enter_context(nc.sbuf_tensor("cbias1", [128, 16], F32))
        with contextlib.ExitStack() as phx:
            xT = phx.enter_context(nc.sbuf_tensor("xT1", [128, 8 * T], BF16))
            with k.scope():
                emit_ln_transpose(k, x1_in[:, :], xT, "l1",
                                  ln_scale_bc=ln1s_bc[:, :],
                                  ln_shift_bc=ln1b_bc[:, :])

            wq_sb = phx.enter_context(nc.sbuf_tensor("wq_sb", [128, 8 * D], BF16))
            wk_sb = phx.enter_context(nc.sbuf_tensor("wk_sb", [128, 8 * D], BF16))
            wv_sb = phx.enter_context(nc.sbuf_tensor("wv_sb", [128, 8 * D], BF16))
            s_wl = k.sem("g_wld")
            with nc.Block() as blk:
                @blk.gpsimd
                def _(g_raw):
                    g = k.eng(g_raw, "gpsimd")
                    for kt in range(8):
                        g.dma(wq_sb[:, kt * D:(kt + 1) * D],
                              wq_in[kt * 128:(kt + 1) * 128, :], s_wl)
                        g.dma(wk_sb[:, kt * D:(kt + 1) * D],
                              wk_in[kt * 128:(kt + 1) * 128, :], s_wl)
                        g.dma(wv_sb[:, kt * D:(kt + 1) * D],
                              wv_in[kt * 128:(kt + 1) * 128, :], s_wl)
                    g_raw.wait_ge(s_wl.h, s_wl.v)

            def lhsT_qk(kind, kt, mb):
                w = wq_sb if kind == "q" else wk_sb
                return w[:, kt * D + mb * 128:kt * D + mb * 128 + 128]

            def rhs_v(kt, nb):
                return wv_sb[:, kt * D + nb * 512:kt * D + nb * 512 + 512]

            with k.scope():
                emit_qkv(k, xT, lhsT_qk, rhs_v, qt_hbm, kt_hbm, v_sb, "a1")
        with k.scope():
            emit_att_bounds(k, qt_hbm, kt_hbm, cbias, "a1")
        with k.scope():
            emit_attention(k, qt_hbm, kt_hbm, v_sb, cbias, ctxT_hbm, "a1")

    with contextlib.ExitStack() as ph3:
        wo_sb = ph3.enter_context(nc.sbuf_tensor("wo_sb", [128, 8 * D], BF16))
        bo_bc = ph3.enter_context(nc.sbuf_tensor("bo_bc", [128, 1024], F32))
        with k.scope():
            bcast_rows(k, [(bo_in.rearrange("(o d) -> o d", o=1), bo_bc)], "bo")
        s_wl2 = k.sem("g_wld")
        with nc.Block() as blk:
            @blk.gpsimd
            def _(g_raw):
                g = k.eng(g_raw, "gpsimd")
                for kt in range(8):
                    g.dma(wo_sb[:, kt * D:(kt + 1) * D],
                          wo_in[kt * 128:(kt + 1) * 128, :], s_wl2)
                g_raw.wait_ge(s_wl2.h, s_wl2.v)

        def rhs_o(kt, nb):
            return wo_sb[:, kt * D + nb * 512:kt * D + nb * 512 + 512]

        with k.scope():
            emit_proj_residual_ln(k, ctxT_hbm, rhs_o, u_hbm, "a1",
                                  resid_hbm=x1_in[:, :], bias_bc=bo_bc[:, :],
                                  do_ln=False)

    with contextlib.ExitStack() as ph4:
        ln2s_bc = ph4.enter_context(nc.sbuf_tensor("ln2s_bc", [128, 1024], F32))
        ln2b_bc = ph4.enter_context(nc.sbuf_tensor("ln2b_bc", [128, 1024], F32))
        with k.scope():
            bcast_rows(k, [
                (l2s_in.rearrange("(o d) -> o d", o=1), ln2s_bc),
                (l2b_in.rearrange("(o d) -> o d", o=1), ln2b_bc),
            ], "l2")
        xT2 = ph4.enter_context(nc.sbuf_tensor("xT2", [128, 8 * T], BF16))
        gw_w_sb = ph4.enter_context(nc.sbuf_tensor("gw_w_sb", [128, 8 * 8], F32))
        gw_sb = ph4.enter_context(nc.sbuf_tensor("gw_sb", [128, 8 * 32], F32))
        s_gl = k.sem("s_gld")
        with nc.Block() as blk:
            @blk.sync
            def _(s_raw):
                s = k.eng(s_raw, "sync")
                gv = gw_w_sb.rearrange("p (kt e) -> p kt e", e=8)
                gs = gw_in.rearrange("(kt p) e -> p kt e", p=128)
                s.dma(gv[:, 0:4, :], gs[:, 0:4, :], s_gl)
                s.dma(gv[:, 4:8, :], gs[:, 4:8, :], s_gl)
                s_raw.wait_ge(s_gl.h, s_gl.v)
        with k.scope():
            emit_ln_transpose(k, u_hbm[:, :], xT2, "l2",
                              ln_scale_bc=ln2s_bc[:, :], ln_shift_bc=ln2b_bc[:, :],
                              gate=(gw_w_sb, gw_sb))
        with k.scope():
            emit_moe(k, xT2, gw_sb, w1_in, w2_in, w3_in, acc_hbm, "moe")

    with k.scope():
        emit_add3_ln(k, x1_in[:, :], u_hbm[:, :], acc_hbm[:, :], x2_out, "f")


def build_l0ffn():
    k = K("l0ffn")
    x1_in = k.dram("x1_in", [T, D], F32, kind="ExternalInput")
    wq_in = k.dram("wq_in", [D, D], F32, kind="ExternalInput")
    wk_in = k.dram("wk_in", [D, D], F32, kind="ExternalInput")
    wv_in = k.dram("wv_in", [D, D], F32, kind="ExternalInput")
    wo_in = k.dram("wo_in", [D, D], F32, kind="ExternalInput")
    bo_in = k.dram("bo_in", [D], F32, kind="ExternalInput")
    l1s_in = k.dram("ln1s_in", [D], F32, kind="ExternalInput")
    l1b_in = k.dram("ln1b_in", [D], F32, kind="ExternalInput")
    l2s_in = k.dram("ln2s_in", [D], F32, kind="ExternalInput")
    l2b_in = k.dram("ln2b_in", [D], F32, kind="ExternalInput")
    gw_in = k.dram("gatew_in", [D, 8], F32, kind="ExternalInput")
    w1_in = k.dram("moew1_in", [8, 1024, 2048], F32, kind="ExternalInput")
    w2_in = k.dram("moew2_in", [8, 1024, 2048], F32, kind="ExternalInput")
    w3_in = k.dram("moew3_in", [8, 2048, 1024], F32, kind="ExternalInput")
    x2_out = k.dram("x2_out", [T, D], F32, kind="ExternalOutput")
    declare_and_load_consts(k)
    body_l0ffn(k, x1_in, wq_in, wk_in, wv_in, wo_in, bo_in, l1s_in, l1b_in,
               l2s_in, l2b_in, gw_in, w1_in, w2_in, w3_in, x2_out)
    return k


def run_l0ffn(x1_full, wq, wk, wv, wo, bo, l1s, l1b, l2s, l2b, gw, w1, w2, w3):
    k = build_l0ffn()
    consts = host_consts()
    in_maps = []
    for c in range(NCORES):
        m = dict(consts)
        m["x1_in"] = np.ascontiguousarray(x1_full[c])
        m.update(wq_in=wq, wk_in=wk, wv_in=wv, wo_in=wo, bo_in=bo,
                 ln1s_in=l1s, ln1b_in=l1b, ln2s_in=l2s, ln2b_in=l2b,
                 gatew_in=gw, moew1_in=w1, moew2_in=w2, moew3_in=w3)
        in_maps.append(m)
    res = run_bass_kernel_spmd(k.nc, in_maps, core_ids=list(range(NCORES)))
    k.close()
    return np.stack([res.results[c]["x2_out"] for c in range(NCORES)])



def build_full():
    k = K("full")

    def fin(name, shape):
        return k.dram(name, shape, F32, kind="ExternalInput")

    x_in = fin("L0_attn_X", [T, D])
    a_q = fin("L0_attn_q_proj", [D, D])
    a_k = fin("L0_attn_k_proj", [D, D])
    a_v = fin("L0_attn_v_proj", [D, D])
    a_o = fin("L0_attn_o_proj", [D, D])
    f_l1s = fin("L0_ffn_ln1_scale", [D])
    f_l1b = fin("L0_ffn_ln1_shift", [D])
    f_wq = fin("L0_ffn_wq", [D, D])
    f_wk = fin("L0_ffn_wk", [D, D])
    f_wv = fin("L0_ffn_wv", [D, D])
    f_wo = fin("L0_ffn_wo", [D, D])
    f_bo = fin("L0_ffn_bo", [D])
    f_l2s = fin("L0_ffn_ln2_scale", [D])
    f_l2b = fin("L0_ffn_ln2_shift", [D])
    f_gw = fin("L0_ffn_gate_w", [D, 8])
    f_w1 = fin("L0_ffn_w1", [8, 1024, 2048])
    f_w2 = fin("L0_ffn_w2", [8, 1024, 2048])
    f_w3 = fin("L0_ffn_w3", [8, 2048, 1024])
    m_win = fin("L1_attn_W_in", [4096, 1024])
    m_cw = fin("L1_attn_conv_w", [2048, 4])
    m_cb = fin("L1_attn_conv_b", [2048])
    m_wx = fin("L1_attn_W_xproj", [96, 2048])
    m_wdt = fin("L1_attn_W_dt", [2048, 64])
    m_bdt = fin("L1_attn_b_dt", [2048])
    m_alog = fin("L1_attn_A_log", [2048, 16])
    m_dp = fin("L1_attn_Dp", [2048])
    m_wout = fin("L1_attn_W_out", [1024, 2048])
    t_w1 = fin("L1_ffn_w1_w", [4096, 1024])
    t_s1 = fin("L1_ffn_w1_s", [4096])
    t_w2 = fin("L1_ffn_w2_w", [4096, 1024])
    t_s2 = fin("L1_ffn_w2_s", [4096])
    t_h1 = fin("L1_ffn_h1_w", [4096, 4096])
    t_sh1 = fin("L1_ffn_h1_s", [4096])
    t_h2 = fin("L1_ffn_h2_w", [4096, 4096])
    t_sh2 = fin("L1_ffn_h2_s", [4096])
    t_w3 = fin("L1_ffn_w3_w", [1024, 4096])
    t_s3 = fin("L1_ffn_w3_s", [1024])
    sel_in = k.dram("c_sel_bf", [32, 32 * 128], BF16, kind="ExternalInput")
    y_out = k.dram("y_out", [T, D], F32, kind="ExternalOutput")

    x1 = k.dram("scr_x1", [T, D], F32)
    x2 = k.dram("scr_x2", [T, D], F32)
    x3 = k.dram("scr_x3", [T, D], F32)

    declare_and_load_consts(k)
    pfx = "lf"
    w1t_hbm = k.dram(f"{pfx}_scr_w1t", [32, 128, 1024], BF16)
    w2t_hbm = k.dram(f"{pfx}_scr_w2t", [32, 128, 1024], BF16)
    h1t_hbm = k.dram(f"{pfx}_scr_h1t", [32, 128, 4096], BF16)
    h2t_hbm = k.dram(f"{pfx}_scr_h2t", [32, 128, 4096], BF16)
    w3t_hbm = k.dram(f"{pfx}_scr_w3t", [8, 128, 4096], BF16)
    wprep = [
        (t_w1[:, :], w1t_hbm, 4096, 1024),
        (t_w2[:, :], w2t_hbm, 4096, 1024),
        (t_h1[:, :], h1t_hbm, 4096, 4096),
        (t_h2[:, :], h2t_hbm, 4096, 4096),
        (t_w3[:, :], w3t_hbm, 1024, 4096),
    ]
    body_l0attn(k, x_in, a_q, a_k, a_v, a_o, x1)
    body_l0ffn(k, x1, f_wq, f_wk, f_wv, f_wo, f_bo, f_l1s, f_l1b, f_l2s,
               f_l2b, f_gw, f_w1, f_w2, f_w3, x2)
    body_mamba(k, x2, m_win, m_cw, m_cb, m_wx, m_wdt, m_bdt, m_alog, m_dp,
               m_wout, sel_in, x3, wprep=wprep)
    body_l1ffn(k, x3, t_w1, t_w2, t_h1, t_h2, t_w3, t_s1, t_s2, t_sh1, t_sh2,
               t_s3, y_out,
               pre_w=(w1t_hbm, w2t_hbm, h1t_hbm, h2t_hbm, w3t_hbm))
    return k


def kernel(**inputs):
    g = {n: _f32(v) for n, v in inputs.items()}
    k = build_full()
    consts = host_consts()
    consts["c_sel_bf"] = host_sel_const()
    in_maps = []
    for c in range(NCORES):
        m = dict(consts)
        for n, v in g.items():
            if n == "L0_attn_X":
                m[n] = np.ascontiguousarray(v[c])
            else:
                m[n] = v
        in_maps.append(m)
    res = run_bass_kernel_spmd(k.nc, in_maps, core_ids=list(range(NCORES)))
    k.close()
    return np.stack([res.results[c]["y_out"] for c in range(NCORES)])


def kernel_unfused(**inputs):
    g = {n: _f32(v) for n, v in inputs.items()}
    x1 = run_l0attn(g["L0_attn_X"], g["L0_attn_q_proj"], g["L0_attn_k_proj"],
                    g["L0_attn_v_proj"], g["L0_attn_o_proj"])
    x2 = run_l0ffn(x1, g["L0_ffn_wq"], g["L0_ffn_wk"], g["L0_ffn_wv"],
                   g["L0_ffn_wo"], g["L0_ffn_bo"], g["L0_ffn_ln1_scale"],
                   g["L0_ffn_ln1_shift"], g["L0_ffn_ln2_scale"],
                   g["L0_ffn_ln2_shift"], g["L0_ffn_gate_w"], g["L0_ffn_w1"],
                   g["L0_ffn_w2"], g["L0_ffn_w3"])
    x3 = run_mamba(x2, g["L1_attn_W_in"], g["L1_attn_conv_w"],
                   g["L1_attn_conv_b"], g["L1_attn_W_xproj"], g["L1_attn_W_dt"],
                   g["L1_attn_b_dt"], g["L1_attn_A_log"], g["L1_attn_Dp"],
                   g["L1_attn_W_out"])
    y = run_l1ffn(x3, g["L1_ffn_w1_w"], g["L1_ffn_w1_s"], g["L1_ffn_w2_w"],
                  g["L1_ffn_w2_s"], g["L1_ffn_h1_w"], g["L1_ffn_h1_s"],
                  g["L1_ffn_h2_w"], g["L1_ffn_h2_s"], g["L1_ffn_w3_w"],
                  g["L1_ffn_w3_s"])
    return y
